# version 02
# baseline (speedup 1.0000x reference)
import contextlib
import numpy as np
import concourse.bass as bass
import concourse.mybir as mybir
from concourse.bass_utils import run_bass_kernel_spmd

F32 = mybir.dt.float32
BF16 = mybir.dt.bfloat16
AF = mybir.ActivationFunctionType
ALU = mybir.AluOpType
AX = mybir.AxisListType

NCORE = 8
T = 4096
D = 1024
NDT = D // 128
TCH = 512
NCH = T // TCH



class Prog:

    ENG = ("tensor", "vector", "scalar", "gpsimd", "sync")
    POOL = 16
    PARANOID = False

    def __init__(self):
        self.tasks = {e: [] for e in self.ENG}
        self.writer = {}
        self.readers = {}
        self.floor = {e: 0 for e in self.ENG}
        self.dma_count = {e: 0 for e in self.ENG}
        self.dma_ord = {}

    def task(self, engine, emit, reads=(), writes=(), deps=(), dma=False,
             fence=False):
        my_idx = len(self.tasks[engine])
        dd = {}

        def add(pe, pidx):
            p_dma = self.tasks[pe][pidx][3]
            p_fence = self.tasks[pe][pidx][4]
            if (pe == engine and not p_dma and not dma and not p_fence
                    and not Prog.PARANOID):
                return
            cur = dd.get(pe, -1)
            if pidx > cur and not p_dma:
                dd[pe] = pidx
            elif p_dma:
                dd.setdefault(("D", pe), set()).add(pidx) if False else None
                key = ("D", pe, self.dma_ord[(pe, pidx)] % Prog.POOL)
                cur2 = dd.get(key, -1)
                if pidx > cur2:
                    dd[key] = pidx

        for e, i in deps:
            add(e, i)
        for b in reads:
            w = self.writer.get(b)
            if w:
                add(w[0], w[1])
            if b.startswith("PS"):
                for r in self.readers.get(b, ()):
                    add(r[0], r[1])
        for b in writes:
            w = self.writer.get(b)
            if w:
                add(w[0], w[1])
            for r in self.readers.get(b, ()):
                add(r[0], r[1])
        for e in self.ENG:
            if e != engine and self.floor[e] > 0:
                add(e, self.floor[e] - 1)
        self.tasks[engine].append([emit, dd, 16 if dma else 1, dma, fence])
        if dma:
            self.dma_ord[(engine, my_idx)] = self.dma_count[engine]
            self.dma_count[engine] += 1
        for b in reads:
            self.readers.setdefault(b, []).append((engine, my_idx, dma))
        for b in writes:
            self.writer[b] = (engine, my_idx, dma)
            self.readers[b] = []
        return (engine, my_idx)

    def barrier(self):
        for e in self.ENG:
            self.floor[e] = len(self.tasks[e])

    def emit(self, nc, block, sems, dpools):
        cum = {}
        for e in self.ENG:
            c, lst = 0, []
            for t in self.tasks[e]:
                c += 0 if t[3] else 1
                lst.append(c)
            cum[e] = lst

        def mk(e):
            def body(eng):
                waited = {}
                for idx, (emit_fn, dd, wgt, is_dma, _f) in enumerate(self.tasks[e]):
                    for key in sorted(dd, key=str):
                        pidx = dd[key]
                        if isinstance(key, tuple):
                            _, pe, slot = key
                            o = self.dma_ord[(pe, pidx)]
                            sem = dpools[pe][o % Prog.POOL]
                            val = 16 * (o // Prog.POOL + 1)
                        else:
                            pe = key
                            if self.tasks[pe][pidx][3]:
                                continue
                            sem = sems[pe]
                            val = cum[pe][pidx]
                        wk = id(sem)
                        if waited.get(wk, -1) < val:
                            eng.wait_ge(sem, val)
                            waited[wk] = val
                    if is_dma:
                        o = self.dma_ord[(e, idx)]
                        sem = dpools[e][o % Prog.POOL]
                        if o >= Prog.POOL:
                            gen = o // Prog.POOL
                            wk = id(sem)
                            if waited.get(wk, -1) < 16 * gen:
                                eng.wait_ge(sem, 16 * gen)
                                waited[wk] = 16 * gen
                        ins = emit_fn(eng)
                        assert ins is not None
                        ins.then_inc(sem, 16)
                    else:
                        ins = emit_fn(eng)
                        assert ins is not None
                        ins.then_inc(sems[e], wgt)
            return body

        for e in self.ENG:
            getattr(block, e)(mk(e))

    def final_waits(self):
        return [(e, len(self.tasks[e]) - 1) for e in self.ENG if self.tasks[e]]


class BB:
    def __init__(self, name):
        self.nc = bass.Bass("TRN2", target_bir_lowering=False)
        self.name = name
        self.es = contextlib.ExitStack()
        self.p = Prog()
        self.consts = {}
        self._cap = {}
        self._sb = {}

    def dram_in(self, name, shape, dtype=F32):
        name = getattr(self, "_pfx", "") + name
        return self.nc.dram_tensor(name, list(shape), dtype, kind="ExternalInput").ap()

    def dram_out(self, name, shape, dtype=F32):
        name = getattr(self, "_pfx", "") + name
        return self.nc.dram_tensor(name, list(shape), dtype, kind="ExternalOutput").ap()

    def dram_tmp(self, name, shape, dtype=F32):
        name = getattr(self, "_pfx", "") + name
        return self.nc.dram_tensor(name, list(shape), dtype, kind="Internal").ap()

    def finalize(self):
        nc = self.nc
        with contextlib.ExitStack() as es2:
            sems = {}
            for e in Prog.ENG:
                sems[e] = es2.enter_context(nc.semaphore(f"s_{e}"))
            dpools = {}
            for e in Prog.ENG:
                if self.p.dma_count[e] > 0:
                    n = min(Prog.POOL, max(1, self.p.dma_count[e]))
                    n = Prog.POOL
                    dpools[e] = [es2.enter_context(nc.semaphore(f"d_{e}{i}"))
                                 for i in range(n)]
            block = es2.enter_context(nc.Block())
            fw = self.p.final_waits()
            self.p.task("sync", lambda e: e.nop(), deps=fw)
            self.p.emit(nc, block, sems, dpools)
        self.es.close()
        return nc


_SB_GLOBALS = {"ident_f32", "ident_bf", "ones1x128", "ones128x1", "ones1x128bf",
               "ones1x512bf", "kbuf", "ARENA", "SMALLS"}


def _bb_sbuf(self, name, shape, dtype=F32):
    if name in self._sb:
        return self._sb[name]
    if name in _SB_GLOBALS:
        t = self.es.enter_context(self.nc.sbuf_tensor(name, list(shape), dtype))
        ap = t[:]
        self._sb[name] = ap
        return ap
    if not hasattr(self, "_smbase"):
        self._smbase = self.sbuf("SMALLS", [128, 24 * 1024], mybir.dt.uint8)
        self._smoff = 0
        self._smnames = set()
    esz = {F32: 4, BF16: 2}[dtype]
    nb = int(np.prod(shape[1:])) * esz
    nb = (nb + 127) // 128 * 128
    off = self._smoff
    self._smoff += nb
    assert self._smoff <= 24 * 1024, f"SMALLS overflow at {name}"
    v = self._smbase[:, off:off + nb].bitcast(dtype)
    v = v[:, 0:int(np.prod(shape[1:]))]
    if len(shape) == 3:
        v = v.rearrange("p (a b) -> p a b", b=shape[2])
    if shape[0] != 128:
        v = v[0:shape[0]]
    self._sb[name] = v
    self._smnames.add(name)
    return v


def _bb_sm_reset(self):
    if hasattr(self, "_smbase"):
        for n in self._smnames:
            self._sb.pop(n, None)
        self._smnames = set()
        self._smoff = 0


BB.sm_reset = _bb_sm_reset


def _bb_psum(self, name, shape, dtype=F32):
    if name in self._sb:
        return self._sb[name]
    t = self.es.enter_context(self.nc.psum_tensor(name, list(shape), dtype))
    ap = t[:]
    self._sb[name] = ap
    return ap


def _bb_const(self, name, arr):
    if name not in self.consts:
        self.consts[name] = arr
        import ml_dtypes
        dt = BF16 if arr.dtype == ml_dtypes.bfloat16 else F32
        self._cap[name] = self.nc.dram_tensor(name, list(arr.shape), dt,
                                              kind="ExternalInput").ap()
    return self._cap[name]


BB.sbuf = _bb_sbuf
BB.psum = _bb_psum
BB.const = _bb_const


def _mk(name):
    def f(self, *a, **kw):
        return getattr(self, "_" + name)(*a, **kw)
    return f


def _bb_dma(self, q, out, in_, reads=(), writes=(), deps=(), accum=False):
    if accum:
        return self.p.task(
            q, lambda e: e.dma_start(out=out, in_=in_, accum_op=ALU.add),
            reads=reads, writes=writes, deps=deps, dma=True)
    return self.p.task(q, lambda e: e.dma_start(out=out, in_=in_),
                       reads=reads, writes=writes, deps=deps, dma=True)


def _bb_mm(self, out, lhsT, rhs, start=True, stop=True, reads=(), writes=(), deps=()):
    return self.p.task(
        "tensor", lambda e: e.matmul(out, lhsT, rhs, start=start, stop=stop),
        reads=reads, writes=writes, deps=deps)


def _bb_mm_acc(self, pairs, reads=(), writes=(), deps=()):
    def emit(e):
        n = len(pairs)
        ins = None
        for i, (o, l, r) in enumerate(pairs):
            ins = e.matmul(o, l, r, start=(i == 0), stop=(i == n - 1))
        return ins
    return self.p.task("tensor", emit, reads=reads, writes=writes, deps=deps)


def _bb_mm_multi(self, triples, start, stop, reads=(), writes=(), deps=()):
    def emit(e):
        ins = None
        for (o, l, r) in triples:
            ins = e.matmul(o, l, r, start=start, stop=stop)
        return ins
    return self.p.task("tensor", emit, reads=reads, writes=writes, deps=deps)


def _bb_act(self, out, in_, func, bias=None, scale=None, accum_out=None,
            reads=(), writes=(), deps=()):
    kw = {}
    if bias is not None:
        kw["bias"] = bias
    if scale is not None:
        kw["scale"] = scale
    if accum_out is not None:
        kw["accum_out"] = accum_out
    return self.p.task("scalar", lambda e: e.activation(out, in_, func, **kw),
                       reads=reads, writes=writes, deps=deps)


def _bb_tt(self, out, a, b, op, reads=(), writes=(), deps=(), eng="vector"):
    return self.p.task(eng, lambda e: e.tensor_tensor(out, a, b, op=op),
                       reads=reads, writes=writes, deps=deps)


def _bb_ts(self, out, a, s1, s2=None, op0=ALU.mult, op1=None, reads=(), writes=(),
           deps=(), eng="vector"):
    if op1 is None:
        return self.p.task(eng, lambda e: e.tensor_scalar(out, a, s1, None, op0=op0),
                           reads=reads, writes=writes, deps=deps)
    return self.p.task(
        eng, lambda e: e.tensor_scalar(out, a, s1, s2, op0=op0, op1=op1),
        reads=reads, writes=writes, deps=deps)


def _bb_stt(self, out, in0, scalar, in1, op0, op1, reads=(), writes=(), deps=(),
            eng="vector"):
    return self.p.task(
        eng,
        lambda e: e.scalar_tensor_tensor(out, in0, scalar, in1, op0=op0, op1=op1),
        reads=reads, writes=writes, deps=deps)


def _bb_copy(self, out, in_, reads=(), writes=(), deps=(), eng="scalar"):
    if eng == "scalar":
        return self.p.task(eng, lambda e: e.copy(out, in_),
                           reads=reads, writes=writes, deps=deps)
    return self.p.task(eng, lambda e: e.tensor_copy(out, in_),
                       reads=reads, writes=writes, deps=deps)


def _bb_recip(self, out, in_, reads=(), writes=(), deps=()):
    return self.p.task("vector", lambda e: e.reciprocal(out, in_),
                       reads=reads, writes=writes, deps=deps, fence=True)


def _bb_transpose(self, out, in_, ident, reads=(), writes=(), deps=()):
    return self.p.task("tensor", lambda e: e.transpose(out, in_, ident),
                       reads=reads, writes=writes, deps=deps)


def _bb_memset(self, ap, val, reads=(), writes=(), deps=(), eng="vector"):
    return self.p.task(eng, lambda e: e.memset(ap, val),
                       reads=reads, writes=writes, deps=deps, fence=True)


def _bb_reduce(self, out, in_, op, reads=(), writes=(), deps=(), negate=None):
    return self.p.task(
        "vector", lambda e: e.tensor_reduce(out, in_, axis=AX.X, op=op, negate=negate),
        reads=reads, writes=writes, deps=deps)


def _bb_scan(self, out, a, b, init, reads=(), writes=(), deps=()):
    return self.p.task(
        "vector", lambda e: e.tensor_tensor_scan(out, a, b, init, ALU.mult, ALU.add),
        reads=reads, writes=writes, deps=deps)


for _n in ["dma", "mm", "mm_acc", "mm_multi", "act", "tt", "ts", "stt", "copy",
           "recip", "transpose", "memset", "reduce", "scan"]:
    setattr(BB, _n, globals()["_bb_" + _n])

class Arena:

    def __init__(self, bb, bytes_pp, name="ARENA"):
        self.bb = bb
        self.base = bb.sbuf(name, [128, bytes_pp], mybir.dt.uint8)
        self.front = 0
        self.back = bytes_pp
        self.hwm = 0

    def take(self, shape, dtype=F32, back=False):
        esz = 4 if dtype == F32 else 2
        n = int(np.prod(shape[1:])) * esz
        n = (n + 127) // 128 * 128
        if back:
            self.back -= n
            off = self.back
        else:
            off = self.front
            self.front += n
        assert self.front <= self.back, \
            f"arena overflow front={self.front} back={self.back}"
        self.hwm = max(self.hwm, self.front + (self.base.shape[1] - self.back))
        v = self.base[:, off:off + n].bitcast(dtype)
        if len(shape) == 2:
            v = v[:, 0:shape[1]] if v.shape[1] != shape[1] else v
        else:
            v = v[:, 0:int(np.prod(shape[1:]))]
            v = v.rearrange("p (a b) -> p a b", b=shape[2]) if len(shape) == 3 else v
        if shape[0] != 128:
            v = v[0:shape[0]]
        return v

    def reset_front(self):
        self.front = 0

    def reset_back(self):
        self.back = self.base.shape[1]


_I128_F32 = np.eye(128, dtype=np.float32)


def _bb_kcol(self, val, rows=128):
    if not hasattr(self, "_kvals"):
        self._kvals = {}
        self._kbuf = self.sbuf("kbuf", [128, 16], F32)
    if val not in self._kvals:
        i = len(self._kvals)
        assert i < 16
        self.memset(self._kbuf[:, i:i + 1], float(val), writes=("kbuf",))
        self._kvals[val] = i
    i = self._kvals[val]
    return self._kbuf[0:rows, i:i + 1]


BB.kcol = _bb_kcol


def setup_common(bb, ps7_bf16=False):
    if getattr(bb, "_setup_done", False):
        return
    bb._setup_done = True
    bb.ident_f32 = bb.sbuf("ident_f32", [128, 128], F32)
    bb.ones_1x128 = bb.sbuf("ones1x128", [1, 128], F32)
    bb.ones_128x1 = bb.sbuf("ones128x1", [128, 1], F32)
    bb.ones_1x128_bf = bb.sbuf("ones1x128bf", [1, 128], BF16)
    bb.PS = [bb.psum(f"PS{i}", [128, 512], F32) for i in range(7)]
    psbf = bb.psum("PS7", [128, 1024], BF16)
    bb.PSBF = psbf
    bb.PS.append(psbf.bitcast(F32))
    import ml_dtypes
    bb.ident_bf = bb.sbuf("ident_bf", [128, 128], BF16)
    bb.dma("gpsimd", bb.ident_bf,
           bb.const("c_ident_bf", _I128_F32.astype(ml_dtypes.bfloat16)),
           writes=("ident_bf",))
    bb.dma("gpsimd", bb.ident_f32, bb.const("c_ident_f32", _I128_F32),
           writes=("ident_f32",))
    bb.memset(bb.ones_1x128, 1.0, writes=("ones1x128",))
    bb.memset(bb.ones_128x1, 1.0, writes=("ones128x1",))
    bb.memset(bb.ones_1x128_bf, 1.0, writes=("ones1x128bf",))
    bb.ones_1x512_bf = bb.sbuf("ones1x512bf", [1, 512], BF16)
    bb.memset(bb.ones_1x512_bf, 1.0, writes=("ones1x512bf",))


def vec_to_cols(bb, dram_vec, length, tag):
    m = length // 128
    assert m * 128 == length and 1 <= m <= 128
    vt = bb.sbuf(f"{tag}_r", [m, 128], F32)
    cols = bb.sbuf(f"{tag}_c", [128, m], F32)
    ld = bb.dma("gpsimd", vt, dram_vec.rearrange("(m p) -> m p", p=128),
                writes=(f"{tag}_r",))
    tr = bb.transpose(bb.PS[0][0:128, 0:m], vt, bb.ident_f32[0:m, 0:m],
                      reads=(f"{tag}_r", "ident_f32"), writes=("PS0",), deps=[ld])
    bb.copy(cols, bb.PS[0][0:128, 0:m], reads=("PS0",), writes=(f"{tag}_c",), deps=[tr])
    return cols


def ln_chunked(bb, xck, xckbuf, outs, tag, ch, scratch, eps=1e-5, cw=TCH):
    j = ch % 2
    sq = scratch[f"sq{j}"][:, 0:cw]
    rows = scratch[f"rw{j}"][0:1]
    r_s1, r_s2 = rows[:, 0:cw], rows[:, cw:2 * cw]
    r_var, r_rstd = rows[:, 2 * cw:3 * cw], rows[:, 3 * cw:4 * cw]
    r_rstdf, r_tmp = rows[:, 4 * cw:5 * cw], rows[:, 5 * cw:6 * cw]
    B = f"{tag}{j}_"
    PS0, PS1, PS2 = bb.PS[0], bb.PS[1], bb.PS[2]
    for i in range(NDT):
        sqt = bb.act(sq, xck[:, i, :], AF.Square, reads=(xckbuf,), writes=(B + "sq",))
        bb.mm(PS0[0:1, 0:cw], bb.ones_128x1, xck[:, i, :], start=(i == 0),
              stop=(i == NDT - 1), reads=(xckbuf, "ones128x1"), writes=("PS0",))
        bb.mm(PS1[0:1, 0:cw], bb.ones_128x1, sq, start=(i == 0), stop=(i == NDT - 1),
              reads=(B + "sq", "ones128x1"), writes=("PS1",), deps=[sqt])
    bb.ts(r_s1, PS0[0:1, 0:cw], 1.0 / D, None, reads=("PS0",), writes=(B + "s1",))
    bb.ts(r_s2, PS1[0:1, 0:cw], 1.0 / D, None, reads=("PS1",), writes=(B + "s2",))
    bb.tt(r_tmp, r_s1, r_s1, ALU.mult, reads=(B + "s1",), writes=(B + "tmp",))
    bb.tt(r_var, r_s2, r_tmp, ALU.subtract, reads=(B + "s2", B + "tmp"),
          writes=(B + "var",))
    bb.act(r_rstd, r_var, AF.Sqrt, bias=bb.kcol(eps, rows=1),
           reads=(B + "var", "kbuf"), writes=(B + "rstd",))
    bb.recip(r_rstd, r_rstd, reads=(B + "rstd",), writes=(B + "rstd",))
    need_f = any(o["mode"] == "gbf" for o in outs)
    if need_f:
        bb.ts(r_tmp, r_var, eps, None, op0=ALU.add, reads=(B + "var",),
              writes=(B + "tmp",))
        bb.recip(r_tmp, r_tmp, reads=(B + "tmp",), writes=(B + "tmp",))
        bb.tt(r_tmp, r_tmp, r_var, ALU.mult, reads=(B + "tmp", B + "var"),
              writes=(B + "tmp",))
        bb.act(r_tmp, r_tmp, AF.Sqrt, bias=bb.kcol(eps, rows=1),
               reads=(B + "tmp", "kbuf"), writes=(B + "tmp",))
        bb.recip(r_tmp, r_tmp, reads=(B + "tmp",), writes=(B + "tmp",))
        bb.tt(r_rstdf, r_rstd, r_tmp, ALU.mult, reads=(B + "rstd", B + "tmp"),
              writes=(B + "rstdf",))
    bm = bb.mm(PS0[:, 0:cw], bb.ones_1x128, r_s1, reads=(B + "s1", "ones1x128"),
               writes=("PS0",))
    br = bb.mm(PS1[:, 0:cw], bb.ones_1x128, r_rstd, reads=(B + "rstd", "ones1x128"),
               writes=("PS1",))
    brf = None
    if need_f:
        brf = bb.mm(PS2[:, 0:cw], bb.ones_1x128, r_rstdf,
                    reads=(B + "rstdf", "ones1x128"), writes=("PS2",))
    xm = scratch[f"xm{j}"][:, 0:cw]
    tmpa = scratch[f"ta{j}"][:, 0:cw]
    for i in range(NDT):
        bb.tt(xm, xck[:, i, :], PS0[:, 0:cw], ALU.subtract,
              reads=(xckbuf, "PS0"), writes=(B + "xm",), deps=[bm])
        for o in outs:
            dst, dstbuf = o["sink"](i)
            if o["mode"] == "plain":
                bb.tt(dst, xm, PS1[:, 0:cw], ALU.mult, reads=(B + "xm", "PS1"),
                      writes=(dstbuf,), deps=[br])
            else:
                psx, bwait, psname = (PS1, br, "PS1") if o["mode"] == "gb" else (PS2, brf, "PS2")
                bb.tt(tmpa, xm, psx[:, 0:cw], ALU.mult, reads=(B + "xm", psname),
                      writes=(B + "ta",), deps=[bwait])
                bb.ts(dst, tmpa, o["gamma"][:, i:i + 1], o["beta"][:, i:i + 1],
                      op0=ALU.mult, op1=ALU.add,
                      reads=(B + "ta",) + tuple(o.get("gbufs", ())), writes=(dstbuf,))
            if o.get("post"):
                o["post"](i, dst, dstbuf)



def _ln_scratch(bb, AR, cw=TCH):
    return {
        "sq0": AR.take([128, cw]), "sq1": AR.take([128, cw]),
        "rw0": AR.take([128, 6 * cw]), "rw1": AR.take([128, 6 * cw]),
        "xm0": AR.take([128, cw]), "xm1": AR.take([128, cw]),
        "ta0": AR.take([128, cw]), "ta1": AR.take([128, cw]),
    }


def build_block1(bb=None, xT=None, yT=None):
    if bb is None:
        bb = BB("b1")
        xT = bb.dram_in("xT", [D, T])
        yT = bb.dram_out("yT", [D, T])
    HID = 2048
    w = {}
    for nm, shp in [("ln_g", [D]), ("ln_b", [D]), ("Wh", [D, 2 * HID]),
                    ("bh", [2 * HID]), ("Wqk", [D, 128]), ("bqk", [128]),
                    ("gam", [2, 128]), ("bet", [2, 128]), ("Wo", [HID, D]),
                    ("bo", [D]), ("W1", [D, 4096]), ("b1", [4096]),
                    ("W2", [4096, D]), ("b2", [D])]:
        w[nm] = bb.dram_in("w_" + nm, shp)
    xn1_s = bb.dram_tmp("xn1_s", [D, T], BF16)
    nrm_s = bb.dram_tmp("nrm_s", [D, T], BF16)
    gate_s = bb.dram_tmp("gate_s", [HID, T], BF16)

    setup_common(bb)
    AR = Arena(bb, 176 * 1024)
    mask_sb = bb.sbuf("mask_tri", [128, 128], F32)
    bb.dma("gpsimd", mask_sb,
           bb.const("c_mask_tri", np.triu(np.ones((128, 128), np.float32))),
           writes=("mask_tri",))
    g_cols = vec_to_cols(bb, w["ln_g"], D, "lng")
    b_cols = vec_to_cols(bb, w["ln_b"], D, "lnb")
    b1c = vec_to_cols(bb, w["b1"], 4096, "b1")
    b2c = vec_to_cols(bb, w["b2"], D, "b2")
    boc = vec_to_cols(bb, w["bo"], D, "bo")
    bhc = vec_to_cols(bb, w["bh"], 2 * HID, "bh")
    bqkc = vec_to_cols(bb, w["bqk"], 128, "bqk")
    gamc = bb.sbuf("gamc", [128, 2], F32)
    betc = bb.sbuf("betc", [128, 2], F32)
    gb_r = bb.sbuf("gb_r", [2, 128], F32)
    for (nm, dst, dbuf) in [("gam", gamc, "gamc"), ("bet", betc, "betc")]:
        ldt = bb.dma("sync", gb_r, w[nm], writes=("gb_r",))
        trt = bb.transpose(bb.PS[0][0:128, 0:2], gb_r, bb.ident_f32[0:2, 0:2],
                           reads=("gb_r", "ident_f32"), writes=("PS0",), deps=[ldt])
        bb.copy(dst, bb.PS[0][0:128, 0:2], reads=("PS0",), writes=(dbuf,), deps=[trt])
    bb.p.barrier()

    xck = [AR.take([128, NDT, TCH]) for _ in range(2)]
    ob_xn = [AR.take([128, TCH], BF16) for _ in range(2)]
    ob_nr = [AR.take([128, TCH], BF16) for _ in range(2)]
    lns = _ln_scratch(bb, AR)
    for ch in range(NCH):
        j = ch % 2
        sl = slice(ch * TCH, (ch + 1) * TCH)
        bb.dma("sync", xck[j], xT[:, sl].rearrange("(i p) c -> p i c", p=128),
               writes=(f"xck{j}",))

        def post(dram, dname, q, sl=sl):
            def f(i, ap, buf):
                bb.dma(q, dram[i * 128:(i + 1) * 128, sl], ap, reads=(buf,),
                       writes=(f"{dname}_{i}_{sl.start}",))
            return f

        outs = [
            {"sink": lambda i, j=j: (ob_xn[j], f"oxn{j}"), "mode": "plain",
             "post": post(xn1_s, "xn1_s", "scalar")},
            {"sink": lambda i, j=j: (ob_nr[j], f"onr{j}"), "mode": "gbf",
             "gamma": g_cols, "beta": b_cols, "gbufs": ("lng_c", "lnb_c"),
             "post": post(nrm_s, "nrm_s", "sync")},
        ]
        ln_chunked(bb, xck[j], f"xck{j}", outs, "ln1", ch, lns)
    bb.p.barrier()
    AR.reset_front()

    W1_sb = AR.take([128, NDT * 4096], BF16)
    h_sb = AR.take([128, 32 * TCH], BF16)
    xnck = [AR.take([128, NDT, TCH], BF16) for _ in range(2)]
    w2cb = [AR.take([128, 32, 128], BF16) for _ in range(2)]
    xres = [AR.take([128, TCH]) for _ in range(2)]
    ych = [AR.take([128, TCH]) for _ in range(2)]
    for k in range(NDT):
        bb.dma("gpsimd", W1_sb[:, k * 4096:(k + 1) * 4096],
               w["W1"][k * 128:(k + 1) * 128, :], writes=(f"W1_{k}",))
    ps_h = [bb.PS[3], bb.PS[4]]
    ps_o = [bb.PS[5], bb.PS[6]]
    for ch in range(NCH):
        j = ch % 2
        sl = slice(ch * TCH, (ch + 1) * TCH)
        bb.dma("sync", xnck[j], xn1_s[:, sl].rearrange("(i p) c -> p i c", p=128),
               reads=tuple(f"xn1_s_{i}_{sl.start}" for i in range(NDT)),
               writes=(f"xnck{j}",))
        for m in range(32):
            jm = m % 2
            mmt = bb.mm_acc(
                [(ps_h[jm][:, :],
                  W1_sb[:, k * 4096 + m * 128: k * 4096 + (m + 1) * 128],
                  xnck[j][:, k, :]) for k in range(NDT)],
                reads=tuple(f"W1_{k}" for k in range(NDT)) + (f"xnck{j}",),
                writes=(f"PS{3 + jm}",))
            bb.act(h_sb[:, m * TCH:(m + 1) * TCH], ps_h[jm][:, :], AF.Relu,
                   bias=b1c[:, m:m + 1], reads=(f"PS{3 + jm}", "b1_c"),
                   writes=(f"h{m}",), deps=[mmt])
        for do in range(NDT):
            jo = do % 2
            bb.dma("gpsimd", w2cb[jo],
                   w["W2"][:, do * 128:(do + 1) * 128].rearrange(
                       "(k p) c -> p k c", p=128),
                   writes=(f"w2cb{jo}",))
            bb.dma("sync", xres[jo], xT[do * 128:(do + 1) * 128, sl],
                   writes=(f"xres{jo}",))
            mmo = bb.mm_acc(
                [(ps_o[jo][:, :], w2cb[jo][:, k, :],
                  h_sb[:, k * TCH:(k + 1) * TCH]) for k in range(32)],
                reads=(f"w2cb{jo}",) + tuple(f"h{k}" for k in range(32)),
                writes=(f"PS{5 + jo}",))
            bb.stt(ych[jo], ps_o[jo][:, :], b2c[:, do:do + 1], xres[jo],
                   op0=ALU.add, op1=ALU.add,
                   reads=(f"PS{5 + jo}", "b2_c", f"xres{jo}"),
                   writes=(f"ych{jo}",), deps=[mmo])
            bb.tt(ych[jo], ych[jo], xnck[j][:, do, :], ALU.add,
                  reads=(f"ych{jo}", f"xnck{j}"), writes=(f"ych{jo}",))
            bb.dma("scalar", yT[do * 128:(do + 1) * 128, sl], ych[jo],
                   reads=(f"ych{jo}",), writes=(f"yT_{do}_{ch}",))
    bb.p.barrier()
    AR.reset_front()

    v_res = AR.take([128, 32 * HID], BF16, back=True)
    whcb = [AR.take([128, NDT, TCH], BF16) for _ in range(2)]
    nrmck = [AR.take([128, NDT, 128], BF16) for _ in range(2)]
    nrmch = AR.take([128, NDT, TCH], BF16)
    gch = [AR.take([128, TCH], BF16) for _ in range(2)]
    bhrow_a = AR.take([128, 2 * HID], BF16)
    bhrow = bhrow_a[0:1]
    bb.dma("gpsimd", bhrow, w["bh"].unsqueeze(0), writes=("bhrow",))
    ps_v = [bb.PS[3], bb.PS[4]]
    nT = nrm_s.rearrange("(k p) t -> p k t", p=128)
    for cb in range(4):
        j = cb % 2
        cs = slice(cb * 512, (cb + 1) * 512)
        bb.dma("gpsimd", whcb[j], w["Wh"][:, cs].rearrange("(k p) c -> p k c", p=128),
               writes=(f"whcb{j}",))
        for ts_ in range(32):
            jj = ts_ % 2
            tsl = slice(ts_ * 128, (ts_ + 1) * 128)
            bb.dma("sync", nrmck[jj], nT[:, :, tsl],
                   reads=tuple(f"nrm_s_{k}_{(ts_ // 4) * TCH}" for k in range(NDT)),
                   writes=(f"nrmck{jj}",))
            pairs = [(ps_v[jj][:, :], nrmck[jj][:, k, :],
                      whcb[j][:, k, :]) for k in range(NDT)]
            pairs.append((ps_v[jj][:, :], bb.ones_1x128_bf, bhrow[:, cs]))
            mmv = bb.mm_acc(pairs,
                            reads=(f"nrmck{jj}", f"whcb{j}", "ones1x128bf", "bhrow"),
                            writes=(f"PS{3 + jj}",))
            bb.act(v_res[:, ts_ * HID + cb * 512: ts_ * HID + (cb + 1) * 512],
                   ps_v[jj][:, :], AF.Silu, reads=(f"PS{3 + jj}",),
                   writes=(f"v_{ts_}_{cb}",), deps=[mmv])
    for cb in range(4):
        j = cb % 2
        cs = slice(HID + cb * 512, HID + (cb + 1) * 512)
        bb.dma("gpsimd", whcb[j], w["Wh"][:, cs].rearrange("(k p) c -> p k c", p=128),
               writes=(f"whcb{j}",))
        for ch in range(NCH):
            sl = slice(ch * TCH, (ch + 1) * TCH)
            bb.dma("sync", nrmch, nT[:, :, sl],
                   reads=tuple(f"nrm_s_{k}_{sl.start}" for k in range(NDT)),
                   writes=("nrmch",))
            for gm in range(4):
                jj = (ch * 4 + gm) % 2
                gcol = HID + cb * 512 + gm * 128
                mmg = bb.mm_acc(
                    [(ps_v[jj][:, :], whcb[j][:, k, gm * 128:(gm + 1) * 128],
                      nrmch[:, k, :]) for k in range(NDT)],
                    reads=("nrmch", f"whcb{j}"),
                    writes=(f"PS{3 + jj}",))
                bb.act(gch[jj], ps_v[jj][:, :], AF.Silu,
                       bias=bhc[:, gcol // 128: gcol // 128 + 1],
                       reads=(f"PS{3 + jj}", "bh_c"), writes=(f"gch{jj}",),
                       deps=[mmg])
                bb.dma("scalar", gate_s[gcol - HID: gcol - HID + 128, sl], gch[jj],
                       reads=(f"gch{jj}",), writes=(f"gate_{gcol - HID}_{ch}",))
    bb.p.barrier()
    AR.reset_front()

    q_f = AR.take([128, T], BF16, back=True)
    k_f = AR.take([128, T], BF16, back=True)
    Wqk_sb = AR.take([128, NDT, 128], BF16)
    nrmch3 = [AR.take([128, NDT, TCH], BF16) for _ in range(2)]
    qk32 = [AR.take([128, TCH]) for _ in range(2)]
    bb.dma("gpsimd", Wqk_sb, w["Wqk"].rearrange("(k p) c -> p k c", p=128),
           writes=("Wqk_sb",))
    for ch in range(NCH):
        j = ch % 2
        sl = slice(ch * TCH, (ch + 1) * TCH)
        bb.dma("sync", nrmch3[j], nT[:, :, sl],
               reads=tuple(f"nrm_s_{k}_{sl.start}" for k in range(NDT)),
               writes=(f"nrmch3{j}",))
        mmq = bb.mm_acc(
            [(bb.PS[1][:, :], Wqk_sb[:, k, :], nrmch3[j][:, k, :])
             for k in range(NDT)],
            reads=(f"nrmch3{j}", "Wqk_sb"),
            writes=("PS1",))
        bb.act(qk32[j], bb.PS[1][:, :], AF.Silu, bias=bqkc[:, 0:1],
               reads=("PS1", "bqk_c"), writes=(f"qk32_{j}",), deps=[mmq])
        bb.ts(q_f[:, sl], qk32[j], gamc[:, 0:1], betc[:, 0:1], op0=ALU.mult,
              op1=ALU.add, reads=(f"qk32_{j}", "gamc", "betc"), writes=("q_f",))
        bb.ts(k_f[:, sl], qk32[j], gamc[:, 1:2], betc[:, 1:2], op0=ALU.mult,
              op1=ALU.add, reads=(f"qk32_{j}", "gamc", "betc"), writes=("k_f",))
    bb.p.barrier()
    AR.reset_front()

    ps_sim = [bb.PS[2][:, 0:128], bb.PS[7][:, 0:128]]
    ps_sim_b = ["PS2", "PS7"]
    psY = [bb.PS[3], bb.PS[4], bb.PS[5], bb.PS[6]]
    at32 = [AR.take([128, 128]) for _ in range(2)]
    atbf = [AR.take([128, 128], BF16) for _ in range(2)]
    gateT = AR.take([128, 16, 128], BF16)
    ygT = AR.take([128, 16 * 512], BF16)
    wok = [AR.take([128, 512], BF16) for _ in range(2)]
    yo_x = [AR.take([128, TCH]) for _ in range(2)]
    yo_y = [AR.take([128, TCH]) for _ in range(2)]
    gsT = gate_s.rearrange("(i p) t -> p i t", p=128)
    for grp in range(8):
        for qq in range(4):
            qs = grp * 4 + qq
            qsl = slice(qs * 128, (qs + 1) * 128)
            bb.dma("sync", gateT, gsT[:, :, qsl],
                   reads=tuple(f"gate_{et * 128}_{qs // 4}" for et in range(16)),
                   writes=("gateT",))
            for kc in range(qs + 1):
                sj = kc % 2
                mms = bb.mm(ps_sim[sj], k_f[:, kc * 128:(kc + 1) * 128], q_f[:, qsl],
                            reads=("k_f", "q_f"), writes=(ps_sim_b[sj],))
                last = bb.ts(at32[sj], ps_sim[sj], 0.0, 1.0 / T, op0=ALU.max,
                             op1=ALU.mult, reads=(ps_sim_b[sj],),
                             writes=(f"at32_{sj}",), deps=[mms])
                if kc == qs:
                    last = bb.tt(at32[sj], at32[sj], mask_sb, ALU.mult,
                                 reads=(f"at32_{sj}", "mask_tri"),
                                 writes=(f"at32_{sj}",))
                sqt = bb.act(atbf[sj], at32[sj], AF.Square, reads=(f"at32_{sj}",),
                             writes=(f"atbf_{sj}",), deps=[last])
                bb.mm_multi(
                    [(psY[et // 4][:, (et % 4) * 128:(et % 4 + 1) * 128],
                      v_res[:, kc * HID + et * 128: kc * HID + (et + 1) * 128],
                      atbf[sj]) for et in range(16)],
                    start=(kc == 0), stop=(kc == qs),
                    reads=(f"atbf_{sj}",) + tuple(f"v_{kc}_{cb}" for cb in range(4)),
                    writes=tuple(f"PS{3 + jj}" for jj in range(4)), deps=[sqt])
            for et in range(16):
                bb.tt(ygT[:, et * 512 + qq * 128: et * 512 + (qq + 1) * 128],
                      psY[et // 4][:, (et % 4) * 128:(et % 4 + 1) * 128],
                      gateT[:, et, :], ALU.mult,
                      reads=(f"PS{3 + et // 4}", "gateT"),
                      writes=(f"ygT_{et}_{qq}",))
        gsl = slice(grp * 512, (grp + 1) * 512)
        for wave in range(2):
            for k in range(16):
                jk = k % 2
                bb.dma("gpsimd", wok[jk],
                       w["Wo"][k * 128:(k + 1) * 128, wave * 512:(wave + 1) * 512],
                       writes=(f"wok{jk}",))
                bb.mm_multi(
                    [(psY[dd][:, :], wok[jk][:, dd * 128:(dd + 1) * 128],
                      ygT[:, k * 512:(k + 1) * 512]) for dd in range(4)],
                    start=(k == 0), stop=(k == 15),
                    reads=(f"wok{jk}",) + tuple(f"ygT_{k}_{q2}" for q2 in range(4)),
                    writes=tuple(f"PS{3 + jj}" for jj in range(4)))
            for dd in range(4):
                do = wave * 4 + dd
                jo = dd % 2
                bb.dma("sync", yo_x[jo], yT[do * 128:(do + 1) * 128, gsl],
                       reads=(f"yT_{do}_{grp}",), writes=(f"yo_x{jo}",))
                bb.stt(yo_y[jo], psY[dd][:, :], boc[:, do:do + 1], yo_x[jo],
                       op0=ALU.add, op1=ALU.add,
                       reads=(f"PS{3 + dd}", "bo_c", f"yo_x{jo}"),
                       writes=(f"yo_y{jo}",))
                bb.dma("scalar", yT[do * 128:(do + 1) * 128, gsl], yo_y[jo],
                       reads=(f"yo_y{jo}",), writes=(f"yT_{do}_{grp}",))
    return bb





def short_vec_col(bb, dram_vec, n, tag):
    vr = bb.sbuf(f"{tag}_r", [1, n], F32)
    col = bb.sbuf(f"{tag}_c", [n, 1], F32)
    ld = bb.dma("sync", vr, dram_vec.unsqueeze(0), writes=(f"{tag}_r",))
    tr = bb.transpose(bb.PS[0][0:n, 0:1], vr, bb.ident_f32[0:1, 0:1],
                      reads=(f"{tag}_r", "ident_f32"), writes=("PS0",), deps=[ld])
    bb.copy(col, bb.PS[0][0:n, 0:1], reads=("PS0",), writes=(f"{tag}_c",), deps=[tr])
    return col


def build_block0(stop_after=None, dbg=False, bb=None, xT=None, yT=None):
    if bb is None:
        bb = BB("b0")
        xT = bb.dram_in("xT", [D, T])
        yT = bb.dram_out("yT", [D, T])
    H, DH, CC = 16, 64, 2
    w = {}
    for nm, shp in [("Wq", [D, D]), ("Wk", [D, D]), ("Wv", [D, D]),
                    ("Wbeta", [D, 32]), ("bbeta", [32]), ("Wo", [D, D]),
                    ("bo", [D]), ("ln_g", [D]), ("ln_b", [D]),
                    ("WT", [D, D]), ("fb", [D]), ("V_r", [D, 32]),
                    ("CT", [32, D])]:
        w[nm] = bb.dram_in("w_" + nm, shp)
    dram_dbg = bb.dram_out if dbg else bb.dram_tmp
    xn_s = dram_dbg("xn_s", [D, T])
    xnb_s = bb.dram_tmp("xnb_s", [D, T], BF16)
    Lf_s = dram_dbg("Lf_s", [32, T]) if dbg else None
    qp_s = dram_dbg("qp_s", [T, D]) if dbg else None
    kp_s = dram_dbg("kp_s", [T, D]) if dbg else None
    att_s = dram_dbg("att_s", [T, D]) if dbg else None
    ex_s = dram_dbg("ex_s", [128, 96]) if dbg else None
    py_s = dram_dbg("py_s", [128, 260]) if dbg else None
    ab_s = dram_dbg("ab_s", [128, 256], BF16) if dbg else None
    lr_s = dram_dbg("lr_s", [32, 256]) if dbg else None
    tp_s = dram_dbg("tp_s", [128, 32]) if dbg else None
    va_s = dram_dbg("va_s", [128, 16 * 65], BF16) if dbg else None
    py2_s = dram_dbg("py2_s", [128, 256]) if dbg else None
    dn_s = dram_dbg("dn_s", [128, 16]) if dbg else None

    setup_common(bb)
    AR = Arena(bb, 176 * 1024)
    inv_freq = 1.0 / 10000.0 ** (np.arange(0, 64, 2, dtype=np.float64) / 64)
    tt_ = np.arange(T, dtype=np.float64)
    freqs = np.outer(tt_, inv_freq)
    c_cos = np.cos(freqs).astype(np.float32)
    c_snp = np.concatenate([-np.sin(freqs), np.sin(freqs)], 1).astype(np.float32)
    cosd = bb.const("c_rope_cos", c_cos)
    snpd = bb.const("c_rope_snp", c_snp)
    mask_sb = bb.sbuf("mask_tri", [128, 128], F32)
    bb.dma("gpsimd", mask_sb,
           bb.const("c_mask_tri", np.triu(np.ones((128, 128), np.float32))),
           writes=("mask_tri",))
    g_cols = vec_to_cols(bb, w["ln_g"], D, "lng")
    b_cols = vec_to_cols(bb, w["ln_b"], D, "lnb")
    bo_c = vec_to_cols(bb, w["bo"], D, "bo")
    fb_c = vec_to_cols(bb, w["fb"], D, "fb")
    bbeta_c = short_vec_col(bb, w["bbeta"], 32, "bbeta")
    bb.p.barrier()

    xck = [AR.take([128, NDT, TCH]) for _ in range(2)]
    o_f32 = [AR.take([128, TCH]) for _ in range(2)]
    lns = _ln_scratch(bb, AR)
    for ch in range(NCH):
        j = ch % 2
        sl = slice(ch * TCH, (ch + 1) * TCH)
        bb.dma("sync", xck[j], xT[:, sl].rearrange("(i p) c -> p i c", p=128),
               writes=(f"xck{j}",))

        def post2(i, ap, buf, sl=sl):
            bb.dma("scalar", xn_s[i * 128:(i + 1) * 128, sl], ap, reads=(buf,),
                   writes=(f"xn_s_{i}_{sl.start}",))
            bb.dma("gpsimd", xnb_s[i * 128:(i + 1) * 128, sl], ap, reads=(buf,),
                   writes=(f"xnb_s_{i}_{sl.start}",))

        outs = [{"sink": lambda i, j=j: (o_f32[j], f"of{j}"), "mode": "plain",
                 "post": post2}]
        ln_chunked(bb, xck[j], f"xck{j}", outs, "ln0", ch, lns)
    bb.p.barrier()
    AR.reset_front()

    if stop_after == 0:
        return bb
    WT = AR.take([128, NDT * D], BF16)
    VR = AR.take([128, NDT * 32], BF16)
    CTt = AR.take([32, D], BF16)
    for k in range(NDT):
        bb.dma("gpsimd", WT[:, k * D:(k + 1) * D],
               w["WT"][k * 128:(k + 1) * 128, :], writes=(f"WT{k}",))
        bb.dma("gpsimd", VR[:, k * 32:(k + 1) * 32],
               w["V_r"][k * 128:(k + 1) * 128, :], writes=("VR",))
    bb.dma("gpsimd", CTt, w["CT"], writes=("CTt",))
    xnck = [AR.take([128, NDT, TCH], BF16) for _ in range(2)]
    vrb = [AR.take([32, TCH], BF16) for _ in range(2)]
    xres = [AR.take([128, TCH]) for _ in range(2)]
    ych = [AR.take([128, TCH]) for _ in range(2)]
    ps2 = [bb.PS[3], bb.PS[4]]
    for ch in range(NCH):
        j = ch % 2
        sl = slice(ch * TCH, (ch + 1) * TCH)
        bb.dma("sync", xnck[j], xnb_s[:, sl].rearrange("(i p) c -> p i c", p=128),
               reads=tuple(f"xnb_s_{i}_{sl.start}" for i in range(NDT)),
               writes=(f"xnck{j}",))
        mmv = bb.mm_acc(
            [(bb.PS[5][0:32, :], VR[:, k * 32:(k + 1) * 32], xnck[j][:, k, :])
             for k in range(NDT)],
            reads=("VR", f"xnck{j}"), writes=("PS5",))
        bb.copy(vrb[j], bb.PS[5][0:32, :], reads=("PS5",), writes=(f"vrb{j}",),
                deps=[mmv])
        for do in range(NDT):
            jo = do % 2
            bb.dma("sync", xres[jo], xT[do * 128:(do + 1) * 128, sl],
                   writes=(f"xres{jo}",))
            pairs = [(ps2[jo][:, :], WT[:, k * D + do * 128: k * D + (do + 1) * 128],
                      xnck[j][:, k, :]) for k in range(NDT)]
            pairs.append((ps2[jo][:, :], CTt[:, do * 128:(do + 1) * 128], vrb[j]))
            mmo = bb.mm_acc(pairs,
                            reads=tuple(f"WT{k}" for k in range(NDT))
                            + ("CTt", f"vrb{j}", f"xnck{j}"),
                            writes=(f"PS{3 + jo}",))
            bb.stt(ych[jo], ps2[jo][:, :], fb_c[:, do:do + 1], xres[jo],
                   op0=ALU.add, op1=ALU.add,
                   reads=(f"PS{3 + jo}", "fb_c", f"xres{jo}"),
                   writes=(f"ych{jo}",), deps=[mmo])
            bb.dma("scalar", yT[do * 128:(do + 1) * 128, sl], ych[jo],
                   reads=(f"ych{jo}",), writes=(f"yT_{do}_{ch}",))
    bb.p.barrier()
    AR.reset_front()

    if stop_after == 1:
        return bb
    L_f = AR.take([32, T], back=True)
    WB = AR.take([128, NDT * 32], BF16)
    for k in range(NDT):
        bb.dma("gpsimd", WB[:, k * 32:(k + 1) * 32],
               w["Wbeta"][k * 128:(k + 1) * 128, :], writes=("WB",))
    lgb = AR.take([32, T])
    xnck2 = [AR.take([128, NDT, TCH], BF16) for _ in range(2)]
    for ch in range(NCH):
        j = ch % 2
        sl = slice(ch * TCH, (ch + 1) * TCH)
        bb.dma("sync", xnck2[j], xnb_s[:, sl].rearrange("(i p) c -> p i c", p=128),
               reads=tuple(f"xnb_s_{i}_{sl.start}" for i in range(NDT)),
               writes=(f"xnck2{j}",))
        mmb = bb.mm_acc(
            [(bb.PS[5][0:32, :], WB[:, k * 32:(k + 1) * 32], xnck2[j][:, k, :])
             for k in range(NDT)],
            reads=("WB", f"xnck2{j}"), writes=("PS5",))
        sg = bb.act(lgb[:, sl], bb.PS[5][0:32, :], AF.Sigmoid, bias=bbeta_c,
                    reads=("PS5", "bbeta_c"), writes=("lgb",), deps=[mmb])
        cl = bb.ts(lgb[:, sl], lgb[:, sl], 0.999, 0.001, op0=ALU.min, op1=ALU.max,
                   reads=("lgb",), writes=("lgb",), deps=[sg])
        bb.act(lgb[:, sl], lgb[:, sl], AF.Ln, reads=("lgb",), writes=("lgb",),
               deps=[cl])
    ones32T = AR.take([32, T])
    bb.memset(ones32T, 1.0, writes=("ones32T",))
    bb.scan(L_f, ones32T, lgb, 0.0, reads=("ones32T", "lgb"), writes=("L_f",))
    if dbg:
        bb.dma("scalar", Lf_s, L_f, reads=("L_f",), writes=("Lf_s",))
    bb.p.barrier()
    AR.reset_front()

    if stop_after == 2:
        return bb
    NHC = 32
    WQ = AR.take([128, NDT * D], BF16)
    WK = AR.take([128, NDT * D], BF16)
    WV = AR.take([128, NDT * D], BF16)
    WO = AR.take([128, NDT * D], BF16)
    for k in range(NDT):
        bb.dma("gpsimd", WQ[:, k * D:(k + 1) * D],
               w["Wq"][k * 128:(k + 1) * 128, :], writes=(f"WQ{k}",))
        bb.dma("gpsimd", WK[:, k * D:(k + 1) * D],
               w["Wk"][k * 128:(k + 1) * 128, :], writes=(f"WK{k}",))
        bb.dma("gpsimd", WV[:, k * D:(k + 1) * D],
               w["Wv"][k * 128:(k + 1) * 128, :], writes=(f"WV{k}",))
        bb.dma("gpsimd", WO[:, k * D:(k + 1) * D],
               w["Wo"][k * 128:(k + 1) * 128, :], writes=(f"WO{k}",))
    stt_t = [[AR.take([128, 65], back=True) for _ in range(NDT)] for _ in range(CC)]
    sbf_t = [[AR.take([128, 65], BF16, back=True) for _ in range(NDT)] for _ in range(CC)]
    for c in range(CC):
        for tl in range(NDT):
            bb.memset(stt_t[c][tl], 0.0, writes=(f"st{c}_{tl}",))
            bb.memset(sbf_t[c][tl], 0.0, writes=(f"stb{c}_{tl}",))
    xnkc = [AR.take([128, NDT, 128], BF16) for _ in range(2)]
    xnfc = [AR.take([128, NDT, 128]) for _ in range(2)]
    q32 = AR.take([128, D])
    k32 = AR.take([128, D])
    rt1 = AR.take([128, D])
    rt2 = AR.take([128, D])
    qp = AR.take([128, D])
    kp = AR.take([128, D])
    qt = [AR.take([128, D]) for _ in range(CC)]
    kt = [AR.take([128, D], BF16) for _ in range(CC)]
    v_aug = AR.take([128, H * 65], BF16)
    qf = [AR.take([128, NDT * 128], BF16) for _ in range(CC)]
    kf = AR.take([128, NDT * 128], BF16)
    atbf = [AR.take([128, 128], BF16) for _ in range(2)]
    rope_c = AR.take([128, 32])
    rope_s = AR.take([128, 64])
    lrel = AR.take([32, 128])
    lmp = AR.take([32, 128])
    expP_t = AR.take([128, 32])
    expM_t = AR.take([128, 32])
    expMP_t = AR.take([128, 32])
    prow = AR.take([1, 32])
    pcol = AR.take([32, 64])
    pbc = AR.take([128, 32])
    dens = AR.take([128, 16])
    att_t = AR.take([128, D])
    attf = AR.take([128, NDT * 128], BF16)
    hck = AR.take([128, NDT, 128])
    yrmw = [AR.take([128, NDT, 128]) for _ in range(2)]
    lnsP = _ln_scratch(bb, AR, cw=128)
    ps_tr = [bb.PS[0], bb.PS[1]]
    ps_sim = [bb.PS[2][:, 0:128], bb.PS[7][:, 0:128]]
    ps_sim_b = ["PS2", "PS7"]
    psY = [bb.PS[3], bb.PS[4], bb.PS[5], bb.PS[6]]
    bb.memset(v_aug.rearrange("p (h e) -> p h e", e=65)[:, :, 64:65], 1.0,
              writes=("v_aug",))
    NC32 = T // 128
    for c in range(NC32):
        j = c % 2
        sl = slice(c * 128, (c + 1) * 128)
        bb.dma("sync", xnkc[j], xnb_s[:, sl].rearrange("(i p) t -> p i t", p=128),
               reads=tuple(f"xnb_s_{i}_{(c // 4) * TCH}" for i in range(NDT)),
               writes=(f"xnkc{j}",))
        bb.dma("sync", xnfc[j], xn_s[:, sl].rearrange("(i p) t -> p i t", p=128),
               reads=tuple(f"xn_s_{i}_{(c // 4) * TCH}" for i in range(NDT)),
               writes=(f"xnfc{j}",))
        bb.dma("scalar", rope_c, cosd[sl, :], writes=("rope_c",))
        bb.dma("scalar", rope_s, snpd[sl, :], writes=("rope_s",))
        for (dst, dbuf, Wt, wpre) in [(q32, "q32", WQ, "WQ"), (k32, "k32", WK, "WK")]:
            for n2 in range(2):
                mm = bb.mm_acc(
                    [(ps2[n2][:, :], xnkc[j][:, k, :],
                      Wt[:, k * D + n2 * 512:(k) * D + (n2 + 1) * 512])
                     for k in range(NDT)],
                    reads=(f"xnkc{j}",) + tuple(f"{wpre}{k}" for k in range(NDT)),
                    writes=(f"PS{3 + n2}",))
                bb.copy(dst[:, n2 * 512:(n2 + 1) * 512], ps2[n2][:, :],
                        reads=(f"PS{3 + n2}",), writes=(dbuf,), deps=[mm])
        vap = v_aug.rearrange("p (h e) -> p h e", e=65)
        for n2 in range(2):
            mm = bb.mm_acc(
                [(ps2[n2][:, :], xnkc[j][:, k, :],
                  WV[:, k * D + n2 * 512:(k) * D + (n2 + 1) * 512])
                 for k in range(NDT)],
                reads=(f"xnkc{j}",) + tuple(f"WV{k}" for k in range(NDT)),
                writes=(f"PS{3 + n2}",))
            bb.copy(vap[:, n2 * 8:(n2 + 1) * 8, 0:64],
                    ps2[n2][:, :].rearrange("p (h e) -> p h e", e=64),
                    reads=(f"PS{3 + n2}",), writes=("v_aug",), deps=[mm])
        for (src_, sbuf_, dphi, dpbuf) in [(q32, "q32", qp, "qp"),
                                           (k32, "k32", kp, "kp")]:
            capT = rope_c.unsqueeze(1).unsqueeze(1).broadcast_to((128, 16, 2, 32))
            spT = rope_s.unsqueeze(1).broadcast_to((128, 16, 64))
            sw = src_.rearrange("p (h two cc) -> p h two cc", two=2, cc=32)[:, :, ::-1, :]
            t1 = bb.tt(rt1.rearrange("p (h two cc) -> p h two cc", two=2, cc=32),
                       src_.rearrange("p (h two cc) -> p h two cc", two=2, cc=32),
                       capT, ALU.mult, reads=(sbuf_, "rope_c"), writes=("rt1",))
            t2 = bb.tt(rt2.rearrange("p (h e) -> p h e", e=64), sw, spT, ALU.mult,
                       reads=(sbuf_, "rope_s"), writes=("rt2",))
            t3 = bb.tt(rt1, rt1, rt2, ALU.add, reads=("rt1", "rt2"),
                       writes=("rt1",), deps=[t1, t2])
            t4 = bb.ts(rt2, rt1, 0.0, None, op0=ALU.min, reads=("rt1",),
                       writes=("rt2",), deps=[t3])
            t5 = bb.act(rt2, rt2, AF.Exp, reads=("rt2",), writes=("rt2",),
                        deps=[t4])
            bb.stt(dphi, rt1, 0.0, rt2, op0=ALU.max, op1=ALU.add,
                   reads=("rt1", "rt2"), writes=(dpbuf,), deps=[t5])
        if dbg:
            bb.dma("scalar", qp_s[sl, :], qp, reads=("qp",), writes=("qp_s",))
            bb.dma("scalar", kp_s[sl, :], kp, reads=("kp",), writes=("kp_s",))
        if c == 0:
            lst = bb.kcol(0.0, rows=32)
            lread = ("kbuf", "L_f")
        else:
            lst = L_f[:, c * 128 - 1:c * 128]
            lread = ("L_f",)
        l1 = bb.tt(lrel, L_f[:, sl], lst.broadcast_to((32, 128)), ALU.subtract,
                   reads=lread, writes=("lrel",))

        tr1 = bb.transpose(ps_tr[0][0:128, 0:32], lrel, bb.ident_f32[0:32, 0:32],
                           reads=("lrel", "ident_f32"), writes=("PS0",), deps=[l1])
        if dbg and c == 0:
            bb.dma("scalar", lr_s[:, 0:128], lrel, reads=("lrel",), writes=("lr_s",))
            cpt = bb.copy(expM_t, ps_tr[0][0:128, 0:32], reads=("PS0",),
                          writes=("expM_t",), deps=[tr1])
            bb.dma("scalar", tp_s, expM_t, reads=("expM_t",), writes=("tp_s",))
        e1 = bb.act(expP_t, ps_tr[0][0:128, 0:32], AF.Exp, reads=("PS0",),
                    writes=("expP_t",), deps=[tr1])
        e2 = bb.act(expM_t, ps_tr[0][0:128, 0:32], AF.Exp, scale=-1.0,
                    reads=("PS0",), writes=("expM_t",), deps=[tr1])

        ep = bb.act(pcol[:, 0:1], lrel[:, 127:128], AF.Exp, reads=("lrel",),
                    writes=("pcol",), deps=[l1])
        tr3 = bb.transpose(ps_tr[1][0:1, 32:64], pcol[:, 0:1],
                           bb.ident_f32[0:32, 0:32],
                           reads=("pcol", "ident_f32"), writes=("PS1",), deps=[ep])
        cp3 = bb.copy(prow, ps_tr[1][0:1, 32:64], reads=("PS1",), writes=("prow",),
                      deps=[tr3])
        mmp = bb.mm(ps_tr[1][:, 64:96], bb.ones_1x128, prow,
                    reads=("prow", "ones1x128"), writes=("PS1",), deps=[cp3])
        bb.copy(pbc, ps_tr[1][:, 64:96], reads=("PS1",), writes=("pbc",),
                deps=[mmp])
        bb.tt(expMP_t, expM_t, pbc, ALU.mult, reads=("expM_t", "pbc"),
              writes=("expMP_t",))
        for cc_ in range(CC):
            eP = expP_t[:, cc_:32:2].unsqueeze(2).broadcast_to((128, 16, 64))
            eMP = expMP_t[:, cc_:32:2].unsqueeze(2).broadcast_to((128, 16, 64))
            bb.tt(qt[cc_].rearrange("p (h e) -> p h e", e=64),
                  qp.rearrange("p (h e) -> p h e", e=64), eP, ALU.mult,
                  reads=("qp", "expP_t"), writes=(f"qt{cc_}",))
            bb.tt(kt[cc_].rearrange("p (h e) -> p h e", e=64),
                  kp.rearrange("p (h e) -> p h e", e=64), eMP, ALU.mult,
                  reads=("kp", "expMP_t"), writes=(f"kt{cc_}",))
        for blk in range(NDT):
            jj = blk % 2
            tr = bb.transpose(ps_tr[jj][:, 128:256], kp[:, blk * 128:(blk + 1) * 128],
                              bb.ident_f32,
                              reads=("kp", "ident_f32"), writes=(f"PS{jj}",))
            bb.copy(kf[:, blk * 128:(blk + 1) * 128], ps_tr[jj][:, 128:256],
                    reads=(f"PS{jj}",), writes=("kf",), deps=[tr])
            for cc_ in range(CC):
                tr = bb.transpose(ps_tr[jj][:, 128:256],
                                  qt[cc_][:, blk * 128:(blk + 1) * 128],
                                  bb.ident_f32,
                                  reads=(f"qt{cc_}", "ident_f32"),
                                  writes=(f"PS{jj}",))
                bb.copy(qf[cc_][:, blk * 128:(blk + 1) * 128],
                        ps_tr[jj][:, 128:256],
                        reads=(f"PS{jj}",), writes=(f"qf{cc_}",), deps=[tr])
        for cc_ in range(CC):
            for tl in range(NDT):
                bb.copy(sbf_t[cc_][tl], stt_t[cc_][tl], reads=(f"st{cc_}_{tl}",),
                        writes=(f"stb{cc_}_{tl}",))
        for h in range(H):
            blk = h // 2
            rb = (h % 2) * 64
            ytile = psY[h // 4]
            ycol = (h % 4) * 65
            ypsb = f"PS{3 + h // 4}"
            mms = []
            for cc_ in range(CC):
                sj = (2 * h + cc_) % 2
                mm1 = bb.mm(ps_sim[sj],
                            kf[rb:rb + 64, blk * 128:(blk + 1) * 128],
                            qf[cc_][rb:rb + 64, blk * 128:(blk + 1) * 128],
                            reads=("kf", f"qf{cc_}"), writes=(ps_sim_b[sj],))
                at = bb.stt(atbf[sj], ps_sim[sj], expM_t[:, 2 * h + cc_:2 * h + cc_ + 1],
                            mask_sb, op0=ALU.mult, op1=ALU.mult,
                            reads=(ps_sim_b[sj], "expM_t", "mask_tri"),
                            writes=(f"atbf{sj}",), deps=[mm1])
                pairs = [
                    (ytile[:, ycol:ycol + 65], atbf[sj], vap[:, h, :]),
                    (ytile[:, ycol:ycol + 65],
                     qf[cc_][rb:rb + 64, blk * 128:(blk + 1) * 128],
                     sbf_t[cc_][blk][rb:rb + 64, :]),
                ]
                mm2 = bb.mm_multi([pairs[0]], start=(cc_ == 0), stop=False,
                                  reads=(f"atbf{sj}", "v_aug"), writes=(ypsb,),
                                  deps=[at])
                mm3 = bb.mm_multi([pairs[1]], start=False, stop=(cc_ == CC - 1),
                                  reads=(f"qf{cc_}", f"stb{cc_}_{blk}"),
                                  writes=(ypsb,))
        if dbg and c == 0:
            bb.dma("scalar", ex_s[:, 0:32], expP_t, reads=("expP_t",), writes=("ex_s",))
            bb.dma("scalar", ex_s[:, 32:64], expM_t, reads=("expM_t",), writes=("ex_s",))
            bb.dma("scalar", ex_s[:, 64:96], expMP_t, reads=("expMP_t",), writes=("ex_s",))
            cpy = bb.copy(att_t[:, 0:260], psY[0][:, 0:260], reads=("PS3",),
                          writes=("att_t",), eng="vector")
            bb.dma("scalar", py_s, att_t[:, 0:260], reads=("att_t",), writes=("py_s",))
            bb.dma("scalar", ab_s[:, 0:128], atbf[0], reads=("atbf0",), writes=("ab_s",))
            bb.dma("scalar", ab_s[:, 128:256], atbf[1], reads=("atbf1",), writes=("ab_s",))
        for cc_ in range(CC):
            for tl in range(NDT):
                jj = (cc_ * NDT + tl) % 2
                h0, h1 = 2 * tl, 2 * tl + 1
                dm = bb.mm_multi(
                    [(ps_tr[jj][0:64, 256 + 0:256 + 65],
                      kt[cc_][:, h0 * 64:(h0 + 1) * 64], vap[:, h0, :]),
                     (ps_tr[jj][0:64, 256 + 65:256 + 130],
                      kt[cc_][:, h1 * 64:(h1 + 1) * 64], vap[:, h1, :])],
                    start=True, stop=True,
                    reads=(f"kt{cc_}", "v_aug"), writes=(f"PS{jj}",))
                u1 = bb.tt(stt_t[cc_][tl][0:64, :], stt_t[cc_][tl][0:64, :],
                           pbc[0:64, 2 * h0 + cc_:2 * h0 + cc_ + 1]
                           .broadcast_to((64, 65)), ALU.mult,
                           reads=(f"st{cc_}_{tl}", "pbc"),
                           writes=(f"st{cc_}_{tl}",))
                bb.tt(stt_t[cc_][tl][0:64, :], stt_t[cc_][tl][0:64, :],
                      ps_tr[jj][0:64, 256:256 + 65], ALU.add,
                      reads=(f"st{cc_}_{tl}", f"PS{jj}"),
                      writes=(f"st{cc_}_{tl}",), deps=[dm, u1])
                u2 = bb.tt(stt_t[cc_][tl][64:128, :], stt_t[cc_][tl][64:128, :],
                           pbc[64:128, 2 * h1 + cc_:2 * h1 + cc_ + 1]
                           .broadcast_to((64, 65)), ALU.mult,
                           reads=(f"st{cc_}_{tl}", "pbc"),
                           writes=(f"st{cc_}_{tl}",))
                bb.tt(stt_t[cc_][tl][64:128, :], stt_t[cc_][tl][64:128, :],
                      ps_tr[jj][0:64, 256 + 65:256 + 130], ALU.add,
                      reads=(f"st{cc_}_{tl}", f"PS{jj}"),
                      writes=(f"st{cc_}_{tl}",), deps=[dm, u2])
        for h in range(H):
            bk, ycol = h // 4, (h % 4) * 65
            bb.copy(dens[:, h:h + 1], psY[bk][:, ycol + 64:ycol + 65],
                    reads=(f"PS{3 + bk}",), writes=("dens",), eng="vector")
        r1 = bb.ts(dens, dens, 1e-6, None, op0=ALU.add, reads=("dens",),
                   writes=("dens",))
        r2 = bb.recip(dens, dens, reads=("dens",), writes=("dens",))
        for h in range(H):
            bk, ycol = h // 4, (h % 4) * 65
            bb.tt(att_t[:, h * 64:(h + 1) * 64], psY[bk][:, ycol:ycol + 64],
                  dens[:, h:h + 1].broadcast_to((128, 64)), ALU.mult,
                  reads=(f"PS{3 + bk}", "dens"), writes=("att_t",), deps=[r2])
        if dbg and c == 0:
            bb.dma("scalar", dn_s, dens, reads=("dens",), writes=("dn_s",))
            cp9 = bb.copy(hck[:, 0:2, :], psY[0][:, 0:256], reads=("PS3",),
                          writes=("hck",), eng="vector")
            bb.dma("scalar", tp_s[:, 0:32], hck[:, 0, 0:32], reads=("hck",), writes=("tp_s",))
            bb.dma("scalar", va_s, v_aug, reads=("v_aug",), writes=("va_s",))
            bb.dma("scalar", py2_s, hck[:, 0:2, :], reads=("hck",), writes=("py2_s",))
        if dbg:
            bb.dma("scalar", att_s[sl, :], att_t, reads=("att_t",), writes=("att_s",))
        for blk in range(NDT):
            jj = blk % 2
            tr = bb.transpose(ps_tr[jj][:, 128:256],
                              att_t[:, blk * 128:(blk + 1) * 128], bb.ident_f32,
                              reads=("att_t", "ident_f32"), writes=(f"PS{jj}",))
            bb.copy(attf[:, blk * 128:(blk + 1) * 128], ps_tr[jj][:, 128:256],
                    reads=(f"PS{jj}",), writes=("attf",), deps=[tr])
        for do in range(NDT):
            jj = do % 2
            mo = bb.mm_acc(
                [(ps2[jj][:, 0:128], WO[:, k * D + do * 128: k * D + (do + 1) * 128],
                  attf[:, k * 128:(k + 1) * 128]) for k in range(NDT)],
                reads=tuple(f"WO{k}" for k in range(NDT)) + ("attf",),
                writes=(f"PS{3 + jj}",))
            bb.stt(hck[:, do, :], ps2[jj][:, 0:128], bo_c[:, do:do + 1],
                   xnfc[j][:, do, :], op0=ALU.add, op1=ALU.add,
                   reads=(f"PS{3 + jj}", "bo_c", f"xnfc{j}"), writes=("hck",),
                   deps=[mo])
        bb.dma("sync", yrmw[j], yT[:, sl].rearrange("(i p) t -> p i t", p=128),
               reads=tuple(f"yT_{i}_{c // 4}" for i in range(NDT)),
               writes=(f"yrmw{j}",))

        def postA(i, ap, buf, j=j, sl=sl, c=c):
            a2 = bb.tt(ap, ap, yrmw[j][:, i, :], ALU.add,
                       reads=(buf, f"yrmw{j}"), writes=(buf,))
            bb.dma("scalar", yT[i * 128:(i + 1) * 128, sl], ap, reads=(buf,),
                   writes=(f"yT2_{i}_{c}",))

        outs = [{"sink": lambda i, j=j: (lnsP[f"ta{1 - (c % 2)}"][:, 0:128],
                                         f"lnP{1 - (c % 2)}_ta"),
                 "mode": "gb", "gamma": g_cols, "beta": b_cols,
                 "gbufs": ("lng_c", "lnb_c"), "post": postA}]
        ln_chunked(bb, hck, "hck", outs, "lnP", c, lnsP, cw=128)
    return bb


def _b0_wmap(inputs):
    P = "L0_attn_"
    return {
        "Wq": inputs[P + "Wq"], "Wk": inputs[P + "Wk"], "Wv": inputs[P + "Wv"],
        "Wbeta": inputs[P + "Wbeta"], "bbeta": inputs[P + "bbeta"],
        "Wo": inputs[P + "Wo"], "bo": inputs[P + "bo"],
        "ln_g": inputs[P + "ln_gamma"], "ln_b": inputs[P + "ln_beta"],
        "WT": inputs["L0_ffn_W"].T, "fb": inputs["L0_ffn_b"],
        "V_r": inputs["L0_ffn_V_r"], "CT": inputs["L0_ffn_C"].T,
    }


def _b0_inmap(core, inputs, bb):
    m = {"xT": np.ascontiguousarray(inputs["x_in"][core].T)}
    for k, v in _b0_wmap(inputs).items():
        m["w_" + k] = np.ascontiguousarray(v)
    for cname, arr in bb.consts.items():
        m[cname] = arr
    return m


def run_block0(x_in, inputs):
    bb = build_block0()
    nc = bb.finalize()
    in_maps = [_b0_inmap(c, {**inputs, "x_in": x_in}, bb) for c in range(NCORE)]
    res = run_bass_kernel_spmd(nc, in_maps, core_ids=list(range(NCORE)))
    out = np.stack([res.results[c]["yT"].T for c in range(NCORE)])
    return np.ascontiguousarray(out)



def build_block3(stop_after=None, dbg=False, bb=None, xT=None, yT=None):
    if bb is None:
        bb = BB("b3")
        xT = bb.dram_in("xT", [D, T])
        yT = bb.dram_out("yT", [D, T])

    w = {}
    for nm, shp in [("inT", [D, 2048]), ("conv_w", [D, 2]), ("conv_b", [D]),
                    ("xp_dtT", [D, 64]), ("xp_BT", [D, 16]), ("xp_CT", [D, 16]),
                    ("dtT", [64, D]), ("dt_b", [D]), ("A_log", [D, 16]),
                    ("Dp", [D]), ("opT", [D, D]), ("mn_g", [D]), ("mn_b", [D]),
                    ("n1_g", [D]), ("n1_b", [D]), ("n2_g", [D]), ("n2_b", [D]),
                    ("n3_g", [D]), ("n3_b", [D]), ("w1T", [D, 4096]),
                    ("w2T", [4096, D]), ("feT", [D, D]), ("fe_b", [D]),
                    ("WrT", [D, D]), ("WkT", [D, D]), ("WvT", [D, D]),
                    ("mu_r", [D]), ("mu_k", [D])]:
        w[nm] = bb.dram_in("w_" + nm, shp)
    NST = 16
    dram_dbg = bb.dram_out if dbg else bb.dram_tmp
    xn_s = dram_dbg("xn_s", [D, T])
    xnb_s = bb.dram_tmp("xnb_s", [D, T], BF16)
    min_s = dram_dbg("min_s", [D, T], BF16)
    xin_s = dram_dbg("xin_s", [D, T])
    z_s = dram_dbg("z_s", [D, T])
    xcb_s = dram_dbg("xcb_s", [D, T], BF16)
    xcf_s = dram_dbg("xcf_s", [D, T])
    src_s = dram_dbg("src_s", [D, T])
    ymam_s = dram_dbg("ymam_s", [D, T], BF16)

    setup_common(bb)
    AR = Arena(bb, 176 * 1024)
    cols = {}
    for nm in ["conv_b", "dt_b", "Dp", "mn_g", "mn_b", "n1_g", "n1_b", "n2_g",
               "n2_b", "n3_g", "n3_b", "fe_b", "mu_r", "mu_k"]:
        cols[nm] = vec_to_cols(bb, w[nm], D, nm)
    convw = bb.sbuf("convw", [128, NDT * 2], F32)
    bb.dma("sync", convw.rearrange("p (i c) -> p i c", c=2),
           w["conv_w"].rearrange("(i p) c -> p i c", p=128), writes=("convw",))
    Alog_sb = bb.sbuf("Alog_sb", [128, NDT * NST], F32)
    negA = bb.sbuf("negA", [128, NDT * NST], F32)
    lda = bb.dma("sync", Alog_sb.rearrange("p (i n) -> p i n", n=NST),
                 w["A_log"].rearrange("(i p) n -> p i n", p=128),
                 writes=("Alog_sb",))
    ea = bb.act(negA, Alog_sb, AF.Exp, reads=("Alog_sb",), writes=("negA",),
                deps=[lda])
    bb.ts(negA, negA, -1.0, None, reads=("negA",), writes=("negA",), deps=[ea])
    om_r = bb.sbuf("om_r", [128, NDT], F32)
    om_k = bb.sbuf("om_k", [128, NDT], F32)
    bb.ts(om_r, cols["mu_r"], -1.0, 1.0, op0=ALU.mult, op1=ALU.add,
          reads=("mu_r_c",), writes=("om_r",))
    bb.ts(om_k, cols["mu_k"], -1.0, 1.0, op0=ALU.mult, op1=ALU.add,
          reads=("mu_k_c",), writes=("om_k",))
    bb.p.barrier()

    xck = [AR.take([128, NDT, TCH]) for _ in range(2)]
    o_f32 = [AR.take([128, TCH]) for _ in range(2)]
    o_bf = [AR.take([128, TCH], BF16) for _ in range(2)]
    o_m = [AR.take([128, TCH], BF16) for _ in range(2)]
    lns = _ln_scratch(bb, AR)
    for ch in range(NCH):
        j = ch % 2
        sl = slice(ch * TCH, (ch + 1) * TCH)
        bb.dma("sync", xck[j], xT[:, sl].rearrange("(i p) c -> p i c", p=128),
               writes=(f"xck{j}",))

        def post(dram, dname, q, sl=sl):
            def f(i, ap, buf):
                bb.dma(q, dram[i * 128:(i + 1) * 128, sl], ap, reads=(buf,),
                       writes=(f"{dname}_{i}_{sl.start}",))
            return f

        def post2(i, ap, buf, sl=sl):
            bb.dma("scalar", xn_s[i * 128:(i + 1) * 128, sl], ap, reads=(buf,),
                   writes=(f"xn_s_{i}_{sl.start}",))
            bb.dma("gpsimd", xnb_s[i * 128:(i + 1) * 128, sl], ap, reads=(buf,),
                   writes=(f"xnb_s_{i}_{sl.start}",))

        outs = [
            {"sink": lambda i, j=j: (o_f32[j], f"of{j}"), "mode": "plain",
             "post": post2},
            {"sink": lambda i, j=j: (o_m[j], f"om{j}"), "mode": "gbf",
             "gamma": cols["n1_g"], "beta": cols["n1_b"],
             "gbufs": ("n1_g_c", "n1_b_c"), "post": post(min_s, "min_s", "sync")},
        ]
        ln_chunked(bb, xck[j], f"xck{j}", outs, "ln0", ch, lns)
    bb.p.barrier()
    AR.reset_front()

    if stop_after == 0:
        return bb
    WI = AR.take([128, NDT * 2048], BF16)
    for k in range(NDT):
        bb.dma("gpsimd", WI[:, k * 2048:(k + 1) * 2048],
               w["inT"][k * 128:(k + 1) * 128, :], writes=(f"WI{k}",))
    mck = [AR.take([128, NDT, TCH], BF16) for _ in range(2)]
    och = [AR.take([128, TCH]) for _ in range(2)]
    ps2 = [bb.PS[3], bb.PS[4]]
    for ch in range(NCH):
        j = ch % 2
        sl = slice(ch * TCH, (ch + 1) * TCH)
        bb.dma("sync", mck[j], min_s[:, sl].rearrange("(i p) c -> p i c", p=128),
               reads=tuple(f"min_s_{i}_{sl.start}" for i in range(NDT)),
               writes=(f"mck{j}",))
        for mt in range(16):
            jm = mt % 2
            mmt = bb.mm_acc(
                [(ps2[jm][:, :], WI[:, k * 2048 + mt * 128: k * 2048 + (mt + 1) * 128],
                  mck[j][:, k, :]) for k in range(NDT)],
                reads=tuple(f"WI{k}" for k in range(NDT)) + (f"mck{j}",),
                writes=(f"PS{3 + jm}",))
            tgt, tname = (xin_s, "xin_s") if mt < 8 else (z_s, "z_s")
            ti = mt if mt < 8 else mt - 8
            cpt = bb.copy(och[jm], ps2[jm][:, :], reads=(f"PS{3 + jm}",),
                          writes=(f"och{jm}",), deps=[mmt])
            bb.dma("scalar", tgt[ti * 128:(ti + 1) * 128, sl], och[jm],
                   reads=(f"och{jm}",), writes=(f"{tname}_{ti}_{sl.start}",))
    bb.p.barrier()
    AR.reset_front()

    if stop_after == 1:
        return bb
    cur = [AR.take([128, NDT, TCH]) for _ in range(2)]
    prv = [AR.take([128, NDT, TCH]) for _ in range(2)]
    xc32 = [AR.take([128, TCH]) for _ in range(2)]
    xcb = [AR.take([128, TCH], BF16) for _ in range(2)]
    for ch in range(NCH):
        j = ch % 2
        sl = slice(ch * TCH, (ch + 1) * TCH)
        bb.dma("sync", cur[j], xin_s[:, sl].rearrange("(i p) c -> p i c", p=128),
               reads=tuple(f"xin_s_{i}_{sl.start}" for i in range(NDT)),
               writes=(f"cur{j}",))
        if ch == 0:
            bb.memset(prv[j][:, :, 0:1], 0.0, writes=(f"prv{j}",))
            bb.dma("sync", prv[j][:, :, 1:TCH],
                   xin_s[:, 0:TCH - 1].rearrange("(i p) c -> p i c", p=128),
                   reads=tuple(f"xin_s_{i}_0" for i in range(NDT)),
                   writes=(f"prv{j}",))
        else:
            bb.dma("sync", prv[j],
                   xin_s[:, ch * TCH - 1:(ch + 1) * TCH - 1].rearrange(
                       "(i p) c -> p i c", p=128),
               reads=tuple(f"xin_s_{i}_{(ch - 1) * TCH}" for i in range(NDT))
               + tuple(f"xin_s_{i}_{sl.start}" for i in range(NDT)),
               writes=(f"prv{j}",))
        for i in range(NDT):
            jj = (ch * NDT + i) % 2
            t1 = bb.ts(xc32[jj], cur[j][:, i, :], convw[:, 2 * i + 1:2 * i + 2], None,
                       reads=(f"cur{j}", "convw"), writes=(f"xc32_{jj}",))
            t2 = bb.stt(xc32[jj], prv[j][:, i, :], convw[:, 2 * i:2 * i + 1], xc32[jj],
                        op0=ALU.mult, op1=ALU.add,
                        reads=(f"prv{j}", "convw", f"xc32_{jj}"),
                        writes=(f"xc32_{jj}",), deps=[t1])
            a1 = bb.act(xc32[jj], xc32[jj], AF.Silu, bias=cols["conv_b"][:, i:i + 1],
                        reads=(f"xc32_{jj}", "conv_b_c"), writes=(f"xc32_{jj}",),
                        deps=[t2])
            bb.dma("scalar", xcf_s[i * 128:(i + 1) * 128, sl], xc32[jj],
                   reads=(f"xc32_{jj}",), writes=(f"xcf_s_{i}_{sl.start}",))
            c1 = bb.copy(xcb[jj], xc32[jj], reads=(f"xc32_{jj}",),
                         writes=(f"xcb{jj}",), eng="vector", deps=[a1])
            bb.dma("gpsimd", xcb_s[i * 128:(i + 1) * 128, sl], xcb[jj],
                   reads=(f"xcb{jj}",), writes=(f"xcb_s_{i}_{sl.start}",))
    bb.p.barrier()
    AR.reset_front()

    if stop_after == 2:
        return bb
    dt_f = AR.take([64, T], BF16, back=True)
    B_f = AR.take([16, T])
    C_f = AR.take([16, T])
    Wdt = AR.take([128, NDT * 64], BF16)
    Wb_ = AR.take([128, NDT * 16], BF16)
    Wc_ = AR.take([128, NDT * 16], BF16)
    for k in range(NDT):
        bb.dma("gpsimd", Wdt[:, k * 64:(k + 1) * 64],
               w["xp_dtT"][k * 128:(k + 1) * 128, :], writes=("Wdt",))
        bb.dma("gpsimd", Wb_[:, k * 16:(k + 1) * 16],
               w["xp_BT"][k * 128:(k + 1) * 128, :], writes=("Wb_",))
        bb.dma("gpsimd", Wc_[:, k * 16:(k + 1) * 16],
               w["xp_CT"][k * 128:(k + 1) * 128, :], writes=("Wc_",))
    b_s = dram_dbg("b_s", [16, T])
    c_s = dram_dbg("c_s", [16, T])
    xcck = [AR.take([128, NDT, TCH], BF16) for _ in range(2)]
    for ch in range(NCH):
        j = ch % 2
        sl = slice(ch * TCH, (ch + 1) * TCH)
        bb.dma("sync", xcck[j], xcb_s[:, sl].rearrange("(i p) c -> p i c", p=128),
               reads=tuple(f"xcb_s_{i}_{sl.start}" for i in range(NDT)),
               writes=(f"xcck{j}",))
        for (lhs, lname, dst, dname, m) in [
                (Wdt, "Wdt", dt_f, "dt_f", 64), (Wb_, "Wb_", B_f, "B_f", 16),
                (Wc_, "Wc_", C_f, "C_f", 16)]:
            mmt = bb.mm_acc(
                [(bb.PS[3][0:m, :], lhs[:, k * m:(k + 1) * m], xcck[j][:, k, :])
                 for k in range(NDT)],
                reads=(lname, f"xcck{j}"), writes=("PS3",))
            bb.copy(dst[0:m, sl], bb.PS[3][0:m, :], reads=("PS3",),
                    writes=(dname,), deps=[mmt])
    bb.dma("scalar", b_s, B_f, reads=("B_f",), writes=("b_s",))
    bb.dma("scalar", c_s, C_f, reads=("C_f",), writes=("c_s",))
    bb.p.barrier()
    AR.reset_front()

    if stop_after == 3:
        return bb
    Wdtp = AR.take([64, D], BF16)
    bb.dma("gpsimd", Wdtp, w["dtT"], writes=("Wdtp",))
    delta = AR.take([128, T])
    u_ = AR.take([128, T])
    dA = AR.take([128, T])
    dBu = AR.take([128, T])
    hsc = AR.take([128, T])
    yac = AR.take([128, T])
    Bb = AR.take([128, T])
    Cb = AR.take([128, T])
    xcd = AR.take([128, T])
    zd = [AR.take([128, TCH]) for _ in range(2)]
    for i in range(NDT):
        for c2 in range(NCH):
            sl = slice(c2 * TCH, (c2 + 1) * TCH)
            mmt = bb.mm(bb.PS[3][:, :], Wdtp[:, i * 128:(i + 1) * 128], dt_f[:, sl],
                        reads=("Wdtp", "dt_f"), writes=("PS3",))
            if stop_after == 44:
                cp_t = bb.copy(delta[:, sl], bb.PS[3][:, :], reads=("PS3",),
                               writes=("delta",), deps=[mmt])
                continue
            a_t = bb.act(u_[:, sl], bb.PS[3][:, :], AF.Abs,
                         bias=cols["dt_b"][:, i:i + 1],
                         reads=("PS3", "dt_b_c"), writes=("u_",), deps=[mmt])
            e_t = bb.act(u_[:, sl], u_[:, sl], AF.Exp, scale=-1.0,
                         reads=("u_",), writes=("u_",), deps=[a_t])
            l_t = bb.act(u_[:, sl], u_[:, sl], AF.Ln, bias=bb.kcol(1.0),
                         reads=("u_", "kbuf"), writes=("u_",), deps=[e_t])
            if stop_after == 45:
                continue
            m_t = bb.ts(delta[:, sl], bb.PS[3][:, :], cols["dt_b"][:, i:i + 1], 0.0,
                        op0=ALU.add, op1=ALU.max,
                        reads=("PS3", "dt_b_c"), writes=("delta",), deps=[mmt])
            if stop_after != 46:
                bb.tt(delta[:, sl], delta[:, sl], u_[:, sl], ALU.add,
                      reads=("delta", "u_"), writes=("delta",), deps=[l_t, m_t])
        bb.dma("sync", xcd, xcf_s[i * 128:(i + 1) * 128, :],
               reads=tuple(f"xcf_s_{i}_{c2 * TCH}" for c2 in range(NCH)),
               writes=("xcd",))
        bb.tt(u_, delta, xcd, ALU.mult, reads=("delta", "xcd"), writes=("u_",))
        if stop_after in (41, 44, 45, 46):
            continue
        for n in range(NST if stop_after != 42 else 1):
            da_t = bb.act(dA, delta, AF.Exp, scale=negA[:, i * NST + n:i * NST + n + 1],
                          reads=("delta", "negA"), writes=("dA",))
            bb.dma("sync", Bb, b_s[n:n + 1, :].partition_broadcast(128).squeeze(1),
                   reads=("b_s",), writes=("Bb",))
            bb.dma("scalar", Cb, c_s[n:n + 1, :].partition_broadcast(128).squeeze(1),
                   reads=("c_s",), writes=("Cb",))
            db_t = bb.tt(dBu, u_, Bb, ALU.mult, reads=("u_", "Bb"), writes=("dBu",))
            sc_t = bb.scan(hsc, dA, dBu, 0.0, reads=("dA", "dBu"), writes=("hsc",),
                           deps=[da_t, db_t])
            if n == 0:
                bb.tt(yac, hsc, Cb, ALU.mult, reads=("hsc", "Cb"), writes=("yac",),
                      deps=[sc_t])
            else:
                hc_t = bb.tt(hsc, hsc, Cb, ALU.mult, reads=("hsc", "Cb"),
                             writes=("hsc",), deps=[sc_t])
                bb.tt(yac, yac, hsc, ALU.add, reads=("yac", "hsc"),
                      writes=("yac",), deps=[hc_t])
        if stop_after in (41, 42, 43, 44, 45, 46):
            continue
        y1 = bb.stt(yac, xcd, cols["Dp"][:, i:i + 1], yac, op0=ALU.mult,
                    op1=ALU.add, reads=("xcd", "Dp_c", "yac"), writes=("yac",))
        for c2 in range(NCH):
            sl = slice(c2 * TCH, (c2 + 1) * TCH)
            jz = c2 % 2
            zl = bb.dma("sync", zd[jz], z_s[i * 128:(i + 1) * 128, sl],
                        reads=(f"z_s_{i}_{sl.start}",), writes=(f"zd{jz}",))
            s_t = bb.act(zd[jz], zd[jz], AF.Silu, reads=(f"zd{jz}",),
                         writes=(f"zd{jz}",), deps=[zl])
            y2 = bb.tt(zd[jz], yac[:, sl], zd[jz], ALU.mult,
                       reads=("yac", f"zd{jz}"), writes=(f"zd{jz}",),
                       deps=[s_t, y1])
            bb.dma("gpsimd", ymam_s[i * 128:(i + 1) * 128, sl], zd[jz],
                   reads=(f"zd{jz}",), writes=(f"ymam_{i}_{c2}",), deps=[y2])
    if stop_after in (41, 42, 43, 44, 45, 46):
        return bb
    bb.p.barrier()
    AR.reset_front()
    AR.reset_back()

    if stop_after == 4:
        return bb
    WO = AR.take([128, NDT * D], BF16)
    for k in range(NDT):
        bb.dma("gpsimd", WO[:, k * D:(k + 1) * D],
               w["opT"][k * 128:(k + 1) * 128, :], writes=(f"WO{k}",))
    ymck = [AR.take([128, NDT, TCH], BF16) for _ in range(2)]
    h2ck = AR.take([128, NDT, TCH])
    xnck5 = [AR.take([128, NDT, TCH]), None]
    xnck5[1] = xnck5[0]
    osrc = [AR.take([128, TCH]) for _ in range(2)]
    lns5 = {k + jj: AR.take([128, 6 * TCH] if k == "rw" else [128, TCH])
            for k in ["sq", "rw", "xm", "ta"] for jj in ["0", "1"]}
    for ch in range(NCH):
        j = ch % 2
        sl = slice(ch * TCH, (ch + 1) * TCH)
        bb.dma("sync", ymck[j], ymam_s[:, sl].rearrange("(i p) c -> p i c", p=128),
               reads=tuple(f"ymam_{i}_{ch}" for i in range(NDT)),
               writes=(f"ymck{j}",))
        bb.dma("sync", xnck5[j], xn_s[:, sl].rearrange("(i p) c -> p i c", p=128),
               reads=tuple(f"xn_s_{i}_{sl.start}" for i in range(NDT)),
               writes=("xnck50",))
        for do in range(NDT):
            jm = do % 2
            mmt = bb.mm_acc(
                [(ps2[jm][:, :], WO[:, k * D + do * 128: k * D + (do + 1) * 128],
                  ymck[j][:, k, :]) for k in range(NDT)],
                reads=tuple(f"WO{k}" for k in range(NDT)) + (f"ymck{j}",),
                writes=(f"PS{3 + jm}",))
            bb.copy(h2ck[:, do, :], ps2[jm][:, :], reads=(f"PS{3 + jm}",),
                    writes=("h2ck",), deps=[mmt])

        def post5(i, ap, buf, sl=sl, j=j):
            a2 = bb.tt(ap, ap, xnck5[j][:, i, :], ALU.add,
                       reads=(buf, "xnck50"), writes=(buf,))
            bb.dma("scalar", src_s[i * 128:(i + 1) * 128, sl], ap, reads=(buf,),
                   writes=(f"src_{i}_{sl.start}",))

        outs = [{"sink": lambda i, j=j: (osrc[j], f"osrc{j}"), "mode": "gb",
                 "gamma": cols["mn_g"], "beta": cols["mn_b"],
                 "gbufs": ("mn_g_c", "mn_b_c"), "post": post5}]
        ln_chunked(bb, h2ck, "h2ck", outs, "ln5", ch, lns5)
    bb.p.barrier()
    AR.reset_front()

    if stop_after == 5:
        return bb
    W1 = AR.take([128, NDT * 4096], BF16)
    for k in range(NDT):
        bb.dma("gpsimd", W1[:, k * 4096:(k + 1) * 4096],
               w["w1T"][k * 128:(k + 1) * 128, :], writes=(f"W1_{k}",))
    sck = [AR.take([128, NDT, TCH]), None]
    sck[1] = sck[0]
    snb = [AR.take([128, NDT, TCH], BF16), None]
    snb[1] = snb[0]
    fh = AR.take([128, 32 * TCH], BF16)
    f32t = [AR.take([128, TCH]) for _ in range(2)]
    w2cb = [AR.take([128, 32, 128], BF16), None]
    w2cb[1] = w2cb[0]
    lns6 = _ln_scratch(bb, AR)
    osn = [AR.take([128, TCH], BF16) for _ in range(2)]
    for ch in range(NCH):
        j = ch % 2
        sl = slice(ch * TCH, (ch + 1) * TCH)
        bb.dma("sync", sck[j], src_s[:, sl].rearrange("(i p) c -> p i c", p=128),
               reads=tuple(f"src_{i}_{sl.start}" for i in range(NDT)),
               writes=("sck0",))
        outs = [{"sink": lambda i, j=j: (snb[j][:, i, :], "snb0"), "mode": "gb",
                 "gamma": cols["n2_g"], "beta": cols["n2_b"],
                 "gbufs": ("n2_g_c", "n2_b_c")}]
        ln_chunked(bb, sck[j], "sck0", outs, "ln6", ch, lns6)
        for m in range(32):
            jm = m % 2
            mmt = bb.mm_acc(
                [(ps2[jm][:, :], W1[:, k * 4096 + m * 128: k * 4096 + (m + 1) * 128],
                  snb[j][:, k, :]) for k in range(NDT)],
                reads=tuple(f"W1_{k}" for k in range(NDT)) + ("snb0",),
                writes=(f"PS{3 + jm}",))
            cpt = bb.copy(f32t[jm], ps2[jm][:, :], reads=(f"PS{3 + jm}",),
                          writes=(f"f32t{jm}",), deps=[mmt])
            bb.stt(fh[:, m * TCH:(m + 1) * TCH], f32t[jm], 0.01, f32t[jm],
                   op0=ALU.mult, op1=ALU.max, reads=(f"f32t{jm}",),
                   writes=(f"fh{m}",), deps=[cpt])
        for do in range(NDT):
            jo = do % 2
            bb.dma("gpsimd", w2cb[jo],
                   w["w2T"][:, do * 128:(do + 1) * 128].rearrange(
                       "(k p) c -> p k c", p=128),
                   writes=("w2cb0",))
            mmo = bb.mm_acc(
                [(bb.PS[5 + jo][:, :], w2cb[jo][:, k, :],
                  fh[:, k * TCH:(k + 1) * TCH]) for k in range(32)],
                reads=("w2cb0",) + tuple(f"fh{k}" for k in range(32)),
                writes=(f"PS{5 + jo}",))
            st = bb.tt(f32t[jo], bb.PS[5 + jo][:, :], sck[j][:, do, :], ALU.add,
                       reads=(f"PS{5 + jo}", "sck0"), writes=(f"f32t{jo}",),
                       deps=[mmo])
            bb.dma("scalar", src_s[do * 128:(do + 1) * 128, sl], f32t[jo],
                   reads=(f"f32t{jo}",), writes=(f"src_{do}_{sl.start}",))
    bb.p.barrier()
    AR.reset_front()

    if stop_after == 6:
        return bb
    srcfin_s = dram_dbg("srcfin_s", [D, T])
    FE = AR.take([128, NDT * D], BF16)
    for k in range(NDT):
        bb.dma("gpsimd", FE[:, k * D:(k + 1) * D],
               w["feT"][k * 128:(k + 1) * 128, :], writes=(f"FE{k}",))
    sck7 = AR.take([128, NDT, TCH])
    sn7 = AR.take([128, NDT, TCH], BF16)
    tmp7 = [AR.take([128, TCH]) for _ in range(2)]
    lns7 = _ln_scratch(bb, AR)
    for ch in range(NCH):
        sl = slice(ch * TCH, (ch + 1) * TCH)
        bb.dma("sync", sck7, src_s[:, sl].rearrange("(i p) c -> p i c", p=128),
               reads=tuple(f"src_{i}_{sl.start}" for i in range(NDT)),
               writes=("sck7",))
        outs = [{"sink": lambda i: (sn7[:, i, :], "sn7"), "mode": "gb",
                 "gamma": cols["n3_g"], "beta": cols["n3_b"],
                 "gbufs": ("n3_g_c", "n3_b_c")}]
        ln_chunked(bb, sck7, "sck7", outs, "ln7", ch, lns7)
        for do in range(NDT):
            jm = do % 2
            mmf = bb.mm_acc(
                [(ps2[jm][:, :], FE[:, k * D + do * 128: k * D + (do + 1) * 128],
                  sn7[:, k, :]) for k in range(NDT)],
                reads=tuple(f"FE{k}" for k in range(NDT)) + ("sn7",),
                writes=(f"PS{3 + jm}",))
            s1 = bb.stt(tmp7[jm], ps2[jm][:, :], cols["fe_b"][:, do:do + 1],
                        sck7[:, do, :], op0=ALU.add, op1=ALU.add,
                        reads=(f"PS{3 + jm}", "fe_b_c", "sck7"),
                        writes=(f"tmp7{jm}",), deps=[mmf])
            bb.dma("scalar", srcfin_s[do * 128:(do + 1) * 128, sl], tmp7[jm],
                   reads=(f"tmp7{jm}",), writes=(f"srcfin_{do}_{ch}",))
    bb.p.barrier()
    AR.reset_front()

    if stop_after == 7:
        return bb
    WR = AR.take([128, NDT * D], BF16)
    WK = AR.take([128, NDT * D], BF16)
    WV = AR.take([128, NDT * D], BF16)
    for k in range(NDT):
        bb.dma("gpsimd", WR[:, k * D:(k + 1) * D],
               w["WrT"][k * 128:(k + 1) * 128, :], writes=(f"WR{k}",))
        bb.dma("gpsimd", WK[:, k * D:(k + 1) * D],
               w["WkT"][k * 128:(k + 1) * 128, :], writes=(f"WK{k}",))
        bb.dma("gpsimd", WV[:, k * D:(k + 1) * D],
               w["WvT"][k * 128:(k + 1) * 128, :], writes=(f"WV{k}",))
    xnc = AR.take([128, NDT, TCH])
    xnp = AR.take([128, NDT, TCH])
    xrb = AR.take([128, NDT, TCH], BF16)
    xkb = AR.take([128, NDT, TCH], BF16)
    k2b = AR.take([128, NDT, TCH], BF16)
    sfc = AR.take([128, NDT, TCH])
    rsg = [AR.take([128, TCH]) for _ in range(2)]
    tmp8 = [AR.take([128, TCH]) for _ in range(2)]
    xres7 = [AR.take([128, TCH]) for _ in range(2)]
    for ch in range(NCH):
        sl = slice(ch * TCH, (ch + 1) * TCH)
        bb.dma("sync", xnc, xn_s[:, sl].rearrange("(i p) c -> p i c", p=128),
               reads=tuple(f"xn_s_{i}_{sl.start}" for i in range(NDT)),
               writes=("xnc",))
        bb.dma("sync", sfc, srcfin_s[:, sl].rearrange("(i p) c -> p i c", p=128),
               reads=tuple(f"srcfin_{i}_{ch}" for i in range(NDT)),
               writes=("sfc",))
        if ch == 0:
            bb.memset(xnp[:, :, 0:1], 0.0, writes=("xnp",))
            bb.dma("sync", xnp[:, :, 1:TCH],
                   xn_s[:, 0:TCH - 1].rearrange("(i p) c -> p i c", p=128),
                   reads=tuple(f"xn_s_{i}_0" for i in range(NDT)),
                   writes=("xnp",))
        else:
            bb.dma("sync", xnp,
                   xn_s[:, ch * TCH - 1:(ch + 1) * TCH - 1].rearrange(
                       "(i p) c -> p i c", p=128),
                   reads=tuple(f"xn_s_{i}_{(ch - 1) * TCH}" for i in range(NDT))
                   + tuple(f"xn_s_{i}_{sl.start}" for i in range(NDT)),
                   writes=("xnp",))
        for i in range(NDT):
            jj = i % 2
            t1 = bb.ts(tmp8[jj], xnp[:, i, :], om_r[:, i:i + 1], None,
                       reads=("xnp", "om_r"), writes=(f"tmp8{jj}",))
            bb.stt(xrb[:, i, :], xnc[:, i, :], cols["mu_r"][:, i:i + 1],
                   tmp8[jj], op0=ALU.mult, op1=ALU.add,
                   reads=("xnc", "mu_r_c", f"tmp8{jj}"), writes=("xrb",),
                   deps=[t1])
            t2 = bb.ts(tmp8[jj], xnp[:, i, :], om_k[:, i:i + 1], None,
                       reads=("xnp", "om_k"), writes=(f"tmp8{jj}",))
            bb.stt(xkb[:, i, :], xnc[:, i, :], cols["mu_k"][:, i:i + 1],
                   tmp8[jj], op0=ALU.mult, op1=ALU.add,
                   reads=("xnc", "mu_k_c", f"tmp8{jj}"), writes=("xkb",),
                   deps=[t2])
        for do in range(NDT):
            jm = do % 2
            mmk = bb.mm_acc(
                [(ps2[jm][:, :], WK[:, k * D + do * 128: k * D + (do + 1) * 128],
                  xkb[:, k, :]) for k in range(NDT)],
                reads=tuple(f"WK{k}" for k in range(NDT)) + ("xkb",),
                writes=(f"PS{3 + jm}",))
            rl = bb.ts(tmp8[jm], ps2[jm][:, :], 0.0, None, op0=ALU.max,
                       reads=(f"PS{3 + jm}",), writes=(f"tmp8{jm}",), deps=[mmk])
            bb.act(k2b[:, do, :], tmp8[jm], AF.Square, reads=(f"tmp8{jm}",),
                   writes=("k2b",), deps=[rl])
        for do in range(NDT):
            jm = do % 2
            bb.dma("sync", xres7[jm], xT[do * 128:(do + 1) * 128, sl],
                   writes=(f"xres7{jm}",))
            mmr = bb.mm_acc(
                [(ps2[jm][:, :], WR[:, k * D + do * 128: k * D + (do + 1) * 128],
                  xrb[:, k, :]) for k in range(NDT)],
                reads=tuple(f"WR{k}" for k in range(NDT)) + ("xrb",),
                writes=(f"PS{3 + jm}",))
            sg = bb.act(rsg[jm], ps2[jm][:, :], AF.Sigmoid,
                        reads=(f"PS{3 + jm}",), writes=(f"rsg{jm}",), deps=[mmr])
            mmv = bb.mm_acc(
                [(bb.PS[5 + jm][:, :], WV[:, k * D + do * 128: k * D + (do + 1) * 128],
                  k2b[:, k, :]) for k in range(NDT)],
                reads=tuple(f"WV{k}" for k in range(NDT)) + ("k2b",),
                writes=(f"PS{5 + jm}",))
            s3 = bb.tt(tmp8[jm], rsg[jm], bb.PS[5 + jm][:, :], ALU.mult,
                       reads=(f"rsg{jm}", f"PS{5 + jm}"), writes=(f"tmp8{jm}",),
                       deps=[mmv, sg])
            s4 = bb.tt(tmp8[jm], tmp8[jm], sfc[:, do, :], ALU.add,
                       reads=(f"tmp8{jm}", "sfc"), writes=(f"tmp8{jm}",),
                       deps=[s3])
            s5 = bb.tt(tmp8[jm], tmp8[jm], xres7[jm], ALU.add,
                       reads=(f"tmp8{jm}", f"xres7{jm}"), writes=(f"tmp8{jm}",),
                       deps=[s4])
            bb.dma("scalar", yT[do * 128:(do + 1) * 128, sl], tmp8[jm],
                   reads=(f"tmp8{jm}",), writes=(f"yT_{do}_{ch}",))
    return bb


def _b3_wmap(inputs):
    P = "L3_attn_"
    xp = inputs[P + "x_proj_w"]
    return {
        "inT": inputs[P + "in_proj_w"].T, "conv_w": inputs[P + "conv_w"],
        "conv_b": inputs[P + "conv_b"], "xp_dtT": xp[0:64].T,
        "xp_BT": xp[64:80].T, "xp_CT": xp[80:96].T,
        "dtT": inputs[P + "dt_proj_w"].T, "dt_b": inputs[P + "dt_proj_b"],
        "A_log": inputs[P + "A_log"], "Dp": inputs[P + "Dp"],
        "opT": inputs[P + "out_proj_w"].T, "mn_g": inputs[P + "mnorm_g"],
        "mn_b": inputs[P + "mnorm_b"], "n1_g": inputs[P + "n1_g"],
        "n1_b": inputs[P + "n1_b"], "n2_g": inputs[P + "n2_g"],
        "n2_b": inputs[P + "n2_b"], "n3_g": inputs[P + "n3_g"],
        "n3_b": inputs[P + "n3_b"], "w1T": inputs[P + "ffn_w1"].T,
        "w2T": inputs[P + "ffn_w2"].T, "feT": inputs[P + "fe_w"].T,
        "fe_b": inputs[P + "fe_b"], "WrT": inputs["L3_ffn_W_r"].T,
        "WkT": inputs["L3_ffn_W_k"].T, "WvT": inputs["L3_ffn_W_v"].T,
        "mu_r": inputs["L3_ffn_mu_r"], "mu_k": inputs["L3_ffn_mu_k"],
    }


def _b3_inmap(core, inputs, bb):
    m = {"xT": np.ascontiguousarray(inputs["x_in"][core].T)}
    for k, v in _b3_wmap(inputs).items():
        m["w_" + k] = np.ascontiguousarray(v)
    for cname, arr in bb.consts.items():
        m[cname] = arr
    return m


def run_block3(x_in, inputs):
    bb = build_block3()
    nc = bb.finalize()
    in_maps = [_b3_inmap(c, {**inputs, "x_in": x_in}, bb) for c in range(NCORE)]
    res = run_bass_kernel_spmd(nc, in_maps, core_ids=list(range(NCORE)))
    out = np.stack([res.results[c]["yT"].T for c in range(NCORE)])
    return np.ascontiguousarray(out)




def _l2_rope_tables():
    DH, W = 64, 256
    half = np.arange(0, DH, 2, dtype=np.float64)
    inv_freq = 1.0 / 10000.0 ** (half / DH)
    t = np.arange(T, dtype=np.float64)
    freqs = np.outer(t, inv_freq)
    base = (half + 0.4 * DH) / (1.4 * DH)
    power = (t - T // 2) / (W // 2)
    scale = base[None, :] ** power[:, None]
    cos, sin = np.cos(freqs), np.sin(freqs)
    CSQ = np.concatenate([cos * scale, cos * scale], 1) * 8.0
    SSQ = np.concatenate([-sin * scale, sin * scale], 1) * 8.0
    CIK = np.concatenate([cos / scale, cos / scale], 1)
    SIK = np.concatenate([-sin / scale, sin / scale], 1)
    return (CSQ.astype(np.float32), SSQ.astype(np.float32),
            CIK.astype(np.float32), SIK.astype(np.float32))


def _l2_masks():
    import ml_dtypes
    p = np.arange(128)[:, None]
    f = np.arange(256)[None, :]
    NEG = np.float32(-1e30)
    cur0 = np.where(f > p, NEG, 0.0).astype(np.float32)
    prev0 = np.where(f < p, NEG, 0.0).astype(np.float32)
    prev1 = np.where(f < 128 + p, NEG, 0.0).astype(np.float32)
    cur1 = np.where(f > 128 + p, NEG, 0.0).astype(np.float32)
    return {"m_cur0": cur0, "m_prev0": prev0, "m_prev1": prev1, "m_cur1": cur1}


def build_block2(stop_after=None, dbg=False, bb=None, xT=None, yT=None):
    if bb is None:
        bb = BB("b2")
        xT = bb.dram_in("xT", [D, T])
        yT = bb.dram_out("yT", [D, T])
    HH, DH, WIN = 16, 64, 256
    DI = 2730
    NHB = 22
    w = {}
    for nm, shp in [("log_a", [D]), ("WbT", [D, D]), ("WcT", [D, D]),
                    ("gam", [D]), ("bet", [D]),
                    ("Wqkv", [6, D, 3072]), ("qs", [6, DH]), ("ks", [6, DH]),
                    ("Wo", [6, 1024, D]), ("ln_g", [6, D]), ("ln_b", [6, D]),
                    ("W1", [6, D, 2 * DI]), ("b1", [6, 2 * DI]),
                    ("W2", [6, DI, D]), ("b2", [6, D])]:
        w[nm] = bb.dram_in("w_" + nm, shp)
    dram_dbg = bb.dram_out if dbg else bb.dram_tmp
    xn_s = dram_dbg("xn_s", [D, T])
    xnb_s = dram_dbg("xnb_s", [D, T], BF16)
    xl_s = [dram_dbg(f"xl{i}_s", [D, T]) for i in range(2)]
    xlb_s = [dram_dbg(f"xlb{i}_s", [D, T], BF16) for i in range(2)]

    setup_common(bb, ps7_bf16=True)
    AR = Arena(bb, 176 * 1024)
    CSQ, SSQ, CIK, SIK = _l2_rope_tables()
    ropes = {nm: bb.const("c_" + nm, v) for nm, v in
             [("CSQ", CSQ), ("SSQ", SSQ), ("CIK", CIK), ("SIK", SIK)]}
    masks = {}
    for nm, v in _l2_masks().items():
        ap = bb.sbuf(nm, [128, 256], F32)
        bb.dma("gpsimd", ap, bb.const("c_" + nm, v), writes=(nm,))
        masks[nm] = ap
    gamc = vec_to_cols(bb, w["gam"], D, "gam")
    betc = vec_to_cols(bb, w["bet"], D, "bet")
    la_c = vec_to_cols(bb, w["log_a"], D, "la")
    a_c = bb.sbuf("a_c", [128, NDT], F32)
    bb.act(a_c, la_c, AF.Exp, reads=("la_c",), writes=("a_c",))
    bb.p.barrier()

    xck = [AR.take([128, NDT, TCH]) for _ in range(2)]
    o_f32 = [AR.take([128, TCH]) for _ in range(2)]
    lns = _ln_scratch(bb, AR)
    for ch in range(NCH):
        j = ch % 2
        sl = slice(ch * TCH, (ch + 1) * TCH)
        bb.dma("sync", xck[j], xT[:, sl].rearrange("(i p) c -> p i c", p=128),
               writes=(f"xck{j}",))

        def post2(i, ap, buf, sl=sl):
            bb.dma("scalar", xn_s[i * 128:(i + 1) * 128, sl], ap, reads=(buf,),
                   writes=(f"xn_s_{i}_{sl.start}",))
            bb.dma("gpsimd", xnb_s[i * 128:(i + 1) * 128, sl], ap, reads=(buf,),
                   writes=(f"xnb_s_{i}_{sl.start}",))
            bb.dma("sync", xl_s[0][i * 128:(i + 1) * 128, sl], ap, reads=(buf,),
                   writes=("XL0",))
            bb.dma("gpsimd", xlb_s[0][i * 128:(i + 1) * 128, sl], ap, reads=(buf,),
                   writes=("XL0B",))

        outs = [{"sink": lambda i, j=j: (o_f32[j], f"of{j}"), "mode": "plain",
                 "post": post2}]
        ln_chunked(bb, xck[j], f"xck{j}", outs, "ln0", ch, lns)
    bb.p.barrier()
    AR.reset_front()
    if stop_after == 0:
        return bb

    ub_s = dram_dbg("ub_s", [D, T])
    WB = AR.take([128, NDT * D], BF16)
    WC = AR.take([128, NDT * D], BF16)
    for k in range(NDT):
        bb.dma("gpsimd", WB[:, k * D:(k + 1) * D],
               w["WbT"][k * 128:(k + 1) * 128, :], writes=(f"WB{k}",))
        bb.dma("gpsimd", WC[:, k * D:(k + 1) * D],
               w["WcT"][k * 128:(k + 1) * 128, :], writes=(f"WC{k}",))
    xnck = [AR.take([128, NDT, TCH], BF16) for _ in range(2)]
    ubc = [AR.take([128, TCH]) for _ in range(2)]
    ps2 = [bb.PS[3], bb.PS[4]]
    for ch in range(NCH):
        j = ch % 2
        sl = slice(ch * TCH, (ch + 1) * TCH)
        bb.dma("sync", xnck[j], xnb_s[:, sl].rearrange("(i p) c -> p i c", p=128),
               reads=tuple(f"xnb_s_{i}_{sl.start}" for i in range(NDT)),
               writes=(f"xnck{j}",))
        for i in range(NDT):
            jm = i % 2
            mmu = bb.mm_acc(
                [(ps2[jm][:, :], WB[:, k * D + i * 128: k * D + (i + 1) * 128],
                  xnck[j][:, k, :]) for k in range(NDT)],
                reads=tuple(f"WB{k}" for k in range(NDT)) + (f"xnck{j}",),
                writes=(f"PS{3 + jm}",))
            cpt = bb.copy(ubc[jm], ps2[jm][:, :], reads=(f"PS{3 + jm}",),
                          writes=(f"ubc{jm}",), deps=[mmu])
            bb.dma("scalar", ub_s[i * 128:(i + 1) * 128, sl], ubc[jm],
                   reads=(f"ubc{jm}",), writes=(f"ub_{i}_{ch}",))
    stb_s = dram_dbg("stb_s", [D, T], BF16)
    ub = AR.take([128, T])
    stbuf = AR.take([128, T])
    for i in range(NDT):
        bb.dma("sync", ub, ub_s[i * 128:(i + 1) * 128, :],
               reads=tuple(f"ub_{i}_{c2}" for c2 in range(NCH)), writes=("ub",))
        sc = bb.scan(stbuf, a_c[:, i:i + 1].broadcast_to((128, T)), ub, 0.0,
                     reads=("a_c", "ub"), writes=("stbuf",))
        bb.dma("gpsimd", stb_s[i * 128:(i + 1) * 128, :], stbuf,
               reads=("stbuf",), writes=(f"stb{i}",))
    yck = AR.take([128, NDT, TCH])
    xres = [AR.take([128, TCH]) for _ in range(2)]
    oln = [AR.take([128, TCH]) for _ in range(2)]
    stc = [AR.take([128, NDT, TCH], BF16) for _ in range(2)]
    lns1 = _ln_scratch(bb, AR)
    for ch in range(NCH):
        j = ch % 2
        sl = slice(ch * TCH, (ch + 1) * TCH)
        bb.dma("sync", stc[j], stb_s[:, sl].rearrange("(i p) c -> p i c", p=128),
               reads=tuple(f"stb{k}" for k in range(NDT)), writes=(f"stc{j}",))
        for do in range(NDT):
            jm = do % 2
            mmy = bb.mm_acc(
                [(ps2[jm][:, :], WC[:, k * D + do * 128: k * D + (do + 1) * 128],
                  stc[j][:, k, :]) for k in range(NDT)],
                reads=tuple(f"WC{k}" for k in range(NDT)) + (f"stc{j}",),
                writes=(f"PS{3 + jm}",))
            bb.copy(yck[:, do, :], ps2[jm][:, :], reads=(f"PS{3 + jm}",),
                    writes=("yck",), deps=[mmy])

        def postY(i, ap, buf, sl=sl, j=j):
            bb.dma("sync", xres[j], xT[i * 128:(i + 1) * 128, sl],
                   writes=(f"xres{j}",))
            bb.tt(ap, ap, xres[j], ALU.add, reads=(buf, f"xres{j}"), writes=(buf,))
            bb.dma("scalar", yT[i * 128:(i + 1) * 128, sl], ap, reads=(buf,),
                   writes=(f"yT_{i}_{ch}",))

        outs = [{"sink": lambda i, j=j: (oln[j], f"oln{j}"), "mode": "gb",
                 "gamma": gamc, "beta": betc, "gbufs": ("gam_c", "bet_c"),
                 "post": postY}]
        ln_chunked(bb, yck, "yck", outs, "ln1", ch, lns1)
    bb.p.barrier()
    AR.reset_front()
    if stop_after == 1:
        return bb

    PSBF = bb.PSBF
    for sub in range(6):
        src_x, src_xb = xl_s[0], xlb_s[0]
        dst_x, dst_xb = xl_s[1], xlb_s[1]
        sp = f"s{sub}_"
        WQKV = AR.take([128, NDT * 3072], BF16)
        WOs = AR.take([128, NDT * D], BF16)
        for k in range(NDT):
            bb.dma("gpsimd", WQKV[:, k * 3072:(k + 1) * 3072],
                   w["Wqkv"][sub, k * 128:(k + 1) * 128, :], writes=(sp + f"WQ{k}",))
            bb.dma("gpsimd", WOs[:, k * D:(k + 1) * D],
                   w["Wo"][sub, k * 128:(k + 1) * 128, :], writes=(sp + f"WO{k}",))
        qsr = AR.take([1, 2 * DH])
        bb.dma("sync", qsr[:, 0:DH], w["qs"][sub].unsqueeze(0), writes=("qsr",))
        bb.dma("sync", qsr[:, DH:2 * DH], w["ks"][sub].unsqueeze(0), writes=("qsr",))
        qs_b = AR.take([128, 2 * DH])
        mmq = bb.mm(bb.PS[0][:, 0:128], bb.ones_1x128, qsr,
                    reads=("qsr", "ones1x128"), writes=("PS0",))
        bb.copy(qs_b, bb.PS[0][:, 0:128], reads=("PS0",), writes=("qs_b",),
                deps=[mmq])
        xsb = [AR.take([128, NDT, WIN], BF16) for _ in range(2)]
        qkvt = AR.take([128, 2, 3072], BF16)
        ropet = AR.take([128, 2, 4 * DH])
        sq_ = AR.take([128, D])
        ssq = AR.take([128, 2 * HH])
        qn_ = AR.take([128, D])
        r1_ = AR.take([128, D])
        r2_ = AR.take([128, D])
        qkf = [AR.take([128, 16 * 128]) for _ in range(2)]
        qf_ = AR.take([128, 16 * 128])
        vtk = [AR.take([128, 2, 1024], BF16) for _ in range(2)]
        at_sb = AR.take([128, 512], BF16)
        dencol = AR.take([128, 8])
        attf = AR.take([128, NDT, WIN], BF16)
        xlc = [AR.take([128, NDT, WIN]) for _ in range(2)]
        och = [AR.take([128, WIN]) for _ in range(2)]
        ps_sim = [bb.PS[2], bb.PS[6]]
        ps_out = [bb.PS[5], bb.PS[5]]
        for wi in range(16):
            cur = wi % 2
            wsl = slice(wi * WIN, (wi + 1) * WIN)
            ld0 = bb.dma("sync", xsb[cur][:, 0:4, :],
                         src_xb[0:512, wsl].rearrange("(i p) t -> p i t", p=128),
                         reads=("XL0B",), writes=(sp + "xsb",))
            if wi == 0:
                bb.memset(xsb[cur][:, 4:8, 0:1], 0.0, writes=(sp + "xsb",))
                bb.dma("sync", xsb[cur][:, 4:8, 1:WIN],
                       src_xb[512:1024, 0:WIN - 1].rearrange(
                           "(i p) t -> p i t", p=128),
                       reads=("XL0B",), writes=(sp + "xsb",))
            else:
                bb.dma("sync", xsb[cur][:, 4:8, :],
                       src_xb[512:1024, wi * WIN - 1:(wi + 1) * WIN - 1].rearrange(
                           "(i p) t -> p i t", p=128),
                       reads=("XL0B",), writes=(sp + "xsb",))
            for tsb in range(2):
                tt0 = wi * WIN + tsb * 128
                bb.dma("scalar", ropet[:, tsb, 0:DH], ropes["CSQ"][tt0:tt0 + 128, :],
                       writes=(sp + "ropet",))
                bb.dma("scalar", ropet[:, tsb, DH:2 * DH],
                       ropes["SSQ"][tt0:tt0 + 128, :], writes=(sp + "ropet",))
                bb.dma("scalar", ropet[:, tsb, 2 * DH:3 * DH],
                       ropes["CIK"][tt0:tt0 + 128, :], writes=(sp + "ropet",))
                bb.dma("scalar", ropet[:, tsb, 3 * DH:4 * DH],
                       ropes["SIK"][tt0:tt0 + 128, :], writes=(sp + "ropet",))
            for tsb in range(2):
                for n2 in range(6):
                    jm = n2 % 2
                    mm = bb.mm_acc(
                        [(ps2[jm][:, :], xsb[cur][:, k, tsb * 128:(tsb + 1) * 128],
                          WQKV[:, k * 3072 + n2 * 512: k * 3072 + (n2 + 1) * 512])
                         for k in range(NDT)],
                        reads=(sp + "xsb",) + tuple(sp + f"WQ{k}" for k in range(NDT)),
                        writes=(f"PS{3 + jm}",))
                    bb.copy(qkvt[:, tsb, n2 * 512:(n2 + 1) * 512], ps2[jm][:, :],
                            reads=(f"PS{3 + jm}",), writes=(sp + "qkvt",), deps=[mm])
                bb.copy(vtk[cur][:, tsb, :], qkvt[:, tsb, 2048:3072],
                        reads=(sp + "qkvt",), writes=(sp + f"v{cur}",), eng="vector")
            for (qoff, tblc, tbls, scoff, dstf, dstbuf) in [
                    (0, 0, DH, 0, qf_, sp + "qf"),
                    (1024, 2 * DH, 3 * DH, DH, qkf[cur], sp + f"kf{cur}")]:
                for tsb in range(2):
                    qsl_ = qkvt[:, tsb, qoff:qoff + 1024]
                    s1 = bb.act(sq_, qsl_, AF.Square, reads=(sp + "qkvt",),
                                writes=(sp + "sq",))
                    s2 = bb.reduce(ssq[:, 0:HH],
                                   sq_.rearrange("p (h e) -> p h e", e=DH),
                                   ALU.add, reads=(sp + "sq",),
                                   writes=(sp + "ssq",), deps=[s1])
                    s3 = bb.act(ssq[:, HH:2 * HH], ssq[:, 0:HH], AF.Sqrt,
                                reads=(sp + "ssq",), writes=(sp + "ssq",), deps=[s2])
                    s4 = bb.ts(ssq[:, HH:2 * HH], ssq[:, HH:2 * HH], 1e-12, None,
                               op0=ALU.max, reads=(sp + "ssq",),
                               writes=(sp + "ssq",), deps=[s3])
                    s5 = bb.recip(ssq[:, HH:2 * HH], ssq[:, HH:2 * HH],
                                  reads=(sp + "ssq",), writes=(sp + "ssq",))
                    inv = ssq[:, HH:2 * HH].unsqueeze(2).broadcast_to((128, HH, DH))
                    n1 = bb.tt(qn_.rearrange("p (h e) -> p h e", e=DH),
                               qsl_.rearrange("p (h e) -> p h e", e=DH), inv,
                               ALU.mult, reads=(sp + "qkvt", sp + "ssq"),
                               writes=(sp + "qn",), deps=[s5])
                    qsb = qs_b[:, scoff:scoff + DH].unsqueeze(1).broadcast_to(
                        (128, HH, DH))
                    n2_ = bb.tt(qn_.rearrange("p (h e) -> p h e", e=DH),
                                qn_.rearrange("p (h e) -> p h e", e=DH), qsb,
                                ALU.mult, reads=(sp + "qn", "qs_b"),
                                writes=(sp + "qn",), deps=[n1])
                    ct = ropet[:, tsb, tblc:tblc + DH].unsqueeze(1).broadcast_to(
                        (128, HH, DH))
                    st_ = ropet[:, tsb, tbls:tbls + DH].unsqueeze(1).broadcast_to(
                        (128, HH, DH))
                    t1 = bb.tt(r1_.rearrange("p (h e) -> p h e", e=DH),
                               qn_.rearrange("p (h e) -> p h e", e=DH), ct,
                               ALU.mult, reads=(sp + "qn", sp + "ropet"),
                               writes=(sp + "r1",), deps=[n2_])
                    swp = qn_.rearrange("p (h two c) -> p h two c", two=2,
                                        c=DH // 2)[:, :, ::-1, :]
                    t2 = bb.tt(r2_.rearrange("p (h two c) -> p h two c", two=2,
                                             c=DH // 2),
                               swp, st_.rearrange("p h (two c) -> p h two c",
                                                  two=2),
                               ALU.mult, reads=(sp + "qn", sp + "ropet"),
                               writes=(sp + "r2",), deps=[n2_])
                    t3 = bb.tt(r1_, r1_, r2_, ALU.add,
                               reads=(sp + "r1", sp + "r2"), writes=(sp + "r1",),
                               deps=[t1, t2])
                    for bk in range(NDT):
                        jj = bk % 2
                        tr = bb.transpose(
                            bb.PS[jj][:, 128:256],
                            r1_[:, bk * 128:(bk + 1) * 128], bb.ident_f32,
                            reads=(sp + "r1", "ident_f32"), writes=(f"PS{jj}",))
                        bb.copy(dstf[:, (bk * 2 + tsb) * 128:(bk * 2 + tsb + 1) * 128],
                                bb.PS[jj][:, 128:256],
                                reads=(f"PS{jj}",), writes=(dstbuf,), deps=[tr])
            for h in range(HH):
                bk, rb = h // 2, (h % 2) * 64
                for tsb in range(2):
                    sj = (h * 2 + tsb) % 2
                    simb = "PS2" if sj == 0 else "PS6"
                    pairs = []
                    if wi > 0:
                        pairs.append((ps_sim[sj][:, 0:256],
                                      qf_[rb:rb + 64, (bk * 2 + tsb) * 128:
                                          (bk * 2 + tsb + 1) * 128],
                                      qkf[1 - cur][rb:rb + 64, bk * 256:(bk + 1) * 256]))
                        pairs.append((ps_sim[sj][:, 0:256], bb.ident_f32,
                                      masks["m_prev0" if tsb == 0 else "m_prev1"]))
                    pairs.append((ps_sim[sj][:, 256:512],
                                  qf_[rb:rb + 64, (bk * 2 + tsb) * 128:
                                      (bk * 2 + tsb + 1) * 128],
                                  qkf[cur][rb:rb + 64, bk * 256:(bk + 1) * 256]))
                    pairs.append((ps_sim[sj][:, 256:512], bb.ident_f32,
                                  masks["m_cur0" if tsb == 0 else "m_cur1"]))
                    def emit_sim(e, pairs=pairs):
                        ins = None
                        offs = [o.offset for (o, _, _) in pairs]
                        for i_, (o, l, r) in enumerate(pairs):
                            first = offs.index(o.offset) == i_
                            last = (len(offs) - 1 - offs[::-1].index(o.offset)) == i_
                            ins = e.matmul(o, l, r, start=first, stop=last)
                        return ins
                    rd = (sp + "qf", sp + f"kf{cur}", "ident_f32",
                          "m_cur0", "m_cur1", "m_prev0", "m_prev1")
                    if wi > 0:
                        rd = rd + (sp + f"kf{1 - cur}",)
                    mms = bb.p.task("tensor", emit_sim, reads=rd, writes=(simb,))
                    lo = 256 if wi == 0 else 0
                    mx = bb.reduce(dencol[:, sj:sj + 1], ps_sim[sj][:, lo:512],
                                   ALU.max, reads=(simb,),
                                   writes=(sp + "dencol",), negate=True,
                                   deps=[mms])
                    ex = bb.act(at_sb[:, lo:512], ps_sim[sj][:, lo:512], AF.Exp,
                                bias=dencol[:, sj:sj + 1],
                                accum_out=dencol[:, 2 + sj:3 + sj],
                                reads=(simb, sp + "dencol"),
                                writes=(sp + "at_sb",), deps=[mx])
                    rc = bb.recip(dencol[:, 2 + sj:3 + sj],
                                  dencol[:, 2 + sj:3 + sj],
                                  reads=(sp + "dencol",), writes=(sp + "dencol",))
                    nr = bb.ts(at_sb[:, lo:512], at_sb[:, lo:512],
                               dencol[:, 2 + sj:3 + sj], None,
                               reads=(sp + "at_sb", sp + "dencol"),
                               writes=(sp + "at_sb",), deps=[ex, rc])
                    nsb = (512 - lo) // 128
                    trs = []
                    for sb_ in range(nsb):
                        tr = bb.transpose(
                            PSBF[:, sj * 512 + sb_ * 128: sj * 512 + (sb_ + 1) * 128],
                            at_sb[:, lo + sb_ * 128: lo + (sb_ + 1) * 128],
                            bb.ident_bf, reads=(sp + "at_sb", "ident_bf"),
                            writes=("PS7",), deps=[nr])
                        trs.append(tr)
                    ev = bb.copy(at_sb[:, lo:512].bitcast(BF16)[:, 0:(512 - lo)]
                                 if False else at_sb[:, lo:512],
                                 PSBF[:, sj * 512: sj * 512 + (512 - lo)],
                                 reads=("PS7",), writes=(sp + "at_sb",),
                                 deps=trs, eng="vector")
                    avp = []
                    for sb_ in range(nsb):
                        vsrc = (vtk[1 - cur] if (wi > 0 and lo + sb_ * 128 < 256)
                                else vtk[cur])
                        vts = ((lo + sb_ * 128) % 256) // 128
                        avp.append((ps_out[sj][0:64, 0:128],
                                    vsrc[:, vts, h * 64:(h + 1) * 64],
                                    at_sb[:, lo + sb_ * 128:lo + (sb_ + 1) * 128]))
                    va_rd = (sp + "at_sb", sp + f"v{cur}")
                    if wi > 0:
                        va_rd = va_rd + (sp + f"v{1 - cur}",)
                    av = bb.mm_acc(avp, reads=va_rd, writes=("PS5",))
                    bb.copy(attf[rb:rb + 64, bk, tsb * 128:(tsb + 1) * 128],
                            ps_out[sj][0:64, 0:128], reads=("PS5",),
                            writes=(sp + "attf",), deps=[av])
            bb.dma("sync", xlc[cur], src_x[:, wsl].rearrange("(i p) t -> p i t", p=128),
                   reads=("XL0",), writes=(sp + "xlc",))
            for do in range(NDT):
                jm = do % 2
                mo = bb.mm_acc(
                    [(ps2[jm][:, 0:WIN], WOs[:, k * D + do * 128: k * D + (do + 1) * 128],
                      attf[:, k, :]) for k in range(NDT)],
                    reads=tuple(sp + f"WO{k}" for k in range(NDT)) + (sp + "attf",),
                    writes=(f"PS{3 + jm}",))
                ad = bb.tt(och[jm], ps2[jm][:, 0:WIN], xlc[cur][:, do, :], ALU.add,
                           reads=(f"PS{3 + jm}", sp + "xlc"), writes=(sp + f"och{jm}",),
                           deps=[mo])
                bb.dma("scalar", dst_x[do * 128:(do + 1) * 128, wsl], och[jm],
                       reads=(sp + f"och{jm}",), writes=("XL1",))
                bb.dma("gpsimd", dst_xb[do * 128:(do + 1) * 128, wsl], och[jm],
                       reads=(sp + f"och{jm}",), writes=("XL1B",))
        bb.p.barrier()
        AR.reset_front()
        if stop_after == 100 + sub:
            return bb

        f_x, f_xb = dst_x, dst_xb
        g_x, g_xb = src_x, src_xb
        W2s = AR.take([128, NHB * D], BF16)
        for k in range(NHB):
            hr = min(128, DI - k * 128)
            bb.dma("gpsimd", W2s[0:hr, k * D:(k + 1) * D],
                   w["W2"][sub, k * 128:k * 128 + hr, :], writes=(sp + f"W2_{k}",))
        b1row = AR.take([1, 2 * DI], BF16)
        bb.dma("gpsimd", b1row, w["b1"][sub].unsqueeze(0), writes=(sp + "b1row",))
        b2c = vec_to_cols(bb, w["b2"][sub], D, sp + "b2")
        gcl = vec_to_cols(bb, w["ln_g"][sub], D, sp + "gc")
        bcl = vec_to_cols(bb, w["ln_b"][sub], D, sp + "bc")
        w1cb = [AR.take([128, NDT, 128], BF16) for _ in range(2)]
        w1gb = [AR.take([128, NDT, 128], BF16) for _ in range(2)]
        xsf = [AR.take([128, NDT, TCH]) for _ in range(2)]
        xnsb = AR.take([128, NDT, TCH], BF16)
        AG = AR.take([128, NHB * TCH], BF16)
        gg = AR.take([128, TCH])
        lnsF = _ln_scratch(bb, AR)
        xrs = [AR.take([128, TCH]) for _ in range(2)]
        ochf = [AR.take([128, TCH]) for _ in range(2)]
        for ch in range(NCH):
            j = ch % 2
            sl = slice(ch * TCH, (ch + 1) * TCH)
            ldx = bb.dma("sync", xsf[j][:, 0:4, :],
                         f_x[0:512, sl].rearrange("(i p) t -> p i t", p=128),
                         reads=("XL1",), writes=(sp + f"xsf{j}",))
            if ch == 0:
                bb.memset(xsf[j][:, 4:8, 0:1], 0.0, writes=(sp + f"xsf{j}",))
                bb.dma("sync", xsf[j][:, 4:8, 1:TCH],
                       f_x[512:1024, 0:TCH - 1].rearrange("(i p) t -> p i t", p=128),
                       reads=("XL1",), writes=(sp + f"xsf{j}",))
            else:
                bb.dma("sync", xsf[j][:, 4:8, :],
                       f_x[512:1024, ch * TCH - 1:(ch + 1) * TCH - 1].rearrange(
                           "(i p) t -> p i t", p=128),
                       reads=("XL1",), writes=(sp + f"xsf{j}",))
            outs = [{"sink": lambda i, j=j: (xnsb[:, i, :], sp + "xnsb"),
                     "mode": "gb", "gamma": gcl, "beta": bcl,
                     "gbufs": (sp + "gc_c", sp + "bc_c")}]
            ln_chunked(bb, xsf[j], sp + f"xsf{j}", outs, sp + "lnF", ch, lnsF)
            for m in range(NHB):
                hr = min(128, DI - m * 128)
                jm = m % 2
                bb.dma("gpsimd", w1cb[jm][:, :, 0:hr],
                       w["W1"][sub][:, m * 128:m * 128 + hr].rearrange(
                           "(k p) c -> p k c", p=128),
                       writes=(sp + f"w1a{jm}",))
                pa = [(ps2[0][0:hr, :], w1cb[jm][:, k, 0:hr], xnsb[:, k, :])
                      for k in range(NDT)]
                pa.append((ps2[0][0:hr, :], b1row[:, m * 128:m * 128 + hr],
                           bb.ones_1x512_bf))
                ma = bb.mm_acc(pa, reads=(sp + f"w1a{jm}", sp + "xnsb",
                                          "ones1x512bf", sp + "b1row"),
                               writes=("PS3",))
                w1g = w1gb[jm]
                bb.dma("gpsimd", w1g[:, :, 0:hr],
                       w["W1"][sub][:, DI + m * 128:DI + m * 128 + hr].rearrange(
                           "(k p) c -> p k c", p=128),
                       writes=(sp + f"w1g{jm}",))
                pg = [(ps2[1][0:hr, :], w1g[:, k, 0:hr], xnsb[:, k, :])
                      for k in range(NDT)]
                pg.append((ps2[1][0:hr, :], b1row[:, DI + m * 128:DI + m * 128 + hr],
                           bb.ones_1x512_bf))
                mg = bb.mm_acc(pg, reads=(sp + f"w1g{jm}", sp + "xnsb",
                                          "ones1x512bf", sp + "b1row"),
                               writes=("PS4",))
                ge = bb.act(gg[0:hr, :], ps2[1][0:hr, :], AF.Gelu,
                            reads=("PS4",), writes=(sp + "gg",), deps=[mg])
                bb.tt(AG[0:hr, m * TCH:(m + 1) * TCH], ps2[0][0:hr, :],
                      gg[0:hr, :], ALU.mult,
                      reads=("PS3", sp + "gg"), writes=(sp + f"AG{m}",),
                      deps=[ma, ge])
            for do in range(NDT):
                jm = do % 2
                m2 = bb.mm_acc(
                    [(bb.PS[5 + jm][:, :], W2s[0:min(128, DI - k * 128),
                                               k * D + do * 128: k * D + (do + 1) * 128],
                      AG[0:min(128, DI - k * 128), k * TCH:(k + 1) * TCH])
                     for k in range(NHB)],
                    reads=tuple(sp + f"W2_{k}" for k in range(NHB))
                    + tuple(sp + f"AG{k}" for k in range(NHB)),
                    writes=(f"PS{5 + jm}",))
                bb.dma("sync", xrs[jm], f_x[do * 128:(do + 1) * 128, sl],
                       reads=("XL1",), writes=(sp + f"xrs{jm}",))
                s1 = bb.stt(ochf[jm], bb.PS[5 + jm][:, :], b2c[:, do:do + 1],
                            xrs[jm], op0=ALU.add, op1=ALU.add,
                            reads=(f"PS{5 + jm}", sp + "b2_c", sp + f"xrs{jm}"),
                            writes=(sp + f"ochf{jm}",), deps=[m2])
                bb.dma("scalar", g_x[do * 128:(do + 1) * 128, sl], ochf[jm],
                       reads=(sp + f"ochf{jm}",), writes=("XL0",))
                bb.dma("gpsimd", g_xb[do * 128:(do + 1) * 128, sl], ochf[jm],
                       reads=(sp + f"ochf{jm}",), writes=("XL0B",))
        bb.p.barrier()
        AR.reset_front()
        if stop_after == 200 + sub:
            return bb

    fin = xl_s[0]
    fck = [AR.take([128, NDT, TCH]) for _ in range(2)]
    yck2 = [AR.take([128, NDT, TCH]) for _ in range(2)]
    for ch in range(NCH):
        j = ch % 2
        sl = slice(ch * TCH, (ch + 1) * TCH)
        bb.dma("sync", fck[j], fin[:, sl].rearrange("(i p) t -> p i t", p=128),
               reads=("XL0",), writes=(f"fck{j}",))
        bb.dma("sync", yck2[j], yT[:, sl].rearrange("(i p) t -> p i t", p=128),
               reads=tuple(f"yT_{i}_{ch}" for i in range(NDT)),
               writes=(f"yck2{j}",))
        for i in range(NDT):
            ad = bb.tt(yck2[j][:, i, :], yck2[j][:, i, :], fck[j][:, i, :],
                       ALU.add, reads=(f"yck2{j}", f"fck{j}"),
                       writes=(f"yck2{j}",))
            bb.dma("scalar", yT[i * 128:(i + 1) * 128, sl], yck2[j][:, i, :],
                   reads=(f"yck2{j}",), writes=(f"yTf_{i}_{ch}",))
    return bb


def _b2_wmap(inputs):
    return {
        "log_a": inputs["L2_attn_log_a"], "WbT": inputs["L2_attn_W_b"].T,
        "WcT": inputs["L2_attn_W_c"].T, "gam": inputs["L2_attn_gamma"],
        "bet": inputs["L2_attn_beta"], "Wqkv": inputs["L2_ffn_Wqkv"],
        "qs": inputs["L2_ffn_q_scale"], "ks": inputs["L2_ffn_k_scale"],
        "Wo": inputs["L2_ffn_Wo"], "ln_g": inputs["L2_ffn_ln_g"],
        "ln_b": inputs["L2_ffn_ln_b"], "W1": inputs["L2_ffn_W1"],
        "b1": inputs["L2_ffn_b1"], "W2": inputs["L2_ffn_W2"],
        "b2": inputs["L2_ffn_b2"],
    }


def _b2_inmap(core, inputs, bb):
    m = {"xT": np.ascontiguousarray(inputs["x_in"][core].T)}
    for k, v in _b2_wmap(inputs).items():
        m["w_" + k] = np.ascontiguousarray(v)
    for cname, arr in bb.consts.items():
        m[cname] = arr
    return m


def run_block2(x_in, inputs, stop_after=None):
    bb = build_block2(stop_after=stop_after)
    nc = bb.finalize()
    in_maps = [_b2_inmap(c, {**inputs, "x_in": x_in}, bb) for c in range(NCORE)]
    res = run_bass_kernel_spmd(nc, in_maps, core_ids=list(range(NCORE)))
    out = np.stack([res.results[c]["yT"].T for c in range(NCORE)])
    return np.ascontiguousarray(out)



def _b1_wmap(inputs):
    names = {"ln_g": "L1_attn_ln_g", "ln_b": "L1_attn_ln_b", "Wh": "L1_attn_Wh",
             "bh": "L1_attn_bh", "Wqk": "L1_attn_Wqk", "bqk": "L1_attn_bqk",
             "gam": "L1_attn_gamma", "bet": "L1_attn_beta", "Wo": "L1_attn_Wo",
             "bo": "L1_attn_bo", "W1": "L1_ffn_W1", "b1": "L1_ffn_b1",
             "W2": "L1_ffn_W2", "b2": "L1_ffn_b2"}
    return {k: inputs[s] for k, s in names.items()}


def _b1_inmap(core, inputs, bb):
    m = {"xT": np.ascontiguousarray(inputs["x_in"][core].T)}
    for k, v in _b1_wmap(inputs).items():
        m["w_" + k] = np.ascontiguousarray(v)
    for cname, arr in bb.consts.items():
        m[cname] = arr
    return m


def run_block1(x_in, inputs):
    bb = build_block1()
    nc = bb.finalize()
    in_maps = [_b1_inmap(c, {**inputs, "x_in": x_in}, bb) for c in range(NCORE)]
    res = run_bass_kernel_spmd(nc, in_maps, core_ids=list(range(NCORE)))
    out = np.stack([res.results[c]["yT"].T for c in range(NCORE)])
    return np.ascontiguousarray(out)


def build_fused():
    bb = BB("fused")
    x0T = bb.dram_in("xT", [D, T])
    x1T = bb.dram_tmp("x1T", [D, T])
    x2T = bb.dram_tmp("x2T", [D, T])
    x3T = bb.dram_tmp("x3T", [D, T])
    yT = bb.dram_out("yT", [D, T])
    setup_common(bb)
    for pfx, fn, xi, yo in [("B0_", build_block0, x0T, x1T),
                            ("B1_", build_block1, x1T, x2T),
                            ("B2_", build_block2, x2T, x3T),
                            ("B3_", build_block3, x3T, yT)]:
        bb._pfx = pfx
        fn(bb=bb, xT=xi, yT=yo)
        bb.p.barrier()
        bb.sm_reset()
    bb._pfx = ""
    return bb


def _fused_inmap(core, inputs, bb):
    m = {"xT": np.ascontiguousarray(inputs["x_in"][core].T)}
    for pfx, wf in [("B0_", _b0_wmap), ("B1_", _b1_wmap), ("B2_", _b2_wmap),
                    ("B3_", _b3_wmap)]:
        for k, v in wf(inputs).items():
            m[pfx + "w_" + k] = np.ascontiguousarray(v)
    for cname, arr in bb.consts.items():
        m[cname] = arr
    return m


_FUSED_CACHE = {}


def kernel(**inputs):
    x = np.ascontiguousarray(inputs["L0_attn_x"], dtype=np.float32)
    if "bb" not in _FUSED_CACHE:
        bbf = build_fused()
        ncf = bbf.finalize()
        _FUSED_CACHE["bb"] = bbf
        _FUSED_CACHE["nc"] = ncf
    bbf, ncf = _FUSED_CACHE["bb"], _FUSED_CACHE["nc"]
    in_maps = [_fused_inmap(c, {**inputs, "x_in": x}, bbf) for c in range(NCORE)]
    res = run_bass_kernel_spmd(ncf, in_maps, core_ids=list(range(NCORE)))
    out = np.stack([res.results[c]["yT"].T for c in range(NCORE)])
    return np.ascontiguousarray(out)


# version 05
# speedup vs baseline: 1.1976x; 1.1976x over previous
import contextlib
import numpy as np
import concourse.bass as bass
import concourse.mybir as mybir
from concourse.bass_utils import run_bass_kernel_spmd

F32 = mybir.dt.float32
BF16 = mybir.dt.bfloat16
AF = mybir.ActivationFunctionType
ALU = mybir.AluOpType
AX = mybir.AxisListType

NCORE = 8
T = 4096
D = 1024
NDT = D // 128
TCH = 512
NCH = T // TCH



class Prog:

    ENG = ("tensor", "vector", "scalar", "gpsimd", "sync")
    POOL = 16
    PARANOID = False

    def __init__(self):
        self.tasks = {e: [] for e in self.ENG}
        self.writer = {}
        self.readers = {}
        self.floor = {e: 0 for e in self.ENG}
        self.dma_count = {e: 0 for e in self.ENG}
        self.dma_ord = {}

    def task(self, engine, emit, reads=(), writes=(), deps=(), dma=False,
             fence=False):
        my_idx = len(self.tasks[engine])
        dd = {}

        def add(pe, pidx):
            p_dma = self.tasks[pe][pidx][3]
            p_fence = self.tasks[pe][pidx][4]
            if (pe == engine and not p_dma and not dma and not p_fence
                    and not Prog.PARANOID):
                return
            cur = dd.get(pe, -1)
            if pidx > cur and not p_dma:
                dd[pe] = pidx
            elif p_dma:
                dd.setdefault(("D", pe), set()).add(pidx) if False else None
                key = ("D", pe, self.dma_ord[(pe, pidx)] % Prog.POOL)
                cur2 = dd.get(key, -1)
                if pidx > cur2:
                    dd[key] = pidx

        for e, i in deps:
            add(e, i)
        for b in reads:
            w = self.writer.get(b)
            if w:
                add(w[0], w[1])
            if b.startswith("PS"):
                for r in self.readers.get(b, ()):
                    add(r[0], r[1])
        for b in writes:
            w = self.writer.get(b)
            if w:
                add(w[0], w[1])
            for r in self.readers.get(b, ()):
                add(r[0], r[1])
        for e in self.ENG:
            if e != engine and self.floor[e] > 0:
                add(e, self.floor[e] - 1)
        self.tasks[engine].append([emit, dd, 16 if dma else 1, dma, fence])
        if dma:
            self.dma_ord[(engine, my_idx)] = self.dma_count[engine]
            self.dma_count[engine] += 1
        for b in reads:
            self.readers.setdefault(b, []).append((engine, my_idx, dma))
        for b in writes:
            self.writer[b] = (engine, my_idx, dma)
            self.readers[b] = []
        return (engine, my_idx)

    def barrier(self):
        for e in self.ENG:
            self.floor[e] = len(self.tasks[e])

    def emit(self, nc, block, sems, dpools):
        cum = {}
        for e in self.ENG:
            c, lst = 0, []
            for t in self.tasks[e]:
                c += 0 if t[3] else 1
                lst.append(c)
            cum[e] = lst

        def mk(e):
            def body(eng):
                waited = {}
                for idx, (emit_fn, dd, wgt, is_dma, _f) in enumerate(self.tasks[e]):
                    for key in sorted(dd, key=str):
                        pidx = dd[key]
                        if isinstance(key, tuple):
                            _, pe, slot = key
                            o = self.dma_ord[(pe, pidx)]
                            sem = dpools[pe][o % Prog.POOL]
                            val = 16 * (o // Prog.POOL + 1)
                        else:
                            pe = key
                            if self.tasks[pe][pidx][3]:
                                continue
                            sem = sems[pe]
                            val = cum[pe][pidx]
                        wk = id(sem)
                        if waited.get(wk, -1) < val:
                            eng.wait_ge(sem, val)
                            waited[wk] = val
                    if is_dma:
                        o = self.dma_ord[(e, idx)]
                        sem = dpools[e][o % Prog.POOL]
                        if o >= Prog.POOL:
                            gen = o // Prog.POOL
                            wk = id(sem)
                            if waited.get(wk, -1) < 16 * gen:
                                eng.wait_ge(sem, 16 * gen)
                                waited[wk] = 16 * gen
                        ins = emit_fn(eng)
                        assert ins is not None
                        ins.then_inc(sem, 16)
                    else:
                        ins = emit_fn(eng)
                        assert ins is not None
                        ins.then_inc(sems[e], wgt)
            return body

        for e in self.ENG:
            getattr(block, e)(mk(e))

    def final_waits(self):
        return [(e, len(self.tasks[e]) - 1) for e in self.ENG if self.tasks[e]]


class BB:
    def __init__(self, name):
        self.nc = bass.Bass("TRN2", target_bir_lowering=False)
        self.name = name
        self.es = contextlib.ExitStack()
        self.p = Prog()
        self.consts = {}
        self._cap = {}
        self._sb = {}

    def dram_in(self, name, shape, dtype=F32):
        name = getattr(self, "_pfx", "") + name
        return self.nc.dram_tensor(name, list(shape), dtype, kind="ExternalInput").ap()

    def dram_out(self, name, shape, dtype=F32):
        name = getattr(self, "_pfx", "") + name
        return self.nc.dram_tensor(name, list(shape), dtype, kind="ExternalOutput").ap()

    def dram_tmp(self, name, shape, dtype=F32):
        name = getattr(self, "_pfx", "") + name
        return self.nc.dram_tensor(name, list(shape), dtype, kind="Internal").ap()

    def finalize(self):
        nc = self.nc
        with contextlib.ExitStack() as es2:
            sems = {}
            for e in Prog.ENG:
                sems[e] = es2.enter_context(nc.semaphore(f"s_{e}"))
            dpools = {}
            for e in Prog.ENG:
                if self.p.dma_count[e] > 0:
                    n = min(Prog.POOL, max(1, self.p.dma_count[e]))
                    n = Prog.POOL
                    dpools[e] = [es2.enter_context(nc.semaphore(f"d_{e}{i}"))
                                 for i in range(n)]
            block = es2.enter_context(nc.Block())
            fw = self.p.final_waits()
            self.p.task("sync", lambda e: e.nop(), deps=fw)
            self.p.emit(nc, block, sems, dpools)
        self.es.close()
        return nc


_SB_GLOBALS = {"ident_f32", "ident_bf", "ones1x128", "ones128x1", "ones1x128bf",
               "ones1x512bf", "kbuf", "ARENA", "SMALLS"}


def _bb_sbuf(self, name, shape, dtype=F32):
    if name in self._sb:
        return self._sb[name]
    if name in _SB_GLOBALS:
        t = self.es.enter_context(self.nc.sbuf_tensor(name, list(shape), dtype))
        ap = t[:]
        self._sb[name] = ap
        return ap
    if not hasattr(self, "_smbase"):
        self._smbase = self.sbuf("SMALLS", [128, 24 * 1024], mybir.dt.uint8)
        self._smoff = 0
        self._smnames = set()
    esz = {F32: 4, BF16: 2}[dtype]
    nb = int(np.prod(shape[1:])) * esz
    nb = (nb + 127) // 128 * 128
    off = self._smoff
    self._smoff += nb
    assert self._smoff <= 24 * 1024, f"SMALLS overflow at {name}"
    v = self._smbase[:, off:off + nb].bitcast(dtype)
    v = v[:, 0:int(np.prod(shape[1:]))]
    if len(shape) == 3:
        v = v.rearrange("p (a b) -> p a b", b=shape[2])
    if shape[0] != 128:
        v = v[0:shape[0]]
    self._sb[name] = v
    self._smnames.add(name)
    return v


def _bb_sm_reset(self):
    if hasattr(self, "_smbase"):
        for n in self._smnames:
            self._sb.pop(n, None)
        self._smnames = set()
        self._smoff = 0


BB.sm_reset = _bb_sm_reset


def _bb_psum(self, name, shape, dtype=F32):
    if name in self._sb:
        return self._sb[name]
    t = self.es.enter_context(self.nc.psum_tensor(name, list(shape), dtype))
    ap = t[:]
    self._sb[name] = ap
    return ap


def _bb_const(self, name, arr):
    if name not in self.consts:
        self.consts[name] = arr
        import ml_dtypes
        dt = BF16 if arr.dtype == ml_dtypes.bfloat16 else F32
        self._cap[name] = self.nc.dram_tensor(name, list(arr.shape), dt,
                                              kind="ExternalInput").ap()
    return self._cap[name]


BB.sbuf = _bb_sbuf
BB.psum = _bb_psum
BB.const = _bb_const


def _mk(name):
    def f(self, *a, **kw):
        return getattr(self, "_" + name)(*a, **kw)
    return f


def _bb_dma(self, q, out, in_, reads=(), writes=(), deps=(), accum=False):
    if accum:
        return self.p.task(
            q, lambda e: e.dma_start(out=out, in_=in_, accum_op=ALU.add),
            reads=reads, writes=writes, deps=deps, dma=True)
    return self.p.task(q, lambda e: e.dma_start(out=out, in_=in_),
                       reads=reads, writes=writes, deps=deps, dma=True)


def _bb_mm(self, out, lhsT, rhs, start=True, stop=True, reads=(), writes=(), deps=()):
    return self.p.task(
        "tensor", lambda e: e.matmul(out, lhsT, rhs, start=start, stop=stop),
        reads=reads, writes=writes, deps=deps)


def _bb_mm_acc(self, pairs, reads=(), writes=(), deps=()):
    def emit(e):
        n = len(pairs)
        ins = None
        for i, (o, l, r) in enumerate(pairs):
            ins = e.matmul(o, l, r, start=(i == 0), stop=(i == n - 1))
        return ins
    return self.p.task("tensor", emit, reads=reads, writes=writes, deps=deps)


def _bb_mm_multi(self, triples, start, stop, reads=(), writes=(), deps=()):
    def emit(e):
        ins = None
        for (o, l, r) in triples:
            ins = e.matmul(o, l, r, start=start, stop=stop)
        return ins
    return self.p.task("tensor", emit, reads=reads, writes=writes, deps=deps)


def _bb_act(self, out, in_, func, bias=None, scale=None, accum_out=None,
            reads=(), writes=(), deps=()):
    kw = {}
    if bias is not None:
        kw["bias"] = bias
    if scale is not None:
        kw["scale"] = scale
    if accum_out is not None:
        kw["accum_out"] = accum_out
    return self.p.task("scalar", lambda e: e.activation(out, in_, func, **kw),
                       reads=reads, writes=writes, deps=deps)


def _bb_tt(self, out, a, b, op, reads=(), writes=(), deps=(), eng="vector"):
    return self.p.task(eng, lambda e: e.tensor_tensor(out, a, b, op=op),
                       reads=reads, writes=writes, deps=deps)


def _bb_ts(self, out, a, s1, s2=None, op0=ALU.mult, op1=None, reads=(), writes=(),
           deps=(), eng="vector"):
    if op1 is None:
        return self.p.task(eng, lambda e: e.tensor_scalar(out, a, s1, None, op0=op0),
                           reads=reads, writes=writes, deps=deps)
    return self.p.task(
        eng, lambda e: e.tensor_scalar(out, a, s1, s2, op0=op0, op1=op1),
        reads=reads, writes=writes, deps=deps)


def _bb_stt(self, out, in0, scalar, in1, op0, op1, reads=(), writes=(), deps=(),
            eng="vector"):
    return self.p.task(
        eng,
        lambda e: e.scalar_tensor_tensor(out, in0, scalar, in1, op0=op0, op1=op1),
        reads=reads, writes=writes, deps=deps)


def _bb_copy(self, out, in_, reads=(), writes=(), deps=(), eng="scalar"):
    if eng == "scalar":
        return self.p.task(eng, lambda e: e.copy(out, in_),
                           reads=reads, writes=writes, deps=deps)
    return self.p.task(eng, lambda e: e.tensor_copy(out, in_),
                       reads=reads, writes=writes, deps=deps)


def _bb_recip(self, out, in_, reads=(), writes=(), deps=()):
    return self.p.task("vector", lambda e: e.reciprocal(out, in_),
                       reads=reads, writes=writes, deps=deps, fence=True)


def _bb_transpose(self, out, in_, ident, reads=(), writes=(), deps=()):
    return self.p.task("tensor", lambda e: e.transpose(out, in_, ident),
                       reads=reads, writes=writes, deps=deps)


def _bb_memset(self, ap, val, reads=(), writes=(), deps=(), eng="vector"):
    return self.p.task(eng, lambda e: e.memset(ap, val),
                       reads=reads, writes=writes, deps=deps, fence=True)


def _bb_reduce(self, out, in_, op, reads=(), writes=(), deps=(), negate=None):
    return self.p.task(
        "vector", lambda e: e.tensor_reduce(out, in_, axis=AX.X, op=op, negate=negate),
        reads=reads, writes=writes, deps=deps)


def _bb_scan(self, out, a, b, init, reads=(), writes=(), deps=()):
    return self.p.task(
        "vector", lambda e: e.tensor_tensor_scan(out, a, b, init, ALU.mult, ALU.add),
        reads=reads, writes=writes, deps=deps)


for _n in ["dma", "mm", "mm_acc", "mm_multi", "act", "tt", "ts", "stt", "copy",
           "recip", "transpose", "memset", "reduce", "scan"]:
    setattr(BB, _n, globals()["_bb_" + _n])

class Arena:

    def __init__(self, bb, bytes_pp, name="ARENA"):
        self.bb = bb
        self.base = bb.sbuf(name, [128, bytes_pp], mybir.dt.uint8)
        self.front = 0
        self.back = bytes_pp
        self.hwm = 0

    def take(self, shape, dtype=F32, back=False):
        esz = 4 if dtype == F32 else 2
        n = int(np.prod(shape[1:])) * esz
        n = (n + 127) // 128 * 128
        if back:
            self.back -= n
            off = self.back
        else:
            off = self.front
            self.front += n
        assert self.front <= self.back, \
            f"arena overflow front={self.front} back={self.back}"
        self.hwm = max(self.hwm, self.front + (self.base.shape[1] - self.back))
        v = self.base[:, off:off + n].bitcast(dtype)
        if len(shape) == 2:
            v = v[:, 0:shape[1]] if v.shape[1] != shape[1] else v
        else:
            v = v[:, 0:int(np.prod(shape[1:]))]
            v = v.rearrange("p (a b) -> p a b", b=shape[2]) if len(shape) == 3 else v
        if shape[0] != 128:
            v = v[0:shape[0]]
        return v

    def reset_front(self):
        self.front = 0

    def reset_back(self):
        self.back = self.base.shape[1]


_I128_F32 = np.eye(128, dtype=np.float32)


def _bb_kcol(self, val, rows=128):
    if not hasattr(self, "_kvals"):
        self._kvals = {}
        self._kbuf = self.sbuf("kbuf", [128, 16], F32)
    if val not in self._kvals:
        i = len(self._kvals)
        assert i < 16
        self.memset(self._kbuf[:, i:i + 1], float(val), writes=("kbuf",))
        self._kvals[val] = i
    i = self._kvals[val]
    return self._kbuf[0:rows, i:i + 1]


BB.kcol = _bb_kcol


def setup_common(bb, ps7_bf16=False):
    if getattr(bb, "_setup_done", False):
        return
    bb._setup_done = True
    bb.ident_f32 = bb.sbuf("ident_f32", [128, 128], F32)
    bb.ones_1x128 = bb.sbuf("ones1x128", [1, 128], F32)
    bb.ones_128x1 = bb.sbuf("ones128x1", [128, 1], F32)
    bb.ones_1x128_bf = bb.sbuf("ones1x128bf", [1, 128], BF16)
    bb.PS = [bb.psum(f"PS{i}", [128, 512], F32) for i in range(7)]
    psbf = bb.psum("PS7", [128, 1024], BF16)
    bb.PSBF = psbf
    bb.PS.append(psbf.bitcast(F32))
    import ml_dtypes
    bb.ident_bf = bb.sbuf("ident_bf", [128, 128], BF16)
    bb.dma("gpsimd", bb.ident_bf,
           bb.const("c_ident_bf", _I128_F32.astype(ml_dtypes.bfloat16)),
           writes=("ident_bf",))
    bb.dma("gpsimd", bb.ident_f32, bb.const("c_ident_f32", _I128_F32),
           writes=("ident_f32",))
    bb.memset(bb.ones_1x128, 1.0, writes=("ones1x128",))
    bb.memset(bb.ones_128x1, 1.0, writes=("ones128x1",))
    bb.memset(bb.ones_1x128_bf, 1.0, writes=("ones1x128bf",))
    bb.ones_1x512_bf = bb.sbuf("ones1x512bf", [1, 512], BF16)
    bb.memset(bb.ones_1x512_bf, 1.0, writes=("ones1x512bf",))


def vec_to_cols(bb, dram_vec, length, tag):
    m = length // 128
    assert m * 128 == length and 1 <= m <= 128
    vt = bb.sbuf(f"{tag}_r", [m, 128], F32)
    cols = bb.sbuf(f"{tag}_c", [128, m], F32)
    ld = bb.dma("gpsimd", vt, dram_vec.rearrange("(m p) -> m p", p=128),
                writes=(f"{tag}_r",))
    tr = bb.transpose(bb.PS[0][0:128, 0:m], vt, bb.ident_f32[0:m, 0:m],
                      reads=(f"{tag}_r", "ident_f32"), writes=("PS0",), deps=[ld])
    bb.copy(cols, bb.PS[0][0:128, 0:m], reads=("PS0",), writes=(f"{tag}_c",), deps=[tr])
    return cols


def ln_chunked(bb, xck, xckbuf, outs, tag, ch, scratch, eps=1e-5, cw=TCH):
    j = ch % 2
    sq = scratch[f"sq{j}"][:, 0:cw]
    rows = scratch[f"rw{j}"][0:1]
    r_s1, r_s2 = rows[:, 0:cw], rows[:, cw:2 * cw]
    r_var, r_rstd = rows[:, 2 * cw:3 * cw], rows[:, 3 * cw:4 * cw]
    r_rstdf, r_tmp = rows[:, 4 * cw:5 * cw], rows[:, 5 * cw:6 * cw]
    B = f"{tag}{j}_"
    PS0, PS1, PS2 = bb.PS[0], bb.PS[1], bb.PS[2]
    for i in range(NDT):
        sqt = bb.act(sq, xck[:, i, :], AF.Square, reads=(xckbuf,), writes=(B + "sq",))
        bb.mm(PS0[0:1, 0:cw], bb.ones_128x1, xck[:, i, :], start=(i == 0),
              stop=(i == NDT - 1), reads=(xckbuf, "ones128x1"), writes=("PS0",))
        bb.mm(PS1[0:1, 0:cw], bb.ones_128x1, sq, start=(i == 0), stop=(i == NDT - 1),
              reads=(B + "sq", "ones128x1"), writes=("PS1",), deps=[sqt])
    bb.ts(r_s1, PS0[0:1, 0:cw], 1.0 / D, None, reads=("PS0",), writes=(B + "s1",))
    bb.ts(r_s2, PS1[0:1, 0:cw], 1.0 / D, None, reads=("PS1",), writes=(B + "s2",))
    bb.tt(r_tmp, r_s1, r_s1, ALU.mult, reads=(B + "s1",), writes=(B + "tmp",))
    bb.tt(r_var, r_s2, r_tmp, ALU.subtract, reads=(B + "s2", B + "tmp"),
          writes=(B + "var",))
    bb.act(r_rstd, r_var, AF.Sqrt, bias=bb.kcol(eps, rows=1),
           reads=(B + "var", "kbuf"), writes=(B + "rstd",))
    bb.recip(r_rstd, r_rstd, reads=(B + "rstd",), writes=(B + "rstd",))
    need_f = any(o["mode"] == "gbf" for o in outs)
    if need_f:
        bb.ts(r_tmp, r_var, eps, None, op0=ALU.add, reads=(B + "var",),
              writes=(B + "tmp",))
        bb.recip(r_tmp, r_tmp, reads=(B + "tmp",), writes=(B + "tmp",))
        bb.tt(r_tmp, r_tmp, r_var, ALU.mult, reads=(B + "tmp", B + "var"),
              writes=(B + "tmp",))
        bb.act(r_tmp, r_tmp, AF.Sqrt, bias=bb.kcol(eps, rows=1),
               reads=(B + "tmp", "kbuf"), writes=(B + "tmp",))
        bb.recip(r_tmp, r_tmp, reads=(B + "tmp",), writes=(B + "tmp",))
        bb.tt(r_rstdf, r_rstd, r_tmp, ALU.mult, reads=(B + "rstd", B + "tmp"),
              writes=(B + "rstdf",))
    bm = bb.mm(PS0[:, 0:cw], bb.ones_1x128, r_s1, reads=(B + "s1", "ones1x128"),
               writes=("PS0",))
    br = bb.mm(PS1[:, 0:cw], bb.ones_1x128, r_rstd, reads=(B + "rstd", "ones1x128"),
               writes=("PS1",))
    brf = None
    if need_f:
        brf = bb.mm(PS2[:, 0:cw], bb.ones_1x128, r_rstdf,
                    reads=(B + "rstdf", "ones1x128"), writes=("PS2",))
    xm = scratch[f"xm{j}"][:, 0:cw]
    tmpa = scratch[f"ta{j}"][:, 0:cw]
    for i in range(NDT):
        bb.tt(xm, xck[:, i, :], PS0[:, 0:cw], ALU.subtract,
              reads=(xckbuf, "PS0"), writes=(B + "xm",), deps=[bm])
        for o in outs:
            dst, dstbuf = o["sink"](i)
            if o["mode"] == "plain":
                bb.tt(dst, xm, PS1[:, 0:cw], ALU.mult, reads=(B + "xm", "PS1"),
                      writes=(dstbuf,), deps=[br])
            else:
                psx, bwait, psname = (PS1, br, "PS1") if o["mode"] == "gb" else (PS2, brf, "PS2")
                bb.tt(tmpa, xm, psx[:, 0:cw], ALU.mult, reads=(B + "xm", psname),
                      writes=(B + "ta",), deps=[bwait])
                bb.ts(dst, tmpa, o["gamma"][:, i:i + 1], o["beta"][:, i:i + 1],
                      op0=ALU.mult, op1=ALU.add,
                      reads=(B + "ta",) + tuple(o.get("gbufs", ())), writes=(dstbuf,))
            if o.get("post"):
                o["post"](i, dst, dstbuf)



def _ln_scratch(bb, AR, cw=TCH):
    return {
        "sq0": AR.take([128, cw]), "sq1": AR.take([128, cw]),
        "rw0": AR.take([128, 6 * cw]), "rw1": AR.take([128, 6 * cw]),
        "xm0": AR.take([128, cw]), "xm1": AR.take([128, cw]),
        "ta0": AR.take([128, cw]), "ta1": AR.take([128, cw]),
    }


def build_block1(bb=None, xT=None, yT=None):
    if bb is None:
        bb = BB("b1")
        xT = bb.dram_in("xT", [D, T])
        yT = bb.dram_out("yT", [D, T])
    HID = 2048
    w = {}
    for nm, shp in [("ln_g", [D]), ("ln_b", [D]), ("Wh", [D, 2 * HID]),
                    ("bh", [2 * HID]), ("Wqk", [D, 128]), ("bqk", [128]),
                    ("gam", [2, 128]), ("bet", [2, 128]), ("Wo", [HID, D]),
                    ("bo", [D]), ("W1", [D, 4096]), ("b1", [4096]),
                    ("W2", [4096, D]), ("b2", [D])]:
        w[nm] = bb.dram_in("w_" + nm, shp)
    xn1_s = bb.dram_tmp("xn1_s", [D, T], BF16)
    nrm_s = bb.dram_tmp("nrm_s", [D, T], BF16)
    gate_s = bb.dram_tmp("gate_s", [HID, T], BF16)

    setup_common(bb)
    AR = Arena(bb, 176 * 1024)
    mask_sb = bb.sbuf("mask_tri", [128, 128], F32)
    bb.dma("gpsimd", mask_sb,
           bb.const("c_mask_tri", np.triu(np.ones((128, 128), np.float32))),
           writes=("mask_tri",))
    g_cols = vec_to_cols(bb, w["ln_g"], D, "lng")
    b_cols = vec_to_cols(bb, w["ln_b"], D, "lnb")
    b1c = vec_to_cols(bb, w["b1"], 4096, "b1")
    b2c = vec_to_cols(bb, w["b2"], D, "b2")
    boc = vec_to_cols(bb, w["bo"], D, "bo")
    bhc = vec_to_cols(bb, w["bh"], 2 * HID, "bh")
    bqkc = vec_to_cols(bb, w["bqk"], 128, "bqk")
    gamc = bb.sbuf("gamc", [128, 2], F32)
    betc = bb.sbuf("betc", [128, 2], F32)
    gb_r = bb.sbuf("gb_r", [2, 128], F32)
    for (nm, dst, dbuf) in [("gam", gamc, "gamc"), ("bet", betc, "betc")]:
        ldt = bb.dma("sync", gb_r, w[nm], writes=("gb_r",))
        trt = bb.transpose(bb.PS[0][0:128, 0:2], gb_r, bb.ident_f32[0:2, 0:2],
                           reads=("gb_r", "ident_f32"), writes=("PS0",), deps=[ldt])
        bb.copy(dst, bb.PS[0][0:128, 0:2], reads=("PS0",), writes=(dbuf,), deps=[trt])
    bb.p.barrier()

    xck = [AR.take([128, NDT, TCH]) for _ in range(2)]
    ob_xn = [AR.take([128, TCH], BF16) for _ in range(2)]
    ob_nr = [AR.take([128, TCH], BF16) for _ in range(2)]
    lns = _ln_scratch(bb, AR)
    for ch in range(NCH):
        j = ch % 2
        sl = slice(ch * TCH, (ch + 1) * TCH)
        bb.dma("sync", xck[j], xT[:, sl].rearrange("(i p) c -> p i c", p=128),
               writes=(f"xck{j}",))

        def post(dram, dname, q, sl=sl):
            def f(i, ap, buf):
                bb.dma(q, dram[i * 128:(i + 1) * 128, sl], ap, reads=(buf,),
                       writes=(f"{dname}_{i}_{sl.start}",))
            return f

        outs = [
            {"sink": lambda i, j=j: (ob_xn[j], f"oxn{j}"), "mode": "plain",
             "post": post(xn1_s, "xn1_s", "scalar")},
            {"sink": lambda i, j=j: (ob_nr[j], f"onr{j}"), "mode": "gbf",
             "gamma": g_cols, "beta": b_cols, "gbufs": ("lng_c", "lnb_c"),
             "post": post(nrm_s, "nrm_s", "sync")},
        ]
        ln_chunked(bb, xck[j], f"xck{j}", outs, "ln1", ch, lns)
    bb.p.barrier()
    AR.reset_front()

    W1_sb = AR.take([128, NDT * 4096], BF16)
    h_sb = AR.take([128, 32 * TCH], BF16)
    xnck = [AR.take([128, NDT, TCH], BF16) for _ in range(2)]
    w2cb = [AR.take([128, 32, 128], BF16) for _ in range(2)]
    xres = [AR.take([128, TCH]) for _ in range(2)]
    ych = [AR.take([128, TCH]) for _ in range(2)]
    for k in range(NDT):
        bb.dma("gpsimd", W1_sb[:, k * 4096:(k + 1) * 4096],
               w["W1"][k * 128:(k + 1) * 128, :], writes=(f"W1_{k}",))
    ps_h = [bb.PS[3], bb.PS[4], bb.PS[5], bb.PS[6]]
    ps_o = [bb.PS[2], bb.PS[7]]
    for ch in range(NCH):
        j = ch % 2
        sl = slice(ch * TCH, (ch + 1) * TCH)
        bb.dma("sync", xnck[j], xn1_s[:, sl].rearrange("(i p) c -> p i c", p=128),
               reads=tuple(f"xn1_s_{i}_{sl.start}" for i in range(NDT)),
               writes=(f"xnck{j}",))
        for m in range(32):
            jm = m % 4
            mmt = bb.mm_acc(
                [(ps_h[jm][:, :],
                  W1_sb[:, k * 4096 + m * 128: k * 4096 + (m + 1) * 128],
                  xnck[j][:, k, :]) for k in range(NDT)],
                reads=tuple(f"W1_{k}" for k in range(NDT)) + (f"xnck{j}",),
                writes=(f"PS{3 + jm}",))
            bb.act(h_sb[:, m * TCH:(m + 1) * TCH], ps_h[jm][:, :], AF.Relu,
                   bias=b1c[:, m:m + 1], reads=(f"PS{3 + jm}", "b1_c"),
                   writes=(f"h{m}",), deps=[mmt])
        for do in range(NDT):
            jo = do % 2
            bb.dma("gpsimd", w2cb[jo],
                   w["W2"][:, do * 128:(do + 1) * 128].rearrange(
                       "(k p) c -> p k c", p=128),
                   writes=(f"w2cb{jo}",))
            bb.dma("sync", xres[jo], xT[do * 128:(do + 1) * 128, sl],
                   writes=(f"xres{jo}",))
            psoname = "PS2" if jo == 0 else "PS7"
            mmo = bb.mm_acc(
                [(ps_o[jo][:, :], w2cb[jo][:, k, :],
                  h_sb[:, k * TCH:(k + 1) * TCH]) for k in range(32)],
                reads=(f"w2cb{jo}",) + tuple(f"h{k}" for k in range(32)),
                writes=(psoname,))
            bb.stt(ych[jo], ps_o[jo][:, :], b2c[:, do:do + 1], xres[jo],
                   op0=ALU.add, op1=ALU.add,
                   reads=(psoname, "b2_c", f"xres{jo}"),
                   writes=(f"ych{jo}",), deps=[mmo])
            bb.tt(ych[jo], ych[jo], xnck[j][:, do, :], ALU.add,
                  reads=(f"ych{jo}", f"xnck{j}"), writes=(f"ych{jo}",))
            bb.dma("scalar", yT[do * 128:(do + 1) * 128, sl], ych[jo],
                   reads=(f"ych{jo}",), writes=(f"yT_{do}_{ch}",))
    bb.p.barrier()
    AR.reset_front()

    v_res = AR.take([128, 32 * HID], BF16, back=True)
    whcb = [AR.take([128, NDT, TCH], BF16) for _ in range(2)]
    nrmck = [AR.take([128, NDT, 128], BF16) for _ in range(2)]
    nrmch = AR.take([128, NDT, TCH], BF16)
    gch = [AR.take([128, TCH], BF16) for _ in range(2)]
    bhrow_a = AR.take([128, 2 * HID], BF16)
    bhrow = bhrow_a[0:1]
    bb.dma("gpsimd", bhrow, w["bh"].unsqueeze(0), writes=("bhrow",))
    ps_v = [bb.PS[3], bb.PS[4]]
    nT = nrm_s.rearrange("(k p) t -> p k t", p=128)
    for cb in range(4):
        j = cb % 2
        cs = slice(cb * 512, (cb + 1) * 512)
        bb.dma("gpsimd", whcb[j], w["Wh"][:, cs].rearrange("(k p) c -> p k c", p=128),
               writes=(f"whcb{j}",))
        for ts_ in range(32):
            jj = ts_ % 2
            tsl = slice(ts_ * 128, (ts_ + 1) * 128)
            bb.dma("sync", nrmck[jj], nT[:, :, tsl],
                   reads=tuple(f"nrm_s_{k}_{(ts_ // 4) * TCH}" for k in range(NDT)),
                   writes=(f"nrmck{jj}",))
            pairs = [(ps_v[jj][:, :], nrmck[jj][:, k, :],
                      whcb[j][:, k, :]) for k in range(NDT)]
            pairs.append((ps_v[jj][:, :], bb.ones_1x128_bf, bhrow[:, cs]))
            mmv = bb.mm_acc(pairs,
                            reads=(f"nrmck{jj}", f"whcb{j}", "ones1x128bf", "bhrow"),
                            writes=(f"PS{3 + jj}",))
            bb.act(v_res[:, ts_ * HID + cb * 512: ts_ * HID + (cb + 1) * 512],
                   ps_v[jj][:, :], AF.Silu, reads=(f"PS{3 + jj}",),
                   writes=(f"v_{ts_}_{cb}",), deps=[mmv])
    for cb in range(4):
        j = cb % 2
        cs = slice(HID + cb * 512, HID + (cb + 1) * 512)
        bb.dma("gpsimd", whcb[j], w["Wh"][:, cs].rearrange("(k p) c -> p k c", p=128),
               writes=(f"whcb{j}",))
        for ch in range(NCH):
            sl = slice(ch * TCH, (ch + 1) * TCH)
            bb.dma("sync", nrmch, nT[:, :, sl],
                   reads=tuple(f"nrm_s_{k}_{sl.start}" for k in range(NDT)),
                   writes=("nrmch",))
            for gm in range(4):
                jj = (ch * 4 + gm) % 2
                gcol = HID + cb * 512 + gm * 128
                mmg = bb.mm_acc(
                    [(ps_v[jj][:, :], whcb[j][:, k, gm * 128:(gm + 1) * 128],
                      nrmch[:, k, :]) for k in range(NDT)],
                    reads=("nrmch", f"whcb{j}"),
                    writes=(f"PS{3 + jj}",))
                bb.act(gch[jj], ps_v[jj][:, :], AF.Silu,
                       bias=bhc[:, gcol // 128: gcol // 128 + 1],
                       reads=(f"PS{3 + jj}", "bh_c"), writes=(f"gch{jj}",),
                       deps=[mmg])
                bb.dma("scalar", gate_s[gcol - HID: gcol - HID + 128, sl], gch[jj],
                       reads=(f"gch{jj}",), writes=(f"gate_{gcol - HID}_{ch}",))
    bb.p.barrier()
    AR.reset_front()

    q_f = AR.take([128, T], BF16, back=True)
    k_f = AR.take([128, T], BF16, back=True)
    Wqk_sb = AR.take([128, NDT, 128], BF16)
    nrmch3 = [AR.take([128, NDT, TCH], BF16) for _ in range(2)]
    qk32 = [AR.take([128, TCH]) for _ in range(2)]
    bb.dma("gpsimd", Wqk_sb, w["Wqk"].rearrange("(k p) c -> p k c", p=128),
           writes=("Wqk_sb",))
    for ch in range(NCH):
        j = ch % 2
        sl = slice(ch * TCH, (ch + 1) * TCH)
        bb.dma("sync", nrmch3[j], nT[:, :, sl],
               reads=tuple(f"nrm_s_{k}_{sl.start}" for k in range(NDT)),
               writes=(f"nrmch3{j}",))
        mmq = bb.mm_acc(
            [(bb.PS[1][:, :], Wqk_sb[:, k, :], nrmch3[j][:, k, :])
             for k in range(NDT)],
            reads=(f"nrmch3{j}", "Wqk_sb"),
            writes=("PS1",))
        bb.act(qk32[j], bb.PS[1][:, :], AF.Silu, bias=bqkc[:, 0:1],
               reads=("PS1", "bqk_c"), writes=(f"qk32_{j}",), deps=[mmq])
        bb.ts(q_f[:, sl], qk32[j], gamc[:, 0:1], betc[:, 0:1], op0=ALU.mult,
              op1=ALU.add, reads=(f"qk32_{j}", "gamc", "betc"), writes=("q_f",))
        bb.ts(k_f[:, sl], qk32[j], gamc[:, 1:2], betc[:, 1:2], op0=ALU.mult,
              op1=ALU.add, reads=(f"qk32_{j}", "gamc", "betc"), writes=("k_f",))
    bb.p.barrier()
    AR.reset_front()

    ps_sim = [bb.PS[2][:, 0:128], bb.PS[7][:, 0:128]]
    ps_sim_b = ["PS2", "PS7"]
    psY = [bb.PS[3], bb.PS[4], bb.PS[5], bb.PS[6]]
    at32 = [AR.take([128, 128]) for _ in range(2)]
    atbf = [AR.take([128, 128], BF16) for _ in range(2)]
    gateT = AR.take([128, 16, 128], BF16)
    ygT = AR.take([128, 16 * 512], BF16)
    wok = [AR.take([128, 512], BF16) for _ in range(2)]
    yo_x = [AR.take([128, TCH]) for _ in range(2)]
    yo_y = [AR.take([128, TCH]) for _ in range(2)]
    gsT = gate_s.rearrange("(i p) t -> p i t", p=128)
    for grp in range(8):
        for qq in range(4):
            qs = grp * 4 + qq
            qsl = slice(qs * 128, (qs + 1) * 128)
            bb.dma("sync", gateT, gsT[:, :, qsl],
                   reads=tuple(f"gate_{et * 128}_{qs // 4}" for et in range(16)),
                   writes=("gateT",))
            for kc in range(qs + 1):
                sj = kc % 2
                mms = bb.mm(ps_sim[sj], k_f[:, kc * 128:(kc + 1) * 128], q_f[:, qsl],
                            reads=("k_f", "q_f"), writes=(ps_sim_b[sj],))
                last = bb.ts(at32[sj], ps_sim[sj], 0.0, 1.0 / T, op0=ALU.max,
                             op1=ALU.mult, reads=(ps_sim_b[sj],),
                             writes=(f"at32_{sj}",), deps=[mms])
                if kc == qs:
                    last = bb.tt(at32[sj], at32[sj], mask_sb, ALU.mult,
                                 reads=(f"at32_{sj}", "mask_tri"),
                                 writes=(f"at32_{sj}",))
                sqt = bb.act(atbf[sj], at32[sj], AF.Square, reads=(f"at32_{sj}",),
                             writes=(f"atbf_{sj}",), deps=[last])
                bb.mm_multi(
                    [(psY[et // 4][:, (et % 4) * 128:(et % 4 + 1) * 128],
                      v_res[:, kc * HID + et * 128: kc * HID + (et + 1) * 128],
                      atbf[sj]) for et in range(16)],
                    start=(kc == 0), stop=(kc == qs),
                    reads=(f"atbf_{sj}",) + tuple(f"v_{kc}_{cb}" for cb in range(4)),
                    writes=tuple(f"PS{3 + jj}" for jj in range(4)), deps=[sqt])
            for et in range(16):
                bb.tt(ygT[:, et * 512 + qq * 128: et * 512 + (qq + 1) * 128],
                      psY[et // 4][:, (et % 4) * 128:(et % 4 + 1) * 128],
                      gateT[:, et, :], ALU.mult,
                      reads=(f"PS{3 + et // 4}", "gateT"),
                      writes=(f"ygT_{et}_{qq}",))
        gsl = slice(grp * 512, (grp + 1) * 512)
        for wave in range(2):
            for k in range(16):
                jk = k % 2
                bb.dma("gpsimd", wok[jk],
                       w["Wo"][k * 128:(k + 1) * 128, wave * 512:(wave + 1) * 512],
                       writes=(f"wok{jk}",))
                bb.mm_multi(
                    [(psY[dd][:, :], wok[jk][:, dd * 128:(dd + 1) * 128],
                      ygT[:, k * 512:(k + 1) * 512]) for dd in range(4)],
                    start=(k == 0), stop=(k == 15),
                    reads=(f"wok{jk}",) + tuple(f"ygT_{k}_{q2}" for q2 in range(4)),
                    writes=tuple(f"PS{3 + jj}" for jj in range(4)))
            for dd in range(4):
                do = wave * 4 + dd
                jo = dd % 2
                bb.dma("sync", yo_x[jo], yT[do * 128:(do + 1) * 128, gsl],
                       reads=(f"yT_{do}_{grp}",), writes=(f"yo_x{jo}",))
                bb.stt(yo_y[jo], psY[dd][:, :], boc[:, do:do + 1], yo_x[jo],
                       op0=ALU.add, op1=ALU.add,
                       reads=(f"PS{3 + dd}", "bo_c", f"yo_x{jo}"),
                       writes=(f"yo_y{jo}",))
                bb.dma("scalar", yT[do * 128:(do + 1) * 128, gsl], yo_y[jo],
                       reads=(f"yo_y{jo}",), writes=(f"yT_{do}_{grp}",))
    return bb





def short_vec_col(bb, dram_vec, n, tag):
    vr = bb.sbuf(f"{tag}_r", [1, n], F32)
    col = bb.sbuf(f"{tag}_c", [n, 1], F32)
    ld = bb.dma("sync", vr, dram_vec.unsqueeze(0), writes=(f"{tag}_r",))
    tr = bb.transpose(bb.PS[0][0:n, 0:1], vr, bb.ident_f32[0:1, 0:1],
                      reads=(f"{tag}_r", "ident_f32"), writes=("PS0",), deps=[ld])
    bb.copy(col, bb.PS[0][0:n, 0:1], reads=("PS0",), writes=(f"{tag}_c",), deps=[tr])
    return col


def build_block0(stop_after=None, dbg=False, bb=None, xT=None, yT=None):
    if bb is None:
        bb = BB("b0")
        xT = bb.dram_in("xT", [D, T])
        yT = bb.dram_out("yT", [D, T])
    H, DH, CC = 16, 64, 2
    w = {}
    for nm, shp in [("Wq", [D, D]), ("Wk", [D, D]), ("Wv", [D, D]),
                    ("Wbeta", [D, 32]), ("bbeta", [32]), ("Wo", [D, D]),
                    ("bo", [D]), ("ln_g", [D]), ("ln_b", [D]),
                    ("WT", [D, D]), ("fb", [D]), ("V_r", [D, 32]),
                    ("CT", [32, D])]:
        w[nm] = bb.dram_in("w_" + nm, shp)
    dram_dbg = bb.dram_out if dbg else bb.dram_tmp
    xn_s = dram_dbg("xn_s", [D, T])
    xnb_s = bb.dram_tmp("xnb_s", [D, T], BF16)
    Lf_s = dram_dbg("Lf_s", [32, T]) if dbg else None
    qp_s = dram_dbg("qp_s", [T, D]) if dbg else None
    kp_s = dram_dbg("kp_s", [T, D]) if dbg else None
    att_s = dram_dbg("att_s", [T, D]) if dbg else None
    ex_s = dram_dbg("ex_s", [128, 96]) if dbg else None
    py_s = dram_dbg("py_s", [128, 260]) if dbg else None
    ab_s = dram_dbg("ab_s", [128, 256], BF16) if dbg else None
    lr_s = dram_dbg("lr_s", [32, 256]) if dbg else None
    tp_s = dram_dbg("tp_s", [128, 32]) if dbg else None
    va_s = dram_dbg("va_s", [128, 16 * 65], BF16) if dbg else None
    py2_s = dram_dbg("py2_s", [128, 256]) if dbg else None
    dn_s = dram_dbg("dn_s", [128, 16]) if dbg else None

    setup_common(bb)
    AR = Arena(bb, 176 * 1024)
    inv_freq = 1.0 / 10000.0 ** (np.arange(0, 64, 2, dtype=np.float64) / 64)
    tt_ = np.arange(T, dtype=np.float64)
    freqs = np.outer(tt_, inv_freq)
    c_cos = np.cos(freqs).astype(np.float32)
    c_snp = np.concatenate([-np.sin(freqs), np.sin(freqs)], 1).astype(np.float32)
    cosd = bb.const("c_rope_cos", c_cos)
    snpd = bb.const("c_rope_snp", c_snp)
    mask_sb = bb.sbuf("mask_tri", [128, 128], F32)
    bb.dma("gpsimd", mask_sb,
           bb.const("c_mask_tri", np.triu(np.ones((128, 128), np.float32))),
           writes=("mask_tri",))
    g_cols = vec_to_cols(bb, w["ln_g"], D, "lng")
    b_cols = vec_to_cols(bb, w["ln_b"], D, "lnb")
    bo_c = vec_to_cols(bb, w["bo"], D, "bo")
    fb_c = vec_to_cols(bb, w["fb"], D, "fb")
    bbeta_c = short_vec_col(bb, w["bbeta"], 32, "bbeta")
    bb.p.barrier()

    xck = [AR.take([128, NDT, TCH]) for _ in range(2)]
    o_f32 = [AR.take([128, TCH]) for _ in range(2)]
    lns = _ln_scratch(bb, AR)
    for ch in range(NCH):
        j = ch % 2
        sl = slice(ch * TCH, (ch + 1) * TCH)
        bb.dma("sync", xck[j], xT[:, sl].rearrange("(i p) c -> p i c", p=128),
               writes=(f"xck{j}",))

        def post2(i, ap, buf, sl=sl):
            bb.dma("scalar", xn_s[i * 128:(i + 1) * 128, sl], ap, reads=(buf,),
                   writes=(f"xn_s_{i}_{sl.start}",))
            bb.dma("gpsimd", xnb_s[i * 128:(i + 1) * 128, sl], ap, reads=(buf,),
                   writes=(f"xnb_s_{i}_{sl.start}",))

        outs = [{"sink": lambda i, j=j: (o_f32[j], f"of{j}"), "mode": "plain",
                 "post": post2}]
        ln_chunked(bb, xck[j], f"xck{j}", outs, "ln0", ch, lns)
    bb.p.barrier()
    AR.reset_front()

    if stop_after == 0:
        return bb
    WT = AR.take([128, NDT * D], BF16)
    VR = AR.take([128, NDT * 32], BF16)
    CTt = AR.take([32, D], BF16)
    for k in range(NDT):
        bb.dma("gpsimd", WT[:, k * D:(k + 1) * D],
               w["WT"][k * 128:(k + 1) * 128, :], writes=(f"WT{k}",))
        bb.dma("gpsimd", VR[:, k * 32:(k + 1) * 32],
               w["V_r"][k * 128:(k + 1) * 128, :], writes=("VR",))
    bb.dma("gpsimd", CTt, w["CT"], writes=("CTt",))
    xnck = [AR.take([128, NDT, TCH], BF16) for _ in range(2)]
    vrb = [AR.take([32, TCH], BF16) for _ in range(2)]
    xres = [AR.take([128, TCH]) for _ in range(2)]
    ych = [AR.take([128, TCH]) for _ in range(2)]
    ps2 = [bb.PS[3], bb.PS[4]]
    for ch in range(NCH):
        j = ch % 2
        sl = slice(ch * TCH, (ch + 1) * TCH)
        bb.dma("sync", xnck[j], xnb_s[:, sl].rearrange("(i p) c -> p i c", p=128),
               reads=tuple(f"xnb_s_{i}_{sl.start}" for i in range(NDT)),
               writes=(f"xnck{j}",))
        mmv = bb.mm_acc(
            [(bb.PS[5][0:32, :], VR[:, k * 32:(k + 1) * 32], xnck[j][:, k, :])
             for k in range(NDT)],
            reads=("VR", f"xnck{j}"), writes=("PS5",))
        bb.copy(vrb[j], bb.PS[5][0:32, :], reads=("PS5",), writes=(f"vrb{j}",),
                deps=[mmv])
        for do in range(NDT):
            jo = do % 2
            bb.dma("sync", xres[jo], xT[do * 128:(do + 1) * 128, sl],
                   writes=(f"xres{jo}",))
            pairs = [(ps2[jo][:, :], WT[:, k * D + do * 128: k * D + (do + 1) * 128],
                      xnck[j][:, k, :]) for k in range(NDT)]
            pairs.append((ps2[jo][:, :], CTt[:, do * 128:(do + 1) * 128], vrb[j]))
            mmo = bb.mm_acc(pairs,
                            reads=tuple(f"WT{k}" for k in range(NDT))
                            + ("CTt", f"vrb{j}", f"xnck{j}"),
                            writes=(f"PS{3 + jo}",))
            bb.stt(ych[jo], ps2[jo][:, :], fb_c[:, do:do + 1], xres[jo],
                   op0=ALU.add, op1=ALU.add,
                   reads=(f"PS{3 + jo}", "fb_c", f"xres{jo}"),
                   writes=(f"ych{jo}",), deps=[mmo])
            bb.dma("scalar", yT[do * 128:(do + 1) * 128, sl], ych[jo],
                   reads=(f"ych{jo}",), writes=(f"yT_{do}_{ch}",))
    bb.p.barrier()
    AR.reset_front()

    if stop_after == 1:
        return bb
    L_f = AR.take([32, T], back=True)
    WB = AR.take([128, NDT * 32], BF16)
    for k in range(NDT):
        bb.dma("gpsimd", WB[:, k * 32:(k + 1) * 32],
               w["Wbeta"][k * 128:(k + 1) * 128, :], writes=("WB",))
    lgb = AR.take([32, T])
    xnck2 = [AR.take([128, NDT, TCH], BF16) for _ in range(2)]
    for ch in range(NCH):
        j = ch % 2
        sl = slice(ch * TCH, (ch + 1) * TCH)
        bb.dma("sync", xnck2[j], xnb_s[:, sl].rearrange("(i p) c -> p i c", p=128),
               reads=tuple(f"xnb_s_{i}_{sl.start}" for i in range(NDT)),
               writes=(f"xnck2{j}",))
        mmb = bb.mm_acc(
            [(bb.PS[5][0:32, :], WB[:, k * 32:(k + 1) * 32], xnck2[j][:, k, :])
             for k in range(NDT)],
            reads=("WB", f"xnck2{j}"), writes=("PS5",))
        sg = bb.act(lgb[:, sl], bb.PS[5][0:32, :], AF.Sigmoid, bias=bbeta_c,
                    reads=("PS5", "bbeta_c"), writes=("lgb",), deps=[mmb])
        cl = bb.ts(lgb[:, sl], lgb[:, sl], 0.999, 0.001, op0=ALU.min, op1=ALU.max,
                   reads=("lgb",), writes=("lgb",), deps=[sg])
        bb.act(lgb[:, sl], lgb[:, sl], AF.Ln, reads=("lgb",), writes=("lgb",),
               deps=[cl])
    ones32T = AR.take([32, T])
    bb.memset(ones32T, 1.0, writes=("ones32T",))
    bb.scan(L_f, ones32T, lgb, 0.0, reads=("ones32T", "lgb"), writes=("L_f",))
    if dbg:
        bb.dma("scalar", Lf_s, L_f, reads=("L_f",), writes=("Lf_s",))
    bb.p.barrier()
    AR.reset_front()

    if stop_after == 2:
        return bb
    NHC = 32
    WQ = AR.take([128, NDT * D], BF16)
    WK = AR.take([128, NDT * D], BF16)
    WV = AR.take([128, NDT * D], BF16)
    WO = AR.take([128, NDT * D], BF16)
    for k in range(NDT):
        bb.dma("gpsimd", WQ[:, k * D:(k + 1) * D],
               w["Wq"][k * 128:(k + 1) * 128, :], writes=(f"WQ{k}",))
        bb.dma("gpsimd", WK[:, k * D:(k + 1) * D],
               w["Wk"][k * 128:(k + 1) * 128, :], writes=(f"WK{k}",))
        bb.dma("gpsimd", WV[:, k * D:(k + 1) * D],
               w["Wv"][k * 128:(k + 1) * 128, :], writes=(f"WV{k}",))
        bb.dma("gpsimd", WO[:, k * D:(k + 1) * D],
               w["Wo"][k * 128:(k + 1) * 128, :], writes=(f"WO{k}",))
    stt_t = [[AR.take([128, 65], back=True) for _ in range(NDT)] for _ in range(CC)]
    sbf_t = [[AR.take([128, 65], BF16, back=True) for _ in range(NDT)] for _ in range(CC)]
    for c in range(CC):
        for tl in range(NDT):
            bb.memset(stt_t[c][tl], 0.0, writes=(f"st{c}_{tl}",))
            bb.memset(sbf_t[c][tl], 0.0, writes=(f"stb{c}_{tl}",))
    xnkc = [AR.take([128, NDT, 128], BF16) for _ in range(2)]
    xnfc = [AR.take([128, NDT, 128]) for _ in range(2)]
    q32 = AR.take([128, D])
    k32 = AR.take([128, D])
    rt1 = AR.take([128, D])
    rt2 = AR.take([128, D])
    qp = AR.take([128, D])
    kp = AR.take([128, D])
    qt = [AR.take([128, D]) for _ in range(CC)]
    kt = [AR.take([128, D], BF16) for _ in range(CC)]
    v_aug = AR.take([128, H * 65], BF16)
    qf = [AR.take([128, NDT * 128], BF16) for _ in range(CC)]
    kf = AR.take([128, NDT * 128], BF16)
    atbf = [AR.take([128, 128], BF16) for _ in range(2)]
    rope_c = AR.take([128, 32])
    rope_s = AR.take([128, 64])
    lrel = AR.take([32, 128])
    lmp = AR.take([32, 128])
    expP_t = AR.take([128, 32])
    expM_t = AR.take([128, 32])
    expMP_t = AR.take([128, 32])
    prow = AR.take([1, 32])
    pcol = AR.take([32, 64])
    pbc = AR.take([128, 32])
    dens = AR.take([128, 16])
    att_t = AR.take([128, D])
    attf = AR.take([128, NDT * 128], BF16)
    hck = AR.take([128, NDT, 128])
    yrmw = [AR.take([128, NDT, 128]) for _ in range(2)]
    lnsP = _ln_scratch(bb, AR, cw=128)
    ps_tr = [bb.PS[0], bb.PS[1]]
    ps_sim = [bb.PS[2][:, 0:128], bb.PS[7][:, 0:128]]
    ps_sim_b = ["PS2", "PS7"]
    psY = [bb.PS[3], bb.PS[4], bb.PS[5], bb.PS[6]]
    bb.memset(v_aug.rearrange("p (h e) -> p h e", e=65)[:, :, 64:65], 1.0,
              writes=("v_aug",))
    NC32 = T // 128
    for c in range(NC32):
        j = c % 2
        sl = slice(c * 128, (c + 1) * 128)
        bb.dma("sync", xnkc[j], xnb_s[:, sl].rearrange("(i p) t -> p i t", p=128),
               reads=tuple(f"xnb_s_{i}_{(c // 4) * TCH}" for i in range(NDT)),
               writes=(f"xnkc{j}",))
        bb.dma("sync", xnfc[j], xn_s[:, sl].rearrange("(i p) t -> p i t", p=128),
               reads=tuple(f"xn_s_{i}_{(c // 4) * TCH}" for i in range(NDT)),
               writes=(f"xnfc{j}",))
        bb.dma("scalar", rope_c, cosd[sl, :], writes=("rope_c",))
        bb.dma("scalar", rope_s, snpd[sl, :], writes=("rope_s",))
        for (dst, dbuf, Wt, wpre) in [(q32, "q32", WQ, "WQ"), (k32, "k32", WK, "WK")]:
            for n2 in range(2):
                mm = bb.mm_acc(
                    [(ps2[n2][:, :], xnkc[j][:, k, :],
                      Wt[:, k * D + n2 * 512:(k) * D + (n2 + 1) * 512])
                     for k in range(NDT)],
                    reads=(f"xnkc{j}",) + tuple(f"{wpre}{k}" for k in range(NDT)),
                    writes=(f"PS{3 + n2}",))
                bb.copy(dst[:, n2 * 512:(n2 + 1) * 512], ps2[n2][:, :],
                        reads=(f"PS{3 + n2}",), writes=(dbuf,), deps=[mm])
        vap = v_aug.rearrange("p (h e) -> p h e", e=65)
        for n2 in range(2):
            mm = bb.mm_acc(
                [(ps2[n2][:, :], xnkc[j][:, k, :],
                  WV[:, k * D + n2 * 512:(k) * D + (n2 + 1) * 512])
                 for k in range(NDT)],
                reads=(f"xnkc{j}",) + tuple(f"WV{k}" for k in range(NDT)),
                writes=(f"PS{3 + n2}",))
            bb.copy(vap[:, n2 * 8:(n2 + 1) * 8, 0:64],
                    ps2[n2][:, :].rearrange("p (h e) -> p h e", e=64),
                    reads=(f"PS{3 + n2}",), writes=("v_aug",), deps=[mm])
        for (src_, sbuf_, dphi, dpbuf) in [(q32, "q32", qp, "qp"),
                                           (k32, "k32", kp, "kp")]:
            capT = rope_c.unsqueeze(1).unsqueeze(1).broadcast_to((128, 16, 2, 32))
            spT = rope_s.unsqueeze(1).broadcast_to((128, 16, 64))
            sw = src_.rearrange("p (h two cc) -> p h two cc", two=2, cc=32)[:, :, ::-1, :]
            t1 = bb.tt(rt1.rearrange("p (h two cc) -> p h two cc", two=2, cc=32),
                       src_.rearrange("p (h two cc) -> p h two cc", two=2, cc=32),
                       capT, ALU.mult, reads=(sbuf_, "rope_c"), writes=("rt1",))
            t2 = bb.tt(rt2.rearrange("p (h e) -> p h e", e=64), sw, spT, ALU.mult,
                       reads=(sbuf_, "rope_s"), writes=("rt2",))
            t3 = bb.tt(rt1, rt1, rt2, ALU.add, reads=("rt1", "rt2"),
                       writes=("rt1",), deps=[t1, t2])
            t4 = bb.ts(rt2, rt1, 0.0, None, op0=ALU.min, reads=("rt1",),
                       writes=("rt2",), deps=[t3])
            t5 = bb.act(rt2, rt2, AF.Exp, reads=("rt2",), writes=("rt2",),
                        deps=[t4])
            bb.stt(dphi, rt1, 0.0, rt2, op0=ALU.max, op1=ALU.add,
                   reads=("rt1", "rt2"), writes=(dpbuf,), deps=[t5])
        if dbg:
            bb.dma("scalar", qp_s[sl, :], qp, reads=("qp",), writes=("qp_s",))
            bb.dma("scalar", kp_s[sl, :], kp, reads=("kp",), writes=("kp_s",))
        if c == 0:
            lst = bb.kcol(0.0, rows=32)
            lread = ("kbuf", "L_f")
        else:
            lst = L_f[:, c * 128 - 1:c * 128]
            lread = ("L_f",)
        l1 = bb.tt(lrel, L_f[:, sl], lst.broadcast_to((32, 128)), ALU.subtract,
                   reads=lread, writes=("lrel",))

        tr1 = bb.transpose(ps_tr[0][0:128, 0:32], lrel, bb.ident_f32[0:32, 0:32],
                           reads=("lrel", "ident_f32"), writes=("PS0",), deps=[l1])
        if dbg and c == 0:
            bb.dma("scalar", lr_s[:, 0:128], lrel, reads=("lrel",), writes=("lr_s",))
            cpt = bb.copy(expM_t, ps_tr[0][0:128, 0:32], reads=("PS0",),
                          writes=("expM_t",), deps=[tr1])
            bb.dma("scalar", tp_s, expM_t, reads=("expM_t",), writes=("tp_s",))
        e1 = bb.act(expP_t, ps_tr[0][0:128, 0:32], AF.Exp, reads=("PS0",),
                    writes=("expP_t",), deps=[tr1])
        e2 = bb.act(expM_t, ps_tr[0][0:128, 0:32], AF.Exp, scale=-1.0,
                    reads=("PS0",), writes=("expM_t",), deps=[tr1])

        ep = bb.act(pcol[:, 0:1], lrel[:, 127:128], AF.Exp, reads=("lrel",),
                    writes=("pcol",), deps=[l1])
        tr3 = bb.transpose(ps_tr[1][0:1, 32:64], pcol[:, 0:1],
                           bb.ident_f32[0:32, 0:32],
                           reads=("pcol", "ident_f32"), writes=("PS1",), deps=[ep])
        cp3 = bb.copy(prow, ps_tr[1][0:1, 32:64], reads=("PS1",), writes=("prow",),
                      deps=[tr3])
        mmp = bb.mm(ps_tr[1][:, 64:96], bb.ones_1x128, prow,
                    reads=("prow", "ones1x128"), writes=("PS1",), deps=[cp3])
        bb.copy(pbc, ps_tr[1][:, 64:96], reads=("PS1",), writes=("pbc",),
                deps=[mmp])
        bb.tt(expMP_t, expM_t, pbc, ALU.mult, reads=("expM_t", "pbc"),
              writes=("expMP_t",))
        for cc_ in range(CC):
            eP = expP_t[:, cc_:32:2].unsqueeze(2).broadcast_to((128, 16, 64))
            eMP = expMP_t[:, cc_:32:2].unsqueeze(2).broadcast_to((128, 16, 64))
            bb.tt(qt[cc_].rearrange("p (h e) -> p h e", e=64),
                  qp.rearrange("p (h e) -> p h e", e=64), eP, ALU.mult,
                  reads=("qp", "expP_t"), writes=(f"qt{cc_}",))
            bb.tt(kt[cc_].rearrange("p (h e) -> p h e", e=64),
                  kp.rearrange("p (h e) -> p h e", e=64), eMP, ALU.mult,
                  reads=("kp", "expMP_t"), writes=(f"kt{cc_}",))
        for blk in range(NDT):
            jj = blk % 2
            tr = bb.transpose(ps_tr[jj][:, 128:256], kp[:, blk * 128:(blk + 1) * 128],
                              bb.ident_f32,
                              reads=("kp", "ident_f32"), writes=(f"PS{jj}",))
            bb.copy(kf[:, blk * 128:(blk + 1) * 128], ps_tr[jj][:, 128:256],
                    reads=(f"PS{jj}",), writes=("kf",), deps=[tr])
            for cc_ in range(CC):
                tr = bb.transpose(ps_tr[jj][:, 128:256],
                                  qt[cc_][:, blk * 128:(blk + 1) * 128],
                                  bb.ident_f32,
                                  reads=(f"qt{cc_}", "ident_f32"),
                                  writes=(f"PS{jj}",))
                bb.copy(qf[cc_][:, blk * 128:(blk + 1) * 128],
                        ps_tr[jj][:, 128:256],
                        reads=(f"PS{jj}",), writes=(f"qf{cc_}",), deps=[tr])
        for cc_ in range(CC):
            for tl in range(NDT):
                bb.copy(sbf_t[cc_][tl], stt_t[cc_][tl], reads=(f"st{cc_}_{tl}",),
                        writes=(f"stb{cc_}_{tl}",))
        for h in range(H):
            blk = h // 2
            rb = (h % 2) * 64
            ytile = psY[h // 4]
            ycol = (h % 4) * 65
            ypsb = f"PS{3 + h // 4}"
            mms = []
            for cc_ in range(CC):
                sj = (2 * h + cc_) % 2
                mm1 = bb.mm(ps_sim[sj],
                            kf[rb:rb + 64, blk * 128:(blk + 1) * 128],
                            qf[cc_][rb:rb + 64, blk * 128:(blk + 1) * 128],
                            reads=("kf", f"qf{cc_}"), writes=(ps_sim_b[sj],))
                at = bb.stt(atbf[sj], ps_sim[sj], expM_t[:, 2 * h + cc_:2 * h + cc_ + 1],
                            mask_sb, op0=ALU.mult, op1=ALU.mult,
                            reads=(ps_sim_b[sj], "expM_t", "mask_tri"),
                            writes=(f"atbf{sj}",), deps=[mm1])
                pairs = [
                    (ytile[:, ycol:ycol + 65], atbf[sj], vap[:, h, :]),
                    (ytile[:, ycol:ycol + 65],
                     qf[cc_][rb:rb + 64, blk * 128:(blk + 1) * 128],
                     sbf_t[cc_][blk][rb:rb + 64, :]),
                ]
                mm2 = bb.mm_multi([pairs[0]], start=(cc_ == 0), stop=False,
                                  reads=(f"atbf{sj}", "v_aug"), writes=(ypsb,),
                                  deps=[at])
                mm3 = bb.mm_multi([pairs[1]], start=False, stop=(cc_ == CC - 1),
                                  reads=(f"qf{cc_}", f"stb{cc_}_{blk}"),
                                  writes=(ypsb,))
        if dbg and c == 0:
            bb.dma("scalar", ex_s[:, 0:32], expP_t, reads=("expP_t",), writes=("ex_s",))
            bb.dma("scalar", ex_s[:, 32:64], expM_t, reads=("expM_t",), writes=("ex_s",))
            bb.dma("scalar", ex_s[:, 64:96], expMP_t, reads=("expMP_t",), writes=("ex_s",))
            cpy = bb.copy(att_t[:, 0:260], psY[0][:, 0:260], reads=("PS3",),
                          writes=("att_t",), eng="vector")
            bb.dma("scalar", py_s, att_t[:, 0:260], reads=("att_t",), writes=("py_s",))
            bb.dma("scalar", ab_s[:, 0:128], atbf[0], reads=("atbf0",), writes=("ab_s",))
            bb.dma("scalar", ab_s[:, 128:256], atbf[1], reads=("atbf1",), writes=("ab_s",))
        for cc_ in range(CC):
            for tl in range(NDT):
                jj = (cc_ * NDT + tl) % 2
                h0, h1 = 2 * tl, 2 * tl + 1
                dm = bb.mm_multi(
                    [(ps_tr[jj][0:64, 256 + 0:256 + 65],
                      kt[cc_][:, h0 * 64:(h0 + 1) * 64], vap[:, h0, :]),
                     (ps_tr[jj][0:64, 256 + 65:256 + 130],
                      kt[cc_][:, h1 * 64:(h1 + 1) * 64], vap[:, h1, :])],
                    start=True, stop=True,
                    reads=(f"kt{cc_}", "v_aug"), writes=(f"PS{jj}",))
                u1 = bb.tt(stt_t[cc_][tl][0:64, :], stt_t[cc_][tl][0:64, :],
                           pbc[0:64, 2 * h0 + cc_:2 * h0 + cc_ + 1]
                           .broadcast_to((64, 65)), ALU.mult,
                           reads=(f"st{cc_}_{tl}", "pbc"),
                           writes=(f"st{cc_}_{tl}",))
                bb.tt(stt_t[cc_][tl][0:64, :], stt_t[cc_][tl][0:64, :],
                      ps_tr[jj][0:64, 256:256 + 65], ALU.add,
                      reads=(f"st{cc_}_{tl}", f"PS{jj}"),
                      writes=(f"st{cc_}_{tl}",), deps=[dm, u1])
                u2 = bb.tt(stt_t[cc_][tl][64:128, :], stt_t[cc_][tl][64:128, :],
                           pbc[64:128, 2 * h1 + cc_:2 * h1 + cc_ + 1]
                           .broadcast_to((64, 65)), ALU.mult,
                           reads=(f"st{cc_}_{tl}", "pbc"),
                           writes=(f"st{cc_}_{tl}",))
                bb.tt(stt_t[cc_][tl][64:128, :], stt_t[cc_][tl][64:128, :],
                      ps_tr[jj][0:64, 256 + 65:256 + 130], ALU.add,
                      reads=(f"st{cc_}_{tl}", f"PS{jj}"),
                      writes=(f"st{cc_}_{tl}",), deps=[dm, u2])
        for h in range(H):
            bk, ycol = h // 4, (h % 4) * 65
            bb.copy(dens[:, h:h + 1], psY[bk][:, ycol + 64:ycol + 65],
                    reads=(f"PS{3 + bk}",), writes=("dens",), eng="vector")
        r1 = bb.ts(dens, dens, 1e-6, None, op0=ALU.add, reads=("dens",),
                   writes=("dens",))
        r2 = bb.recip(dens, dens, reads=("dens",), writes=("dens",))
        for h in range(H):
            bk, ycol = h // 4, (h % 4) * 65
            bb.tt(att_t[:, h * 64:(h + 1) * 64], psY[bk][:, ycol:ycol + 64],
                  dens[:, h:h + 1].broadcast_to((128, 64)), ALU.mult,
                  reads=(f"PS{3 + bk}", "dens"), writes=("att_t",), deps=[r2])
        if dbg and c == 0:
            bb.dma("scalar", dn_s, dens, reads=("dens",), writes=("dn_s",))
            cp9 = bb.copy(hck[:, 0:2, :], psY[0][:, 0:256], reads=("PS3",),
                          writes=("hck",), eng="vector")
            bb.dma("scalar", tp_s[:, 0:32], hck[:, 0, 0:32], reads=("hck",), writes=("tp_s",))
            bb.dma("scalar", va_s, v_aug, reads=("v_aug",), writes=("va_s",))
            bb.dma("scalar", py2_s, hck[:, 0:2, :], reads=("hck",), writes=("py2_s",))
        if dbg:
            bb.dma("scalar", att_s[sl, :], att_t, reads=("att_t",), writes=("att_s",))
        for blk in range(NDT):
            jj = blk % 2
            tr = bb.transpose(ps_tr[jj][:, 128:256],
                              att_t[:, blk * 128:(blk + 1) * 128], bb.ident_f32,
                              reads=("att_t", "ident_f32"), writes=(f"PS{jj}",))
            bb.copy(attf[:, blk * 128:(blk + 1) * 128], ps_tr[jj][:, 128:256],
                    reads=(f"PS{jj}",), writes=("attf",), deps=[tr])
        for do in range(NDT):
            jj = do % 2
            mo = bb.mm_acc(
                [(ps2[jj][:, 0:128], WO[:, k * D + do * 128: k * D + (do + 1) * 128],
                  attf[:, k * 128:(k + 1) * 128]) for k in range(NDT)],
                reads=tuple(f"WO{k}" for k in range(NDT)) + ("attf",),
                writes=(f"PS{3 + jj}",))
            bb.stt(hck[:, do, :], ps2[jj][:, 0:128], bo_c[:, do:do + 1],
                   xnfc[j][:, do, :], op0=ALU.add, op1=ALU.add,
                   reads=(f"PS{3 + jj}", "bo_c", f"xnfc{j}"), writes=("hck",),
                   deps=[mo])
        bb.dma("sync", yrmw[j], yT[:, sl].rearrange("(i p) t -> p i t", p=128),
               reads=tuple(f"yT_{i}_{c // 4}" for i in range(NDT)),
               writes=(f"yrmw{j}",))

        def postA(i, ap, buf, j=j, sl=sl, c=c):
            a2 = bb.tt(ap, ap, yrmw[j][:, i, :], ALU.add,
                       reads=(buf, f"yrmw{j}"), writes=(buf,))
            bb.dma("scalar", yT[i * 128:(i + 1) * 128, sl], ap, reads=(buf,),
                   writes=(f"yT2_{i}_{c}",))

        outs = [{"sink": lambda i, j=j: (lnsP[f"ta{1 - (c % 2)}"][:, 0:128],
                                         f"lnP{1 - (c % 2)}_ta"),
                 "mode": "gb", "gamma": g_cols, "beta": b_cols,
                 "gbufs": ("lng_c", "lnb_c"), "post": postA}]
        ln_chunked(bb, hck, "hck", outs, "lnP", c, lnsP, cw=128)
    return bb


def _b0_wmap(inputs):
    P = "L0_attn_"
    return {
        "Wq": inputs[P + "Wq"], "Wk": inputs[P + "Wk"], "Wv": inputs[P + "Wv"],
        "Wbeta": inputs[P + "Wbeta"], "bbeta": inputs[P + "bbeta"],
        "Wo": inputs[P + "Wo"], "bo": inputs[P + "bo"],
        "ln_g": inputs[P + "ln_gamma"], "ln_b": inputs[P + "ln_beta"],
        "WT": inputs["L0_ffn_W"].T, "fb": inputs["L0_ffn_b"],
        "V_r": inputs["L0_ffn_V_r"], "CT": inputs["L0_ffn_C"].T,
    }


def _b0_inmap(core, inputs, bb):
    m = {"xT": np.ascontiguousarray(inputs["x_in"][core].T)}
    for k, v in _b0_wmap(inputs).items():
        m["w_" + k] = np.ascontiguousarray(v)
    for cname, arr in bb.consts.items():
        m[cname] = arr
    return m


def run_block0(x_in, inputs):
    bb = build_block0()
    nc = bb.finalize()
    in_maps = [_b0_inmap(c, {**inputs, "x_in": x_in}, bb) for c in range(NCORE)]
    res = run_bass_kernel_spmd(nc, in_maps, core_ids=list(range(NCORE)))
    out = np.stack([res.results[c]["yT"].T for c in range(NCORE)])
    return np.ascontiguousarray(out)



def build_block3(stop_after=None, dbg=False, bb=None, xT=None, yT=None):
    if bb is None:
        bb = BB("b3")
        xT = bb.dram_in("xT", [D, T])
        yT = bb.dram_out("yT", [D, T])

    w = {}
    for nm, shp in [("inT", [D, 2048]), ("conv_w", [D, 2]), ("conv_b", [D]),
                    ("xp_dtT", [D, 64]), ("xp_BT", [D, 16]), ("xp_CT", [D, 16]),
                    ("dtT", [64, D]), ("dt_b", [D]), ("A_log", [D, 16]),
                    ("Dp", [D]), ("opT", [D, D]), ("mn_g", [D]), ("mn_b", [D]),
                    ("n1_g", [D]), ("n1_b", [D]), ("n2_g", [D]), ("n2_b", [D]),
                    ("n3_g", [D]), ("n3_b", [D]), ("w1T", [D, 4096]),
                    ("w2T", [4096, D]), ("feT", [D, D]), ("fe_b", [D]),
                    ("WrT", [D, D]), ("WkT", [D, D]), ("WvT", [D, D]),
                    ("mu_r", [D]), ("mu_k", [D])]:
        w[nm] = bb.dram_in("w_" + nm, shp)
    NST = 16
    dram_dbg = bb.dram_out if dbg else bb.dram_tmp
    xn_s = dram_dbg("xn_s", [D, T])
    xnb_s = bb.dram_tmp("xnb_s", [D, T], BF16)
    min_s = dram_dbg("min_s", [D, T], BF16)
    xin_s = dram_dbg("xin_s", [D, T])
    z_s = dram_dbg("z_s", [D, T])
    xcb_s = dram_dbg("xcb_s", [D, T], BF16)
    xcf_s = dram_dbg("xcf_s", [D, T])
    src_s = dram_dbg("src_s", [D, T])
    ymam_s = dram_dbg("ymam_s", [D, T], BF16)

    setup_common(bb)
    AR = Arena(bb, 176 * 1024)
    cols = {}
    for nm in ["conv_b", "dt_b", "Dp", "mn_g", "mn_b", "n1_g", "n1_b", "n2_g",
               "n2_b", "n3_g", "n3_b", "fe_b", "mu_r", "mu_k"]:
        cols[nm] = vec_to_cols(bb, w[nm], D, nm)
    convw = bb.sbuf("convw", [128, NDT * 2], F32)
    bb.dma("sync", convw.rearrange("p (i c) -> p i c", c=2),
           w["conv_w"].rearrange("(i p) c -> p i c", p=128), writes=("convw",))
    Alog_sb = bb.sbuf("Alog_sb", [128, NDT * NST], F32)
    negA = bb.sbuf("negA", [128, NDT * NST], F32)
    lda = bb.dma("sync", Alog_sb.rearrange("p (i n) -> p i n", n=NST),
                 w["A_log"].rearrange("(i p) n -> p i n", p=128),
                 writes=("Alog_sb",))
    ea = bb.act(negA, Alog_sb, AF.Exp, reads=("Alog_sb",), writes=("negA",),
                deps=[lda])
    bb.ts(negA, negA, -1.0, None, reads=("negA",), writes=("negA",), deps=[ea])
    om_r = bb.sbuf("om_r", [128, NDT], F32)
    om_k = bb.sbuf("om_k", [128, NDT], F32)
    bb.ts(om_r, cols["mu_r"], -1.0, 1.0, op0=ALU.mult, op1=ALU.add,
          reads=("mu_r_c",), writes=("om_r",))
    bb.ts(om_k, cols["mu_k"], -1.0, 1.0, op0=ALU.mult, op1=ALU.add,
          reads=("mu_k_c",), writes=("om_k",))
    bb.p.barrier()

    xck = [AR.take([128, NDT, TCH]) for _ in range(2)]
    o_f32 = [AR.take([128, TCH]) for _ in range(2)]
    o_bf = [AR.take([128, TCH], BF16) for _ in range(2)]
    o_m = [AR.take([128, TCH], BF16) for _ in range(2)]
    lns = _ln_scratch(bb, AR)
    for ch in range(NCH):
        j = ch % 2
        sl = slice(ch * TCH, (ch + 1) * TCH)
        bb.dma("sync", xck[j], xT[:, sl].rearrange("(i p) c -> p i c", p=128),
               writes=(f"xck{j}",))

        def post(dram, dname, q, sl=sl):
            def f(i, ap, buf):
                bb.dma(q, dram[i * 128:(i + 1) * 128, sl], ap, reads=(buf,),
                       writes=(f"{dname}_{i}_{sl.start}",))
            return f

        def post2(i, ap, buf, sl=sl):
            bb.dma("scalar", xn_s[i * 128:(i + 1) * 128, sl], ap, reads=(buf,),
                   writes=(f"xn_s_{i}_{sl.start}",))
            bb.dma("gpsimd", xnb_s[i * 128:(i + 1) * 128, sl], ap, reads=(buf,),
                   writes=(f"xnb_s_{i}_{sl.start}",))

        outs = [
            {"sink": lambda i, j=j: (o_f32[j], f"of{j}"), "mode": "plain",
             "post": post2},
            {"sink": lambda i, j=j: (o_m[j], f"om{j}"), "mode": "gbf",
             "gamma": cols["n1_g"], "beta": cols["n1_b"],
             "gbufs": ("n1_g_c", "n1_b_c"), "post": post(min_s, "min_s", "sync")},
        ]
        ln_chunked(bb, xck[j], f"xck{j}", outs, "ln0", ch, lns)
    bb.p.barrier()
    AR.reset_front()

    if stop_after == 0:
        return bb
    WI = AR.take([128, NDT * 2048], BF16)
    for k in range(NDT):
        bb.dma("gpsimd", WI[:, k * 2048:(k + 1) * 2048],
               w["inT"][k * 128:(k + 1) * 128, :], writes=(f"WI{k}",))
    mck = [AR.take([128, NDT, TCH], BF16) for _ in range(2)]
    och = [AR.take([128, TCH]) for _ in range(2)]
    ps2 = [bb.PS[3], bb.PS[4]]
    for ch in range(NCH):
        j = ch % 2
        sl = slice(ch * TCH, (ch + 1) * TCH)
        bb.dma("sync", mck[j], min_s[:, sl].rearrange("(i p) c -> p i c", p=128),
               reads=tuple(f"min_s_{i}_{sl.start}" for i in range(NDT)),
               writes=(f"mck{j}",))
        for mt in range(16):
            jm = mt % 2
            mmt = bb.mm_acc(
                [(ps2[jm][:, :], WI[:, k * 2048 + mt * 128: k * 2048 + (mt + 1) * 128],
                  mck[j][:, k, :]) for k in range(NDT)],
                reads=tuple(f"WI{k}" for k in range(NDT)) + (f"mck{j}",),
                writes=(f"PS{3 + jm}",))
            tgt, tname = (xin_s, "xin_s") if mt < 8 else (z_s, "z_s")
            ti = mt if mt < 8 else mt - 8
            cpt = bb.copy(och[jm], ps2[jm][:, :], reads=(f"PS{3 + jm}",),
                          writes=(f"och{jm}",), deps=[mmt])
            bb.dma("scalar", tgt[ti * 128:(ti + 1) * 128, sl], och[jm],
                   reads=(f"och{jm}",), writes=(f"{tname}_{ti}_{sl.start}",))
    bb.p.barrier()
    AR.reset_front()

    if stop_after == 1:
        return bb
    cur = [AR.take([128, NDT, TCH]) for _ in range(2)]
    prv = [AR.take([128, NDT, TCH]) for _ in range(2)]
    xc32 = [AR.take([128, TCH]) for _ in range(2)]
    xcb = [AR.take([128, TCH], BF16) for _ in range(2)]
    for ch in range(NCH):
        j = ch % 2
        sl = slice(ch * TCH, (ch + 1) * TCH)
        bb.dma("sync", cur[j], xin_s[:, sl].rearrange("(i p) c -> p i c", p=128),
               reads=tuple(f"xin_s_{i}_{sl.start}" for i in range(NDT)),
               writes=(f"cur{j}",))
        if ch == 0:
            bb.memset(prv[j][:, :, 0:1], 0.0, writes=(f"prv{j}",))
            bb.dma("sync", prv[j][:, :, 1:TCH],
                   xin_s[:, 0:TCH - 1].rearrange("(i p) c -> p i c", p=128),
                   reads=tuple(f"xin_s_{i}_0" for i in range(NDT)),
                   writes=(f"prv{j}",))
        else:
            bb.dma("sync", prv[j],
                   xin_s[:, ch * TCH - 1:(ch + 1) * TCH - 1].rearrange(
                       "(i p) c -> p i c", p=128),
               reads=tuple(f"xin_s_{i}_{(ch - 1) * TCH}" for i in range(NDT))
               + tuple(f"xin_s_{i}_{sl.start}" for i in range(NDT)),
               writes=(f"prv{j}",))
        for i in range(NDT):
            jj = (ch * NDT + i) % 2
            t1 = bb.ts(xc32[jj], cur[j][:, i, :], convw[:, 2 * i + 1:2 * i + 2], None,
                       reads=(f"cur{j}", "convw"), writes=(f"xc32_{jj}",))
            t2 = bb.stt(xc32[jj], prv[j][:, i, :], convw[:, 2 * i:2 * i + 1], xc32[jj],
                        op0=ALU.mult, op1=ALU.add,
                        reads=(f"prv{j}", "convw", f"xc32_{jj}"),
                        writes=(f"xc32_{jj}",), deps=[t1])
            a1 = bb.act(xc32[jj], xc32[jj], AF.Silu, bias=cols["conv_b"][:, i:i + 1],
                        reads=(f"xc32_{jj}", "conv_b_c"), writes=(f"xc32_{jj}",),
                        deps=[t2])
            bb.dma("scalar", xcf_s[i * 128:(i + 1) * 128, sl], xc32[jj],
                   reads=(f"xc32_{jj}",), writes=(f"xcf_s_{i}_{sl.start}",))
            c1 = bb.copy(xcb[jj], xc32[jj], reads=(f"xc32_{jj}",),
                         writes=(f"xcb{jj}",), eng="vector", deps=[a1])
            bb.dma("gpsimd", xcb_s[i * 128:(i + 1) * 128, sl], xcb[jj],
                   reads=(f"xcb{jj}",), writes=(f"xcb_s_{i}_{sl.start}",))
    bb.p.barrier()
    AR.reset_front()

    if stop_after == 2:
        return bb
    dt_f = AR.take([64, T], BF16, back=True)
    B_f = AR.take([16, T])
    C_f = AR.take([16, T])
    Wdt = AR.take([128, NDT * 64], BF16)
    Wb_ = AR.take([128, NDT * 16], BF16)
    Wc_ = AR.take([128, NDT * 16], BF16)
    for k in range(NDT):
        bb.dma("gpsimd", Wdt[:, k * 64:(k + 1) * 64],
               w["xp_dtT"][k * 128:(k + 1) * 128, :], writes=("Wdt",))
        bb.dma("gpsimd", Wb_[:, k * 16:(k + 1) * 16],
               w["xp_BT"][k * 128:(k + 1) * 128, :], writes=("Wb_",))
        bb.dma("gpsimd", Wc_[:, k * 16:(k + 1) * 16],
               w["xp_CT"][k * 128:(k + 1) * 128, :], writes=("Wc_",))
    b_s = dram_dbg("b_s", [16, T])
    c_s = dram_dbg("c_s", [16, T])
    xcck = [AR.take([128, NDT, TCH], BF16) for _ in range(2)]
    for ch in range(NCH):
        j = ch % 2
        sl = slice(ch * TCH, (ch + 1) * TCH)
        bb.dma("sync", xcck[j], xcb_s[:, sl].rearrange("(i p) c -> p i c", p=128),
               reads=tuple(f"xcb_s_{i}_{sl.start}" for i in range(NDT)),
               writes=(f"xcck{j}",))
        for (lhs, lname, dst, dname, m) in [
                (Wdt, "Wdt", dt_f, "dt_f", 64), (Wb_, "Wb_", B_f, "B_f", 16),
                (Wc_, "Wc_", C_f, "C_f", 16)]:
            mmt = bb.mm_acc(
                [(bb.PS[3][0:m, :], lhs[:, k * m:(k + 1) * m], xcck[j][:, k, :])
                 for k in range(NDT)],
                reads=(lname, f"xcck{j}"), writes=("PS3",))
            bb.copy(dst[0:m, sl], bb.PS[3][0:m, :], reads=("PS3",),
                    writes=(dname,), deps=[mmt])
    bb.dma("scalar", b_s, B_f, reads=("B_f",), writes=("b_s",))
    bb.dma("scalar", c_s, C_f, reads=("C_f",), writes=("c_s",))
    bb.p.barrier()
    AR.reset_front()

    if stop_after == 3:
        return bb
    Wdtp = AR.take([64, D], BF16)
    bb.dma("gpsimd", Wdtp, w["dtT"], writes=("Wdtp",))
    delta = AR.take([128, T])
    u_ = AR.take([128, T])
    dA = AR.take([128, T])
    dBu = AR.take([128, T])
    hsc = AR.take([128, T])
    yac = AR.take([128, T])
    Bb = AR.take([128, T])
    Cb = AR.take([128, T])
    xcd = AR.take([128, T])
    zd = [AR.take([128, TCH]) for _ in range(2)]
    for i in range(NDT):
        for c2 in range(NCH):
            sl = slice(c2 * TCH, (c2 + 1) * TCH)
            mmt = bb.mm(bb.PS[3][:, :], Wdtp[:, i * 128:(i + 1) * 128], dt_f[:, sl],
                        reads=("Wdtp", "dt_f"), writes=("PS3",))
            if stop_after == 44:
                cp_t = bb.copy(delta[:, sl], bb.PS[3][:, :], reads=("PS3",),
                               writes=("delta",), deps=[mmt])
                continue
            a_t = bb.act(u_[:, sl], bb.PS[3][:, :], AF.Abs,
                         bias=cols["dt_b"][:, i:i + 1],
                         reads=("PS3", "dt_b_c"), writes=("u_",), deps=[mmt])
            e_t = bb.act(u_[:, sl], u_[:, sl], AF.Exp, scale=-1.0,
                         reads=("u_",), writes=("u_",), deps=[a_t])
            l_t = bb.act(u_[:, sl], u_[:, sl], AF.Ln, bias=bb.kcol(1.0),
                         reads=("u_", "kbuf"), writes=("u_",), deps=[e_t])
            if stop_after == 45:
                continue
            m_t = bb.ts(delta[:, sl], bb.PS[3][:, :], cols["dt_b"][:, i:i + 1], 0.0,
                        op0=ALU.add, op1=ALU.max,
                        reads=("PS3", "dt_b_c"), writes=("delta",), deps=[mmt])
            if stop_after != 46:
                bb.tt(delta[:, sl], delta[:, sl], u_[:, sl], ALU.add,
                      reads=("delta", "u_"), writes=("delta",), deps=[l_t, m_t])
        bb.dma("sync", xcd, xcf_s[i * 128:(i + 1) * 128, :],
               reads=tuple(f"xcf_s_{i}_{c2 * TCH}" for c2 in range(NCH)),
               writes=("xcd",))
        bb.tt(u_, delta, xcd, ALU.mult, reads=("delta", "xcd"), writes=("u_",))
        if stop_after in (41, 44, 45, 46):
            continue
        for n in range(NST if stop_after != 42 else 1):
            da_t = bb.act(dA, delta, AF.Exp, scale=negA[:, i * NST + n:i * NST + n + 1],
                          reads=("delta", "negA"), writes=("dA",))
            bb.dma("sync", Bb, b_s[n:n + 1, :].partition_broadcast(128).squeeze(1),
                   reads=("b_s",), writes=("Bb",))
            bb.dma("scalar", Cb, c_s[n:n + 1, :].partition_broadcast(128).squeeze(1),
                   reads=("c_s",), writes=("Cb",))
            db_t = bb.tt(dBu, u_, Bb, ALU.mult, reads=("u_", "Bb"), writes=("dBu",))
            sc_t = bb.scan(hsc, dA, dBu, 0.0, reads=("dA", "dBu"), writes=("hsc",),
                           deps=[da_t, db_t])
            if n == 0:
                bb.tt(yac, hsc, Cb, ALU.mult, reads=("hsc", "Cb"), writes=("yac",),
                      deps=[sc_t])
            else:
                hc_t = bb.tt(hsc, hsc, Cb, ALU.mult, reads=("hsc", "Cb"),
                             writes=("hsc",), deps=[sc_t])
                bb.tt(yac, yac, hsc, ALU.add, reads=("yac", "hsc"),
                      writes=("yac",), deps=[hc_t])
        if stop_after in (41, 42, 43, 44, 45, 46):
            continue
        y1 = bb.stt(yac, xcd, cols["Dp"][:, i:i + 1], yac, op0=ALU.mult,
                    op1=ALU.add, reads=("xcd", "Dp_c", "yac"), writes=("yac",))
        for c2 in range(NCH):
            sl = slice(c2 * TCH, (c2 + 1) * TCH)
            jz = c2 % 2
            zl = bb.dma("sync", zd[jz], z_s[i * 128:(i + 1) * 128, sl],
                        reads=(f"z_s_{i}_{sl.start}",), writes=(f"zd{jz}",))
            s_t = bb.act(zd[jz], zd[jz], AF.Silu, reads=(f"zd{jz}",),
                         writes=(f"zd{jz}",), deps=[zl])
            y2 = bb.tt(zd[jz], yac[:, sl], zd[jz], ALU.mult,
                       reads=("yac", f"zd{jz}"), writes=(f"zd{jz}",),
                       deps=[s_t, y1])
            bb.dma("gpsimd", ymam_s[i * 128:(i + 1) * 128, sl], zd[jz],
                   reads=(f"zd{jz}",), writes=(f"ymam_{i}_{c2}",), deps=[y2])
    if stop_after in (41, 42, 43, 44, 45, 46):
        return bb
    bb.p.barrier()
    AR.reset_front()
    AR.reset_back()

    if stop_after == 4:
        return bb
    WO = AR.take([128, NDT * D], BF16)
    for k in range(NDT):
        bb.dma("gpsimd", WO[:, k * D:(k + 1) * D],
               w["opT"][k * 128:(k + 1) * 128, :], writes=(f"WO{k}",))
    ymck = [AR.take([128, NDT, TCH], BF16) for _ in range(2)]
    h2ck = AR.take([128, NDT, TCH])
    xnck5 = [AR.take([128, NDT, TCH]), None]
    xnck5[1] = xnck5[0]
    osrc = [AR.take([128, TCH]) for _ in range(2)]
    lns5 = {k + jj: AR.take([128, 6 * TCH] if k == "rw" else [128, TCH])
            for k in ["sq", "rw", "xm", "ta"] for jj in ["0", "1"]}
    for ch in range(NCH):
        j = ch % 2
        sl = slice(ch * TCH, (ch + 1) * TCH)
        bb.dma("sync", ymck[j], ymam_s[:, sl].rearrange("(i p) c -> p i c", p=128),
               reads=tuple(f"ymam_{i}_{ch}" for i in range(NDT)),
               writes=(f"ymck{j}",))
        bb.dma("sync", xnck5[j], xn_s[:, sl].rearrange("(i p) c -> p i c", p=128),
               reads=tuple(f"xn_s_{i}_{sl.start}" for i in range(NDT)),
               writes=("xnck50",))
        for do in range(NDT):
            jm = do % 2
            mmt = bb.mm_acc(
                [(ps2[jm][:, :], WO[:, k * D + do * 128: k * D + (do + 1) * 128],
                  ymck[j][:, k, :]) for k in range(NDT)],
                reads=tuple(f"WO{k}" for k in range(NDT)) + (f"ymck{j}",),
                writes=(f"PS{3 + jm}",))
            bb.copy(h2ck[:, do, :], ps2[jm][:, :], reads=(f"PS{3 + jm}",),
                    writes=("h2ck",), deps=[mmt])

        def post5(i, ap, buf, sl=sl, j=j):
            a2 = bb.tt(ap, ap, xnck5[j][:, i, :], ALU.add,
                       reads=(buf, "xnck50"), writes=(buf,))
            bb.dma("scalar", src_s[i * 128:(i + 1) * 128, sl], ap, reads=(buf,),
                   writes=(f"src_{i}_{sl.start}",))

        outs = [{"sink": lambda i, j=j: (osrc[j], f"osrc{j}"), "mode": "gb",
                 "gamma": cols["mn_g"], "beta": cols["mn_b"],
                 "gbufs": ("mn_g_c", "mn_b_c"), "post": post5}]
        ln_chunked(bb, h2ck, "h2ck", outs, "ln5", ch, lns5)
    bb.p.barrier()
    AR.reset_front()

    if stop_after == 5:
        return bb
    W1 = AR.take([128, NDT * 4096], BF16)
    for k in range(NDT):
        bb.dma("gpsimd", W1[:, k * 4096:(k + 1) * 4096],
               w["w1T"][k * 128:(k + 1) * 128, :], writes=(f"W1_{k}",))
    sck = [AR.take([128, NDT, TCH]), None]
    sck[1] = sck[0]
    snb = [AR.take([128, NDT, TCH], BF16), None]
    snb[1] = snb[0]
    fh = AR.take([128, 32 * TCH], BF16)
    f32t = [AR.take([128, TCH]) for _ in range(2)]
    w2cb = [AR.take([128, 32, 128], BF16), None]
    w2cb[1] = w2cb[0]
    lns6 = _ln_scratch(bb, AR)
    osn = [AR.take([128, TCH], BF16) for _ in range(2)]
    for ch in range(NCH):
        j = ch % 2
        sl = slice(ch * TCH, (ch + 1) * TCH)
        bb.dma("sync", sck[j], src_s[:, sl].rearrange("(i p) c -> p i c", p=128),
               reads=tuple(f"src_{i}_{sl.start}" for i in range(NDT)),
               writes=("sck0",))
        outs = [{"sink": lambda i, j=j: (snb[j][:, i, :], "snb0"), "mode": "gb",
                 "gamma": cols["n2_g"], "beta": cols["n2_b"],
                 "gbufs": ("n2_g_c", "n2_b_c")}]
        ln_chunked(bb, sck[j], "sck0", outs, "ln6", ch, lns6)
        for m in range(32):
            jm = m % 2
            mmt = bb.mm_acc(
                [(ps2[jm][:, :], W1[:, k * 4096 + m * 128: k * 4096 + (m + 1) * 128],
                  snb[j][:, k, :]) for k in range(NDT)],
                reads=tuple(f"W1_{k}" for k in range(NDT)) + ("snb0",),
                writes=(f"PS{3 + jm}",))
            cpt = bb.copy(f32t[jm], ps2[jm][:, :], reads=(f"PS{3 + jm}",),
                          writes=(f"f32t{jm}",), deps=[mmt])
            bb.stt(fh[:, m * TCH:(m + 1) * TCH], f32t[jm], 0.01, f32t[jm],
                   op0=ALU.mult, op1=ALU.max, reads=(f"f32t{jm}",),
                   writes=(f"fh{m}",), deps=[cpt])
        for do in range(NDT):
            jo = do % 2
            bb.dma("gpsimd", w2cb[jo],
                   w["w2T"][:, do * 128:(do + 1) * 128].rearrange(
                       "(k p) c -> p k c", p=128),
                   writes=("w2cb0",))
            mmo = bb.mm_acc(
                [(bb.PS[5 + jo][:, :], w2cb[jo][:, k, :],
                  fh[:, k * TCH:(k + 1) * TCH]) for k in range(32)],
                reads=("w2cb0",) + tuple(f"fh{k}" for k in range(32)),
                writes=(f"PS{5 + jo}",))
            st = bb.tt(f32t[jo], bb.PS[5 + jo][:, :], sck[j][:, do, :], ALU.add,
                       reads=(f"PS{5 + jo}", "sck0"), writes=(f"f32t{jo}",),
                       deps=[mmo])
            bb.dma("scalar", src_s[do * 128:(do + 1) * 128, sl], f32t[jo],
                   reads=(f"f32t{jo}",), writes=(f"src_{do}_{sl.start}",))
    bb.p.barrier()
    AR.reset_front()

    if stop_after == 6:
        return bb
    srcfin_s = dram_dbg("srcfin_s", [D, T])
    FE = AR.take([128, NDT * D], BF16)
    for k in range(NDT):
        bb.dma("gpsimd", FE[:, k * D:(k + 1) * D],
               w["feT"][k * 128:(k + 1) * 128, :], writes=(f"FE{k}",))
    sck7 = AR.take([128, NDT, TCH])
    sn7 = AR.take([128, NDT, TCH], BF16)
    tmp7 = [AR.take([128, TCH]) for _ in range(2)]
    lns7 = _ln_scratch(bb, AR)
    for ch in range(NCH):
        sl = slice(ch * TCH, (ch + 1) * TCH)
        bb.dma("sync", sck7, src_s[:, sl].rearrange("(i p) c -> p i c", p=128),
               reads=tuple(f"src_{i}_{sl.start}" for i in range(NDT)),
               writes=("sck7",))
        outs = [{"sink": lambda i: (sn7[:, i, :], "sn7"), "mode": "gb",
                 "gamma": cols["n3_g"], "beta": cols["n3_b"],
                 "gbufs": ("n3_g_c", "n3_b_c")}]
        ln_chunked(bb, sck7, "sck7", outs, "ln7", ch, lns7)
        for do in range(NDT):
            jm = do % 2
            mmf = bb.mm_acc(
                [(ps2[jm][:, :], FE[:, k * D + do * 128: k * D + (do + 1) * 128],
                  sn7[:, k, :]) for k in range(NDT)],
                reads=tuple(f"FE{k}" for k in range(NDT)) + ("sn7",),
                writes=(f"PS{3 + jm}",))
            s1 = bb.stt(tmp7[jm], ps2[jm][:, :], cols["fe_b"][:, do:do + 1],
                        sck7[:, do, :], op0=ALU.add, op1=ALU.add,
                        reads=(f"PS{3 + jm}", "fe_b_c", "sck7"),
                        writes=(f"tmp7{jm}",), deps=[mmf])
            bb.dma("scalar", srcfin_s[do * 128:(do + 1) * 128, sl], tmp7[jm],
                   reads=(f"tmp7{jm}",), writes=(f"srcfin_{do}_{ch}",))
    bb.p.barrier()
    AR.reset_front()

    if stop_after == 7:
        return bb
    WR = AR.take([128, NDT * D], BF16)
    WK = AR.take([128, NDT * D], BF16)
    WV = AR.take([128, NDT * D], BF16)
    for k in range(NDT):
        bb.dma("gpsimd", WR[:, k * D:(k + 1) * D],
               w["WrT"][k * 128:(k + 1) * 128, :], writes=(f"WR{k}",))
        bb.dma("gpsimd", WK[:, k * D:(k + 1) * D],
               w["WkT"][k * 128:(k + 1) * 128, :], writes=(f"WK{k}",))
        bb.dma("gpsimd", WV[:, k * D:(k + 1) * D],
               w["WvT"][k * 128:(k + 1) * 128, :], writes=(f"WV{k}",))
    xnc = AR.take([128, NDT, TCH])
    xnp = AR.take([128, NDT, TCH])
    xrb = AR.take([128, NDT, TCH], BF16)
    xkb = AR.take([128, NDT, TCH], BF16)
    k2b = AR.take([128, NDT, TCH], BF16)
    sfc = AR.take([128, NDT, TCH])
    rsg = [AR.take([128, TCH]) for _ in range(2)]
    tmp8 = [AR.take([128, TCH]) for _ in range(2)]
    xres7 = [AR.take([128, TCH]) for _ in range(2)]
    for ch in range(NCH):
        sl = slice(ch * TCH, (ch + 1) * TCH)
        bb.dma("sync", xnc, xn_s[:, sl].rearrange("(i p) c -> p i c", p=128),
               reads=tuple(f"xn_s_{i}_{sl.start}" for i in range(NDT)),
               writes=("xnc",))
        bb.dma("sync", sfc, srcfin_s[:, sl].rearrange("(i p) c -> p i c", p=128),
               reads=tuple(f"srcfin_{i}_{ch}" for i in range(NDT)),
               writes=("sfc",))
        if ch == 0:
            bb.memset(xnp[:, :, 0:1], 0.0, writes=("xnp",))
            bb.dma("sync", xnp[:, :, 1:TCH],
                   xn_s[:, 0:TCH - 1].rearrange("(i p) c -> p i c", p=128),
                   reads=tuple(f"xn_s_{i}_0" for i in range(NDT)),
                   writes=("xnp",))
        else:
            bb.dma("sync", xnp,
                   xn_s[:, ch * TCH - 1:(ch + 1) * TCH - 1].rearrange(
                       "(i p) c -> p i c", p=128),
                   reads=tuple(f"xn_s_{i}_{(ch - 1) * TCH}" for i in range(NDT))
                   + tuple(f"xn_s_{i}_{sl.start}" for i in range(NDT)),
                   writes=("xnp",))
        for i in range(NDT):
            jj = i % 2
            t1 = bb.ts(tmp8[jj], xnp[:, i, :], om_r[:, i:i + 1], None,
                       reads=("xnp", "om_r"), writes=(f"tmp8{jj}",))
            bb.stt(xrb[:, i, :], xnc[:, i, :], cols["mu_r"][:, i:i + 1],
                   tmp8[jj], op0=ALU.mult, op1=ALU.add,
                   reads=("xnc", "mu_r_c", f"tmp8{jj}"), writes=("xrb",),
                   deps=[t1])
            t2 = bb.ts(tmp8[jj], xnp[:, i, :], om_k[:, i:i + 1], None,
                       reads=("xnp", "om_k"), writes=(f"tmp8{jj}",))
            bb.stt(xkb[:, i, :], xnc[:, i, :], cols["mu_k"][:, i:i + 1],
                   tmp8[jj], op0=ALU.mult, op1=ALU.add,
                   reads=("xnc", "mu_k_c", f"tmp8{jj}"), writes=("xkb",),
                   deps=[t2])
        for do in range(NDT):
            jm = do % 2
            mmk = bb.mm_acc(
                [(ps2[jm][:, :], WK[:, k * D + do * 128: k * D + (do + 1) * 128],
                  xkb[:, k, :]) for k in range(NDT)],
                reads=tuple(f"WK{k}" for k in range(NDT)) + ("xkb",),
                writes=(f"PS{3 + jm}",))
            rl = bb.ts(tmp8[jm], ps2[jm][:, :], 0.0, None, op0=ALU.max,
                       reads=(f"PS{3 + jm}",), writes=(f"tmp8{jm}",), deps=[mmk])
            bb.act(k2b[:, do, :], tmp8[jm], AF.Square, reads=(f"tmp8{jm}",),
                   writes=("k2b",), deps=[rl])
        for do in range(NDT):
            jm = do % 2
            bb.dma("sync", xres7[jm], xT[do * 128:(do + 1) * 128, sl],
                   writes=(f"xres7{jm}",))
            mmr = bb.mm_acc(
                [(ps2[jm][:, :], WR[:, k * D + do * 128: k * D + (do + 1) * 128],
                  xrb[:, k, :]) for k in range(NDT)],
                reads=tuple(f"WR{k}" for k in range(NDT)) + ("xrb",),
                writes=(f"PS{3 + jm}",))
            sg = bb.act(rsg[jm], ps2[jm][:, :], AF.Sigmoid,
                        reads=(f"PS{3 + jm}",), writes=(f"rsg{jm}",), deps=[mmr])
            mmv = bb.mm_acc(
                [(bb.PS[5 + jm][:, :], WV[:, k * D + do * 128: k * D + (do + 1) * 128],
                  k2b[:, k, :]) for k in range(NDT)],
                reads=tuple(f"WV{k}" for k in range(NDT)) + ("k2b",),
                writes=(f"PS{5 + jm}",))
            s3 = bb.tt(tmp8[jm], rsg[jm], bb.PS[5 + jm][:, :], ALU.mult,
                       reads=(f"rsg{jm}", f"PS{5 + jm}"), writes=(f"tmp8{jm}",),
                       deps=[mmv, sg])
            s4 = bb.tt(tmp8[jm], tmp8[jm], sfc[:, do, :], ALU.add,
                       reads=(f"tmp8{jm}", "sfc"), writes=(f"tmp8{jm}",),
                       deps=[s3])
            s5 = bb.tt(tmp8[jm], tmp8[jm], xres7[jm], ALU.add,
                       reads=(f"tmp8{jm}", f"xres7{jm}"), writes=(f"tmp8{jm}",),
                       deps=[s4])
            bb.dma("scalar", yT[do * 128:(do + 1) * 128, sl], tmp8[jm],
                   reads=(f"tmp8{jm}",), writes=(f"yT_{do}_{ch}",))
    return bb


def _b3_wmap(inputs):
    P = "L3_attn_"
    xp = inputs[P + "x_proj_w"]
    return {
        "inT": inputs[P + "in_proj_w"].T, "conv_w": inputs[P + "conv_w"],
        "conv_b": inputs[P + "conv_b"], "xp_dtT": xp[0:64].T,
        "xp_BT": xp[64:80].T, "xp_CT": xp[80:96].T,
        "dtT": inputs[P + "dt_proj_w"].T, "dt_b": inputs[P + "dt_proj_b"],
        "A_log": inputs[P + "A_log"], "Dp": inputs[P + "Dp"],
        "opT": inputs[P + "out_proj_w"].T, "mn_g": inputs[P + "mnorm_g"],
        "mn_b": inputs[P + "mnorm_b"], "n1_g": inputs[P + "n1_g"],
        "n1_b": inputs[P + "n1_b"], "n2_g": inputs[P + "n2_g"],
        "n2_b": inputs[P + "n2_b"], "n3_g": inputs[P + "n3_g"],
        "n3_b": inputs[P + "n3_b"], "w1T": inputs[P + "ffn_w1"].T,
        "w2T": inputs[P + "ffn_w2"].T, "feT": inputs[P + "fe_w"].T,
        "fe_b": inputs[P + "fe_b"], "WrT": inputs["L3_ffn_W_r"].T,
        "WkT": inputs["L3_ffn_W_k"].T, "WvT": inputs["L3_ffn_W_v"].T,
        "mu_r": inputs["L3_ffn_mu_r"], "mu_k": inputs["L3_ffn_mu_k"],
    }


def _b3_inmap(core, inputs, bb):
    m = {"xT": np.ascontiguousarray(inputs["x_in"][core].T)}
    for k, v in _b3_wmap(inputs).items():
        m["w_" + k] = np.ascontiguousarray(v)
    for cname, arr in bb.consts.items():
        m[cname] = arr
    return m


def run_block3(x_in, inputs):
    bb = build_block3()
    nc = bb.finalize()
    in_maps = [_b3_inmap(c, {**inputs, "x_in": x_in}, bb) for c in range(NCORE)]
    res = run_bass_kernel_spmd(nc, in_maps, core_ids=list(range(NCORE)))
    out = np.stack([res.results[c]["yT"].T for c in range(NCORE)])
    return np.ascontiguousarray(out)




def _l2_rope_tables():
    DH, W = 64, 256
    half = np.arange(0, DH, 2, dtype=np.float64)
    inv_freq = 1.0 / 10000.0 ** (half / DH)
    t = np.arange(T, dtype=np.float64)
    freqs = np.outer(t, inv_freq)
    base = (half + 0.4 * DH) / (1.4 * DH)
    power = (t - T // 2) / (W // 2)
    scale = base[None, :] ** power[:, None]
    cos, sin = np.cos(freqs), np.sin(freqs)
    CSQ = np.concatenate([cos * scale, cos * scale], 1) * 8.0
    SSQ = np.concatenate([-sin * scale, sin * scale], 1) * 8.0
    CIK = np.concatenate([cos / scale, cos / scale], 1)
    SIK = np.concatenate([-sin / scale, sin / scale], 1)
    return (CSQ.astype(np.float32), SSQ.astype(np.float32),
            CIK.astype(np.float32), SIK.astype(np.float32))


def _l2_masks():
    import ml_dtypes
    p = np.arange(128)[:, None]
    f = np.arange(256)[None, :]
    NEG = np.float32(-1e30)
    cur0 = np.where(f > p, NEG, 0.0).astype(np.float32)
    prev0 = np.where(f < p, NEG, 0.0).astype(np.float32)
    prev1 = np.where(f < 128 + p, NEG, 0.0).astype(np.float32)
    cur1 = np.where(f > 128 + p, NEG, 0.0).astype(np.float32)
    return {"m_cur0": cur0, "m_prev0": prev0, "m_prev1": prev1, "m_cur1": cur1}


def build_block2(stop_after=None, dbg=False, bb=None, xT=None, yT=None):
    if bb is None:
        bb = BB("b2")
        xT = bb.dram_in("xT", [D, T])
        yT = bb.dram_out("yT", [D, T])
    HH, DH, WIN = 16, 64, 256
    DI = 2730
    NHB = 22
    w = {}
    for nm, shp in [("log_a", [D]), ("WbT", [D, D]), ("WcT", [D, D]),
                    ("gam", [D]), ("bet", [D]),
                    ("Wqkv", [6, D, 3072]), ("qs", [6, DH]), ("ks", [6, DH]),
                    ("Wo", [6, 1024, D]), ("ln_g", [6, D]), ("ln_b", [6, D]),
                    ("W1", [6, D, 2 * DI]), ("b1", [6, 2 * DI]),
                    ("W2", [6, DI, D]), ("b2", [6, D])]:
        w[nm] = bb.dram_in("w_" + nm, shp)
    dram_dbg = bb.dram_out if dbg else bb.dram_tmp
    xn_s = dram_dbg("xn_s", [D, T])
    xnb_s = dram_dbg("xnb_s", [D, T], BF16)
    xl_s = [dram_dbg(f"xl{i}_s", [D, T]) for i in range(2)]
    xlb_s = [dram_dbg(f"xlb{i}_s", [D, T], BF16) for i in range(2)]

    setup_common(bb, ps7_bf16=True)
    AR = Arena(bb, 176 * 1024)
    CSQ, SSQ, CIK, SIK = _l2_rope_tables()
    ropes = {nm: bb.const("c_" + nm, v) for nm, v in
             [("CSQ", CSQ), ("SSQ", SSQ), ("CIK", CIK), ("SIK", SIK)]}
    import ml_dtypes
    masks = {}
    for nm, v in _l2_masks().items():
        ap = bb.sbuf(nm, [128, 256], BF16)
        bb.dma("gpsimd", ap, bb.const("c_" + nm, v.astype(ml_dtypes.bfloat16)),
               writes=(nm,))
        masks[nm] = ap
    gamc = vec_to_cols(bb, w["gam"], D, "gam")
    betc = vec_to_cols(bb, w["bet"], D, "bet")
    la_c = vec_to_cols(bb, w["log_a"], D, "la")
    a_c = bb.sbuf("a_c", [128, NDT], F32)
    bb.act(a_c, la_c, AF.Exp, reads=("la_c",), writes=("a_c",))
    bb.p.barrier()

    xck = [AR.take([128, NDT, TCH]) for _ in range(2)]
    o_f32 = [AR.take([128, TCH]) for _ in range(2)]
    lns = _ln_scratch(bb, AR)
    for ch in range(NCH):
        j = ch % 2
        sl = slice(ch * TCH, (ch + 1) * TCH)
        bb.dma("sync", xck[j], xT[:, sl].rearrange("(i p) c -> p i c", p=128),
               writes=(f"xck{j}",))

        def post2(i, ap, buf, sl=sl):
            bb.dma("scalar", xn_s[i * 128:(i + 1) * 128, sl], ap, reads=(buf,),
                   writes=(f"xn_s_{i}_{sl.start}",))
            bb.dma("gpsimd", xnb_s[i * 128:(i + 1) * 128, sl], ap, reads=(buf,),
                   writes=(f"xnb_s_{i}_{sl.start}",))
            bb.dma("sync", xl_s[0][i * 128:(i + 1) * 128, sl], ap, reads=(buf,),
                   writes=("XL0",))
            bb.dma("gpsimd", xlb_s[0][i * 128:(i + 1) * 128, sl], ap, reads=(buf,),
                   writes=("XL0B",))

        outs = [{"sink": lambda i, j=j: (o_f32[j], f"of{j}"), "mode": "plain",
                 "post": post2}]
        ln_chunked(bb, xck[j], f"xck{j}", outs, "ln0", ch, lns)
    bb.p.barrier()
    AR.reset_front()
    if stop_after == 0:
        return bb

    ub_s = dram_dbg("ub_s", [D, T])
    WB = AR.take([128, NDT * D], BF16)
    WC = AR.take([128, NDT * D], BF16)
    for k in range(NDT):
        bb.dma("gpsimd", WB[:, k * D:(k + 1) * D],
               w["WbT"][k * 128:(k + 1) * 128, :], writes=(f"WB{k}",))
        bb.dma("gpsimd", WC[:, k * D:(k + 1) * D],
               w["WcT"][k * 128:(k + 1) * 128, :], writes=(f"WC{k}",))
    xnck = [AR.take([128, NDT, TCH], BF16) for _ in range(2)]
    ubc = [AR.take([128, TCH]) for _ in range(2)]
    ps2 = [bb.PS[3], bb.PS[4]]
    for ch in range(NCH):
        j = ch % 2
        sl = slice(ch * TCH, (ch + 1) * TCH)
        bb.dma("sync", xnck[j], xnb_s[:, sl].rearrange("(i p) c -> p i c", p=128),
               reads=tuple(f"xnb_s_{i}_{sl.start}" for i in range(NDT)),
               writes=(f"xnck{j}",))
        for i in range(NDT):
            jm = i % 2
            mmu = bb.mm_acc(
                [(ps2[jm][:, :], WB[:, k * D + i * 128: k * D + (i + 1) * 128],
                  xnck[j][:, k, :]) for k in range(NDT)],
                reads=tuple(f"WB{k}" for k in range(NDT)) + (f"xnck{j}",),
                writes=(f"PS{3 + jm}",))
            cpt = bb.copy(ubc[jm], ps2[jm][:, :], reads=(f"PS{3 + jm}",),
                          writes=(f"ubc{jm}",), deps=[mmu])
            bb.dma("scalar", ub_s[i * 128:(i + 1) * 128, sl], ubc[jm],
                   reads=(f"ubc{jm}",), writes=(f"ub_{i}_{ch}",))
    stb_s = dram_dbg("stb_s", [D, T], BF16)
    ub = AR.take([128, T])
    stbuf = AR.take([128, T])
    for i in range(NDT):
        bb.dma("sync", ub, ub_s[i * 128:(i + 1) * 128, :],
               reads=tuple(f"ub_{i}_{c2}" for c2 in range(NCH)), writes=("ub",))
        sc = bb.scan(stbuf, a_c[:, i:i + 1].broadcast_to((128, T)), ub, 0.0,
                     reads=("a_c", "ub"), writes=("stbuf",))
        bb.dma("gpsimd", stb_s[i * 128:(i + 1) * 128, :], stbuf,
               reads=("stbuf",), writes=(f"stb{i}",))
    yck = AR.take([128, NDT, TCH])
    xres = [AR.take([128, TCH]) for _ in range(2)]
    oln = [AR.take([128, TCH]) for _ in range(2)]
    stc = [AR.take([128, NDT, TCH], BF16) for _ in range(2)]
    lns1 = _ln_scratch(bb, AR)
    for ch in range(NCH):
        j = ch % 2
        sl = slice(ch * TCH, (ch + 1) * TCH)
        bb.dma("sync", stc[j], stb_s[:, sl].rearrange("(i p) c -> p i c", p=128),
               reads=tuple(f"stb{k}" for k in range(NDT)), writes=(f"stc{j}",))
        for do in range(NDT):
            jm = do % 2
            mmy = bb.mm_acc(
                [(ps2[jm][:, :], WC[:, k * D + do * 128: k * D + (do + 1) * 128],
                  stc[j][:, k, :]) for k in range(NDT)],
                reads=tuple(f"WC{k}" for k in range(NDT)) + (f"stc{j}",),
                writes=(f"PS{3 + jm}",))
            bb.copy(yck[:, do, :], ps2[jm][:, :], reads=(f"PS{3 + jm}",),
                    writes=("yck",), deps=[mmy])

        def postY(i, ap, buf, sl=sl, j=j):
            bb.dma("sync", xres[j], xT[i * 128:(i + 1) * 128, sl],
                   writes=(f"xres{j}",))
            bb.tt(ap, ap, xres[j], ALU.add, reads=(buf, f"xres{j}"), writes=(buf,))
            bb.dma("scalar", yT[i * 128:(i + 1) * 128, sl], ap, reads=(buf,),
                   writes=(f"yT_{i}_{ch}",))

        outs = [{"sink": lambda i, j=j: (oln[j], f"oln{j}"), "mode": "gb",
                 "gamma": gamc, "beta": betc, "gbufs": ("gam_c", "bet_c"),
                 "post": postY}]
        ln_chunked(bb, yck, "yck", outs, "ln1", ch, lns1)
    bb.p.barrier()
    AR.reset_front()
    if stop_after == 1:
        return bb

    PSBF = bb.PSBF
    for sub in range(6):
        src_x, src_xb = xl_s[0], xlb_s[0]
        dst_x, dst_xb = xl_s[1], xlb_s[1]
        sp = f"s{sub}_"
        WQKV = AR.take([128, NDT * 3072], BF16)
        WOs = AR.take([128, NDT * D], BF16)
        for k in range(NDT):
            bb.dma("gpsimd", WQKV[:, k * 3072:(k + 1) * 3072],
                   w["Wqkv"][sub, k * 128:(k + 1) * 128, :], writes=(sp + f"WQ{k}",))
            bb.dma("gpsimd", WOs[:, k * D:(k + 1) * D],
                   w["Wo"][sub, k * 128:(k + 1) * 128, :], writes=(sp + f"WO{k}",))
        qsr = AR.take([1, 2 * DH])
        bb.dma("sync", qsr[:, 0:DH], w["qs"][sub].unsqueeze(0), writes=("qsr",))
        bb.dma("sync", qsr[:, DH:2 * DH], w["ks"][sub].unsqueeze(0), writes=("qsr",))
        qs_b = AR.take([128, 2 * DH])
        mmq = bb.mm(bb.PS[0][:, 0:128], bb.ones_1x128, qsr,
                    reads=("qsr", "ones1x128"), writes=("PS0",))
        bb.copy(qs_b, bb.PS[0][:, 0:128], reads=("PS0",), writes=("qs_b",),
                deps=[mmq])
        xsb = [AR.take([128, NDT, WIN], BF16) for _ in range(2)]
        qkvt = AR.take([128, 2, 3072], BF16)
        ropet = AR.take([128, 2, 4 * DH])
        sq_ = AR.take([128, D])
        ssq = AR.take([128, 2 * HH])
        qn_ = AR.take([128, D])
        r1_ = AR.take([128, D])
        r2_ = AR.take([128, D])
        qkf = [AR.take([128, 16 * 128], BF16) for _ in range(2)]
        qf_ = AR.take([128, 16 * 128], BF16)
        vtk = [AR.take([128, 2, 1024], BF16) for _ in range(2)]
        at_sb = AR.take([128, 512], BF16)
        dencol = AR.take([128, 8])
        attf = AR.take([128, NDT, WIN], BF16)
        xlc = [AR.take([128, NDT, WIN]) for _ in range(2)]
        och = [AR.take([128, WIN]) for _ in range(2)]
        ps_sim = [bb.PS[2], bb.PS[6]]
        ps_out = [bb.PS[5], bb.PS[5]]
        for wi in range(16):
            cur = wi % 2
            wsl = slice(wi * WIN, (wi + 1) * WIN)
            ld0 = bb.dma("sync", xsb[cur][:, 0:4, :],
                         src_xb[0:512, wsl].rearrange("(i p) t -> p i t", p=128),
                         reads=("XL0B",), writes=(sp + "xsb",))
            if wi == 0:
                bb.memset(xsb[cur][:, 4:8, 0:1], 0.0, writes=(sp + "xsb",))
                bb.dma("sync", xsb[cur][:, 4:8, 1:WIN],
                       src_xb[512:1024, 0:WIN - 1].rearrange(
                           "(i p) t -> p i t", p=128),
                       reads=("XL0B",), writes=(sp + "xsb",))
            else:
                bb.dma("sync", xsb[cur][:, 4:8, :],
                       src_xb[512:1024, wi * WIN - 1:(wi + 1) * WIN - 1].rearrange(
                           "(i p) t -> p i t", p=128),
                       reads=("XL0B",), writes=(sp + "xsb",))
            for tsb in range(2):
                tt0 = wi * WIN + tsb * 128
                bb.dma("scalar", ropet[:, tsb, 0:DH], ropes["CSQ"][tt0:tt0 + 128, :],
                       writes=(sp + "ropet",))
                bb.dma("scalar", ropet[:, tsb, DH:2 * DH],
                       ropes["SSQ"][tt0:tt0 + 128, :], writes=(sp + "ropet",))
                bb.dma("scalar", ropet[:, tsb, 2 * DH:3 * DH],
                       ropes["CIK"][tt0:tt0 + 128, :], writes=(sp + "ropet",))
                bb.dma("scalar", ropet[:, tsb, 3 * DH:4 * DH],
                       ropes["SIK"][tt0:tt0 + 128, :], writes=(sp + "ropet",))
            for tsb in range(2):
                for n2 in range(6):
                    jm = n2 % 2
                    mm = bb.mm_acc(
                        [(ps2[jm][:, :], xsb[cur][:, k, tsb * 128:(tsb + 1) * 128],
                          WQKV[:, k * 3072 + n2 * 512: k * 3072 + (n2 + 1) * 512])
                         for k in range(NDT)],
                        reads=(sp + "xsb",) + tuple(sp + f"WQ{k}" for k in range(NDT)),
                        writes=(f"PS{3 + jm}",))
                    bb.copy(qkvt[:, tsb, n2 * 512:(n2 + 1) * 512], ps2[jm][:, :],
                            reads=(f"PS{3 + jm}",), writes=(sp + "qkvt",), deps=[mm])
                bb.copy(vtk[cur][:, tsb, :], qkvt[:, tsb, 2048:3072],
                        reads=(sp + "qkvt",), writes=(sp + f"v{cur}",), eng="vector")
            for (qoff, tblc, tbls, scoff, dstf, dstbuf) in [
                    (0, 0, DH, 0, qf_, sp + "qf"),
                    (1024, 2 * DH, 3 * DH, DH, qkf[cur], sp + f"kf{cur}")]:
                for tsb in range(2):
                    qsl_ = qkvt[:, tsb, qoff:qoff + 1024]
                    s1 = bb.act(sq_, qsl_, AF.Square, reads=(sp + "qkvt",),
                                writes=(sp + "sq",))
                    s2 = bb.reduce(ssq[:, 0:HH],
                                   sq_.rearrange("p (h e) -> p h e", e=DH),
                                   ALU.add, reads=(sp + "sq",),
                                   writes=(sp + "ssq",), deps=[s1])
                    s3 = bb.act(ssq[:, HH:2 * HH], ssq[:, 0:HH], AF.Sqrt,
                                reads=(sp + "ssq",), writes=(sp + "ssq",), deps=[s2])
                    s4 = bb.ts(ssq[:, HH:2 * HH], ssq[:, HH:2 * HH], 1e-12, None,
                               op0=ALU.max, reads=(sp + "ssq",),
                               writes=(sp + "ssq",), deps=[s3])
                    s5 = bb.recip(ssq[:, HH:2 * HH], ssq[:, HH:2 * HH],
                                  reads=(sp + "ssq",), writes=(sp + "ssq",))
                    inv = ssq[:, HH:2 * HH].unsqueeze(2).broadcast_to((128, HH, DH))
                    n1 = bb.tt(qn_.rearrange("p (h e) -> p h e", e=DH),
                               qsl_.rearrange("p (h e) -> p h e", e=DH), inv,
                               ALU.mult, reads=(sp + "qkvt", sp + "ssq"),
                               writes=(sp + "qn",), deps=[s5])
                    qsb = qs_b[:, scoff:scoff + DH].unsqueeze(1).broadcast_to(
                        (128, HH, DH))
                    n2_ = bb.tt(qn_.rearrange("p (h e) -> p h e", e=DH),
                                qn_.rearrange("p (h e) -> p h e", e=DH), qsb,
                                ALU.mult, reads=(sp + "qn", "qs_b"),
                                writes=(sp + "qn",), deps=[n1])
                    ct = ropet[:, tsb, tblc:tblc + DH].unsqueeze(1).broadcast_to(
                        (128, HH, DH))
                    st_ = ropet[:, tsb, tbls:tbls + DH].unsqueeze(1).broadcast_to(
                        (128, HH, DH))
                    t1 = bb.tt(r1_.rearrange("p (h e) -> p h e", e=DH),
                               qn_.rearrange("p (h e) -> p h e", e=DH), ct,
                               ALU.mult, reads=(sp + "qn", sp + "ropet"),
                               writes=(sp + "r1",), deps=[n2_])
                    swp = qn_.rearrange("p (h two c) -> p h two c", two=2,
                                        c=DH // 2)[:, :, ::-1, :]
                    t2 = bb.tt(r2_.rearrange("p (h two c) -> p h two c", two=2,
                                             c=DH // 2),
                               swp, st_.rearrange("p h (two c) -> p h two c",
                                                  two=2),
                               ALU.mult, reads=(sp + "qn", sp + "ropet"),
                               writes=(sp + "r2",), deps=[n2_])
                    t3 = bb.tt(r1_, r1_, r2_, ALU.add,
                               reads=(sp + "r1", sp + "r2"), writes=(sp + "r1",),
                               deps=[t1, t2])
                    for bk in range(NDT):
                        jj = bk % 2
                        tr = bb.transpose(
                            bb.PS[jj][:, 128:256],
                            r1_[:, bk * 128:(bk + 1) * 128], bb.ident_f32,
                            reads=(sp + "r1", "ident_f32"), writes=(f"PS{jj}",))
                        bb.copy(dstf[:, (bk * 2 + tsb) * 128:(bk * 2 + tsb + 1) * 128],
                                bb.PS[jj][:, 128:256],
                                reads=(f"PS{jj}",), writes=(dstbuf,), deps=[tr])
            for h in range(HH):
                bk, rb = h // 2, (h % 2) * 64
                for tsb in range(2):
                    sj = (h * 2 + tsb) % 2
                    simb = "PS2" if sj == 0 else "PS6"
                    pairs = []
                    if wi > 0:
                        pairs.append((ps_sim[sj][:, 0:256],
                                      qf_[rb:rb + 64, (bk * 2 + tsb) * 128:
                                          (bk * 2 + tsb + 1) * 128],
                                      qkf[1 - cur][rb:rb + 64, bk * 256:(bk + 1) * 256]))
                        pairs.append((ps_sim[sj][:, 0:256], bb.ident_bf,
                                      masks["m_prev0" if tsb == 0 else "m_prev1"]))
                    pairs.append((ps_sim[sj][:, 256:512],
                                  qf_[rb:rb + 64, (bk * 2 + tsb) * 128:
                                      (bk * 2 + tsb + 1) * 128],
                                  qkf[cur][rb:rb + 64, bk * 256:(bk + 1) * 256]))
                    pairs.append((ps_sim[sj][:, 256:512], bb.ident_bf,
                                  masks["m_cur0" if tsb == 0 else "m_cur1"]))
                    def emit_sim(e, pairs=pairs):
                        ins = None
                        offs = [o.offset for (o, _, _) in pairs]
                        for i_, (o, l, r) in enumerate(pairs):
                            first = offs.index(o.offset) == i_
                            last = (len(offs) - 1 - offs[::-1].index(o.offset)) == i_
                            ins = e.matmul(o, l, r, start=first, stop=last)
                        return ins
                    rd = (sp + "qf", sp + f"kf{cur}", "ident_bf",
                          "m_cur0", "m_cur1", "m_prev0", "m_prev1")
                    if wi > 0:
                        rd = rd + (sp + f"kf{1 - cur}",)
                    mms = bb.p.task("tensor", emit_sim, reads=rd, writes=(simb,))
                    lo = 256 if wi == 0 else 0
                    mx = bb.reduce(dencol[:, sj:sj + 1], ps_sim[sj][:, lo:512],
                                   ALU.max, reads=(simb,),
                                   writes=(sp + "dencol",), negate=True,
                                   deps=[mms])
                    ex = bb.act(at_sb[:, lo:512], ps_sim[sj][:, lo:512], AF.Exp,
                                bias=dencol[:, sj:sj + 1],
                                accum_out=dencol[:, 2 + sj:3 + sj],
                                reads=(simb, sp + "dencol"),
                                writes=(sp + "at_sb",), deps=[mx])
                    rc = bb.recip(dencol[:, 2 + sj:3 + sj],
                                  dencol[:, 2 + sj:3 + sj],
                                  reads=(sp + "dencol",), writes=(sp + "dencol",))
                    nr = bb.ts(at_sb[:, lo:512], at_sb[:, lo:512],
                               dencol[:, 2 + sj:3 + sj], None,
                               reads=(sp + "at_sb", sp + "dencol"),
                               writes=(sp + "at_sb",), deps=[ex, rc])
                    nsb = (512 - lo) // 128
                    trs = []
                    for sb_ in range(nsb):
                        tr = bb.transpose(
                            PSBF[:, sj * 512 + sb_ * 128: sj * 512 + (sb_ + 1) * 128],
                            at_sb[:, lo + sb_ * 128: lo + (sb_ + 1) * 128],
                            bb.ident_bf, reads=(sp + "at_sb", "ident_bf"),
                            writes=("PS7",), deps=[nr])
                        trs.append(tr)
                    ev = bb.copy(at_sb[:, lo:512].bitcast(BF16)[:, 0:(512 - lo)]
                                 if False else at_sb[:, lo:512],
                                 PSBF[:, sj * 512: sj * 512 + (512 - lo)],
                                 reads=("PS7",), writes=(sp + "at_sb",),
                                 deps=trs, eng="vector")
                    avp = []
                    for sb_ in range(nsb):
                        vsrc = (vtk[1 - cur] if (wi > 0 and lo + sb_ * 128 < 256)
                                else vtk[cur])
                        vts = ((lo + sb_ * 128) % 256) // 128
                        avp.append((ps_out[sj][0:64, 0:128],
                                    vsrc[:, vts, h * 64:(h + 1) * 64],
                                    at_sb[:, lo + sb_ * 128:lo + (sb_ + 1) * 128]))
                    va_rd = (sp + "at_sb", sp + f"v{cur}")
                    if wi > 0:
                        va_rd = va_rd + (sp + f"v{1 - cur}",)
                    av = bb.mm_acc(avp, reads=va_rd, writes=("PS5",))
                    bb.copy(attf[rb:rb + 64, bk, tsb * 128:(tsb + 1) * 128],
                            ps_out[sj][0:64, 0:128], reads=("PS5",),
                            writes=(sp + "attf",), deps=[av])
            bb.dma("sync", xlc[cur], src_x[:, wsl].rearrange("(i p) t -> p i t", p=128),
                   reads=("XL0",), writes=(sp + "xlc",))
            for do in range(NDT):
                jm = do % 2
                mo = bb.mm_acc(
                    [(ps2[jm][:, 0:WIN], WOs[:, k * D + do * 128: k * D + (do + 1) * 128],
                      attf[:, k, :]) for k in range(NDT)],
                    reads=tuple(sp + f"WO{k}" for k in range(NDT)) + (sp + "attf",),
                    writes=(f"PS{3 + jm}",))
                ad = bb.tt(och[jm], ps2[jm][:, 0:WIN], xlc[cur][:, do, :], ALU.add,
                           reads=(f"PS{3 + jm}", sp + "xlc"), writes=(sp + f"och{jm}",),
                           deps=[mo])
                bb.dma("scalar", dst_x[do * 128:(do + 1) * 128, wsl], och[jm],
                       reads=(sp + f"och{jm}",), writes=("XL1",))
                bb.dma("gpsimd", dst_xb[do * 128:(do + 1) * 128, wsl], och[jm],
                       reads=(sp + f"och{jm}",), writes=("XL1B",))
        bb.p.barrier()
        AR.reset_front()
        if stop_after == 100 + sub:
            return bb

        f_x, f_xb = dst_x, dst_xb
        g_x, g_xb = src_x, src_xb
        W2s = AR.take([128, NHB * D], BF16)
        for k in range(NHB):
            hr = min(128, DI - k * 128)
            bb.dma("gpsimd", W2s[0:hr, k * D:(k + 1) * D],
                   w["W2"][sub, k * 128:k * 128 + hr, :], writes=(sp + f"W2_{k}",))
        b1row = AR.take([1, 2 * DI], BF16)
        bb.dma("gpsimd", b1row, w["b1"][sub].unsqueeze(0), writes=(sp + "b1row",))
        b2c = vec_to_cols(bb, w["b2"][sub], D, sp + "b2")
        gcl = vec_to_cols(bb, w["ln_g"][sub], D, sp + "gc")
        bcl = vec_to_cols(bb, w["ln_b"][sub], D, sp + "bc")
        w1cb = [AR.take([128, NDT, 128], BF16) for _ in range(4)]
        w1gb = [AR.take([128, NDT, 128], BF16) for _ in range(4)]
        xsf = [AR.take([128, NDT, TCH])] * 2
        xnsb = AR.take([128, NDT, TCH], BF16)
        AG = AR.take([128, NHB * TCH], BF16)
        gg = AR.take([128, TCH])
        lnsF = _ln_scratch(bb, AR)
        xrs = [AR.take([128, TCH]) for _ in range(2)]
        ochf = [AR.take([128, TCH]) for _ in range(2)]
        for ch in range(NCH):
            j = ch % 2
            sl = slice(ch * TCH, (ch + 1) * TCH)
            ldx = bb.dma("sync", xsf[j][:, 0:4, :],
                         f_x[0:512, sl].rearrange("(i p) t -> p i t", p=128),
                         reads=("XL1",), writes=(sp + "xsf0",))
            if ch == 0:
                bb.memset(xsf[j][:, 4:8, 0:1], 0.0, writes=(sp + "xsf0",))
                bb.dma("sync", xsf[j][:, 4:8, 1:TCH],
                       f_x[512:1024, 0:TCH - 1].rearrange("(i p) t -> p i t", p=128),
                       reads=("XL1",), writes=(sp + "xsf0",))
            else:
                bb.dma("sync", xsf[j][:, 4:8, :],
                       f_x[512:1024, ch * TCH - 1:(ch + 1) * TCH - 1].rearrange(
                           "(i p) t -> p i t", p=128),
                       reads=("XL1",), writes=(sp + "xsf0",))
            outs = [{"sink": lambda i, j=j: (xnsb[:, i, :], sp + "xnsb"),
                     "mode": "gb", "gamma": gcl, "beta": bcl,
                     "gbufs": (sp + "gc_c", sp + "bc_c")}]
            ln_chunked(bb, xsf[j], sp + "xsf0", outs, sp + "lnF", ch, lnsF)
            for m in range(NHB):
                hr = min(128, DI - m * 128)
                jm = m % 4
                bb.dma("gpsimd", w1cb[jm][:, :, 0:hr],
                       w["W1"][sub][:, m * 128:m * 128 + hr].rearrange(
                           "(k p) c -> p k c", p=128),
                       writes=(sp + f"w1a{jm}",))
                psa = bb.PS[3] if jm % 2 == 0 else bb.PS[5]
                psaname = "PS3" if jm % 2 == 0 else "PS5"
                pa = [(psa[0:hr, :], w1cb[jm][:, k, 0:hr], xnsb[:, k, :])
                      for k in range(NDT)]
                pa.append((psa[0:hr, :], b1row[:, m * 128:m * 128 + hr],
                           bb.ones_1x512_bf))
                ma = bb.mm_acc(pa, reads=(sp + f"w1a{jm}", sp + "xnsb",
                                          "ones1x512bf", sp + "b1row"),
                               writes=(psaname,))
                w1g = w1gb[jm]
                bb.dma("gpsimd", w1g[:, :, 0:hr],
                       w["W1"][sub][:, DI + m * 128:DI + m * 128 + hr].rearrange(
                           "(k p) c -> p k c", p=128),
                       writes=(sp + f"w1g{jm}",))
                psg = bb.PS[4] if jm % 2 == 0 else bb.PS[6]
                psgname = "PS4" if jm % 2 == 0 else "PS6"
                pg = [(psg[0:hr, :], w1g[:, k, 0:hr], xnsb[:, k, :])
                      for k in range(NDT)]
                pg.append((psg[0:hr, :], b1row[:, DI + m * 128:DI + m * 128 + hr],
                           bb.ones_1x512_bf))
                mg = bb.mm_acc(pg, reads=(sp + f"w1g{jm}", sp + "xnsb",
                                          "ones1x512bf", sp + "b1row"),
                               writes=(psgname,))
                ge = bb.act(gg[0:hr, :], psg[0:hr, :], AF.Gelu,
                            reads=(psgname,), writes=(sp + "gg",), deps=[mg])
                bb.tt(AG[0:hr, m * TCH:(m + 1) * TCH], psa[0:hr, :],
                      gg[0:hr, :], ALU.mult,
                      reads=(psaname, sp + "gg"), writes=(sp + f"AG{m}",),
                      deps=[ma, ge])
            for do in range(NDT):
                jm = do % 2
                psw = bb.PS[2] if jm == 0 else bb.PS[7]
                pswname = "PS2" if jm == 0 else "PS7"
                m2 = bb.mm_acc(
                    [(psw[:, :], W2s[0:min(128, DI - k * 128),
                                     k * D + do * 128: k * D + (do + 1) * 128],
                      AG[0:min(128, DI - k * 128), k * TCH:(k + 1) * TCH])
                     for k in range(NHB)],
                    reads=tuple(sp + f"W2_{k}" for k in range(NHB))
                    + tuple(sp + f"AG{k}" for k in range(NHB)),
                    writes=(pswname,))
                bb.dma("sync", xrs[jm], f_x[do * 128:(do + 1) * 128, sl],
                       reads=("XL1",), writes=(sp + f"xrs{jm}",))
                s1 = bb.stt(ochf[jm], psw[:, :], b2c[:, do:do + 1],
                            xrs[jm], op0=ALU.add, op1=ALU.add,
                            reads=(pswname, sp + "b2_c", sp + f"xrs{jm}"),
                            writes=(sp + f"ochf{jm}",), deps=[m2])
                bb.dma("scalar", g_x[do * 128:(do + 1) * 128, sl], ochf[jm],
                       reads=(sp + f"ochf{jm}",), writes=("XL0",))
                bb.dma("gpsimd", g_xb[do * 128:(do + 1) * 128, sl], ochf[jm],
                       reads=(sp + f"ochf{jm}",), writes=("XL0B",))
        bb.p.barrier()
        AR.reset_front()
        if stop_after == 200 + sub:
            return bb

    fin = xl_s[0]
    fck = [AR.take([128, NDT, TCH]) for _ in range(2)]
    yck2 = [AR.take([128, NDT, TCH]) for _ in range(2)]
    for ch in range(NCH):
        j = ch % 2
        sl = slice(ch * TCH, (ch + 1) * TCH)
        bb.dma("sync", fck[j], fin[:, sl].rearrange("(i p) t -> p i t", p=128),
               reads=("XL0",), writes=(f"fck{j}",))
        bb.dma("sync", yck2[j], yT[:, sl].rearrange("(i p) t -> p i t", p=128),
               reads=tuple(f"yT_{i}_{ch}" for i in range(NDT)),
               writes=(f"yck2{j}",))
        for i in range(NDT):
            ad = bb.tt(yck2[j][:, i, :], yck2[j][:, i, :], fck[j][:, i, :],
                       ALU.add, reads=(f"yck2{j}", f"fck{j}"),
                       writes=(f"yck2{j}",))
            bb.dma("scalar", yT[i * 128:(i + 1) * 128, sl], yck2[j][:, i, :],
                   reads=(f"yck2{j}",), writes=(f"yTf_{i}_{ch}",))
    return bb


def _b2_wmap(inputs):
    return {
        "log_a": inputs["L2_attn_log_a"], "WbT": inputs["L2_attn_W_b"].T,
        "WcT": inputs["L2_attn_W_c"].T, "gam": inputs["L2_attn_gamma"],
        "bet": inputs["L2_attn_beta"], "Wqkv": inputs["L2_ffn_Wqkv"],
        "qs": inputs["L2_ffn_q_scale"], "ks": inputs["L2_ffn_k_scale"],
        "Wo": inputs["L2_ffn_Wo"], "ln_g": inputs["L2_ffn_ln_g"],
        "ln_b": inputs["L2_ffn_ln_b"], "W1": inputs["L2_ffn_W1"],
        "b1": inputs["L2_ffn_b1"], "W2": inputs["L2_ffn_W2"],
        "b2": inputs["L2_ffn_b2"],
    }


def _b2_inmap(core, inputs, bb):
    m = {"xT": np.ascontiguousarray(inputs["x_in"][core].T)}
    for k, v in _b2_wmap(inputs).items():
        m["w_" + k] = np.ascontiguousarray(v)
    for cname, arr in bb.consts.items():
        m[cname] = arr
    return m


def run_block2(x_in, inputs, stop_after=None):
    bb = build_block2(stop_after=stop_after)
    nc = bb.finalize()
    in_maps = [_b2_inmap(c, {**inputs, "x_in": x_in}, bb) for c in range(NCORE)]
    res = run_bass_kernel_spmd(nc, in_maps, core_ids=list(range(NCORE)))
    out = np.stack([res.results[c]["yT"].T for c in range(NCORE)])
    return np.ascontiguousarray(out)



def _b1_wmap(inputs):
    names = {"ln_g": "L1_attn_ln_g", "ln_b": "L1_attn_ln_b", "Wh": "L1_attn_Wh",
             "bh": "L1_attn_bh", "Wqk": "L1_attn_Wqk", "bqk": "L1_attn_bqk",
             "gam": "L1_attn_gamma", "bet": "L1_attn_beta", "Wo": "L1_attn_Wo",
             "bo": "L1_attn_bo", "W1": "L1_ffn_W1", "b1": "L1_ffn_b1",
             "W2": "L1_ffn_W2", "b2": "L1_ffn_b2"}
    return {k: inputs[s] for k, s in names.items()}


def _b1_inmap(core, inputs, bb):
    m = {"xT": np.ascontiguousarray(inputs["x_in"][core].T)}
    for k, v in _b1_wmap(inputs).items():
        m["w_" + k] = np.ascontiguousarray(v)
    for cname, arr in bb.consts.items():
        m[cname] = arr
    return m


def run_block1(x_in, inputs):
    bb = build_block1()
    nc = bb.finalize()
    in_maps = [_b1_inmap(c, {**inputs, "x_in": x_in}, bb) for c in range(NCORE)]
    res = run_bass_kernel_spmd(nc, in_maps, core_ids=list(range(NCORE)))
    out = np.stack([res.results[c]["yT"].T for c in range(NCORE)])
    return np.ascontiguousarray(out)


def build_fused():
    bb = BB("fused")
    x0T = bb.dram_in("xT", [D, T])
    x1T = bb.dram_tmp("x1T", [D, T])
    x2T = bb.dram_tmp("x2T", [D, T])
    x3T = bb.dram_tmp("x3T", [D, T])
    yT = bb.dram_out("yT", [D, T])
    setup_common(bb)
    for pfx, fn, xi, yo in [("B0_", build_block0, x0T, x1T),
                            ("B1_", build_block1, x1T, x2T),
                            ("B2_", build_block2, x2T, x3T),
                            ("B3_", build_block3, x3T, yT)]:
        bb._pfx = pfx
        fn(bb=bb, xT=xi, yT=yo)
        bb.p.barrier()
        bb.sm_reset()
    bb._pfx = ""
    return bb


def _fused_inmap(core, inputs, bb):
    m = {"xT": np.ascontiguousarray(inputs["x_in"][core].T)}
    for pfx, wf in [("B0_", _b0_wmap), ("B1_", _b1_wmap), ("B2_", _b2_wmap),
                    ("B3_", _b3_wmap)]:
        for k, v in wf(inputs).items():
            m[pfx + "w_" + k] = np.ascontiguousarray(v)
    for cname, arr in bb.consts.items():
        m[cname] = arr
    return m


_FUSED_CACHE = {}


def kernel(**inputs):
    x = np.ascontiguousarray(inputs["L0_attn_x"], dtype=np.float32)
    if "bb" not in _FUSED_CACHE:
        bbf = build_fused()
        ncf = bbf.finalize()
        _FUSED_CACHE["bb"] = bbf
        _FUSED_CACHE["nc"] = ncf
    bbf, ncf = _FUSED_CACHE["bb"], _FUSED_CACHE["nc"]
    in_maps = [_fused_inmap(c, {**inputs, "x_in": x}, bbf) for c in range(NCORE)]
    res = run_bass_kernel_spmd(ncf, in_maps, core_ids=list(range(NCORE)))
    out = np.stack([res.results[c]["yT"].T for c in range(NCORE)])
    return np.ascontiguousarray(out)


# version 07 (final)
# speedup vs baseline: 1.2029x; 1.0044x over previous
import contextlib
import numpy as np
import concourse.bass as bass
import concourse.mybir as mybir
from concourse.bass_utils import run_bass_kernel_spmd

F32 = mybir.dt.float32
BF16 = mybir.dt.bfloat16
AF = mybir.ActivationFunctionType
ALU = mybir.AluOpType
AX = mybir.AxisListType

NCORE = 8
T = 4096
D = 1024
NDT = D // 128
TCH = 512
NCH = T // TCH



class Prog:

    ENG = ("tensor", "vector", "scalar", "gpsimd", "sync")
    POOL = 16
    PARANOID = False

    def __init__(self):
        self.tasks = {e: [] for e in self.ENG}
        self.writer = {}
        self.readers = {}
        self.floor = {e: 0 for e in self.ENG}
        self.dma_count = {e: 0 for e in self.ENG}
        self.dma_ord = {}

    def task(self, engine, emit, reads=(), writes=(), deps=(), dma=False,
             fence=False):
        my_idx = len(self.tasks[engine])
        dd = {}

        def add(pe, pidx):
            p_dma = self.tasks[pe][pidx][3]
            p_fence = self.tasks[pe][pidx][4]
            if (pe == engine and not p_dma and not dma and not p_fence
                    and not Prog.PARANOID):
                return
            cur = dd.get(pe, -1)
            if pidx > cur and not p_dma:
                dd[pe] = pidx
            elif p_dma:
                dd.setdefault(("D", pe), set()).add(pidx) if False else None
                key = ("D", pe, self.dma_ord[(pe, pidx)] % Prog.POOL)
                cur2 = dd.get(key, -1)
                if pidx > cur2:
                    dd[key] = pidx

        for e, i in deps:
            add(e, i)
        for b in reads:
            w = self.writer.get(b)
            if w:
                add(w[0], w[1])
            if b.startswith("PS"):
                for r in self.readers.get(b, ()):
                    add(r[0], r[1])
        for b in writes:
            w = self.writer.get(b)
            if w:
                add(w[0], w[1])
            for r in self.readers.get(b, ()):
                add(r[0], r[1])
        for e in self.ENG:
            if e != engine and self.floor[e] > 0:
                add(e, self.floor[e] - 1)
        self.tasks[engine].append([emit, dd, 16 if dma else 1, dma, fence])
        if dma:
            self.dma_ord[(engine, my_idx)] = self.dma_count[engine]
            self.dma_count[engine] += 1
        for b in reads:
            self.readers.setdefault(b, []).append((engine, my_idx, dma))
        for b in writes:
            self.writer[b] = (engine, my_idx, dma)
            self.readers[b] = []
        return (engine, my_idx)

    def barrier(self):
        for e in self.ENG:
            self.floor[e] = len(self.tasks[e])

    def emit(self, nc, block, sems, dpools):
        cum = {}
        for e in self.ENG:
            c, lst = 0, []
            for t in self.tasks[e]:
                c += 0 if t[3] else 1
                lst.append(c)
            cum[e] = lst

        def mk(e):
            def body(eng):
                waited = {}
                for idx, (emit_fn, dd, wgt, is_dma, _f) in enumerate(self.tasks[e]):
                    for key in sorted(dd, key=str):
                        pidx = dd[key]
                        if isinstance(key, tuple):
                            _, pe, slot = key
                            o = self.dma_ord[(pe, pidx)]
                            sem = dpools[pe][o % Prog.POOL]
                            val = 16 * (o // Prog.POOL + 1)
                        else:
                            pe = key
                            if self.tasks[pe][pidx][3]:
                                continue
                            sem = sems[pe]
                            val = cum[pe][pidx]
                        wk = id(sem)
                        if waited.get(wk, -1) < val:
                            eng.wait_ge(sem, val)
                            waited[wk] = val
                    if is_dma:
                        o = self.dma_ord[(e, idx)]
                        sem = dpools[e][o % Prog.POOL]
                        if o >= Prog.POOL:
                            gen = o // Prog.POOL
                            wk = id(sem)
                            if waited.get(wk, -1) < 16 * gen:
                                eng.wait_ge(sem, 16 * gen)
                                waited[wk] = 16 * gen
                        ins = emit_fn(eng)
                        assert ins is not None
                        ins.then_inc(sem, 16)
                    else:
                        ins = emit_fn(eng)
                        assert ins is not None
                        ins.then_inc(sems[e], wgt)
            return body

        for e in self.ENG:
            getattr(block, e)(mk(e))

    def final_waits(self):
        return [(e, len(self.tasks[e]) - 1) for e in self.ENG if self.tasks[e]]


class BB:
    def __init__(self, name):
        self.nc = bass.Bass("TRN2", target_bir_lowering=False)
        self.name = name
        self.es = contextlib.ExitStack()
        self.p = Prog()
        self.consts = {}
        self._cap = {}
        self._sb = {}

    def dram_in(self, name, shape, dtype=F32):
        name = getattr(self, "_pfx", "") + name
        return self.nc.dram_tensor(name, list(shape), dtype, kind="ExternalInput").ap()

    def dram_out(self, name, shape, dtype=F32):
        name = getattr(self, "_pfx", "") + name
        return self.nc.dram_tensor(name, list(shape), dtype, kind="ExternalOutput").ap()

    def dram_tmp(self, name, shape, dtype=F32):
        name = getattr(self, "_pfx", "") + name
        return self.nc.dram_tensor(name, list(shape), dtype, kind="Internal").ap()

    def finalize(self):
        nc = self.nc
        with contextlib.ExitStack() as es2:
            sems = {}
            for e in Prog.ENG:
                sems[e] = es2.enter_context(nc.semaphore(f"s_{e}"))
            dpools = {}
            for e in Prog.ENG:
                if self.p.dma_count[e] > 0:
                    n = min(Prog.POOL, max(1, self.p.dma_count[e]))
                    n = Prog.POOL
                    dpools[e] = [es2.enter_context(nc.semaphore(f"d_{e}{i}"))
                                 for i in range(n)]
            block = es2.enter_context(nc.Block())
            fw = self.p.final_waits()
            self.p.task("sync", lambda e: e.nop(), deps=fw)
            self.p.emit(nc, block, sems, dpools)
        self.es.close()
        return nc


_SB_GLOBALS = {"ident_f32", "ident_bf", "ones1x128", "ones128x1", "ones1x128bf",
               "ones1x512bf", "kbuf", "ARENA", "SMALLS"}


def _bb_sbuf(self, name, shape, dtype=F32):
    if name in self._sb:
        return self._sb[name]
    if name in _SB_GLOBALS:
        t = self.es.enter_context(self.nc.sbuf_tensor(name, list(shape), dtype))
        ap = t[:]
        self._sb[name] = ap
        return ap
    if not hasattr(self, "_smbase"):
        self._smbase = self.sbuf("SMALLS", [128, 24 * 1024], mybir.dt.uint8)
        self._smoff = 0
        self._smnames = set()
    esz = {F32: 4, BF16: 2}[dtype]
    nb = int(np.prod(shape[1:])) * esz
    nb = (nb + 127) // 128 * 128
    off = self._smoff
    self._smoff += nb
    assert self._smoff <= 24 * 1024, f"SMALLS overflow at {name}"
    v = self._smbase[:, off:off + nb].bitcast(dtype)
    v = v[:, 0:int(np.prod(shape[1:]))]
    if len(shape) == 3:
        v = v.rearrange("p (a b) -> p a b", b=shape[2])
    if shape[0] != 128:
        v = v[0:shape[0]]
    self._sb[name] = v
    self._smnames.add(name)
    return v


def _bb_sm_reset(self):
    if hasattr(self, "_smbase"):
        for n in self._smnames:
            self._sb.pop(n, None)
        self._smnames = set()
        self._smoff = 0


BB.sm_reset = _bb_sm_reset


def _bb_psum(self, name, shape, dtype=F32):
    if name in self._sb:
        return self._sb[name]
    t = self.es.enter_context(self.nc.psum_tensor(name, list(shape), dtype))
    ap = t[:]
    self._sb[name] = ap
    return ap


def _bb_const(self, name, arr):
    if name not in self.consts:
        self.consts[name] = arr
        import ml_dtypes
        dt = BF16 if arr.dtype == ml_dtypes.bfloat16 else F32
        self._cap[name] = self.nc.dram_tensor(name, list(arr.shape), dt,
                                              kind="ExternalInput").ap()
    return self._cap[name]


BB.sbuf = _bb_sbuf
BB.psum = _bb_psum
BB.const = _bb_const


def _mk(name):
    def f(self, *a, **kw):
        return getattr(self, "_" + name)(*a, **kw)
    return f


def _bb_dma(self, q, out, in_, reads=(), writes=(), deps=(), accum=False):
    if accum:
        return self.p.task(
            q, lambda e: e.dma_start(out=out, in_=in_, accum_op=ALU.add),
            reads=reads, writes=writes, deps=deps, dma=True)
    return self.p.task(q, lambda e: e.dma_start(out=out, in_=in_),
                       reads=reads, writes=writes, deps=deps, dma=True)


def _bb_mm(self, out, lhsT, rhs, start=True, stop=True, reads=(), writes=(), deps=()):
    return self.p.task(
        "tensor", lambda e: e.matmul(out, lhsT, rhs, start=start, stop=stop),
        reads=reads, writes=writes, deps=deps)


def _bb_mm_acc(self, pairs, reads=(), writes=(), deps=()):
    def emit(e):
        n = len(pairs)
        ins = None
        for i, (o, l, r) in enumerate(pairs):
            ins = e.matmul(o, l, r, start=(i == 0), stop=(i == n - 1))
        return ins
    return self.p.task("tensor", emit, reads=reads, writes=writes, deps=deps)


def _bb_mm_multi(self, triples, start, stop, reads=(), writes=(), deps=()):
    def emit(e):
        ins = None
        for (o, l, r) in triples:
            ins = e.matmul(o, l, r, start=start, stop=stop)
        return ins
    return self.p.task("tensor", emit, reads=reads, writes=writes, deps=deps)


def _bb_act(self, out, in_, func, bias=None, scale=None, accum_out=None,
            reads=(), writes=(), deps=()):
    kw = {}
    if bias is not None:
        kw["bias"] = bias
    if scale is not None:
        kw["scale"] = scale
    if accum_out is not None:
        kw["accum_out"] = accum_out
    return self.p.task("scalar", lambda e: e.activation(out, in_, func, **kw),
                       reads=reads, writes=writes, deps=deps)


def _bb_tt(self, out, a, b, op, reads=(), writes=(), deps=(), eng="vector"):
    return self.p.task(eng, lambda e: e.tensor_tensor(out, a, b, op=op),
                       reads=reads, writes=writes, deps=deps)


def _bb_ts(self, out, a, s1, s2=None, op0=ALU.mult, op1=None, reads=(), writes=(),
           deps=(), eng="vector"):
    if op1 is None:
        return self.p.task(eng, lambda e: e.tensor_scalar(out, a, s1, None, op0=op0),
                           reads=reads, writes=writes, deps=deps)
    return self.p.task(
        eng, lambda e: e.tensor_scalar(out, a, s1, s2, op0=op0, op1=op1),
        reads=reads, writes=writes, deps=deps)


def _bb_stt(self, out, in0, scalar, in1, op0, op1, reads=(), writes=(), deps=(),
            eng="vector"):
    return self.p.task(
        eng,
        lambda e: e.scalar_tensor_tensor(out, in0, scalar, in1, op0=op0, op1=op1),
        reads=reads, writes=writes, deps=deps)


def _bb_copy(self, out, in_, reads=(), writes=(), deps=(), eng="scalar"):
    if eng == "scalar":
        return self.p.task(eng, lambda e: e.copy(out, in_),
                           reads=reads, writes=writes, deps=deps)
    return self.p.task(eng, lambda e: e.tensor_copy(out, in_),
                       reads=reads, writes=writes, deps=deps)


def _bb_recip(self, out, in_, reads=(), writes=(), deps=()):
    return self.p.task("vector", lambda e: e.reciprocal(out, in_),
                       reads=reads, writes=writes, deps=deps, fence=True)


def _bb_transpose(self, out, in_, ident, reads=(), writes=(), deps=()):
    return self.p.task("tensor", lambda e: e.transpose(out, in_, ident),
                       reads=reads, writes=writes, deps=deps)


def _bb_memset(self, ap, val, reads=(), writes=(), deps=(), eng="vector"):
    return self.p.task(eng, lambda e: e.memset(ap, val),
                       reads=reads, writes=writes, deps=deps, fence=True)


def _bb_reduce(self, out, in_, op, reads=(), writes=(), deps=(), negate=None):
    return self.p.task(
        "vector", lambda e: e.tensor_reduce(out, in_, axis=AX.X, op=op, negate=negate),
        reads=reads, writes=writes, deps=deps)


def _bb_scan(self, out, a, b, init, reads=(), writes=(), deps=()):
    return self.p.task(
        "vector", lambda e: e.tensor_tensor_scan(out, a, b, init, ALU.mult, ALU.add),
        reads=reads, writes=writes, deps=deps)


for _n in ["dma", "mm", "mm_acc", "mm_multi", "act", "tt", "ts", "stt", "copy",
           "recip", "transpose", "memset", "reduce", "scan"]:
    setattr(BB, _n, globals()["_bb_" + _n])

class Arena:

    def __init__(self, bb, bytes_pp, name="ARENA"):
        self.bb = bb
        self.base = bb.sbuf(name, [128, bytes_pp], mybir.dt.uint8)
        self.front = 0
        self.back = bytes_pp
        self.hwm = 0

    def take(self, shape, dtype=F32, back=False):
        esz = 4 if dtype == F32 else 2
        n = int(np.prod(shape[1:])) * esz
        n = (n + 127) // 128 * 128
        if back:
            self.back -= n
            off = self.back
        else:
            off = self.front
            self.front += n
        assert self.front <= self.back, \
            f"arena overflow front={self.front} back={self.back}"
        self.hwm = max(self.hwm, self.front + (self.base.shape[1] - self.back))
        v = self.base[:, off:off + n].bitcast(dtype)
        if len(shape) == 2:
            v = v[:, 0:shape[1]] if v.shape[1] != shape[1] else v
        else:
            v = v[:, 0:int(np.prod(shape[1:]))]
            v = v.rearrange("p (a b) -> p a b", b=shape[2]) if len(shape) == 3 else v
        if shape[0] != 128:
            v = v[0:shape[0]]
        return v

    def reset_front(self):
        self.front = 0

    def reset_back(self):
        self.back = self.base.shape[1]


_I128_F32 = np.eye(128, dtype=np.float32)


def _bb_kcol(self, val, rows=128):
    if not hasattr(self, "_kvals"):
        self._kvals = {}
        self._kbuf = self.sbuf("kbuf", [128, 16], F32)
    if val not in self._kvals:
        i = len(self._kvals)
        assert i < 16
        self.memset(self._kbuf[:, i:i + 1], float(val), writes=("kbuf",))
        self._kvals[val] = i
    i = self._kvals[val]
    return self._kbuf[0:rows, i:i + 1]


BB.kcol = _bb_kcol


def setup_common(bb, ps7_bf16=False):
    if getattr(bb, "_setup_done", False):
        return
    bb._setup_done = True
    bb.ident_f32 = bb.sbuf("ident_f32", [128, 128], F32)
    bb.ones_1x128 = bb.sbuf("ones1x128", [1, 128], F32)
    bb.ones_128x1 = bb.sbuf("ones128x1", [128, 1], F32)
    bb.ones_1x128_bf = bb.sbuf("ones1x128bf", [1, 128], BF16)
    bb.PS = [bb.psum(f"PS{i}", [128, 512], F32) for i in range(7)]
    psbf = bb.psum("PS7", [128, 1024], BF16)
    bb.PSBF = psbf
    bb.PS.append(psbf.bitcast(F32))
    import ml_dtypes
    bb.ident_bf = bb.sbuf("ident_bf", [128, 128], BF16)
    bb.dma("gpsimd", bb.ident_bf,
           bb.const("c_ident_bf", _I128_F32.astype(ml_dtypes.bfloat16)),
           writes=("ident_bf",))
    bb.dma("gpsimd", bb.ident_f32, bb.const("c_ident_f32", _I128_F32),
           writes=("ident_f32",))
    bb.memset(bb.ones_1x128, 1.0, writes=("ones1x128",))
    bb.memset(bb.ones_128x1, 1.0, writes=("ones128x1",))
    bb.memset(bb.ones_1x128_bf, 1.0, writes=("ones1x128bf",))
    bb.ones_1x512_bf = bb.sbuf("ones1x512bf", [1, 512], BF16)
    bb.memset(bb.ones_1x512_bf, 1.0, writes=("ones1x512bf",))


def vec_to_cols(bb, dram_vec, length, tag):
    m = length // 128
    assert m * 128 == length and 1 <= m <= 128
    vt = bb.sbuf(f"{tag}_r", [m, 128], F32)
    cols = bb.sbuf(f"{tag}_c", [128, m], F32)
    ld = bb.dma("gpsimd", vt, dram_vec.rearrange("(m p) -> m p", p=128),
                writes=(f"{tag}_r",))
    tr = bb.transpose(bb.PS[0][0:128, 0:m], vt, bb.ident_f32[0:m, 0:m],
                      reads=(f"{tag}_r", "ident_f32"), writes=("PS0",), deps=[ld])
    bb.copy(cols, bb.PS[0][0:128, 0:m], reads=("PS0",), writes=(f"{tag}_c",), deps=[tr])
    return cols


def ln_chunked(bb, xck, xckbuf, outs, tag, ch, scratch, eps=1e-5, cw=TCH):
    j = ch % 2
    sq = scratch[f"sq{j}"][:, 0:cw]
    rows = scratch[f"rw{j}"][0:1]
    r_s1, r_s2 = rows[:, 0:cw], rows[:, cw:2 * cw]
    r_var, r_rstd = rows[:, 2 * cw:3 * cw], rows[:, 3 * cw:4 * cw]
    r_rstdf, r_tmp = rows[:, 4 * cw:5 * cw], rows[:, 5 * cw:6 * cw]
    B = f"{tag}{j}_"
    PS0, PS1, PS2 = bb.PS[0], bb.PS[1], bb.PS[2]
    for i in range(NDT):
        sqt = bb.act(sq, xck[:, i, :], AF.Square, reads=(xckbuf,), writes=(B + "sq",))
        bb.mm(PS0[0:1, 0:cw], bb.ones_128x1, xck[:, i, :], start=(i == 0),
              stop=(i == NDT - 1), reads=(xckbuf, "ones128x1"), writes=("PS0",))
        bb.mm(PS1[0:1, 0:cw], bb.ones_128x1, sq, start=(i == 0), stop=(i == NDT - 1),
              reads=(B + "sq", "ones128x1"), writes=("PS1",), deps=[sqt])
    bb.ts(r_s1, PS0[0:1, 0:cw], 1.0 / D, None, reads=("PS0",), writes=(B + "s1",))
    bb.ts(r_s2, PS1[0:1, 0:cw], 1.0 / D, None, reads=("PS1",), writes=(B + "s2",))
    bb.tt(r_tmp, r_s1, r_s1, ALU.mult, reads=(B + "s1",), writes=(B + "tmp",))
    bb.tt(r_var, r_s2, r_tmp, ALU.subtract, reads=(B + "s2", B + "tmp"),
          writes=(B + "var",))
    bb.act(r_rstd, r_var, AF.Sqrt, bias=bb.kcol(eps, rows=1),
           reads=(B + "var", "kbuf"), writes=(B + "rstd",))
    bb.recip(r_rstd, r_rstd, reads=(B + "rstd",), writes=(B + "rstd",))
    need_f = any(o["mode"] == "gbf" for o in outs)
    if need_f:
        bb.ts(r_tmp, r_var, eps, None, op0=ALU.add, reads=(B + "var",),
              writes=(B + "tmp",))
        bb.recip(r_tmp, r_tmp, reads=(B + "tmp",), writes=(B + "tmp",))
        bb.tt(r_tmp, r_tmp, r_var, ALU.mult, reads=(B + "tmp", B + "var"),
              writes=(B + "tmp",))
        bb.act(r_tmp, r_tmp, AF.Sqrt, bias=bb.kcol(eps, rows=1),
               reads=(B + "tmp", "kbuf"), writes=(B + "tmp",))
        bb.recip(r_tmp, r_tmp, reads=(B + "tmp",), writes=(B + "tmp",))
        bb.tt(r_rstdf, r_rstd, r_tmp, ALU.mult, reads=(B + "rstd", B + "tmp"),
              writes=(B + "rstdf",))
    bm = bb.mm(PS0[:, 0:cw], bb.ones_1x128, r_s1, reads=(B + "s1", "ones1x128"),
               writes=("PS0",))
    br = bb.mm(PS1[:, 0:cw], bb.ones_1x128, r_rstd, reads=(B + "rstd", "ones1x128"),
               writes=("PS1",))
    brf = None
    if need_f:
        brf = bb.mm(PS2[:, 0:cw], bb.ones_1x128, r_rstdf,
                    reads=(B + "rstdf", "ones1x128"), writes=("PS2",))
    xm = scratch[f"xm{j}"][:, 0:cw]
    tmpa = scratch[f"ta{j}"][:, 0:cw]
    for i in range(NDT):
        bb.tt(xm, xck[:, i, :], PS0[:, 0:cw], ALU.subtract,
              reads=(xckbuf, "PS0"), writes=(B + "xm",), deps=[bm])
        for o in outs:
            dst, dstbuf = o["sink"](i)
            if o["mode"] == "plain":
                bb.tt(dst, xm, PS1[:, 0:cw], ALU.mult, reads=(B + "xm", "PS1"),
                      writes=(dstbuf,), deps=[br])
            else:
                psx, bwait, psname = (PS1, br, "PS1") if o["mode"] == "gb" else (PS2, brf, "PS2")
                bb.tt(tmpa, xm, psx[:, 0:cw], ALU.mult, reads=(B + "xm", psname),
                      writes=(B + "ta",), deps=[bwait])
                bb.ts(dst, tmpa, o["gamma"][:, i:i + 1], o["beta"][:, i:i + 1],
                      op0=ALU.mult, op1=ALU.add,
                      reads=(B + "ta",) + tuple(o.get("gbufs", ())), writes=(dstbuf,))
            if o.get("post"):
                o["post"](i, dst, dstbuf)



def _ln_scratch(bb, AR, cw=TCH):
    return {
        "sq0": AR.take([128, cw]), "sq1": AR.take([128, cw]),
        "rw0": AR.take([128, 6 * cw]), "rw1": AR.take([128, 6 * cw]),
        "xm0": AR.take([128, cw]), "xm1": AR.take([128, cw]),
        "ta0": AR.take([128, cw]), "ta1": AR.take([128, cw]),
    }


def build_block1(bb=None, xT=None, yT=None):
    if bb is None:
        bb = BB("b1")
        xT = bb.dram_in("xT", [D, T])
        yT = bb.dram_out("yT", [D, T])
    HID = 2048
    w = {}
    for nm, shp in [("ln_g", [D]), ("ln_b", [D]), ("Wh", [D, 2 * HID]),
                    ("bh", [2 * HID]), ("Wqk", [D, 128]), ("bqk", [128]),
                    ("gam", [2, 128]), ("bet", [2, 128]), ("Wo", [HID, D]),
                    ("bo", [D]), ("W1", [D, 4096]), ("b1", [4096]),
                    ("W2", [4096, D]), ("b2", [D])]:
        w[nm] = bb.dram_in("w_" + nm, shp)
    xn1_s = bb.dram_tmp("xn1_s", [D, T], BF16)
    nrm_s = bb.dram_tmp("nrm_s", [D, T], BF16)
    gate_s = bb.dram_tmp("gate_s", [HID, T], BF16)

    setup_common(bb)
    AR = Arena(bb, 176 * 1024)
    mask_sb = bb.sbuf("mask_tri", [128, 128], F32)
    bb.dma("gpsimd", mask_sb,
           bb.const("c_mask_tri", np.triu(np.ones((128, 128), np.float32))),
           writes=("mask_tri",))
    g_cols = vec_to_cols(bb, w["ln_g"], D, "lng")
    b_cols = vec_to_cols(bb, w["ln_b"], D, "lnb")
    b1c = vec_to_cols(bb, w["b1"], 4096, "b1")
    b2c = vec_to_cols(bb, w["b2"], D, "b2")
    boc = vec_to_cols(bb, w["bo"], D, "bo")
    bhc = vec_to_cols(bb, w["bh"], 2 * HID, "bh")
    bqkc = vec_to_cols(bb, w["bqk"], 128, "bqk")
    gamc = bb.sbuf("gamc", [128, 2], F32)
    betc = bb.sbuf("betc", [128, 2], F32)
    gb_r = bb.sbuf("gb_r", [2, 128], F32)
    for (nm, dst, dbuf) in [("gam", gamc, "gamc"), ("bet", betc, "betc")]:
        ldt = bb.dma("sync", gb_r, w[nm], writes=("gb_r",))
        trt = bb.transpose(bb.PS[0][0:128, 0:2], gb_r, bb.ident_f32[0:2, 0:2],
                           reads=("gb_r", "ident_f32"), writes=("PS0",), deps=[ldt])
        bb.copy(dst, bb.PS[0][0:128, 0:2], reads=("PS0",), writes=(dbuf,), deps=[trt])
    bb.p.barrier()

    xck = [AR.take([128, NDT, TCH]) for _ in range(2)]
    ob_xn = [AR.take([128, TCH], BF16) for _ in range(2)]
    ob_nr = [AR.take([128, TCH], BF16) for _ in range(2)]
    lns = _ln_scratch(bb, AR)
    for ch in range(NCH):
        j = ch % 2
        sl = slice(ch * TCH, (ch + 1) * TCH)
        bb.dma("sync", xck[j], xT[:, sl].rearrange("(i p) c -> p i c", p=128),
               writes=(f"xck{j}",))

        def post(dram, dname, q, sl=sl):
            def f(i, ap, buf):
                bb.dma(q, dram[i * 128:(i + 1) * 128, sl], ap, reads=(buf,),
                       writes=(f"{dname}_{i}_{sl.start}",))
            return f

        outs = [
            {"sink": lambda i, j=j: (ob_xn[j], f"oxn{j}"), "mode": "plain",
             "post": post(xn1_s, "xn1_s", "scalar")},
            {"sink": lambda i, j=j: (ob_nr[j], f"onr{j}"), "mode": "gbf",
             "gamma": g_cols, "beta": b_cols, "gbufs": ("lng_c", "lnb_c"),
             "post": post(nrm_s, "nrm_s", "sync")},
        ]
        ln_chunked(bb, xck[j], f"xck{j}", outs, "ln1", ch, lns)
    bb.p.barrier()
    AR.reset_front()

    W1_sb = AR.take([128, NDT * 4096], BF16)
    h_sb = AR.take([128, 32 * TCH], BF16)
    xnck = [AR.take([128, NDT, TCH], BF16) for _ in range(2)]
    w2cb = [AR.take([128, 32, 128], BF16) for _ in range(2)]
    xres = [AR.take([128, TCH]) for _ in range(2)]
    ych = [AR.take([128, TCH]) for _ in range(2)]
    for k in range(NDT):
        bb.dma("gpsimd", W1_sb[:, k * 4096:(k + 1) * 4096],
               w["W1"][k * 128:(k + 1) * 128, :], writes=(f"W1_{k}",))
    ps_h = [bb.PS[3], bb.PS[4], bb.PS[5], bb.PS[6]]
    ps_o = [bb.PS[2], bb.PS[7]]
    for ch in range(NCH):
        j = ch % 2
        sl = slice(ch * TCH, (ch + 1) * TCH)
        bb.dma("sync", xnck[j], xn1_s[:, sl].rearrange("(i p) c -> p i c", p=128),
               reads=tuple(f"xn1_s_{i}_{sl.start}" for i in range(NDT)),
               writes=(f"xnck{j}",))
        for m in range(32):
            jm = m % 4
            mmt = bb.mm_acc(
                [(ps_h[jm][:, :],
                  W1_sb[:, k * 4096 + m * 128: k * 4096 + (m + 1) * 128],
                  xnck[j][:, k, :]) for k in range(NDT)],
                reads=tuple(f"W1_{k}" for k in range(NDT)) + (f"xnck{j}",),
                writes=(f"PS{3 + jm}",))
            bb.act(h_sb[:, m * TCH:(m + 1) * TCH], ps_h[jm][:, :], AF.Relu,
                   bias=b1c[:, m:m + 1], reads=(f"PS{3 + jm}", "b1_c"),
                   writes=(f"h{m}",), deps=[mmt])
        for do in range(NDT):
            jo = do % 2
            bb.dma("gpsimd", w2cb[jo],
                   w["W2"][:, do * 128:(do + 1) * 128].rearrange(
                       "(k p) c -> p k c", p=128),
                   writes=(f"w2cb{jo}",))
            bb.dma("sync", xres[jo], xT[do * 128:(do + 1) * 128, sl],
                   writes=(f"xres{jo}",))
            psoname = "PS2" if jo == 0 else "PS7"
            mmo = bb.mm_acc(
                [(ps_o[jo][:, :], w2cb[jo][:, k, :],
                  h_sb[:, k * TCH:(k + 1) * TCH]) for k in range(32)],
                reads=(f"w2cb{jo}",) + tuple(f"h{k}" for k in range(32)),
                writes=(psoname,))
            bb.stt(ych[jo], ps_o[jo][:, :], b2c[:, do:do + 1], xres[jo],
                   op0=ALU.add, op1=ALU.add,
                   reads=(psoname, "b2_c", f"xres{jo}"),
                   writes=(f"ych{jo}",), deps=[mmo])
            bb.tt(ych[jo], ych[jo], xnck[j][:, do, :], ALU.add,
                  reads=(f"ych{jo}", f"xnck{j}"), writes=(f"ych{jo}",))
            bb.dma("scalar", yT[do * 128:(do + 1) * 128, sl], ych[jo],
                   reads=(f"ych{jo}",), writes=(f"yT_{do}_{ch}",))
    bb.p.barrier()
    AR.reset_front()

    v_res = AR.take([128, 32 * HID], BF16, back=True)
    whcb = [AR.take([128, NDT, TCH], BF16) for _ in range(2)]
    nrmck = [AR.take([128, NDT, 128], BF16) for _ in range(2)]
    nrmch = AR.take([128, NDT, TCH], BF16)
    gch = [AR.take([128, TCH], BF16) for _ in range(2)]
    bhrow_a = AR.take([128, 2 * HID], BF16)
    bhrow = bhrow_a[0:1]
    bb.dma("gpsimd", bhrow, w["bh"].unsqueeze(0), writes=("bhrow",))
    ps_v = [bb.PS[3], bb.PS[4]]
    nT = nrm_s.rearrange("(k p) t -> p k t", p=128)
    for cb in range(4):
        j = cb % 2
        cs = slice(cb * 512, (cb + 1) * 512)
        bb.dma("gpsimd", whcb[j], w["Wh"][:, cs].rearrange("(k p) c -> p k c", p=128),
               writes=(f"whcb{j}",))
        for ts_ in range(32):
            jj = ts_ % 2
            tsl = slice(ts_ * 128, (ts_ + 1) * 128)
            bb.dma("sync", nrmck[jj], nT[:, :, tsl],
                   reads=tuple(f"nrm_s_{k}_{(ts_ // 4) * TCH}" for k in range(NDT)),
                   writes=(f"nrmck{jj}",))
            pairs = [(ps_v[jj][:, :], nrmck[jj][:, k, :],
                      whcb[j][:, k, :]) for k in range(NDT)]
            pairs.append((ps_v[jj][:, :], bb.ones_1x128_bf, bhrow[:, cs]))
            mmv = bb.mm_acc(pairs,
                            reads=(f"nrmck{jj}", f"whcb{j}", "ones1x128bf", "bhrow"),
                            writes=(f"PS{3 + jj}",))
            bb.act(v_res[:, ts_ * HID + cb * 512: ts_ * HID + (cb + 1) * 512],
                   ps_v[jj][:, :], AF.Silu, reads=(f"PS{3 + jj}",),
                   writes=(f"v_{ts_}_{cb}",), deps=[mmv])
    for cb in range(4):
        j = cb % 2
        cs = slice(HID + cb * 512, HID + (cb + 1) * 512)
        bb.dma("gpsimd", whcb[j], w["Wh"][:, cs].rearrange("(k p) c -> p k c", p=128),
               writes=(f"whcb{j}",))
        for ch in range(NCH):
            sl = slice(ch * TCH, (ch + 1) * TCH)
            bb.dma("sync", nrmch, nT[:, :, sl],
                   reads=tuple(f"nrm_s_{k}_{sl.start}" for k in range(NDT)),
                   writes=("nrmch",))
            for gm in range(4):
                jj = (ch * 4 + gm) % 2
                gcol = HID + cb * 512 + gm * 128
                mmg = bb.mm_acc(
                    [(ps_v[jj][:, :], whcb[j][:, k, gm * 128:(gm + 1) * 128],
                      nrmch[:, k, :]) for k in range(NDT)],
                    reads=("nrmch", f"whcb{j}"),
                    writes=(f"PS{3 + jj}",))
                bb.act(gch[jj], ps_v[jj][:, :], AF.Silu,
                       bias=bhc[:, gcol // 128: gcol // 128 + 1],
                       reads=(f"PS{3 + jj}", "bh_c"), writes=(f"gch{jj}",),
                       deps=[mmg])
                bb.dma("scalar", gate_s[gcol - HID: gcol - HID + 128, sl], gch[jj],
                       reads=(f"gch{jj}",), writes=(f"gate_{gcol - HID}_{ch}",))
    bb.p.barrier()
    AR.reset_front()

    q_f = AR.take([128, T], BF16, back=True)
    k_f = AR.take([128, T], BF16, back=True)
    Wqk_sb = AR.take([128, NDT, 128], BF16)
    nrmch3 = [AR.take([128, NDT, TCH], BF16) for _ in range(2)]
    qk32 = [AR.take([128, TCH]) for _ in range(2)]
    bb.dma("gpsimd", Wqk_sb, w["Wqk"].rearrange("(k p) c -> p k c", p=128),
           writes=("Wqk_sb",))
    for ch in range(NCH):
        j = ch % 2
        sl = slice(ch * TCH, (ch + 1) * TCH)
        bb.dma("sync", nrmch3[j], nT[:, :, sl],
               reads=tuple(f"nrm_s_{k}_{sl.start}" for k in range(NDT)),
               writes=(f"nrmch3{j}",))
        mmq = bb.mm_acc(
            [(bb.PS[1][:, :], Wqk_sb[:, k, :], nrmch3[j][:, k, :])
             for k in range(NDT)],
            reads=(f"nrmch3{j}", "Wqk_sb"),
            writes=("PS1",))
        bb.act(qk32[j], bb.PS[1][:, :], AF.Silu, bias=bqkc[:, 0:1],
               reads=("PS1", "bqk_c"), writes=(f"qk32_{j}",), deps=[mmq])
        bb.ts(q_f[:, sl], qk32[j], gamc[:, 0:1], betc[:, 0:1], op0=ALU.mult,
              op1=ALU.add, reads=(f"qk32_{j}", "gamc", "betc"), writes=("q_f",))
        bb.ts(k_f[:, sl], qk32[j], gamc[:, 1:2], betc[:, 1:2], op0=ALU.mult,
              op1=ALU.add, reads=(f"qk32_{j}", "gamc", "betc"), writes=("k_f",))
    bb.p.barrier()
    AR.reset_front()

    ps_sim = [bb.PS[2][:, 0:128], bb.PS[7][:, 0:128]]
    ps_sim_b = ["PS2", "PS7"]
    psY = [bb.PS[3], bb.PS[4], bb.PS[5], bb.PS[6]]
    at32 = [AR.take([128, 128]) for _ in range(2)]
    atbf = [AR.take([128, 128], BF16) for _ in range(2)]
    gateT = AR.take([128, 16, 128], BF16)
    ygT = AR.take([128, 16 * 512], BF16)
    wok = [AR.take([128, 512], BF16) for _ in range(2)]
    yo_x = [AR.take([128, TCH]) for _ in range(2)]
    yo_y = [AR.take([128, TCH]) for _ in range(2)]
    gsT = gate_s.rearrange("(i p) t -> p i t", p=128)
    for grp in range(8):
        for qq in range(4):
            qs = grp * 4 + qq
            qsl = slice(qs * 128, (qs + 1) * 128)
            bb.dma("sync", gateT, gsT[:, :, qsl],
                   reads=tuple(f"gate_{et * 128}_{qs // 4}" for et in range(16)),
                   writes=("gateT",))
            for kc in range(qs + 1):
                sj = kc % 2
                mms = bb.mm(ps_sim[sj], k_f[:, kc * 128:(kc + 1) * 128], q_f[:, qsl],
                            reads=("k_f", "q_f"), writes=(ps_sim_b[sj],))
                last = bb.ts(at32[sj], ps_sim[sj], 0.0, 1.0 / T, op0=ALU.max,
                             op1=ALU.mult, reads=(ps_sim_b[sj],),
                             writes=(f"at32_{sj}",), deps=[mms])
                if kc == qs:
                    last = bb.tt(at32[sj], at32[sj], mask_sb, ALU.mult,
                                 reads=(f"at32_{sj}", "mask_tri"),
                                 writes=(f"at32_{sj}",))
                sqt = bb.act(atbf[sj], at32[sj], AF.Square, reads=(f"at32_{sj}",),
                             writes=(f"atbf_{sj}",), deps=[last])
                bb.mm_multi(
                    [(psY[et // 4][:, (et % 4) * 128:(et % 4 + 1) * 128],
                      v_res[:, kc * HID + et * 128: kc * HID + (et + 1) * 128],
                      atbf[sj]) for et in range(16)],
                    start=(kc == 0), stop=(kc == qs),
                    reads=(f"atbf_{sj}",) + tuple(f"v_{kc}_{cb}" for cb in range(4)),
                    writes=tuple(f"PS{3 + jj}" for jj in range(4)), deps=[sqt])
            for et in range(16):
                bb.tt(ygT[:, et * 512 + qq * 128: et * 512 + (qq + 1) * 128],
                      psY[et // 4][:, (et % 4) * 128:(et % 4 + 1) * 128],
                      gateT[:, et, :], ALU.mult,
                      reads=(f"PS{3 + et // 4}", "gateT"),
                      writes=(f"ygT_{et}_{qq}",))
        gsl = slice(grp * 512, (grp + 1) * 512)
        for wave in range(2):
            for k in range(16):
                jk = k % 2
                bb.dma("gpsimd", wok[jk],
                       w["Wo"][k * 128:(k + 1) * 128, wave * 512:(wave + 1) * 512],
                       writes=(f"wok{jk}",))
                bb.mm_multi(
                    [(psY[dd][:, :], wok[jk][:, dd * 128:(dd + 1) * 128],
                      ygT[:, k * 512:(k + 1) * 512]) for dd in range(4)],
                    start=(k == 0), stop=(k == 15),
                    reads=(f"wok{jk}",) + tuple(f"ygT_{k}_{q2}" for q2 in range(4)),
                    writes=tuple(f"PS{3 + jj}" for jj in range(4)))
            for dd in range(4):
                do = wave * 4 + dd
                jo = dd % 2
                bb.dma("sync", yo_x[jo], yT[do * 128:(do + 1) * 128, gsl],
                       reads=(f"yT_{do}_{grp}",), writes=(f"yo_x{jo}",))
                bb.stt(yo_y[jo], psY[dd][:, :], boc[:, do:do + 1], yo_x[jo],
                       op0=ALU.add, op1=ALU.add,
                       reads=(f"PS{3 + dd}", "bo_c", f"yo_x{jo}"),
                       writes=(f"yo_y{jo}",))
                bb.dma("scalar", yT[do * 128:(do + 1) * 128, gsl], yo_y[jo],
                       reads=(f"yo_y{jo}",), writes=(f"yT_{do}_{grp}",))
    return bb





def short_vec_col(bb, dram_vec, n, tag):
    vr = bb.sbuf(f"{tag}_r", [1, n], F32)
    col = bb.sbuf(f"{tag}_c", [n, 1], F32)
    ld = bb.dma("sync", vr, dram_vec.unsqueeze(0), writes=(f"{tag}_r",))
    tr = bb.transpose(bb.PS[0][0:n, 0:1], vr, bb.ident_f32[0:1, 0:1],
                      reads=(f"{tag}_r", "ident_f32"), writes=("PS0",), deps=[ld])
    bb.copy(col, bb.PS[0][0:n, 0:1], reads=("PS0",), writes=(f"{tag}_c",), deps=[tr])
    return col


def build_block0(stop_after=None, dbg=False, bb=None, xT=None, yT=None):
    if bb is None:
        bb = BB("b0")
        xT = bb.dram_in("xT", [D, T])
        yT = bb.dram_out("yT", [D, T])
    H, DH, CC = 16, 64, 2
    w = {}
    for nm, shp in [("Wq", [D, D]), ("Wk", [D, D]), ("Wv", [D, D]),
                    ("Wbeta", [D, 32]), ("bbeta", [32]), ("Wo", [D, D]),
                    ("bo", [D]), ("ln_g", [D]), ("ln_b", [D]),
                    ("WT", [D, D]), ("fb", [D]), ("V_r", [D, 32]),
                    ("CT", [32, D])]:
        w[nm] = bb.dram_in("w_" + nm, shp)
    dram_dbg = bb.dram_out if dbg else bb.dram_tmp
    xn_s = dram_dbg("xn_s", [D, T])
    xnb_s = bb.dram_tmp("xnb_s", [D, T], BF16)
    Lf_s = dram_dbg("Lf_s", [32, T]) if dbg else None
    qp_s = dram_dbg("qp_s", [T, D]) if dbg else None
    kp_s = dram_dbg("kp_s", [T, D]) if dbg else None
    att_s = dram_dbg("att_s", [T, D]) if dbg else None
    ex_s = dram_dbg("ex_s", [128, 96]) if dbg else None
    py_s = dram_dbg("py_s", [128, 260]) if dbg else None
    ab_s = dram_dbg("ab_s", [128, 256], BF16) if dbg else None
    lr_s = dram_dbg("lr_s", [32, 256]) if dbg else None
    tp_s = dram_dbg("tp_s", [128, 32]) if dbg else None
    va_s = dram_dbg("va_s", [128, 16 * 65], BF16) if dbg else None
    py2_s = dram_dbg("py2_s", [128, 256]) if dbg else None
    dn_s = dram_dbg("dn_s", [128, 16]) if dbg else None

    setup_common(bb)
    AR = Arena(bb, 176 * 1024)
    inv_freq = 1.0 / 10000.0 ** (np.arange(0, 64, 2, dtype=np.float64) / 64)
    tt_ = np.arange(T, dtype=np.float64)
    freqs = np.outer(tt_, inv_freq)
    c_cos = np.cos(freqs).astype(np.float32)
    c_snp = np.concatenate([-np.sin(freqs), np.sin(freqs)], 1).astype(np.float32)
    cosd = bb.const("c_rope_cos", c_cos)
    snpd = bb.const("c_rope_snp", c_snp)
    mask_sb = bb.sbuf("mask_tri", [128, 128], F32)
    bb.dma("gpsimd", mask_sb,
           bb.const("c_mask_tri", np.triu(np.ones((128, 128), np.float32))),
           writes=("mask_tri",))
    g_cols = vec_to_cols(bb, w["ln_g"], D, "lng")
    b_cols = vec_to_cols(bb, w["ln_b"], D, "lnb")
    bo_c = vec_to_cols(bb, w["bo"], D, "bo")
    fb_c = vec_to_cols(bb, w["fb"], D, "fb")
    bbeta_c = short_vec_col(bb, w["bbeta"], 32, "bbeta")
    bb.p.barrier()

    xck = [AR.take([128, NDT, TCH]) for _ in range(2)]
    o_f32 = [AR.take([128, TCH]) for _ in range(2)]
    lns = _ln_scratch(bb, AR)
    for ch in range(NCH):
        j = ch % 2
        sl = slice(ch * TCH, (ch + 1) * TCH)
        bb.dma("sync", xck[j], xT[:, sl].rearrange("(i p) c -> p i c", p=128),
               writes=(f"xck{j}",))

        def post2(i, ap, buf, sl=sl):
            bb.dma("scalar", xn_s[i * 128:(i + 1) * 128, sl], ap, reads=(buf,),
                   writes=(f"xn_s_{i}_{sl.start}",))
            bb.dma("gpsimd", xnb_s[i * 128:(i + 1) * 128, sl], ap, reads=(buf,),
                   writes=(f"xnb_s_{i}_{sl.start}",))

        outs = [{"sink": lambda i, j=j: (o_f32[j], f"of{j}"), "mode": "plain",
                 "post": post2}]
        ln_chunked(bb, xck[j], f"xck{j}", outs, "ln0", ch, lns)
    bb.p.barrier()
    AR.reset_front()

    if stop_after == 0:
        return bb
    WT = AR.take([128, NDT * D], BF16)
    VR = AR.take([128, NDT * 32], BF16)
    CTt = AR.take([32, D], BF16)
    for k in range(NDT):
        bb.dma("gpsimd", WT[:, k * D:(k + 1) * D],
               w["WT"][k * 128:(k + 1) * 128, :], writes=(f"WT{k}",))
        bb.dma("gpsimd", VR[:, k * 32:(k + 1) * 32],
               w["V_r"][k * 128:(k + 1) * 128, :], writes=("VR",))
    bb.dma("gpsimd", CTt, w["CT"], writes=("CTt",))
    xnck = [AR.take([128, NDT, TCH], BF16) for _ in range(2)]
    vrb = [AR.take([32, TCH], BF16) for _ in range(2)]
    xres = [AR.take([128, TCH]) for _ in range(2)]
    ych = [AR.take([128, TCH]) for _ in range(2)]
    ps2 = [bb.PS[3], bb.PS[4]]
    for ch in range(NCH):
        j = ch % 2
        sl = slice(ch * TCH, (ch + 1) * TCH)
        bb.dma("sync", xnck[j], xnb_s[:, sl].rearrange("(i p) c -> p i c", p=128),
               reads=tuple(f"xnb_s_{i}_{sl.start}" for i in range(NDT)),
               writes=(f"xnck{j}",))
        mmv = bb.mm_acc(
            [(bb.PS[5][0:32, :], VR[:, k * 32:(k + 1) * 32], xnck[j][:, k, :])
             for k in range(NDT)],
            reads=("VR", f"xnck{j}"), writes=("PS5",))
        bb.copy(vrb[j], bb.PS[5][0:32, :], reads=("PS5",), writes=(f"vrb{j}",),
                deps=[mmv])
        for do in range(NDT):
            jo = do % 2
            bb.dma("sync", xres[jo], xT[do * 128:(do + 1) * 128, sl],
                   writes=(f"xres{jo}",))
            pairs = [(ps2[jo][:, :], WT[:, k * D + do * 128: k * D + (do + 1) * 128],
                      xnck[j][:, k, :]) for k in range(NDT)]
            pairs.append((ps2[jo][:, :], CTt[:, do * 128:(do + 1) * 128], vrb[j]))
            mmo = bb.mm_acc(pairs,
                            reads=tuple(f"WT{k}" for k in range(NDT))
                            + ("CTt", f"vrb{j}", f"xnck{j}"),
                            writes=(f"PS{3 + jo}",))
            bb.stt(ych[jo], ps2[jo][:, :], fb_c[:, do:do + 1], xres[jo],
                   op0=ALU.add, op1=ALU.add,
                   reads=(f"PS{3 + jo}", "fb_c", f"xres{jo}"),
                   writes=(f"ych{jo}",), deps=[mmo])
            bb.dma("scalar", yT[do * 128:(do + 1) * 128, sl], ych[jo],
                   reads=(f"ych{jo}",), writes=(f"yT_{do}_{ch}",))
    bb.p.barrier()
    AR.reset_front()

    if stop_after == 1:
        return bb
    L_f = AR.take([32, T], back=True)
    WB = AR.take([128, NDT * 32], BF16)
    for k in range(NDT):
        bb.dma("gpsimd", WB[:, k * 32:(k + 1) * 32],
               w["Wbeta"][k * 128:(k + 1) * 128, :], writes=("WB",))
    lgb = AR.take([32, T])
    xnck2 = [AR.take([128, NDT, TCH], BF16) for _ in range(2)]
    for ch in range(NCH):
        j = ch % 2
        sl = slice(ch * TCH, (ch + 1) * TCH)
        bb.dma("sync", xnck2[j], xnb_s[:, sl].rearrange("(i p) c -> p i c", p=128),
               reads=tuple(f"xnb_s_{i}_{sl.start}" for i in range(NDT)),
               writes=(f"xnck2{j}",))
        mmb = bb.mm_acc(
            [(bb.PS[5][0:32, :], WB[:, k * 32:(k + 1) * 32], xnck2[j][:, k, :])
             for k in range(NDT)],
            reads=("WB", f"xnck2{j}"), writes=("PS5",))
        sg = bb.act(lgb[:, sl], bb.PS[5][0:32, :], AF.Sigmoid, bias=bbeta_c,
                    reads=("PS5", "bbeta_c"), writes=("lgb",), deps=[mmb])
        cl = bb.ts(lgb[:, sl], lgb[:, sl], 0.999, 0.001, op0=ALU.min, op1=ALU.max,
                   reads=("lgb",), writes=("lgb",), deps=[sg])
        bb.act(lgb[:, sl], lgb[:, sl], AF.Ln, reads=("lgb",), writes=("lgb",),
               deps=[cl])
    ones32T = AR.take([32, T])
    bb.memset(ones32T, 1.0, writes=("ones32T",))
    bb.scan(L_f, ones32T, lgb, 0.0, reads=("ones32T", "lgb"), writes=("L_f",))
    if dbg:
        bb.dma("scalar", Lf_s, L_f, reads=("L_f",), writes=("Lf_s",))
    bb.p.barrier()
    AR.reset_front()

    if stop_after == 2:
        return bb
    NHC = 32
    WQ = AR.take([128, NDT * D], BF16)
    WK = AR.take([128, NDT * D], BF16)
    WV = AR.take([128, NDT * D], BF16)
    WO = AR.take([128, NDT * D], BF16)
    for k in range(NDT):
        bb.dma("gpsimd", WQ[:, k * D:(k + 1) * D],
               w["Wq"][k * 128:(k + 1) * 128, :], writes=(f"WQ{k}",))
        bb.dma("gpsimd", WK[:, k * D:(k + 1) * D],
               w["Wk"][k * 128:(k + 1) * 128, :], writes=(f"WK{k}",))
        bb.dma("gpsimd", WV[:, k * D:(k + 1) * D],
               w["Wv"][k * 128:(k + 1) * 128, :], writes=(f"WV{k}",))
        bb.dma("gpsimd", WO[:, k * D:(k + 1) * D],
               w["Wo"][k * 128:(k + 1) * 128, :], writes=(f"WO{k}",))
    stt_t = [[AR.take([128, 65], back=True) for _ in range(NDT)] for _ in range(CC)]
    sbf_t = [[AR.take([128, 65], BF16, back=True) for _ in range(NDT)] for _ in range(CC)]
    for c in range(CC):
        for tl in range(NDT):
            bb.memset(stt_t[c][tl], 0.0, writes=(f"st{c}_{tl}",))
            bb.memset(sbf_t[c][tl], 0.0, writes=(f"stb{c}_{tl}",))
    xnkc = [AR.take([128, NDT, 128], BF16) for _ in range(2)]
    xnfc = [AR.take([128, NDT, 128]) for _ in range(2)]
    q32 = AR.take([128, D])
    k32 = AR.take([128, D])
    rt1 = AR.take([128, D])
    rt2 = AR.take([128, D])
    qp = AR.take([128, D])
    kp = AR.take([128, D])
    qt = [AR.take([128, D]) for _ in range(CC)]
    kt = [AR.take([128, D], BF16) for _ in range(CC)]
    v_aug = AR.take([128, H * 65], BF16)
    qf = [AR.take([128, NDT * 128], BF16) for _ in range(CC)]
    kf = AR.take([128, NDT * 128], BF16)
    atbf = [AR.take([128, 128], BF16) for _ in range(2)]
    rope_c = AR.take([128, 32])
    rope_s = AR.take([128, 64])
    lrel = AR.take([32, 128])
    lmp = AR.take([32, 128])
    expP_t = AR.take([128, 32])
    expM_t = AR.take([128, 32])
    expMP_t = AR.take([128, 32])
    prow = AR.take([1, 32])
    pcol = AR.take([32, 64])
    pbc = AR.take([128, 32])
    dens = AR.take([128, 16])
    att_t = AR.take([128, D])
    attf = AR.take([128, NDT * 128], BF16)
    hck = AR.take([128, NDT, 128])
    yrmw = [AR.take([128, NDT, 128]) for _ in range(2)]
    lnsP = _ln_scratch(bb, AR, cw=128)
    ps_tr = [bb.PS[0], bb.PS[1]]
    ps_sim = [bb.PS[2][:, 0:128], bb.PS[7][:, 0:128]]
    ps_sim_b = ["PS2", "PS7"]
    psY = [bb.PS[3], bb.PS[4], bb.PS[5], bb.PS[6]]
    bb.memset(v_aug.rearrange("p (h e) -> p h e", e=65)[:, :, 64:65], 1.0,
              writes=("v_aug",))
    NC32 = T // 128
    for c in range(NC32):
        j = c % 2
        sl = slice(c * 128, (c + 1) * 128)
        bb.dma("sync", xnkc[j], xnb_s[:, sl].rearrange("(i p) t -> p i t", p=128),
               reads=tuple(f"xnb_s_{i}_{(c // 4) * TCH}" for i in range(NDT)),
               writes=(f"xnkc{j}",))
        bb.dma("sync", xnfc[j], xn_s[:, sl].rearrange("(i p) t -> p i t", p=128),
               reads=tuple(f"xn_s_{i}_{(c // 4) * TCH}" for i in range(NDT)),
               writes=(f"xnfc{j}",))
        bb.dma("scalar", rope_c, cosd[sl, :], writes=("rope_c",))
        bb.dma("scalar", rope_s, snpd[sl, :], writes=("rope_s",))
        for (dst, dbuf, Wt, wpre) in [(q32, "q32", WQ, "WQ"), (k32, "k32", WK, "WK")]:
            for n2 in range(2):
                mm = bb.mm_acc(
                    [(ps2[n2][:, :], xnkc[j][:, k, :],
                      Wt[:, k * D + n2 * 512:(k) * D + (n2 + 1) * 512])
                     for k in range(NDT)],
                    reads=(f"xnkc{j}",) + tuple(f"{wpre}{k}" for k in range(NDT)),
                    writes=(f"PS{3 + n2}",))
                bb.copy(dst[:, n2 * 512:(n2 + 1) * 512], ps2[n2][:, :],
                        reads=(f"PS{3 + n2}",), writes=(dbuf,), deps=[mm])
        vap = v_aug.rearrange("p (h e) -> p h e", e=65)
        for n2 in range(2):
            mm = bb.mm_acc(
                [(ps2[n2][:, :], xnkc[j][:, k, :],
                  WV[:, k * D + n2 * 512:(k) * D + (n2 + 1) * 512])
                 for k in range(NDT)],
                reads=(f"xnkc{j}",) + tuple(f"WV{k}" for k in range(NDT)),
                writes=(f"PS{3 + n2}",))
            bb.copy(vap[:, n2 * 8:(n2 + 1) * 8, 0:64],
                    ps2[n2][:, :].rearrange("p (h e) -> p h e", e=64),
                    reads=(f"PS{3 + n2}",), writes=("v_aug",), deps=[mm])
        for (src_, sbuf_, dphi, dpbuf) in [(q32, "q32", qp, "qp"),
                                           (k32, "k32", kp, "kp")]:
            capT = rope_c.unsqueeze(1).unsqueeze(1).broadcast_to((128, 16, 2, 32))
            spT = rope_s.unsqueeze(1).broadcast_to((128, 16, 64))
            sw = src_.rearrange("p (h two cc) -> p h two cc", two=2, cc=32)[:, :, ::-1, :]
            t1 = bb.tt(rt1.rearrange("p (h two cc) -> p h two cc", two=2, cc=32),
                       src_.rearrange("p (h two cc) -> p h two cc", two=2, cc=32),
                       capT, ALU.mult, reads=(sbuf_, "rope_c"), writes=("rt1",))
            t2 = bb.tt(rt2.rearrange("p (h e) -> p h e", e=64), sw, spT, ALU.mult,
                       reads=(sbuf_, "rope_s"), writes=("rt2",))
            t3 = bb.tt(rt1, rt1, rt2, ALU.add, reads=("rt1", "rt2"),
                       writes=("rt1",), deps=[t1, t2])
            t4 = bb.ts(rt2, rt1, 0.0, None, op0=ALU.min, reads=("rt1",),
                       writes=("rt2",), deps=[t3])
            t5 = bb.act(rt2, rt2, AF.Exp, reads=("rt2",), writes=("rt2",),
                        deps=[t4])
            bb.stt(dphi, rt1, 0.0, rt2, op0=ALU.max, op1=ALU.add,
                   reads=("rt1", "rt2"), writes=(dpbuf,), deps=[t5])
        if dbg:
            bb.dma("scalar", qp_s[sl, :], qp, reads=("qp",), writes=("qp_s",))
            bb.dma("scalar", kp_s[sl, :], kp, reads=("kp",), writes=("kp_s",))
        if c == 0:
            lst = bb.kcol(0.0, rows=32)
            lread = ("kbuf", "L_f")
        else:
            lst = L_f[:, c * 128 - 1:c * 128]
            lread = ("L_f",)
        l1 = bb.tt(lrel, L_f[:, sl], lst.broadcast_to((32, 128)), ALU.subtract,
                   reads=lread, writes=("lrel",))

        tr1 = bb.transpose(ps_tr[0][0:128, 0:32], lrel, bb.ident_f32[0:32, 0:32],
                           reads=("lrel", "ident_f32"), writes=("PS0",), deps=[l1])
        if dbg and c == 0:
            bb.dma("scalar", lr_s[:, 0:128], lrel, reads=("lrel",), writes=("lr_s",))
            cpt = bb.copy(expM_t, ps_tr[0][0:128, 0:32], reads=("PS0",),
                          writes=("expM_t",), deps=[tr1])
            bb.dma("scalar", tp_s, expM_t, reads=("expM_t",), writes=("tp_s",))
        e1 = bb.act(expP_t, ps_tr[0][0:128, 0:32], AF.Exp, reads=("PS0",),
                    writes=("expP_t",), deps=[tr1])
        e2 = bb.act(expM_t, ps_tr[0][0:128, 0:32], AF.Exp, scale=-1.0,
                    reads=("PS0",), writes=("expM_t",), deps=[tr1])

        ep = bb.act(pcol[:, 0:1], lrel[:, 127:128], AF.Exp, reads=("lrel",),
                    writes=("pcol",), deps=[l1])
        tr3 = bb.transpose(ps_tr[1][0:1, 32:64], pcol[:, 0:1],
                           bb.ident_f32[0:32, 0:32],
                           reads=("pcol", "ident_f32"), writes=("PS1",), deps=[ep])
        cp3 = bb.copy(prow, ps_tr[1][0:1, 32:64], reads=("PS1",), writes=("prow",),
                      deps=[tr3])
        mmp = bb.mm(ps_tr[1][:, 64:96], bb.ones_1x128, prow,
                    reads=("prow", "ones1x128"), writes=("PS1",), deps=[cp3])
        bb.copy(pbc, ps_tr[1][:, 64:96], reads=("PS1",), writes=("pbc",),
                deps=[mmp])
        bb.tt(expMP_t, expM_t, pbc, ALU.mult, reads=("expM_t", "pbc"),
              writes=("expMP_t",))
        for cc_ in range(CC):
            eP = expP_t[:, cc_:32:2].unsqueeze(2).broadcast_to((128, 16, 64))
            eMP = expMP_t[:, cc_:32:2].unsqueeze(2).broadcast_to((128, 16, 64))
            bb.tt(qt[cc_].rearrange("p (h e) -> p h e", e=64),
                  qp.rearrange("p (h e) -> p h e", e=64), eP, ALU.mult,
                  reads=("qp", "expP_t"), writes=(f"qt{cc_}",))
            bb.tt(kt[cc_].rearrange("p (h e) -> p h e", e=64),
                  kp.rearrange("p (h e) -> p h e", e=64), eMP, ALU.mult,
                  reads=("kp", "expMP_t"), writes=(f"kt{cc_}",))
        for blk in range(NDT):
            jj = blk % 2
            tr = bb.transpose(ps_tr[jj][:, 128:256], kp[:, blk * 128:(blk + 1) * 128],
                              bb.ident_f32,
                              reads=("kp", "ident_f32"), writes=(f"PS{jj}",))
            bb.copy(kf[:, blk * 128:(blk + 1) * 128], ps_tr[jj][:, 128:256],
                    reads=(f"PS{jj}",), writes=("kf",), deps=[tr])
            for cc_ in range(CC):
                tr = bb.transpose(ps_tr[jj][:, 128:256],
                                  qt[cc_][:, blk * 128:(blk + 1) * 128],
                                  bb.ident_f32,
                                  reads=(f"qt{cc_}", "ident_f32"),
                                  writes=(f"PS{jj}",))
                bb.copy(qf[cc_][:, blk * 128:(blk + 1) * 128],
                        ps_tr[jj][:, 128:256],
                        reads=(f"PS{jj}",), writes=(f"qf{cc_}",), deps=[tr])
        for cc_ in range(CC):
            for tl in range(NDT):
                bb.copy(sbf_t[cc_][tl], stt_t[cc_][tl], reads=(f"st{cc_}_{tl}",),
                        writes=(f"stb{cc_}_{tl}",))
        for h in range(H):
            blk = h // 2
            rb = (h % 2) * 64
            ytile = psY[h // 4]
            ycol = (h % 4) * 65
            ypsb = f"PS{3 + h // 4}"
            mms = []
            for cc_ in range(CC):
                sj = (2 * h + cc_) % 2
                mm1 = bb.mm(ps_sim[sj],
                            kf[rb:rb + 64, blk * 128:(blk + 1) * 128],
                            qf[cc_][rb:rb + 64, blk * 128:(blk + 1) * 128],
                            reads=("kf", f"qf{cc_}"), writes=(ps_sim_b[sj],))
                at = bb.stt(atbf[sj], ps_sim[sj], expM_t[:, 2 * h + cc_:2 * h + cc_ + 1],
                            mask_sb, op0=ALU.mult, op1=ALU.mult,
                            reads=(ps_sim_b[sj], "expM_t", "mask_tri"),
                            writes=(f"atbf{sj}",), deps=[mm1])
                pairs = [
                    (ytile[:, ycol:ycol + 65], atbf[sj], vap[:, h, :]),
                    (ytile[:, ycol:ycol + 65],
                     qf[cc_][rb:rb + 64, blk * 128:(blk + 1) * 128],
                     sbf_t[cc_][blk][rb:rb + 64, :]),
                ]
                mm2 = bb.mm_multi([pairs[0]], start=(cc_ == 0), stop=False,
                                  reads=(f"atbf{sj}", "v_aug"), writes=(ypsb,),
                                  deps=[at])
                mm3 = bb.mm_multi([pairs[1]], start=False, stop=(cc_ == CC - 1),
                                  reads=(f"qf{cc_}", f"stb{cc_}_{blk}"),
                                  writes=(ypsb,))
        if dbg and c == 0:
            bb.dma("scalar", ex_s[:, 0:32], expP_t, reads=("expP_t",), writes=("ex_s",))
            bb.dma("scalar", ex_s[:, 32:64], expM_t, reads=("expM_t",), writes=("ex_s",))
            bb.dma("scalar", ex_s[:, 64:96], expMP_t, reads=("expMP_t",), writes=("ex_s",))
            cpy = bb.copy(att_t[:, 0:260], psY[0][:, 0:260], reads=("PS3",),
                          writes=("att_t",), eng="vector")
            bb.dma("scalar", py_s, att_t[:, 0:260], reads=("att_t",), writes=("py_s",))
            bb.dma("scalar", ab_s[:, 0:128], atbf[0], reads=("atbf0",), writes=("ab_s",))
            bb.dma("scalar", ab_s[:, 128:256], atbf[1], reads=("atbf1",), writes=("ab_s",))
        for cc_ in range(CC):
            for tl in range(NDT):
                jj = (cc_ * NDT + tl) % 2
                h0, h1 = 2 * tl, 2 * tl + 1
                dm = bb.mm_multi(
                    [(ps_tr[jj][0:64, 256 + 0:256 + 65],
                      kt[cc_][:, h0 * 64:(h0 + 1) * 64], vap[:, h0, :]),
                     (ps_tr[jj][0:64, 256 + 65:256 + 130],
                      kt[cc_][:, h1 * 64:(h1 + 1) * 64], vap[:, h1, :])],
                    start=True, stop=True,
                    reads=(f"kt{cc_}", "v_aug"), writes=(f"PS{jj}",))
                u1 = bb.tt(stt_t[cc_][tl][0:64, :], stt_t[cc_][tl][0:64, :],
                           pbc[0:64, 2 * h0 + cc_:2 * h0 + cc_ + 1]
                           .broadcast_to((64, 65)), ALU.mult,
                           reads=(f"st{cc_}_{tl}", "pbc"),
                           writes=(f"st{cc_}_{tl}",))
                bb.tt(stt_t[cc_][tl][0:64, :], stt_t[cc_][tl][0:64, :],
                      ps_tr[jj][0:64, 256:256 + 65], ALU.add,
                      reads=(f"st{cc_}_{tl}", f"PS{jj}"),
                      writes=(f"st{cc_}_{tl}",), deps=[dm, u1])
                u2 = bb.tt(stt_t[cc_][tl][64:128, :], stt_t[cc_][tl][64:128, :],
                           pbc[64:128, 2 * h1 + cc_:2 * h1 + cc_ + 1]
                           .broadcast_to((64, 65)), ALU.mult,
                           reads=(f"st{cc_}_{tl}", "pbc"),
                           writes=(f"st{cc_}_{tl}",))
                bb.tt(stt_t[cc_][tl][64:128, :], stt_t[cc_][tl][64:128, :],
                      ps_tr[jj][0:64, 256 + 65:256 + 130], ALU.add,
                      reads=(f"st{cc_}_{tl}", f"PS{jj}"),
                      writes=(f"st{cc_}_{tl}",), deps=[dm, u2])
        for h in range(H):
            bk, ycol = h // 4, (h % 4) * 65
            bb.copy(dens[:, h:h + 1], psY[bk][:, ycol + 64:ycol + 65],
                    reads=(f"PS{3 + bk}",), writes=("dens",), eng="vector")
        r1 = bb.ts(dens, dens, 1e-6, None, op0=ALU.add, reads=("dens",),
                   writes=("dens",))
        r2 = bb.recip(dens, dens, reads=("dens",), writes=("dens",))
        for h in range(H):
            bk, ycol = h // 4, (h % 4) * 65
            bb.tt(att_t[:, h * 64:(h + 1) * 64], psY[bk][:, ycol:ycol + 64],
                  dens[:, h:h + 1].broadcast_to((128, 64)), ALU.mult,
                  reads=(f"PS{3 + bk}", "dens"), writes=("att_t",), deps=[r2])
        if dbg and c == 0:
            bb.dma("scalar", dn_s, dens, reads=("dens",), writes=("dn_s",))
            cp9 = bb.copy(hck[:, 0:2, :], psY[0][:, 0:256], reads=("PS3",),
                          writes=("hck",), eng="vector")
            bb.dma("scalar", tp_s[:, 0:32], hck[:, 0, 0:32], reads=("hck",), writes=("tp_s",))
            bb.dma("scalar", va_s, v_aug, reads=("v_aug",), writes=("va_s",))
            bb.dma("scalar", py2_s, hck[:, 0:2, :], reads=("hck",), writes=("py2_s",))
        if dbg:
            bb.dma("scalar", att_s[sl, :], att_t, reads=("att_t",), writes=("att_s",))
        for blk in range(NDT):
            jj = blk % 2
            tr = bb.transpose(ps_tr[jj][:, 128:256],
                              att_t[:, blk * 128:(blk + 1) * 128], bb.ident_f32,
                              reads=("att_t", "ident_f32"), writes=(f"PS{jj}",))
            bb.copy(attf[:, blk * 128:(blk + 1) * 128], ps_tr[jj][:, 128:256],
                    reads=(f"PS{jj}",), writes=("attf",), deps=[tr])
        for do in range(NDT):
            jj = do % 2
            mo = bb.mm_acc(
                [(ps2[jj][:, 0:128], WO[:, k * D + do * 128: k * D + (do + 1) * 128],
                  attf[:, k * 128:(k + 1) * 128]) for k in range(NDT)],
                reads=tuple(f"WO{k}" for k in range(NDT)) + ("attf",),
                writes=(f"PS{3 + jj}",))
            bb.stt(hck[:, do, :], ps2[jj][:, 0:128], bo_c[:, do:do + 1],
                   xnfc[j][:, do, :], op0=ALU.add, op1=ALU.add,
                   reads=(f"PS{3 + jj}", "bo_c", f"xnfc{j}"), writes=("hck",),
                   deps=[mo])
        bb.dma("sync", yrmw[j], yT[:, sl].rearrange("(i p) t -> p i t", p=128),
               reads=tuple(f"yT_{i}_{c // 4}" for i in range(NDT)),
               writes=(f"yrmw{j}",))

        def postA(i, ap, buf, j=j, sl=sl, c=c):
            a2 = bb.tt(ap, ap, yrmw[j][:, i, :], ALU.add,
                       reads=(buf, f"yrmw{j}"), writes=(buf,))
            bb.dma("scalar", yT[i * 128:(i + 1) * 128, sl], ap, reads=(buf,),
                   writes=(f"yT2_{i}_{c}",))

        outs = [{"sink": lambda i, j=j: (lnsP[f"ta{1 - (c % 2)}"][:, 0:128],
                                         f"lnP{1 - (c % 2)}_ta"),
                 "mode": "gb", "gamma": g_cols, "beta": b_cols,
                 "gbufs": ("lng_c", "lnb_c"), "post": postA}]
        ln_chunked(bb, hck, "hck", outs, "lnP", c, lnsP, cw=128)
    return bb


def _b0_wmap(inputs):
    P = "L0_attn_"
    return {
        "Wq": inputs[P + "Wq"], "Wk": inputs[P + "Wk"], "Wv": inputs[P + "Wv"],
        "Wbeta": inputs[P + "Wbeta"], "bbeta": inputs[P + "bbeta"],
        "Wo": inputs[P + "Wo"], "bo": inputs[P + "bo"],
        "ln_g": inputs[P + "ln_gamma"], "ln_b": inputs[P + "ln_beta"],
        "WT": inputs["L0_ffn_W"].T, "fb": inputs["L0_ffn_b"],
        "V_r": inputs["L0_ffn_V_r"], "CT": inputs["L0_ffn_C"].T,
    }


def _b0_inmap(core, inputs, bb):
    m = {"xT": np.ascontiguousarray(inputs["x_in"][core].T)}
    for k, v in _b0_wmap(inputs).items():
        m["w_" + k] = np.ascontiguousarray(v)
    for cname, arr in bb.consts.items():
        m[cname] = arr
    return m


def run_block0(x_in, inputs):
    bb = build_block0()
    nc = bb.finalize()
    in_maps = [_b0_inmap(c, {**inputs, "x_in": x_in}, bb) for c in range(NCORE)]
    res = run_bass_kernel_spmd(nc, in_maps, core_ids=list(range(NCORE)))
    out = np.stack([res.results[c]["yT"].T for c in range(NCORE)])
    return np.ascontiguousarray(out)



def build_block3(stop_after=None, dbg=False, bb=None, xT=None, yT=None):
    if bb is None:
        bb = BB("b3")
        xT = bb.dram_in("xT", [D, T])
        yT = bb.dram_out("yT", [D, T])

    w = {}
    for nm, shp in [("inT", [D, 2048]), ("conv_w", [D, 2]), ("conv_b", [D]),
                    ("xp_dtT", [D, 64]), ("xp_BT", [D, 16]), ("xp_CT", [D, 16]),
                    ("dtT", [64, D]), ("dt_b", [D]), ("A_log", [D, 16]),
                    ("Dp", [D]), ("opT", [D, D]), ("mn_g", [D]), ("mn_b", [D]),
                    ("n1_g", [D]), ("n1_b", [D]), ("n2_g", [D]), ("n2_b", [D]),
                    ("n3_g", [D]), ("n3_b", [D]), ("w1T", [D, 4096]),
                    ("w2T", [4096, D]), ("feT", [D, D]), ("fe_b", [D]),
                    ("WrT", [D, D]), ("WkT", [D, D]), ("WvT", [D, D]),
                    ("mu_r", [D]), ("mu_k", [D])]:
        w[nm] = bb.dram_in("w_" + nm, shp)
    NST = 16
    dram_dbg = bb.dram_out if dbg else bb.dram_tmp
    xn_s = dram_dbg("xn_s", [D, T])
    xnb_s = bb.dram_tmp("xnb_s", [D, T], BF16)
    min_s = dram_dbg("min_s", [D, T], BF16)
    xin_s = dram_dbg("xin_s", [D, T])
    z_s = dram_dbg("z_s", [D, T])
    xcb_s = dram_dbg("xcb_s", [D, T], BF16)
    xcf_s = dram_dbg("xcf_s", [D, T])
    src_s = dram_dbg("src_s", [D, T])
    ymam_s = dram_dbg("ymam_s", [D, T], BF16)

    setup_common(bb)
    AR = Arena(bb, 176 * 1024)
    cols = {}
    for nm in ["conv_b", "dt_b", "Dp", "mn_g", "mn_b", "n1_g", "n1_b", "n2_g",
               "n2_b", "n3_g", "n3_b", "fe_b", "mu_r", "mu_k"]:
        cols[nm] = vec_to_cols(bb, w[nm], D, nm)
    convw = bb.sbuf("convw", [128, NDT * 2], F32)
    bb.dma("sync", convw.rearrange("p (i c) -> p i c", c=2),
           w["conv_w"].rearrange("(i p) c -> p i c", p=128), writes=("convw",))
    Alog_sb = bb.sbuf("Alog_sb", [128, NDT * NST], F32)
    negA = bb.sbuf("negA", [128, NDT * NST], F32)
    lda = bb.dma("sync", Alog_sb.rearrange("p (i n) -> p i n", n=NST),
                 w["A_log"].rearrange("(i p) n -> p i n", p=128),
                 writes=("Alog_sb",))
    ea = bb.act(negA, Alog_sb, AF.Exp, reads=("Alog_sb",), writes=("negA",),
                deps=[lda])
    bb.ts(negA, negA, -1.0, None, reads=("negA",), writes=("negA",), deps=[ea])
    om_r = bb.sbuf("om_r", [128, NDT], F32)
    om_k = bb.sbuf("om_k", [128, NDT], F32)
    bb.ts(om_r, cols["mu_r"], -1.0, 1.0, op0=ALU.mult, op1=ALU.add,
          reads=("mu_r_c",), writes=("om_r",))
    bb.ts(om_k, cols["mu_k"], -1.0, 1.0, op0=ALU.mult, op1=ALU.add,
          reads=("mu_k_c",), writes=("om_k",))
    bb.p.barrier()

    xck = [AR.take([128, NDT, TCH]) for _ in range(2)]
    o_f32 = [AR.take([128, TCH]) for _ in range(2)]
    o_bf = [AR.take([128, TCH], BF16) for _ in range(2)]
    o_m = [AR.take([128, TCH], BF16) for _ in range(2)]
    lns = _ln_scratch(bb, AR)
    for ch in range(NCH):
        j = ch % 2
        sl = slice(ch * TCH, (ch + 1) * TCH)
        bb.dma("sync", xck[j], xT[:, sl].rearrange("(i p) c -> p i c", p=128),
               writes=(f"xck{j}",))

        def post(dram, dname, q, sl=sl):
            def f(i, ap, buf):
                bb.dma(q, dram[i * 128:(i + 1) * 128, sl], ap, reads=(buf,),
                       writes=(f"{dname}_{i}_{sl.start}",))
            return f

        def post2(i, ap, buf, sl=sl):
            bb.dma("scalar", xn_s[i * 128:(i + 1) * 128, sl], ap, reads=(buf,),
                   writes=(f"xn_s_{i}_{sl.start}",))
            bb.dma("gpsimd", xnb_s[i * 128:(i + 1) * 128, sl], ap, reads=(buf,),
                   writes=(f"xnb_s_{i}_{sl.start}",))

        outs = [
            {"sink": lambda i, j=j: (o_f32[j], f"of{j}"), "mode": "plain",
             "post": post2},
            {"sink": lambda i, j=j: (o_m[j], f"om{j}"), "mode": "gbf",
             "gamma": cols["n1_g"], "beta": cols["n1_b"],
             "gbufs": ("n1_g_c", "n1_b_c"), "post": post(min_s, "min_s", "sync")},
        ]
        ln_chunked(bb, xck[j], f"xck{j}", outs, "ln0", ch, lns)
    bb.p.barrier()
    AR.reset_front()

    if stop_after == 0:
        return bb
    WI = AR.take([128, NDT * 2048], BF16)
    for k in range(NDT):
        bb.dma("gpsimd", WI[:, k * 2048:(k + 1) * 2048],
               w["inT"][k * 128:(k + 1) * 128, :], writes=(f"WI{k}",))
    mck = [AR.take([128, NDT, TCH], BF16) for _ in range(2)]
    och = [AR.take([128, TCH]) for _ in range(2)]
    ps2 = [bb.PS[3], bb.PS[4]]
    for ch in range(NCH):
        j = ch % 2
        sl = slice(ch * TCH, (ch + 1) * TCH)
        bb.dma("sync", mck[j], min_s[:, sl].rearrange("(i p) c -> p i c", p=128),
               reads=tuple(f"min_s_{i}_{sl.start}" for i in range(NDT)),
               writes=(f"mck{j}",))
        for mt in range(16):
            jm = mt % 2
            mmt = bb.mm_acc(
                [(ps2[jm][:, :], WI[:, k * 2048 + mt * 128: k * 2048 + (mt + 1) * 128],
                  mck[j][:, k, :]) for k in range(NDT)],
                reads=tuple(f"WI{k}" for k in range(NDT)) + (f"mck{j}",),
                writes=(f"PS{3 + jm}",))
            tgt, tname = (xin_s, "xin_s") if mt < 8 else (z_s, "z_s")
            ti = mt if mt < 8 else mt - 8
            cpt = bb.copy(och[jm], ps2[jm][:, :], reads=(f"PS{3 + jm}",),
                          writes=(f"och{jm}",), deps=[mmt])
            bb.dma("scalar", tgt[ti * 128:(ti + 1) * 128, sl], och[jm],
                   reads=(f"och{jm}",), writes=(f"{tname}_{ti}_{sl.start}",))
    bb.p.barrier()
    AR.reset_front()

    if stop_after == 1:
        return bb
    cur = [AR.take([128, NDT, TCH]) for _ in range(2)]
    prv = [AR.take([128, NDT, TCH]) for _ in range(2)]
    xc32 = [AR.take([128, TCH]) for _ in range(2)]
    xcb = [AR.take([128, TCH], BF16) for _ in range(2)]
    for ch in range(NCH):
        j = ch % 2
        sl = slice(ch * TCH, (ch + 1) * TCH)
        bb.dma("sync", cur[j], xin_s[:, sl].rearrange("(i p) c -> p i c", p=128),
               reads=tuple(f"xin_s_{i}_{sl.start}" for i in range(NDT)),
               writes=(f"cur{j}",))
        if ch == 0:
            bb.memset(prv[j][:, :, 0:1], 0.0, writes=(f"prv{j}",))
            bb.dma("sync", prv[j][:, :, 1:TCH],
                   xin_s[:, 0:TCH - 1].rearrange("(i p) c -> p i c", p=128),
                   reads=tuple(f"xin_s_{i}_0" for i in range(NDT)),
                   writes=(f"prv{j}",))
        else:
            bb.dma("sync", prv[j],
                   xin_s[:, ch * TCH - 1:(ch + 1) * TCH - 1].rearrange(
                       "(i p) c -> p i c", p=128),
               reads=tuple(f"xin_s_{i}_{(ch - 1) * TCH}" for i in range(NDT))
               + tuple(f"xin_s_{i}_{sl.start}" for i in range(NDT)),
               writes=(f"prv{j}",))
        for i in range(NDT):
            jj = (ch * NDT + i) % 2
            t1 = bb.ts(xc32[jj], cur[j][:, i, :], convw[:, 2 * i + 1:2 * i + 2], None,
                       reads=(f"cur{j}", "convw"), writes=(f"xc32_{jj}",))
            t2 = bb.stt(xc32[jj], prv[j][:, i, :], convw[:, 2 * i:2 * i + 1], xc32[jj],
                        op0=ALU.mult, op1=ALU.add,
                        reads=(f"prv{j}", "convw", f"xc32_{jj}"),
                        writes=(f"xc32_{jj}",), deps=[t1])
            a1 = bb.act(xc32[jj], xc32[jj], AF.Silu, bias=cols["conv_b"][:, i:i + 1],
                        reads=(f"xc32_{jj}", "conv_b_c"), writes=(f"xc32_{jj}",),
                        deps=[t2])
            bb.dma("scalar", xcf_s[i * 128:(i + 1) * 128, sl], xc32[jj],
                   reads=(f"xc32_{jj}",), writes=(f"xcf_s_{i}_{sl.start}",))
            c1 = bb.copy(xcb[jj], xc32[jj], reads=(f"xc32_{jj}",),
                         writes=(f"xcb{jj}",), eng="vector", deps=[a1])
            bb.dma("gpsimd", xcb_s[i * 128:(i + 1) * 128, sl], xcb[jj],
                   reads=(f"xcb{jj}",), writes=(f"xcb_s_{i}_{sl.start}",))
    bb.p.barrier()
    AR.reset_front()

    if stop_after == 2:
        return bb
    dt_f = AR.take([64, T], BF16, back=True)
    B_f = AR.take([16, T])
    C_f = AR.take([16, T])
    Wdt = AR.take([128, NDT * 64], BF16)
    Wb_ = AR.take([128, NDT * 16], BF16)
    Wc_ = AR.take([128, NDT * 16], BF16)
    for k in range(NDT):
        bb.dma("gpsimd", Wdt[:, k * 64:(k + 1) * 64],
               w["xp_dtT"][k * 128:(k + 1) * 128, :], writes=("Wdt",))
        bb.dma("gpsimd", Wb_[:, k * 16:(k + 1) * 16],
               w["xp_BT"][k * 128:(k + 1) * 128, :], writes=("Wb_",))
        bb.dma("gpsimd", Wc_[:, k * 16:(k + 1) * 16],
               w["xp_CT"][k * 128:(k + 1) * 128, :], writes=("Wc_",))
    b_s = dram_dbg("b_s", [16, T])
    c_s = dram_dbg("c_s", [16, T])
    xcck = [AR.take([128, NDT, TCH], BF16) for _ in range(2)]
    for ch in range(NCH):
        j = ch % 2
        sl = slice(ch * TCH, (ch + 1) * TCH)
        bb.dma("sync", xcck[j], xcb_s[:, sl].rearrange("(i p) c -> p i c", p=128),
               reads=tuple(f"xcb_s_{i}_{sl.start}" for i in range(NDT)),
               writes=(f"xcck{j}",))
        for (lhs, lname, dst, dname, m) in [
                (Wdt, "Wdt", dt_f, "dt_f", 64), (Wb_, "Wb_", B_f, "B_f", 16),
                (Wc_, "Wc_", C_f, "C_f", 16)]:
            mmt = bb.mm_acc(
                [(bb.PS[3][0:m, :], lhs[:, k * m:(k + 1) * m], xcck[j][:, k, :])
                 for k in range(NDT)],
                reads=(lname, f"xcck{j}"), writes=("PS3",))
            bb.copy(dst[0:m, sl], bb.PS[3][0:m, :], reads=("PS3",),
                    writes=(dname,), deps=[mmt])
    bb.dma("scalar", b_s, B_f, reads=("B_f",), writes=("b_s",))
    bb.dma("scalar", c_s, C_f, reads=("C_f",), writes=("c_s",))
    bb.p.barrier()
    AR.reset_front()

    if stop_after == 3:
        return bb
    Wdtp = AR.take([64, D], BF16)
    bb.dma("gpsimd", Wdtp, w["dtT"], writes=("Wdtp",))
    delta = AR.take([128, T])
    u_ = AR.take([128, T])
    dA = AR.take([128, T])
    dBu = AR.take([128, T])
    hsc = AR.take([128, T])
    yac = AR.take([128, T])
    Bb = AR.take([128, T])
    Cb = AR.take([128, T])
    xcd = AR.take([128, T])
    zd = [AR.take([128, TCH]) for _ in range(2)]
    for i in range(NDT):
        for c2 in range(NCH):
            sl = slice(c2 * TCH, (c2 + 1) * TCH)
            mmt = bb.mm(bb.PS[3][:, :], Wdtp[:, i * 128:(i + 1) * 128], dt_f[:, sl],
                        reads=("Wdtp", "dt_f"), writes=("PS3",))
            if stop_after == 44:
                cp_t = bb.copy(delta[:, sl], bb.PS[3][:, :], reads=("PS3",),
                               writes=("delta",), deps=[mmt])
                continue
            a_t = bb.act(u_[:, sl], bb.PS[3][:, :], AF.Abs,
                         bias=cols["dt_b"][:, i:i + 1],
                         reads=("PS3", "dt_b_c"), writes=("u_",), deps=[mmt])
            e_t = bb.act(u_[:, sl], u_[:, sl], AF.Exp, scale=-1.0,
                         reads=("u_",), writes=("u_",), deps=[a_t])
            l_t = bb.act(u_[:, sl], u_[:, sl], AF.Ln, bias=bb.kcol(1.0),
                         reads=("u_", "kbuf"), writes=("u_",), deps=[e_t])
            if stop_after == 45:
                continue
            m_t = bb.ts(delta[:, sl], bb.PS[3][:, :], cols["dt_b"][:, i:i + 1], 0.0,
                        op0=ALU.add, op1=ALU.max,
                        reads=("PS3", "dt_b_c"), writes=("delta",), deps=[mmt])
            if stop_after != 46:
                bb.tt(delta[:, sl], delta[:, sl], u_[:, sl], ALU.add,
                      reads=("delta", "u_"), writes=("delta",), deps=[l_t, m_t])
        bb.dma("sync", xcd, xcf_s[i * 128:(i + 1) * 128, :],
               reads=tuple(f"xcf_s_{i}_{c2 * TCH}" for c2 in range(NCH)),
               writes=("xcd",))
        bb.tt(u_, delta, xcd, ALU.mult, reads=("delta", "xcd"), writes=("u_",))
        if stop_after in (41, 44, 45, 46):
            continue
        for n in range(NST if stop_after != 42 else 1):
            da_t = bb.act(dA, delta, AF.Exp, scale=negA[:, i * NST + n:i * NST + n + 1],
                          reads=("delta", "negA"), writes=("dA",))
            bb.dma("sync", Bb, b_s[n:n + 1, :].partition_broadcast(128).squeeze(1),
                   reads=("b_s",), writes=("Bb",))
            bb.dma("scalar", Cb, c_s[n:n + 1, :].partition_broadcast(128).squeeze(1),
                   reads=("c_s",), writes=("Cb",))
            db_t = bb.tt(dBu, u_, Bb, ALU.mult, reads=("u_", "Bb"), writes=("dBu",))
            sc_t = bb.scan(hsc, dA, dBu, 0.0, reads=("dA", "dBu"), writes=("hsc",),
                           deps=[da_t, db_t])
            if n == 0:
                bb.tt(yac, hsc, Cb, ALU.mult, reads=("hsc", "Cb"), writes=("yac",),
                      deps=[sc_t])
            else:
                hc_t = bb.tt(hsc, hsc, Cb, ALU.mult, reads=("hsc", "Cb"),
                             writes=("hsc",), deps=[sc_t])
                bb.tt(yac, yac, hsc, ALU.add, reads=("yac", "hsc"),
                      writes=("yac",), deps=[hc_t])
        if stop_after in (41, 42, 43, 44, 45, 46):
            continue
        y1 = bb.stt(yac, xcd, cols["Dp"][:, i:i + 1], yac, op0=ALU.mult,
                    op1=ALU.add, reads=("xcd", "Dp_c", "yac"), writes=("yac",))
        for c2 in range(NCH):
            sl = slice(c2 * TCH, (c2 + 1) * TCH)
            jz = c2 % 2
            zl = bb.dma("sync", zd[jz], z_s[i * 128:(i + 1) * 128, sl],
                        reads=(f"z_s_{i}_{sl.start}",), writes=(f"zd{jz}",))
            s_t = bb.act(zd[jz], zd[jz], AF.Silu, reads=(f"zd{jz}",),
                         writes=(f"zd{jz}",), deps=[zl])
            y2 = bb.tt(zd[jz], yac[:, sl], zd[jz], ALU.mult,
                       reads=("yac", f"zd{jz}"), writes=(f"zd{jz}",),
                       deps=[s_t, y1])
            bb.dma("gpsimd", ymam_s[i * 128:(i + 1) * 128, sl], zd[jz],
                   reads=(f"zd{jz}",), writes=(f"ymam_{i}_{c2}",), deps=[y2])
    if stop_after in (41, 42, 43, 44, 45, 46):
        return bb
    bb.p.barrier()
    AR.reset_front()
    AR.reset_back()

    if stop_after == 4:
        return bb
    WO = AR.take([128, NDT * D], BF16)
    for k in range(NDT):
        bb.dma("gpsimd", WO[:, k * D:(k + 1) * D],
               w["opT"][k * 128:(k + 1) * 128, :], writes=(f"WO{k}",))
    ymck = [AR.take([128, NDT, TCH], BF16) for _ in range(2)]
    h2ck = AR.take([128, NDT, TCH])
    xnck5 = [AR.take([128, NDT, TCH]), None]
    xnck5[1] = xnck5[0]
    osrc = [AR.take([128, TCH]) for _ in range(2)]
    lns5 = {k + jj: AR.take([128, 6 * TCH] if k == "rw" else [128, TCH])
            for k in ["sq", "rw", "xm", "ta"] for jj in ["0", "1"]}
    for ch in range(NCH):
        j = ch % 2
        sl = slice(ch * TCH, (ch + 1) * TCH)
        bb.dma("sync", ymck[j], ymam_s[:, sl].rearrange("(i p) c -> p i c", p=128),
               reads=tuple(f"ymam_{i}_{ch}" for i in range(NDT)),
               writes=(f"ymck{j}",))
        bb.dma("sync", xnck5[j], xn_s[:, sl].rearrange("(i p) c -> p i c", p=128),
               reads=tuple(f"xn_s_{i}_{sl.start}" for i in range(NDT)),
               writes=("xnck50",))
        for do in range(NDT):
            jm = do % 2
            mmt = bb.mm_acc(
                [(ps2[jm][:, :], WO[:, k * D + do * 128: k * D + (do + 1) * 128],
                  ymck[j][:, k, :]) for k in range(NDT)],
                reads=tuple(f"WO{k}" for k in range(NDT)) + (f"ymck{j}",),
                writes=(f"PS{3 + jm}",))
            bb.copy(h2ck[:, do, :], ps2[jm][:, :], reads=(f"PS{3 + jm}",),
                    writes=("h2ck",), deps=[mmt])

        def post5(i, ap, buf, sl=sl, j=j):
            a2 = bb.tt(ap, ap, xnck5[j][:, i, :], ALU.add,
                       reads=(buf, "xnck50"), writes=(buf,))
            bb.dma("scalar", src_s[i * 128:(i + 1) * 128, sl], ap, reads=(buf,),
                   writes=(f"src_{i}_{sl.start}",))

        outs = [{"sink": lambda i, j=j: (osrc[j], f"osrc{j}"), "mode": "gb",
                 "gamma": cols["mn_g"], "beta": cols["mn_b"],
                 "gbufs": ("mn_g_c", "mn_b_c"), "post": post5}]
        ln_chunked(bb, h2ck, "h2ck", outs, "ln5", ch, lns5)
    bb.p.barrier()
    AR.reset_front()

    if stop_after == 5:
        return bb
    W1 = AR.take([128, NDT * 4096], BF16)
    for k in range(NDT):
        bb.dma("gpsimd", W1[:, k * 4096:(k + 1) * 4096],
               w["w1T"][k * 128:(k + 1) * 128, :], writes=(f"W1_{k}",))
    sck = [AR.take([128, NDT, TCH]), None]
    sck[1] = sck[0]
    snb = [AR.take([128, NDT, TCH], BF16), None]
    snb[1] = snb[0]
    fh = AR.take([128, 32 * TCH], BF16)
    f32t = [AR.take([128, TCH]) for _ in range(2)]
    w2cb = [AR.take([128, 32, 128], BF16), None]
    w2cb[1] = w2cb[0]
    lns6 = _ln_scratch(bb, AR)
    osn = [AR.take([128, TCH], BF16) for _ in range(2)]
    for ch in range(NCH):
        j = ch % 2
        sl = slice(ch * TCH, (ch + 1) * TCH)
        bb.dma("sync", sck[j], src_s[:, sl].rearrange("(i p) c -> p i c", p=128),
               reads=tuple(f"src_{i}_{sl.start}" for i in range(NDT)),
               writes=("sck0",))
        outs = [{"sink": lambda i, j=j: (snb[j][:, i, :], "snb0"), "mode": "gb",
                 "gamma": cols["n2_g"], "beta": cols["n2_b"],
                 "gbufs": ("n2_g_c", "n2_b_c")}]
        ln_chunked(bb, sck[j], "sck0", outs, "ln6", ch, lns6)
        for m in range(32):
            jm = m % 2
            mmt = bb.mm_acc(
                [(ps2[jm][:, :], W1[:, k * 4096 + m * 128: k * 4096 + (m + 1) * 128],
                  snb[j][:, k, :]) for k in range(NDT)],
                reads=tuple(f"W1_{k}" for k in range(NDT)) + ("snb0",),
                writes=(f"PS{3 + jm}",))
            cpt = bb.copy(f32t[jm], ps2[jm][:, :], reads=(f"PS{3 + jm}",),
                          writes=(f"f32t{jm}",), deps=[mmt])
            bb.stt(fh[:, m * TCH:(m + 1) * TCH], f32t[jm], 0.01, f32t[jm],
                   op0=ALU.mult, op1=ALU.max, reads=(f"f32t{jm}",),
                   writes=(f"fh{m}",), deps=[cpt])
        for do in range(NDT):
            jo = do % 2
            bb.dma("gpsimd", w2cb[jo],
                   w["w2T"][:, do * 128:(do + 1) * 128].rearrange(
                       "(k p) c -> p k c", p=128),
                   writes=("w2cb0",))
            mmo = bb.mm_acc(
                [(bb.PS[5 + jo][:, :], w2cb[jo][:, k, :],
                  fh[:, k * TCH:(k + 1) * TCH]) for k in range(32)],
                reads=("w2cb0",) + tuple(f"fh{k}" for k in range(32)),
                writes=(f"PS{5 + jo}",))
            st = bb.tt(f32t[jo], bb.PS[5 + jo][:, :], sck[j][:, do, :], ALU.add,
                       reads=(f"PS{5 + jo}", "sck0"), writes=(f"f32t{jo}",),
                       deps=[mmo])
            bb.dma("scalar", src_s[do * 128:(do + 1) * 128, sl], f32t[jo],
                   reads=(f"f32t{jo}",), writes=(f"src_{do}_{sl.start}",))
    bb.p.barrier()
    AR.reset_front()

    if stop_after == 6:
        return bb
    srcfin_s = dram_dbg("srcfin_s", [D, T])
    FE = AR.take([128, NDT * D], BF16)
    for k in range(NDT):
        bb.dma("gpsimd", FE[:, k * D:(k + 1) * D],
               w["feT"][k * 128:(k + 1) * 128, :], writes=(f"FE{k}",))
    sck7 = AR.take([128, NDT, TCH])
    sn7 = AR.take([128, NDT, TCH], BF16)
    tmp7 = [AR.take([128, TCH]) for _ in range(2)]
    lns7 = _ln_scratch(bb, AR)
    for ch in range(NCH):
        sl = slice(ch * TCH, (ch + 1) * TCH)
        bb.dma("sync", sck7, src_s[:, sl].rearrange("(i p) c -> p i c", p=128),
               reads=tuple(f"src_{i}_{sl.start}" for i in range(NDT)),
               writes=("sck7",))
        outs = [{"sink": lambda i: (sn7[:, i, :], "sn7"), "mode": "gb",
                 "gamma": cols["n3_g"], "beta": cols["n3_b"],
                 "gbufs": ("n3_g_c", "n3_b_c")}]
        ln_chunked(bb, sck7, "sck7", outs, "ln7", ch, lns7)
        for do in range(NDT):
            jm = do % 2
            mmf = bb.mm_acc(
                [(ps2[jm][:, :], FE[:, k * D + do * 128: k * D + (do + 1) * 128],
                  sn7[:, k, :]) for k in range(NDT)],
                reads=tuple(f"FE{k}" for k in range(NDT)) + ("sn7",),
                writes=(f"PS{3 + jm}",))
            s1 = bb.stt(tmp7[jm], ps2[jm][:, :], cols["fe_b"][:, do:do + 1],
                        sck7[:, do, :], op0=ALU.add, op1=ALU.add,
                        reads=(f"PS{3 + jm}", "fe_b_c", "sck7"),
                        writes=(f"tmp7{jm}",), deps=[mmf])
            bb.dma("scalar", srcfin_s[do * 128:(do + 1) * 128, sl], tmp7[jm],
                   reads=(f"tmp7{jm}",), writes=(f"srcfin_{do}_{ch}",))
    bb.p.barrier()
    AR.reset_front()

    if stop_after == 7:
        return bb
    WR = AR.take([128, NDT * D], BF16)
    WK = AR.take([128, NDT * D], BF16)
    WV = AR.take([128, NDT * D], BF16)
    for k in range(NDT):
        bb.dma("gpsimd", WR[:, k * D:(k + 1) * D],
               w["WrT"][k * 128:(k + 1) * 128, :], writes=(f"WR{k}",))
        bb.dma("gpsimd", WK[:, k * D:(k + 1) * D],
               w["WkT"][k * 128:(k + 1) * 128, :], writes=(f"WK{k}",))
        bb.dma("gpsimd", WV[:, k * D:(k + 1) * D],
               w["WvT"][k * 128:(k + 1) * 128, :], writes=(f"WV{k}",))
    xnc = AR.take([128, NDT, TCH])
    xnp = AR.take([128, NDT, TCH])
    xrb = AR.take([128, NDT, TCH], BF16)
    xkb = AR.take([128, NDT, TCH], BF16)
    k2b = AR.take([128, NDT, TCH], BF16)
    sfc = AR.take([128, NDT, TCH])
    rsg = [AR.take([128, TCH]) for _ in range(2)]
    tmp8 = [AR.take([128, TCH]) for _ in range(2)]
    xres7 = [AR.take([128, TCH]) for _ in range(2)]
    for ch in range(NCH):
        sl = slice(ch * TCH, (ch + 1) * TCH)
        bb.dma("sync", xnc, xn_s[:, sl].rearrange("(i p) c -> p i c", p=128),
               reads=tuple(f"xn_s_{i}_{sl.start}" for i in range(NDT)),
               writes=("xnc",))
        bb.dma("sync", sfc, srcfin_s[:, sl].rearrange("(i p) c -> p i c", p=128),
               reads=tuple(f"srcfin_{i}_{ch}" for i in range(NDT)),
               writes=("sfc",))
        if ch == 0:
            bb.memset(xnp[:, :, 0:1], 0.0, writes=("xnp",))
            bb.dma("sync", xnp[:, :, 1:TCH],
                   xn_s[:, 0:TCH - 1].rearrange("(i p) c -> p i c", p=128),
                   reads=tuple(f"xn_s_{i}_0" for i in range(NDT)),
                   writes=("xnp",))
        else:
            bb.dma("sync", xnp,
                   xn_s[:, ch * TCH - 1:(ch + 1) * TCH - 1].rearrange(
                       "(i p) c -> p i c", p=128),
                   reads=tuple(f"xn_s_{i}_{(ch - 1) * TCH}" for i in range(NDT))
                   + tuple(f"xn_s_{i}_{sl.start}" for i in range(NDT)),
                   writes=("xnp",))
        for i in range(NDT):
            jj = i % 2
            t1 = bb.ts(tmp8[jj], xnp[:, i, :], om_r[:, i:i + 1], None,
                       reads=("xnp", "om_r"), writes=(f"tmp8{jj}",))
            bb.stt(xrb[:, i, :], xnc[:, i, :], cols["mu_r"][:, i:i + 1],
                   tmp8[jj], op0=ALU.mult, op1=ALU.add,
                   reads=("xnc", "mu_r_c", f"tmp8{jj}"), writes=("xrb",),
                   deps=[t1])
            t2 = bb.ts(tmp8[jj], xnp[:, i, :], om_k[:, i:i + 1], None,
                       reads=("xnp", "om_k"), writes=(f"tmp8{jj}",))
            bb.stt(xkb[:, i, :], xnc[:, i, :], cols["mu_k"][:, i:i + 1],
                   tmp8[jj], op0=ALU.mult, op1=ALU.add,
                   reads=("xnc", "mu_k_c", f"tmp8{jj}"), writes=("xkb",),
                   deps=[t2])
        for do in range(NDT):
            jm = do % 2
            mmk = bb.mm_acc(
                [(ps2[jm][:, :], WK[:, k * D + do * 128: k * D + (do + 1) * 128],
                  xkb[:, k, :]) for k in range(NDT)],
                reads=tuple(f"WK{k}" for k in range(NDT)) + ("xkb",),
                writes=(f"PS{3 + jm}",))
            rl = bb.ts(tmp8[jm], ps2[jm][:, :], 0.0, None, op0=ALU.max,
                       reads=(f"PS{3 + jm}",), writes=(f"tmp8{jm}",), deps=[mmk])
            bb.act(k2b[:, do, :], tmp8[jm], AF.Square, reads=(f"tmp8{jm}",),
                   writes=("k2b",), deps=[rl])
        for do in range(NDT):
            jm = do % 2
            bb.dma("sync", xres7[jm], xT[do * 128:(do + 1) * 128, sl],
                   writes=(f"xres7{jm}",))
            mmr = bb.mm_acc(
                [(ps2[jm][:, :], WR[:, k * D + do * 128: k * D + (do + 1) * 128],
                  xrb[:, k, :]) for k in range(NDT)],
                reads=tuple(f"WR{k}" for k in range(NDT)) + ("xrb",),
                writes=(f"PS{3 + jm}",))
            sg = bb.act(rsg[jm], ps2[jm][:, :], AF.Sigmoid,
                        reads=(f"PS{3 + jm}",), writes=(f"rsg{jm}",), deps=[mmr])
            mmv = bb.mm_acc(
                [(bb.PS[5 + jm][:, :], WV[:, k * D + do * 128: k * D + (do + 1) * 128],
                  k2b[:, k, :]) for k in range(NDT)],
                reads=tuple(f"WV{k}" for k in range(NDT)) + ("k2b",),
                writes=(f"PS{5 + jm}",))
            s3 = bb.tt(tmp8[jm], rsg[jm], bb.PS[5 + jm][:, :], ALU.mult,
                       reads=(f"rsg{jm}", f"PS{5 + jm}"), writes=(f"tmp8{jm}",),
                       deps=[mmv, sg])
            s4 = bb.tt(tmp8[jm], tmp8[jm], sfc[:, do, :], ALU.add,
                       reads=(f"tmp8{jm}", "sfc"), writes=(f"tmp8{jm}",),
                       deps=[s3])
            s5 = bb.tt(tmp8[jm], tmp8[jm], xres7[jm], ALU.add,
                       reads=(f"tmp8{jm}", f"xres7{jm}"), writes=(f"tmp8{jm}",),
                       deps=[s4])
            bb.dma("scalar", yT[do * 128:(do + 1) * 128, sl], tmp8[jm],
                   reads=(f"tmp8{jm}",), writes=(f"yT_{do}_{ch}",))
    return bb


def _b3_wmap(inputs):
    P = "L3_attn_"
    xp = inputs[P + "x_proj_w"]
    return {
        "inT": inputs[P + "in_proj_w"].T, "conv_w": inputs[P + "conv_w"],
        "conv_b": inputs[P + "conv_b"], "xp_dtT": xp[0:64].T,
        "xp_BT": xp[64:80].T, "xp_CT": xp[80:96].T,
        "dtT": inputs[P + "dt_proj_w"].T, "dt_b": inputs[P + "dt_proj_b"],
        "A_log": inputs[P + "A_log"], "Dp": inputs[P + "Dp"],
        "opT": inputs[P + "out_proj_w"].T, "mn_g": inputs[P + "mnorm_g"],
        "mn_b": inputs[P + "mnorm_b"], "n1_g": inputs[P + "n1_g"],
        "n1_b": inputs[P + "n1_b"], "n2_g": inputs[P + "n2_g"],
        "n2_b": inputs[P + "n2_b"], "n3_g": inputs[P + "n3_g"],
        "n3_b": inputs[P + "n3_b"], "w1T": inputs[P + "ffn_w1"].T,
        "w2T": inputs[P + "ffn_w2"].T, "feT": inputs[P + "fe_w"].T,
        "fe_b": inputs[P + "fe_b"], "WrT": inputs["L3_ffn_W_r"].T,
        "WkT": inputs["L3_ffn_W_k"].T, "WvT": inputs["L3_ffn_W_v"].T,
        "mu_r": inputs["L3_ffn_mu_r"], "mu_k": inputs["L3_ffn_mu_k"],
    }


def _b3_inmap(core, inputs, bb):
    m = {"xT": np.ascontiguousarray(inputs["x_in"][core].T)}
    for k, v in _b3_wmap(inputs).items():
        m["w_" + k] = np.ascontiguousarray(v)
    for cname, arr in bb.consts.items():
        m[cname] = arr
    return m


def run_block3(x_in, inputs):
    bb = build_block3()
    nc = bb.finalize()
    in_maps = [_b3_inmap(c, {**inputs, "x_in": x_in}, bb) for c in range(NCORE)]
    res = run_bass_kernel_spmd(nc, in_maps, core_ids=list(range(NCORE)))
    out = np.stack([res.results[c]["yT"].T for c in range(NCORE)])
    return np.ascontiguousarray(out)




def _l2_rope_tables():
    DH, W = 64, 256
    half = np.arange(0, DH, 2, dtype=np.float64)
    inv_freq = 1.0 / 10000.0 ** (half / DH)
    t = np.arange(T, dtype=np.float64)
    freqs = np.outer(t, inv_freq)
    base = (half + 0.4 * DH) / (1.4 * DH)
    power = (t - T // 2) / (W // 2)
    scale = base[None, :] ** power[:, None]
    cos, sin = np.cos(freqs), np.sin(freqs)
    CSQ = np.concatenate([cos * scale, cos * scale], 1) * 8.0
    SSQ = np.concatenate([-sin * scale, sin * scale], 1) * 8.0
    CIK = np.concatenate([cos / scale, cos / scale], 1)
    SIK = np.concatenate([-sin / scale, sin / scale], 1)
    return (CSQ.astype(np.float32), SSQ.astype(np.float32),
            CIK.astype(np.float32), SIK.astype(np.float32))


def _l2_masks():
    import ml_dtypes
    p = np.arange(128)[:, None]
    f = np.arange(256)[None, :]
    NEG = np.float32(-1e30)
    cur0 = np.where(f > p, NEG, 0.0).astype(np.float32)
    prev0 = np.where(f < p, NEG, 0.0).astype(np.float32)
    prev1 = np.where(f < 128 + p, NEG, 0.0).astype(np.float32)
    cur1 = np.where(f > 128 + p, NEG, 0.0).astype(np.float32)
    return {"m_cur0": cur0, "m_prev0": prev0, "m_prev1": prev1, "m_cur1": cur1}


def build_block2(stop_after=None, dbg=False, bb=None, xT=None, yT=None):
    if bb is None:
        bb = BB("b2")
        xT = bb.dram_in("xT", [D, T])
        yT = bb.dram_out("yT", [D, T])
    HH, DH, WIN = 16, 64, 256
    DI = 2730
    NHB = 22
    w = {}
    for nm, shp in [("log_a", [D]), ("WbT", [D, D]), ("WcT", [D, D]),
                    ("gam", [D]), ("bet", [D]),
                    ("Wqkv", [6, D, 3072]), ("qs", [6, DH]), ("ks", [6, DH]),
                    ("Wo", [6, 1024, D]), ("ln_g", [6, D]), ("ln_b", [6, D]),
                    ("W1", [6, D, 2 * DI]), ("b1", [6, 2 * DI]),
                    ("W2", [6, DI, D]), ("b2", [6, D])]:
        w[nm] = bb.dram_in("w_" + nm, shp)
    dram_dbg = bb.dram_out if dbg else bb.dram_tmp
    xn_s = dram_dbg("xn_s", [D, T])
    xnb_s = dram_dbg("xnb_s", [D, T], BF16)
    xl_s = [dram_dbg(f"xl{i}_s", [D, T]) for i in range(2)]
    xlb_s = [dram_dbg(f"xlb{i}_s", [D, T], BF16) for i in range(2)]

    setup_common(bb, ps7_bf16=True)
    AR = Arena(bb, 176 * 1024)
    CSQ, SSQ, CIK, SIK = _l2_rope_tables()
    ropes = {nm: bb.const("c_" + nm, v) for nm, v in
             [("CSQ", CSQ), ("SSQ", SSQ), ("CIK", CIK), ("SIK", SIK)]}
    import ml_dtypes
    masks = {}
    for nm, v in _l2_masks().items():
        ap = bb.sbuf(nm, [128, 256], BF16)
        bb.dma("gpsimd", ap, bb.const("c_" + nm, v.astype(ml_dtypes.bfloat16)),
               writes=(nm,))
        masks[nm] = ap
    gamc = vec_to_cols(bb, w["gam"], D, "gam")
    betc = vec_to_cols(bb, w["bet"], D, "bet")
    la_c = vec_to_cols(bb, w["log_a"], D, "la")
    a_c = bb.sbuf("a_c", [128, NDT], F32)
    bb.act(a_c, la_c, AF.Exp, reads=("la_c",), writes=("a_c",))
    bb.p.barrier()

    xck = [AR.take([128, NDT, TCH]) for _ in range(2)]
    o_f32 = [AR.take([128, TCH]) for _ in range(2)]
    lns = _ln_scratch(bb, AR)
    for ch in range(NCH):
        j = ch % 2
        sl = slice(ch * TCH, (ch + 1) * TCH)
        bb.dma("sync", xck[j], xT[:, sl].rearrange("(i p) c -> p i c", p=128),
               writes=(f"xck{j}",))

        def post2(i, ap, buf, sl=sl):
            bb.dma("scalar", xn_s[i * 128:(i + 1) * 128, sl], ap, reads=(buf,),
                   writes=(f"xn_s_{i}_{sl.start}",))
            bb.dma("gpsimd", xnb_s[i * 128:(i + 1) * 128, sl], ap, reads=(buf,),
                   writes=(f"xnb_s_{i}_{sl.start}",))
            bb.dma("sync", xl_s[0][i * 128:(i + 1) * 128, sl], ap, reads=(buf,),
                   writes=("XL0",))
            bb.dma("gpsimd", xlb_s[0][i * 128:(i + 1) * 128, sl], ap, reads=(buf,),
                   writes=("XL0B",))

        outs = [{"sink": lambda i, j=j: (o_f32[j], f"of{j}"), "mode": "plain",
                 "post": post2}]
        ln_chunked(bb, xck[j], f"xck{j}", outs, "ln0", ch, lns)
    bb.p.barrier()
    AR.reset_front()
    if stop_after == 0:
        return bb

    ub_s = dram_dbg("ub_s", [D, T])
    WB = AR.take([128, NDT * D], BF16)
    WC = AR.take([128, NDT * D], BF16)
    for k in range(NDT):
        bb.dma("gpsimd", WB[:, k * D:(k + 1) * D],
               w["WbT"][k * 128:(k + 1) * 128, :], writes=(f"WB{k}",))
        bb.dma("gpsimd", WC[:, k * D:(k + 1) * D],
               w["WcT"][k * 128:(k + 1) * 128, :], writes=(f"WC{k}",))
    xnck = [AR.take([128, NDT, TCH], BF16) for _ in range(2)]
    ubc = [AR.take([128, TCH]) for _ in range(2)]
    ps2 = [bb.PS[3], bb.PS[4]]
    for ch in range(NCH):
        j = ch % 2
        sl = slice(ch * TCH, (ch + 1) * TCH)
        bb.dma("sync", xnck[j], xnb_s[:, sl].rearrange("(i p) c -> p i c", p=128),
               reads=tuple(f"xnb_s_{i}_{sl.start}" for i in range(NDT)),
               writes=(f"xnck{j}",))
        for i in range(NDT):
            jm = i % 2
            mmu = bb.mm_acc(
                [(ps2[jm][:, :], WB[:, k * D + i * 128: k * D + (i + 1) * 128],
                  xnck[j][:, k, :]) for k in range(NDT)],
                reads=tuple(f"WB{k}" for k in range(NDT)) + (f"xnck{j}",),
                writes=(f"PS{3 + jm}",))
            cpt = bb.copy(ubc[jm], ps2[jm][:, :], reads=(f"PS{3 + jm}",),
                          writes=(f"ubc{jm}",), deps=[mmu])
            bb.dma("scalar", ub_s[i * 128:(i + 1) * 128, sl], ubc[jm],
                   reads=(f"ubc{jm}",), writes=(f"ub_{i}_{ch}",))
    stb_s = dram_dbg("stb_s", [D, T], BF16)
    ub = AR.take([128, T])
    stbuf = AR.take([128, T])
    for i in range(NDT):
        bb.dma("sync", ub, ub_s[i * 128:(i + 1) * 128, :],
               reads=tuple(f"ub_{i}_{c2}" for c2 in range(NCH)), writes=("ub",))
        sc = bb.scan(stbuf, a_c[:, i:i + 1].broadcast_to((128, T)), ub, 0.0,
                     reads=("a_c", "ub"), writes=("stbuf",))
        bb.dma("gpsimd", stb_s[i * 128:(i + 1) * 128, :], stbuf,
               reads=("stbuf",), writes=(f"stb{i}",))
    yck = AR.take([128, NDT, TCH])
    xres = [AR.take([128, TCH]) for _ in range(2)]
    oln = [AR.take([128, TCH]) for _ in range(2)]
    stc = [AR.take([128, NDT, TCH], BF16) for _ in range(2)]
    lns1 = _ln_scratch(bb, AR)
    for ch in range(NCH):
        j = ch % 2
        sl = slice(ch * TCH, (ch + 1) * TCH)
        bb.dma("sync", stc[j], stb_s[:, sl].rearrange("(i p) c -> p i c", p=128),
               reads=tuple(f"stb{k}" for k in range(NDT)), writes=(f"stc{j}",))
        for do in range(NDT):
            jm = do % 2
            mmy = bb.mm_acc(
                [(ps2[jm][:, :], WC[:, k * D + do * 128: k * D + (do + 1) * 128],
                  stc[j][:, k, :]) for k in range(NDT)],
                reads=tuple(f"WC{k}" for k in range(NDT)) + (f"stc{j}",),
                writes=(f"PS{3 + jm}",))
            bb.copy(yck[:, do, :], ps2[jm][:, :], reads=(f"PS{3 + jm}",),
                    writes=("yck",), deps=[mmy])

        def postY(i, ap, buf, sl=sl, j=j):
            bb.dma("sync", xres[j], xT[i * 128:(i + 1) * 128, sl],
                   writes=(f"xres{j}",))
            bb.tt(ap, ap, xres[j], ALU.add, reads=(buf, f"xres{j}"), writes=(buf,))
            bb.dma("scalar", yT[i * 128:(i + 1) * 128, sl], ap, reads=(buf,),
                   writes=(f"yT_{i}_{ch}",))

        outs = [{"sink": lambda i, j=j: (oln[j], f"oln{j}"), "mode": "gb",
                 "gamma": gamc, "beta": betc, "gbufs": ("gam_c", "bet_c"),
                 "post": postY}]
        ln_chunked(bb, yck, "yck", outs, "ln1", ch, lns1)
    bb.p.barrier()
    AR.reset_front()
    if stop_after == 1:
        return bb

    PSBF = bb.PSBF
    for sub in range(6):
        src_x, src_xb = xl_s[0], xlb_s[0]
        dst_x, dst_xb = xl_s[1], xlb_s[1]
        sp = f"s{sub}_"
        WQKV = AR.take([128, NDT * 3072], BF16)
        WOs = AR.take([128, NDT * D], BF16)
        for k in range(NDT):
            bb.dma("gpsimd", WQKV[:, k * 3072:(k + 1) * 3072],
                   w["Wqkv"][sub, k * 128:(k + 1) * 128, :], writes=(sp + f"WQ{k}",))
            bb.dma("gpsimd", WOs[:, k * D:(k + 1) * D],
                   w["Wo"][sub, k * 128:(k + 1) * 128, :], writes=(sp + f"WO{k}",))
        qsr = AR.take([1, 2 * DH])
        bb.dma("sync", qsr[:, 0:DH], w["qs"][sub].unsqueeze(0), writes=("qsr",))
        bb.dma("sync", qsr[:, DH:2 * DH], w["ks"][sub].unsqueeze(0), writes=("qsr",))
        qs_b = AR.take([128, 2 * DH])
        mmq = bb.mm(bb.PS[0][:, 0:128], bb.ones_1x128, qsr,
                    reads=("qsr", "ones1x128"), writes=("PS0",))
        bb.copy(qs_b, bb.PS[0][:, 0:128], reads=("PS0",), writes=("qs_b",),
                deps=[mmq])
        xsb = [AR.take([128, NDT, WIN], BF16) for _ in range(2)]
        qkvt2 = [AR.take([128, 2, 3072], BF16) for _ in range(2)]
        ropet2 = [AR.take([128, 2, 4 * DH]) for _ in range(2)]
        sq2 = [AR.take([128, D])] * 2
        ssq2 = [AR.take([128, 2 * HH]) for _ in range(2)]
        qn2 = [AR.take([128, D]) for _ in range(2)]
        r12 = [AR.take([128, D]) for _ in range(2)]
        r22 = [AR.take([128, D])] * 2
        qkf = [AR.take([128, 16 * 128], BF16) for _ in range(2)]
        qf_ = AR.take([128, 16 * 128], BF16)
        vtk = [AR.take([128, 2, 1024], BF16) for _ in range(2)]
        at_sb = AR.take([128, 512], BF16)
        dencol = AR.take([128, 8])
        attf = AR.take([128, NDT, WIN], BF16)
        xlc = [AR.take([128, NDT, WIN])] * 2
        och = [AR.take([128, WIN]) for _ in range(2)]
        ps_sim = [bb.PS[2], bb.PS[6]]
        ps_out = [bb.PS[5], bb.PS[5]]
        for wi in range(16):
            cur = wi % 2
            qkvt, ropet, sq_, ssq = qkvt2[cur], ropet2[cur], sq2[cur], ssq2[cur]
            qn_, r1_, r2_ = qn2[cur], r12[cur], r22[cur]
            W_ = f"w{cur}_"
            wsl = slice(wi * WIN, (wi + 1) * WIN)
            ld0 = bb.dma("sync", xsb[cur][:, 0:4, :],
                         src_xb[0:512, wsl].rearrange("(i p) t -> p i t", p=128),
                         reads=("XL0B",), writes=(sp + "xsb",))
            if wi == 0:
                bb.memset(xsb[cur][:, 4:8, 0:1], 0.0, writes=(sp + "xsb",))
                bb.dma("sync", xsb[cur][:, 4:8, 1:WIN],
                       src_xb[512:1024, 0:WIN - 1].rearrange(
                           "(i p) t -> p i t", p=128),
                       reads=("XL0B",), writes=(sp + "xsb",))
            else:
                bb.dma("sync", xsb[cur][:, 4:8, :],
                       src_xb[512:1024, wi * WIN - 1:(wi + 1) * WIN - 1].rearrange(
                           "(i p) t -> p i t", p=128),
                       reads=("XL0B",), writes=(sp + "xsb",))
            for tsb in range(2):
                tt0 = wi * WIN + tsb * 128
                bb.dma("scalar", ropet[:, tsb, 0:DH], ropes["CSQ"][tt0:tt0 + 128, :],
                       writes=(sp + W_ + "ropet",))
                bb.dma("scalar", ropet[:, tsb, DH:2 * DH],
                       ropes["SSQ"][tt0:tt0 + 128, :], writes=(sp + W_ + "ropet",))
                bb.dma("scalar", ropet[:, tsb, 2 * DH:3 * DH],
                       ropes["CIK"][tt0:tt0 + 128, :], writes=(sp + W_ + "ropet",))
                bb.dma("scalar", ropet[:, tsb, 3 * DH:4 * DH],
                       ropes["SIK"][tt0:tt0 + 128, :], writes=(sp + W_ + "ropet",))
            for tsb in range(2):
                for n2 in range(6):
                    jm = n2 % 2
                    mm = bb.mm_acc(
                        [(ps2[jm][:, :], xsb[cur][:, k, tsb * 128:(tsb + 1) * 128],
                          WQKV[:, k * 3072 + n2 * 512: k * 3072 + (n2 + 1) * 512])
                         for k in range(NDT)],
                        reads=(sp + "xsb",) + tuple(sp + f"WQ{k}" for k in range(NDT)),
                        writes=(f"PS{3 + jm}",))
                    bb.copy(qkvt[:, tsb, n2 * 512:(n2 + 1) * 512], ps2[jm][:, :],
                            reads=(f"PS{3 + jm}",), writes=(sp + W_ + "qkvt",), deps=[mm])
                bb.copy(vtk[cur][:, tsb, :], qkvt[:, tsb, 2048:3072],
                        reads=(sp + W_ + "qkvt",), writes=(sp + f"v{cur}",), eng="vector")
            for (qoff, tblc, tbls, scoff, dstf, dstbuf) in [
                    (0, 0, DH, 0, qf_, sp + "qf"),
                    (1024, 2 * DH, 3 * DH, DH, qkf[cur], sp + f"kf{cur}")]:
                for tsb in range(2):
                    qsl_ = qkvt[:, tsb, qoff:qoff + 1024]
                    s1 = bb.act(sq_, qsl_, AF.Square, reads=(sp + W_ + "qkvt",),
                                writes=(sp + "sqS",))
                    s2 = bb.reduce(ssq[:, 0:HH],
                                   sq_.rearrange("p (h e) -> p h e", e=DH),
                                   ALU.add, reads=(sp + "sqS",),
                                   writes=(sp + W_ + "ssq",), deps=[s1])
                    s3 = bb.act(ssq[:, HH:2 * HH], ssq[:, 0:HH], AF.Sqrt,
                                reads=(sp + W_ + "ssq",), writes=(sp + W_ + "ssq",), deps=[s2])
                    s4 = bb.ts(ssq[:, HH:2 * HH], ssq[:, HH:2 * HH], 1e-12, None,
                               op0=ALU.max, reads=(sp + W_ + "ssq",),
                               writes=(sp + W_ + "ssq",), deps=[s3])
                    s5 = bb.recip(ssq[:, HH:2 * HH], ssq[:, HH:2 * HH],
                                  reads=(sp + W_ + "ssq",), writes=(sp + W_ + "ssq",))
                    inv = ssq[:, HH:2 * HH].unsqueeze(2).broadcast_to((128, HH, DH))
                    n1 = bb.tt(qn_.rearrange("p (h e) -> p h e", e=DH),
                               qsl_.rearrange("p (h e) -> p h e", e=DH), inv,
                               ALU.mult, reads=(sp + W_ + "qkvt", sp + W_ + "ssq"),
                               writes=(sp + W_ + "qn",), deps=[s5])
                    qsb = qs_b[:, scoff:scoff + DH].unsqueeze(1).broadcast_to(
                        (128, HH, DH))
                    n2_ = bb.tt(qn_.rearrange("p (h e) -> p h e", e=DH),
                                qn_.rearrange("p (h e) -> p h e", e=DH), qsb,
                                ALU.mult, reads=(sp + W_ + "qn", "qs_b"),
                                writes=(sp + W_ + "qn",), deps=[n1])
                    ct = ropet[:, tsb, tblc:tblc + DH].unsqueeze(1).broadcast_to(
                        (128, HH, DH))
                    st_ = ropet[:, tsb, tbls:tbls + DH].unsqueeze(1).broadcast_to(
                        (128, HH, DH))
                    t1 = bb.tt(r1_.rearrange("p (h e) -> p h e", e=DH),
                               qn_.rearrange("p (h e) -> p h e", e=DH), ct,
                               ALU.mult, reads=(sp + W_ + "qn", sp + W_ + "ropet"),
                               writes=(sp + W_ + "r1",), deps=[n2_])
                    swp = qn_.rearrange("p (h two c) -> p h two c", two=2,
                                        c=DH // 2)[:, :, ::-1, :]
                    t2 = bb.tt(r2_.rearrange("p (h two c) -> p h two c", two=2,
                                             c=DH // 2),
                               swp, st_.rearrange("p h (two c) -> p h two c",
                                                  two=2),
                               ALU.mult, reads=(sp + W_ + "qn", sp + W_ + "ropet"),
                               writes=(sp + "r2S",), deps=[n2_])
                    t3 = bb.tt(r1_, r1_, r2_, ALU.add,
                               reads=(sp + W_ + "r1", sp + "r2S"), writes=(sp + W_ + "r1",),
                               deps=[t1, t2])
                    for bk in range(NDT):
                        jj = bk % 2
                        tr = bb.transpose(
                            bb.PS[jj][:, 128:256],
                            r1_[:, bk * 128:(bk + 1) * 128], bb.ident_f32,
                            reads=(sp + W_ + "r1", "ident_f32"), writes=(f"PS{jj}",))
                        bb.copy(dstf[:, (bk * 2 + tsb) * 128:(bk * 2 + tsb + 1) * 128],
                                bb.PS[jj][:, 128:256],
                                reads=(f"PS{jj}",), writes=(dstbuf,), deps=[tr])
            for h in range(HH):
                bk, rb = h // 2, (h % 2) * 64
                for tsb in range(2):
                    sj = (h * 2 + tsb) % 2
                    simb = "PS2" if sj == 0 else "PS6"
                    pairs = []
                    if wi > 0:
                        pairs.append((ps_sim[sj][:, 0:256],
                                      qf_[rb:rb + 64, (bk * 2 + tsb) * 128:
                                          (bk * 2 + tsb + 1) * 128],
                                      qkf[1 - cur][rb:rb + 64, bk * 256:(bk + 1) * 256]))
                        pairs.append((ps_sim[sj][:, 0:256], bb.ident_bf,
                                      masks["m_prev0" if tsb == 0 else "m_prev1"]))
                    pairs.append((ps_sim[sj][:, 256:512],
                                  qf_[rb:rb + 64, (bk * 2 + tsb) * 128:
                                      (bk * 2 + tsb + 1) * 128],
                                  qkf[cur][rb:rb + 64, bk * 256:(bk + 1) * 256]))
                    pairs.append((ps_sim[sj][:, 256:512], bb.ident_bf,
                                  masks["m_cur0" if tsb == 0 else "m_cur1"]))
                    def emit_sim(e, pairs=pairs):
                        ins = None
                        offs = [o.offset for (o, _, _) in pairs]
                        for i_, (o, l, r) in enumerate(pairs):
                            first = offs.index(o.offset) == i_
                            last = (len(offs) - 1 - offs[::-1].index(o.offset)) == i_
                            ins = e.matmul(o, l, r, start=first, stop=last)
                        return ins
                    rd = (sp + "qf", sp + f"kf{cur}", "ident_bf",
                          "m_cur0", "m_cur1", "m_prev0", "m_prev1")
                    if wi > 0:
                        rd = rd + (sp + f"kf{1 - cur}",)
                    mms = bb.p.task("tensor", emit_sim, reads=rd, writes=(simb,))
                    lo = 256 if wi == 0 else 0
                    mx = bb.reduce(dencol[:, sj:sj + 1], ps_sim[sj][:, lo:512],
                                   ALU.max, reads=(simb,),
                                   writes=(sp + "dencol",), negate=True,
                                   deps=[mms])
                    ex = bb.act(at_sb[:, lo:512], ps_sim[sj][:, lo:512], AF.Exp,
                                bias=dencol[:, sj:sj + 1],
                                accum_out=dencol[:, 2 + sj:3 + sj],
                                reads=(simb, sp + "dencol"),
                                writes=(sp + "at_sb",), deps=[mx])
                    rc = bb.recip(dencol[:, 2 + sj:3 + sj],
                                  dencol[:, 2 + sj:3 + sj],
                                  reads=(sp + "dencol",), writes=(sp + "dencol",))
                    nr = bb.ts(at_sb[:, lo:512], at_sb[:, lo:512],
                               dencol[:, 2 + sj:3 + sj], None,
                               reads=(sp + "at_sb", sp + "dencol"),
                               writes=(sp + "at_sb",), deps=[ex, rc])
                    nsb = (512 - lo) // 128
                    trs = []
                    for sb_ in range(nsb):
                        tr = bb.transpose(
                            PSBF[:, sj * 512 + sb_ * 128: sj * 512 + (sb_ + 1) * 128],
                            at_sb[:, lo + sb_ * 128: lo + (sb_ + 1) * 128],
                            bb.ident_bf, reads=(sp + "at_sb", "ident_bf"),
                            writes=("PS7",), deps=[nr])
                        trs.append(tr)
                    ev = bb.copy(at_sb[:, lo:512].bitcast(BF16)[:, 0:(512 - lo)]
                                 if False else at_sb[:, lo:512],
                                 PSBF[:, sj * 512: sj * 512 + (512 - lo)],
                                 reads=("PS7",), writes=(sp + "at_sb",),
                                 deps=trs, eng="vector")
                    avp = []
                    for sb_ in range(nsb):
                        vsrc = (vtk[1 - cur] if (wi > 0 and lo + sb_ * 128 < 256)
                                else vtk[cur])
                        vts = ((lo + sb_ * 128) % 256) // 128
                        avp.append((ps_out[sj][0:64, 0:128],
                                    vsrc[:, vts, h * 64:(h + 1) * 64],
                                    at_sb[:, lo + sb_ * 128:lo + (sb_ + 1) * 128]))
                    va_rd = (sp + "at_sb", sp + f"v{cur}")
                    if wi > 0:
                        va_rd = va_rd + (sp + f"v{1 - cur}",)
                    av = bb.mm_acc(avp, reads=va_rd, writes=("PS5",))
                    bb.copy(attf[rb:rb + 64, bk, tsb * 128:(tsb + 1) * 128],
                            ps_out[sj][0:64, 0:128], reads=("PS5",),
                            writes=(sp + "attf",), deps=[av])
            bb.dma("sync", xlc[cur], src_x[:, wsl].rearrange("(i p) t -> p i t", p=128),
                   reads=("XL0",), writes=(sp + "xlcS",))
            for do in range(NDT):
                jm = do % 2
                mo = bb.mm_acc(
                    [(ps2[jm][:, 0:WIN], WOs[:, k * D + do * 128: k * D + (do + 1) * 128],
                      attf[:, k, :]) for k in range(NDT)],
                    reads=tuple(sp + f"WO{k}" for k in range(NDT)) + (sp + "attf",),
                    writes=(f"PS{3 + jm}",))
                ad = bb.tt(och[jm], ps2[jm][:, 0:WIN], xlc[cur][:, do, :], ALU.add,
                           reads=(f"PS{3 + jm}", sp + "xlcS"), writes=(sp + f"och{jm}",),
                           deps=[mo])
                bb.dma("scalar", dst_x[do * 128:(do + 1) * 128, wsl], och[jm],
                       reads=(sp + f"och{jm}",), writes=("XL1",))
                bb.dma("gpsimd", dst_xb[do * 128:(do + 1) * 128, wsl], och[jm],
                       reads=(sp + f"och{jm}",), writes=("XL1B",))
        bb.p.barrier()
        AR.reset_front()
        if stop_after == 100 + sub:
            return bb

        f_x, f_xb = dst_x, dst_xb
        g_x, g_xb = src_x, src_xb
        W2s = AR.take([128, NHB * D], BF16)
        for k in range(NHB):
            hr = min(128, DI - k * 128)
            bb.dma("gpsimd", W2s[0:hr, k * D:(k + 1) * D],
                   w["W2"][sub, k * 128:k * 128 + hr, :], writes=(sp + f"W2_{k}",))
        b1row = AR.take([1, 2 * DI], BF16)
        bb.dma("gpsimd", b1row, w["b1"][sub].unsqueeze(0), writes=(sp + "b1row",))
        b2c = vec_to_cols(bb, w["b2"][sub], D, sp + "b2")
        gcl = vec_to_cols(bb, w["ln_g"][sub], D, sp + "gc")
        bcl = vec_to_cols(bb, w["ln_b"][sub], D, sp + "bc")
        w1cb = [AR.take([128, NDT, 128], BF16) for _ in range(4)]
        w1gb = [AR.take([128, NDT, 128], BF16) for _ in range(4)]
        xsf = [AR.take([128, NDT, TCH])] * 2
        xnsb = AR.take([128, NDT, TCH], BF16)
        AG = AR.take([128, NHB * TCH], BF16)
        gg = AR.take([128, TCH])
        lnsF = _ln_scratch(bb, AR)
        xrs = [AR.take([128, TCH]) for _ in range(2)]
        ochf = [AR.take([128, TCH]) for _ in range(2)]
        for ch in range(NCH):
            j = ch % 2
            sl = slice(ch * TCH, (ch + 1) * TCH)
            ldx = bb.dma("sync", xsf[j][:, 0:4, :],
                         f_x[0:512, sl].rearrange("(i p) t -> p i t", p=128),
                         reads=("XL1",), writes=(sp + "xsf0",))
            if ch == 0:
                bb.memset(xsf[j][:, 4:8, 0:1], 0.0, writes=(sp + "xsf0",))
                bb.dma("sync", xsf[j][:, 4:8, 1:TCH],
                       f_x[512:1024, 0:TCH - 1].rearrange("(i p) t -> p i t", p=128),
                       reads=("XL1",), writes=(sp + "xsf0",))
            else:
                bb.dma("sync", xsf[j][:, 4:8, :],
                       f_x[512:1024, ch * TCH - 1:(ch + 1) * TCH - 1].rearrange(
                           "(i p) t -> p i t", p=128),
                       reads=("XL1",), writes=(sp + "xsf0",))
            outs = [{"sink": lambda i, j=j: (xnsb[:, i, :], sp + "xnsb"),
                     "mode": "gb", "gamma": gcl, "beta": bcl,
                     "gbufs": (sp + "gc_c", sp + "bc_c")}]
            ln_chunked(bb, xsf[j], sp + "xsf0", outs, sp + "lnF", ch, lnsF)
            for m in range(NHB):
                hr = min(128, DI - m * 128)
                jm = m % 4
                bb.dma("gpsimd", w1cb[jm][:, :, 0:hr],
                       w["W1"][sub][:, m * 128:m * 128 + hr].rearrange(
                           "(k p) c -> p k c", p=128),
                       writes=(sp + f"w1a{jm}",))
                psa = bb.PS[3] if jm % 2 == 0 else bb.PS[5]
                psaname = "PS3" if jm % 2 == 0 else "PS5"
                pa = [(psa[0:hr, :], w1cb[jm][:, k, 0:hr], xnsb[:, k, :])
                      for k in range(NDT)]
                pa.append((psa[0:hr, :], b1row[:, m * 128:m * 128 + hr],
                           bb.ones_1x512_bf))
                ma = bb.mm_acc(pa, reads=(sp + f"w1a{jm}", sp + "xnsb",
                                          "ones1x512bf", sp + "b1row"),
                               writes=(psaname,))
                w1g = w1gb[jm]
                bb.dma("gpsimd", w1g[:, :, 0:hr],
                       w["W1"][sub][:, DI + m * 128:DI + m * 128 + hr].rearrange(
                           "(k p) c -> p k c", p=128),
                       writes=(sp + f"w1g{jm}",))
                psg = bb.PS[4] if jm % 2 == 0 else bb.PS[6]
                psgname = "PS4" if jm % 2 == 0 else "PS6"
                pg = [(psg[0:hr, :], w1g[:, k, 0:hr], xnsb[:, k, :])
                      for k in range(NDT)]
                pg.append((psg[0:hr, :], b1row[:, DI + m * 128:DI + m * 128 + hr],
                           bb.ones_1x512_bf))
                mg = bb.mm_acc(pg, reads=(sp + f"w1g{jm}", sp + "xnsb",
                                          "ones1x512bf", sp + "b1row"),
                               writes=(psgname,))
                ge = bb.act(gg[0:hr, :], psg[0:hr, :], AF.Gelu,
                            reads=(psgname,), writes=(sp + "gg",), deps=[mg])
                bb.tt(AG[0:hr, m * TCH:(m + 1) * TCH], psa[0:hr, :],
                      gg[0:hr, :], ALU.mult,
                      reads=(psaname, sp + "gg"), writes=(sp + f"AG{m}",),
                      deps=[ma, ge])
            for do in range(NDT):
                jm = do % 2
                psw = bb.PS[2] if jm == 0 else bb.PS[7]
                pswname = "PS2" if jm == 0 else "PS7"
                m2 = bb.mm_acc(
                    [(psw[:, :], W2s[0:min(128, DI - k * 128),
                                     k * D + do * 128: k * D + (do + 1) * 128],
                      AG[0:min(128, DI - k * 128), k * TCH:(k + 1) * TCH])
                     for k in range(NHB)],
                    reads=tuple(sp + f"W2_{k}" for k in range(NHB))
                    + tuple(sp + f"AG{k}" for k in range(NHB)),
                    writes=(pswname,))
                bb.dma("sync", xrs[jm], f_x[do * 128:(do + 1) * 128, sl],
                       reads=("XL1",), writes=(sp + f"xrs{jm}",))
                s1 = bb.stt(ochf[jm], psw[:, :], b2c[:, do:do + 1],
                            xrs[jm], op0=ALU.add, op1=ALU.add,
                            reads=(pswname, sp + "b2_c", sp + f"xrs{jm}"),
                            writes=(sp + f"ochf{jm}",), deps=[m2])
                bb.dma("scalar", g_x[do * 128:(do + 1) * 128, sl], ochf[jm],
                       reads=(sp + f"ochf{jm}",), writes=("XL0",))
                bb.dma("gpsimd", g_xb[do * 128:(do + 1) * 128, sl], ochf[jm],
                       reads=(sp + f"ochf{jm}",), writes=("XL0B",))
        bb.p.barrier()
        AR.reset_front()
        if stop_after == 200 + sub:
            return bb

    fin = xl_s[0]
    fck = [AR.take([128, NDT, TCH]) for _ in range(2)]
    yck2 = [AR.take([128, NDT, TCH]) for _ in range(2)]
    for ch in range(NCH):
        j = ch % 2
        sl = slice(ch * TCH, (ch + 1) * TCH)
        bb.dma("sync", fck[j], fin[:, sl].rearrange("(i p) t -> p i t", p=128),
               reads=("XL0",), writes=(f"fck{j}",))
        bb.dma("sync", yck2[j], yT[:, sl].rearrange("(i p) t -> p i t", p=128),
               reads=tuple(f"yT_{i}_{ch}" for i in range(NDT)),
               writes=(f"yck2{j}",))
        for i in range(NDT):
            ad = bb.tt(yck2[j][:, i, :], yck2[j][:, i, :], fck[j][:, i, :],
                       ALU.add, reads=(f"yck2{j}", f"fck{j}"),
                       writes=(f"yck2{j}",))
            bb.dma("scalar", yT[i * 128:(i + 1) * 128, sl], yck2[j][:, i, :],
                   reads=(f"yck2{j}",), writes=(f"yTf_{i}_{ch}",))
    return bb


def _b2_wmap(inputs):
    return {
        "log_a": inputs["L2_attn_log_a"], "WbT": inputs["L2_attn_W_b"].T,
        "WcT": inputs["L2_attn_W_c"].T, "gam": inputs["L2_attn_gamma"],
        "bet": inputs["L2_attn_beta"], "Wqkv": inputs["L2_ffn_Wqkv"],
        "qs": inputs["L2_ffn_q_scale"], "ks": inputs["L2_ffn_k_scale"],
        "Wo": inputs["L2_ffn_Wo"], "ln_g": inputs["L2_ffn_ln_g"],
        "ln_b": inputs["L2_ffn_ln_b"], "W1": inputs["L2_ffn_W1"],
        "b1": inputs["L2_ffn_b1"], "W2": inputs["L2_ffn_W2"],
        "b2": inputs["L2_ffn_b2"],
    }


def _b2_inmap(core, inputs, bb):
    m = {"xT": np.ascontiguousarray(inputs["x_in"][core].T)}
    for k, v in _b2_wmap(inputs).items():
        m["w_" + k] = np.ascontiguousarray(v)
    for cname, arr in bb.consts.items():
        m[cname] = arr
    return m


def run_block2(x_in, inputs, stop_after=None):
    bb = build_block2(stop_after=stop_after)
    nc = bb.finalize()
    in_maps = [_b2_inmap(c, {**inputs, "x_in": x_in}, bb) for c in range(NCORE)]
    res = run_bass_kernel_spmd(nc, in_maps, core_ids=list(range(NCORE)))
    out = np.stack([res.results[c]["yT"].T for c in range(NCORE)])
    return np.ascontiguousarray(out)



def _b1_wmap(inputs):
    names = {"ln_g": "L1_attn_ln_g", "ln_b": "L1_attn_ln_b", "Wh": "L1_attn_Wh",
             "bh": "L1_attn_bh", "Wqk": "L1_attn_Wqk", "bqk": "L1_attn_bqk",
             "gam": "L1_attn_gamma", "bet": "L1_attn_beta", "Wo": "L1_attn_Wo",
             "bo": "L1_attn_bo", "W1": "L1_ffn_W1", "b1": "L1_ffn_b1",
             "W2": "L1_ffn_W2", "b2": "L1_ffn_b2"}
    return {k: inputs[s] for k, s in names.items()}


def _b1_inmap(core, inputs, bb):
    m = {"xT": np.ascontiguousarray(inputs["x_in"][core].T)}
    for k, v in _b1_wmap(inputs).items():
        m["w_" + k] = np.ascontiguousarray(v)
    for cname, arr in bb.consts.items():
        m[cname] = arr
    return m


def run_block1(x_in, inputs):
    bb = build_block1()
    nc = bb.finalize()
    in_maps = [_b1_inmap(c, {**inputs, "x_in": x_in}, bb) for c in range(NCORE)]
    res = run_bass_kernel_spmd(nc, in_maps, core_ids=list(range(NCORE)))
    out = np.stack([res.results[c]["yT"].T for c in range(NCORE)])
    return np.ascontiguousarray(out)


def build_fused():
    bb = BB("fused")
    x0T = bb.dram_in("xT", [D, T])
    x1T = bb.dram_tmp("x1T", [D, T])
    x2T = bb.dram_tmp("x2T", [D, T])
    x3T = bb.dram_tmp("x3T", [D, T])
    yT = bb.dram_out("yT", [D, T])
    setup_common(bb)
    for pfx, fn, xi, yo in [("B0_", build_block0, x0T, x1T),
                            ("B1_", build_block1, x1T, x2T),
                            ("B2_", build_block2, x2T, x3T),
                            ("B3_", build_block3, x3T, yT)]:
        bb._pfx = pfx
        fn(bb=bb, xT=xi, yT=yo)
        bb.p.barrier()
        bb.sm_reset()
    bb._pfx = ""
    return bb


def _fused_inmap(core, inputs, bb):
    m = {"xT": np.ascontiguousarray(inputs["x_in"][core].T)}
    for pfx, wf in [("B0_", _b0_wmap), ("B1_", _b1_wmap), ("B2_", _b2_wmap),
                    ("B3_", _b3_wmap)]:
        for k, v in wf(inputs).items():
            m[pfx + "w_" + k] = np.ascontiguousarray(v)
    for cname, arr in bb.consts.items():
        m[cname] = arr
    return m


_FUSED_CACHE = {}


def kernel(**inputs):
    x = np.ascontiguousarray(inputs["L0_attn_x"], dtype=np.float32)
    if "bb" not in _FUSED_CACHE:
        bbf = build_fused()
        ncf = bbf.finalize()
        _FUSED_CACHE["bb"] = bbf
        _FUSED_CACHE["nc"] = ncf
    bbf, ncf = _FUSED_CACHE["bb"], _FUSED_CACHE["nc"]
    in_maps = [_fused_inmap(c, {**inputs, "x_in": x}, bbf) for c in range(NCORE)]
    res = run_bass_kernel_spmd(ncf, in_maps, core_ids=list(range(NCORE)))
    out = np.stack([res.results[c]["yT"].T for c in range(NCORE)])
    return np.ascontiguousarray(out)
